# version 06
# baseline (speedup 1.0000x reference)
import numpy as np
import concourse.bass as bass
import concourse.mybir as mybir
from concourse.bass_utils import run_bass_kernel_spmd

F32 = mybir.dt.float32
BF16 = mybir.dt.bfloat16
I32 = mybir.dt.int32
I8 = mybir.dt.int8
F32R = mybir.dt.float32r
ALU = mybir.AluOpType
ACT = mybir.ActivationFunctionType
AX = mybir.AxisListType

B, T, C = 4, 2048, 2048
NCORES = 8
TOWN = 1024
NT = T // 128
H, RR, DH = 8, 2, 128
HQ = H * RR
FF = 8192
INTER, NST, DTR, KCONV = 4096, 16, 128, 4
CHH = INTER // 2
GOFF = 6.906768
QMAX = 15.0
SHRINKS = np.linspace(0.4, 1.0, 16)
SVF = [float(np.float32(v)) for v in SHRINKS]
ISVF = [float(np.float32(1.0 / np.float64(np.float32(v)))) for v in SHRINKS]
LN_EPS = 1e-5
RMS_EPS = 1e-6
SQD = 1.0 / float(np.sqrt(DH))

PHASES = "all"


def own_tiles(h):
    return [g for g in range(NT) if g % 2 == h]


class Rec:
    def __init__(self, eng):
        self._eng = eng
        self.last = None
        self.last_is_dma = False

    def __getattr__(self, name):
        attr = getattr(self._eng, name)
        if not callable(attr):
            return attr

        def call(*a, **kw):
            r = attr(*a, **kw)
            if r is not None and hasattr(r, "then_inc"):
                if name in ("wait_ge", "nop", "sem_inc", "wait_op", "drain"):
                    return r
                self.last = r
                self.last_is_dma = name in ("dma_start", "dma_start_transpose")
            return r

        return call


class Sem:
    INCD = set()

    def __init__(self, nc, name):
        self.h = nc.alloc_semaphore(name=name)
        self.n = 0

    def inc(self, instr, k=1):
        instr.then_inc(self.h, k)
        Sem.INCD.add(id(instr))
        self.n += k
        return self.n

    @staticmethod
    def has_inc(instr):
        return id(instr) in Sem.INCD


def build_program():
    nc = bass.Bass("TRN2", target_bir_lowering=False, debug=False)
    p = Prog(nc)
    p.build()
    return nc, p


class Prog:
    def __init__(self, nc):
        self.nc = nc
        self.last = {}
        Sem.INCD = set()
        self.keepalive = []

    def _sect(self, engname, fn):
        prog = self
        emit = getattr(self.block, engname)

        def body(eng):
            rec = Rec(eng)
            fn(rec)
            if rec.last is not None:
                prog.last[engname] = (rec.last, rec.last_is_dma)
                prog.keepalive.append(rec.last)

        emit(body)

    def t(self, fn): self._sect("tensor", fn)
    def v(self, fn): self._sect("vector", fn)
    def s(self, fn): self._sect("scalar", fn)
    def g(self, fn): self._sect("gpsimd", fn)
    def y(self, fn): self._sect("sync", fn)

    ENG_SEM = {"tensor": "tt", "vector": "vv", "scalar": "ss", "gpsimd": "gg"}
    DMA_SEM = {"sync": "ds", "gpsimd": "dg", "vector": "dv", "scalar": "da",
               "tensor": "ds"}

    def barrier(self):
        for engname, (instr, is_dma) in list(self.last.items()):
            if Sem.has_inc(instr):
                continue
            if is_dma:
                getattr(self, self.DMA_SEM[engname]).inc(instr, 16)
            else:
                getattr(self, self.ENG_SEM[engname]).inc(instr, 1)
        self.last = {}
        waits = [(s.h, s.n) for s in
                 (self.tt, self.vv, self.ss, self.gg, self.ds, self.dg,
                  self.dv, self.da, self.dsa, self.dsc, self.cc,
                  self.dso, self.dsp, *self.dsw, *self.dsx2, *self.dskv,
                  *self.dsm, *self.dsn) if s.n > 0]
        for engname in ("tensor", "vector", "scalar", "gpsimd", "sync"):
            def mk(engname=engname):
                def body(eng):
                    for hh, n in waits:
                        eng.wait_ge(hh, n)
                getattr(self.block, engname)(body)
            mk()

    def mm_stream(self, items, nbuf=4):
        tt = self.tt
        W = self.W
        dsw = self.dsw
        base_tt = tt.n
        base_w = [dsw[k].n for k in range(4)]

        def sy(y):
            for i, it in enumerate(items):
                if i == 0:
                    y.wait_ge(tt.h, base_tt)
                if i >= nbuf:
                    y.wait_ge(tt.h, base_tt + (i - nbuf + 1))
                wbuf = W[i % nbuf]
                if it["w"].dtype == BF16:
                    wbuf = wbuf[:].bitcast(BF16)
                wslice = wbuf[:, 0:it["w"].shape[1]]
                y.dma_start(out=wslice, in_=it["w"])
                dsw[i % nbuf].inc(y.last, 16)
        self.y(sy)

        def te(t):
            for i, it in enumerate(items):
                t.wait_ge(dsw[i % nbuf].h, base_w[i % nbuf] + 16 * (i // nbuf + 1))
                for sem, cnt in it.get("mwait", ()):
                    t.wait_ge(sem.h, cnt)
                wbuf = W[i % nbuf]
                if it["w"].dtype == BF16:
                    wbuf = wbuf[:].bitcast(BF16)
                wslice = wbuf[:, 0:it["w"].shape[1]]
                t.matmul(it["o"], wslice, it["m"],
                         start=it.get("start", True), stop=it.get("stop", True))
                tt.inc(t.last)
        self.t(te)
        return tt.n

    def slot(self, i, w=2048):
        return self.R[:, i * 2048:i * 2048 + w]

    def half(self, i, j):
        return self.R[:, i * 2048 + j * 1024:i * 2048 + j * 1024 + 1024]

    def build(self):
        nc = self.nc
        D = {}
        self.D = D

        def din(name, shape, dtype=F32):
            D[name] = nc.dram_tensor(name, shape, dtype, kind="ExternalInput").ap()

        def dout(name, shape, dtype=F32):
            D[name] = nc.dram_tensor(name, shape, dtype, kind="ExternalOutput").ap()

        def dint(name, shape, dtype=F32, **kw):
            D[name] = nc.dram_tensor(name, shape, dtype, **kw).ap()

        din("xT", [C, T])
        din("xT_own", [C, TOWN])
        din("cmask", [128, 256])
        din("hsel", [8, 2])
        din("hpick", [8, 1024])
        din("Wq", [C, HQ * DH])
        din("Wk", [C, H * DH])
        din("Wv", [C, H * DH])
        din("Wg", [C, H])
        din("Wc", [HQ * DH, C])
        din("bc", [C, 1])
        din("w1", [FF, C])
        din("w2", [C, FF])
        din("inproj_sl", [C, 2 * CHH])
        din("bccols", [64, 2])
        din("convw", [CHH, KCONV])
        din("convb", [CHH, 1])
        din("xproj", [CHH, DTR + 2 * NST])
        din("dtw", [DTR, CHH])
        din("dtb", [CHH, 1])
        din("Alog", [CHH, NST])
        din("Dpar", [CHH, 1])
        din("outproj", [CHH, C])
        din("w1_sh", [FF // NCORES, C])
        din("w2_sh", [C // NCORES, FF])
        din("wfcT_sh", [C, FF // NCORES])
        din("wprT_sh", [FF, C // NCORES])
        din("qt_ind", [128, 512])
        din("qt_bst", [8, 512])
        din("qt_svp", [128, 128])
        din("qt_isvp", [128, 128])
        dout("out", [C, TOWN])

        dint("bt_w1", [C, FF // NCORES], BF16)
        dint("bt_w1_all", [NCORES, C, FF // NCORES], BF16)
        dint("bt_w2", [FF, C // NCORES], BF16)
        dint("bt_w2_all", [NCORES, FF, C // NCORES], BF16)
        dint("bs_sc", [1280, 1], F32)
        dint("bs_sc_all", [NCORES, 1280, 1], F32)
        dint("bq_fc", [C, FF // NCORES], F32)
        dint("bq_fc_all", [NCORES, C, FF // NCORES], F32)
        dint("bq_pr", [FF, C // NCORES], F32)
        dint("bq_pr_all", [NCORES, FF, C // NCORES], F32)
        dint("bx2", [C, TOWN], F32)
        dint("bx2_all", [2, C, TOWN], F32)
        dint("bssm", [DTR + 2 * NST, T], F32)
        dint("bssm_all", [DTR + 2 * NST, T], F32)
        dint("bop", [2, C, TOWN], F32)
        dint("bop_out", [C, TOWN], F32)
        dint("st_hid", [CHH, T], F32)
        dint("st_gate", [CHH, T], F32)
        dint("st_hs", [CHH, T], F32)
        dint("st_dt", [CHH, T], F32)
        dint("st_yg", [CHH, T], F32)
        dint("st_hh", [FF, TOWN], F32)
        dint("st_kh", [FF, TOWN], BF16)
        dint("q_fc_f32", [C, FF // NCORES], F32)
        dint("q_pr_f32", [FF, C // NCORES], F32)
        dint("q_fc_f32_all", [NCORES, C, FF // NCORES], F32)
        dint("q_pr_f32_all", [NCORES, FF, C // NCORES], F32)
        dint("q_fc_hi", [C, FF // NCORES], BF16)
        dint("q_fc_lo", [C, FF // NCORES], BF16)
        dint("q_pr_hi", [FF, C // NCORES], BF16)
        dint("q_pr_lo", [FF, C // NCORES], BF16)
        dint("q_fc_hi_all", [NCORES, C, FF // NCORES], BF16)
        dint("q_fc_lo_all", [NCORES, C, FF // NCORES], BF16)
        dint("q_pr_hi_all", [NCORES, FF, C // NCORES], BF16)
        dint("q_pr_lo_all", [NCORES, FF, C // NCORES], BF16)
        dint("st_h2lo", [FF, TOWN], BF16)
        dint("md", [4096, 1], F32)
        dint("st_x2own", [C, TOWN], F32)

        sb = nc.alloc_sbuf_tensor
        self.NSLOT = 17
        self.R = sb("R", [128, self.NSLOT * 2048], F32)
        self.W = [sb(f"W{i}", [128, 512], F32) for i in range(4)]
        self.W4 = [sb(f"W4{i}", [128, 256], F32) for i in range(2)]
        self.W5 = [sb(f"W5{i}", [128, 8], F32) for i in range(2)]
        self.SA = sb("SA", [128, 2048], F32)
        self.SB_ = sb("SB", [128, 2048], F32)
        self.SC = sb("SC", [128, 2048], F32)
        self.SD = sb("SD", [128, 2048], F32)
        self.SBF = sb("SBF", [128, 2048], BF16)
        self.MISC = sb("MISC", [128, 64], F32)
        self.ONER = sb("ONER", [1, 128], F32)
        self.MISC2 = sb("MISC2", [128, 64], F32)
        self.CSP = sb("CSP", [8, 2048], F32)
        self.CSPO = sb("CSPO", [8, 1024], F32)
        self.CSPT = sb("CSPT", [128, 128], F32)
        self.CMASK = sb("CMASK", [128, 256], F32)
        self.HSEL = sb("HSEL", [8, 2], F32)
        self.BCC = sb("BCC", [64, 2], F32)
        self.HPICK = sb("HPICK", [8, 1024], F32)
        self.IDENT = sb("IDENT", [128, 128], F32)
        self.IDENTB = sb("IDENTB", [128, 128], BF16)
        self.ER = sb("ER", [128, 1024], F32R)
        self.QTIT = sb("QTIT", [128, 512], F32R)
        ps = nc.alloc_psum_tensor
        self.PB = [ps(f"PB{i}", [128, 512], F32) for i in range(6)]
        self.PN = [ps(f"PN{i}", [128, 512], F32) for i in range(2)]

        for nm in ("ds", "dg", "dv", "da", "dsa", "dsc", "tt", "vv", "ss", "gg", "cc"):
            setattr(self, nm, Sem(nc, nm))
        self.dsw = [Sem(nc, f"dsw{i}") for i in range(4)]
        self.dsx2 = [Sem(nc, f"dsx{i}") for i in range(2)]
        self.dskv = [Sem(nc, f"dskv{i}") for i in range(2)]
        self.dsm = [Sem(nc, f"dsm{i}") for i in range(2)]
        self.dso = Sem(nc, "dso")
        self.dsp = Sem(nc, "dsp")
        self.dsn = [Sem(nc, f"dsn{i}") for i in range(2)]
        self.dsq = [Sem(nc, f"dsq{i}") for i in range(2)]

        self.block = nc.Block()
        self.block.__enter__()

        self.ph_init()
        self.barrier()
        self.ph_l0_attn()
        self.barrier()
        if PHASES in ("x2", "x3", "all"):
            self.ph_quant_ternary()
            self.barrier()
            self.ph_l0_ffn()
            self.barrier()
        if PHASES in ("x3", "all"):
            self.ph_l1_attn()
            self.barrier()
        if PHASES == "all":
            self.ph_quant_turbo()
            self.barrier()
            self.ph_l1_ffn()
            self.barrier()
        self.ph_out()

        self.block.__exit__(None, None, None)

    def ph_init(self):
        D = self.D
        ds, gg, vv = self.ds, self.gg, self.vv
        IDENT, IDENTB, MISC = self.IDENT, self.IDENTB, self.MISC

        dsc = self.dsc

        def sy(y):
            y.dma_start(out=self.CMASK[:], in_=D["cmask"][:])
            dsc.inc(y.last, 16)
            y.dma_start(out=self.HSEL[:], in_=D["hsel"][:])
            dsc.inc(y.last, 16)
            y.dma_start(out=self.HPICK[:], in_=D["hpick"][:])
            dsc.inc(y.last, 16)
        self.y(sy)

        def gp(g):
            g.iota(self.SA[:, 0:128], pattern=[[-1, 128]], base=0,
                   channel_multiplier=1, allow_small_or_imprecise_dtypes=True)
            g.memset(self.MISC2[:, 16:17], 0.0)
            gg.inc(g.last)
            g.wait_ge(gg.h, gg.n)
            g.tensor_scalar(IDENT[:], self.SA[:, 0:128], 0.0, None, ALU.is_equal)
            gg.inc(g.last)
        self.g(gp)

        def ve(v):
            v.wait_ge(gg.h, gg.n)
            v.tensor_copy(IDENTB[:], IDENT[:])
            vv.inc(v.last)
            v.memset(MISC[:, 0:1], 1.0)
            vv.inc(v.last)
            v.memset(self.ONER[0:1, :], 1.0)
            vv.inc(v.last)
            v.memset(MISC[0:8, 1:2], -GOFF)
            vv.inc(v.last)
            v.memset(MISC[0:1, 2:3], LN_EPS)
            vv.inc(v.last)
            v.memset(MISC[0:1, 3:4], RMS_EPS)
            vv.inc(v.last)
        self.v(ve)

    def ph_l0_attn(self):
        D = self.D
        ds, tt, vv, ss = self.ds, self.tt, self.vv, self.ss
        PB = self.PB
        SD, MISC2 = self.SD, self.MISC2

        q_ap = lambda hr: self.half(hr // 2, hr % 2)
        xo_ap = lambda c: self.half(9 + c // 2, c % 2)
        x1_ap = lambda ch: self.half(9 + ch // 2, ch % 2)
        self.x1_ap = x1_ap

        dsa = self.dsa

        def syA(y):
            for c in range(16):
                y.dma_start(out=xo_ap(c), in_=D["xT_own"][c * 128:(c + 1) * 128, :])
                dsa.inc(y.last, 16)
        self.y(syA)
        dsA = dsa.n

        vvB0 = vv.n
        items = []
        for hrg in range(8):
            for c in range(16):
                for j in range(2):
                    hr = hrg * 2 + j
                    for f in range(2):
                        mw = []
                        if c == 0:
                            mw = [(dsa, dsA)]
                            if hrg > 0:
                                mw.append((vv, vvB0 + hrg * 4))
                        items.append(dict(
                            w=D["Wq"][c * 128:(c + 1) * 128, hr * 128:(hr + 1) * 128],
                            m=xo_ap(c)[:, f * 512:(f + 1) * 512],
                            o=PB[2 + j * 2 + f][:],
                            start=(c == 0), stop=(c == 15), mwait=mw))
        ttB0 = tt.n
        self.mm_stream(items)

        def veB(v):
            for hrg in range(8):
                v.wait_ge(tt.h, ttB0 + (hrg + 1) * 64)
                for j in range(2):
                    hr = hrg * 2 + j
                    for f in range(2):
                        v.tensor_copy(q_ap(hr)[:, f * 512:(f + 1) * 512],
                                      PB[2 + j * 2 + f][:])
                        vv.inc(v.last)
        self.v(veB)

        for wv in range(4):
            self._attn_wave(wv)

        def syC(y):
            y.dma_start(out=MISC2[:, 0:16],
                        in_=D["bc"].rearrange("(i p) one -> p (i one)", p=128),
                        allow_slow_non_contiguous=True)
            self.dsc.inc(y.last, 16)
        self.y(syC)

        vvC0 = vv.n
        sdx_base = [self.dsx2[k].n for k in range(2)]
        for ch in range(16):
            dhalf = SD[:, (ch % 2) * 1024:(ch % 2) * 1024 + 1024]

            def syD(y, ch=ch, dhalf=dhalf):
                if ch < 2:
                    y.wait_ge(vv.h, vvC0)
                else:
                    y.wait_ge(vv.h, vvC0 + (ch - 1) * 2)
                y.dma_start(out=dhalf, in_=D["xT_own"][ch * 128:(ch + 1) * 128, :])
                self.dsx2[ch % 2].inc(y.last, 16)
            self.y(syD)
            dsD = (ch % 2, sdx_base[ch % 2] + 16 * (ch // 2 + 1))

            items = []
            for hr in range(16):
                for f in range(2):
                    items.append(dict(
                        w=D["Wc"][hr * 128:(hr + 1) * 128, ch * 128:(ch + 1) * 128],
                        m=q_ap(hr)[:, f * 512:(f + 1) * 512],
                        o=PB[f][:],
                        start=(hr == 0), stop=(hr == 15),
                        mwait=[(vv, vvC0 + ch * 2)] if hr == 0 else []))
            self.mm_stream(items)
            ttb = tt.n

            def veD(v, ch=ch, dhalf=dhalf, ttb=ttb, dsD=dsD):
                v.wait_ge(tt.h, ttb)
                v.wait_ge(self.dsx2[dsD[0]].h, dsD[1])
                v.wait_ge(self.dsc.h, self.dsc.n)
                for f in range(2):
                    v.scalar_tensor_tensor(
                        x1_ap(ch)[:, f * 512:(f + 1) * 512],
                        PB[f][:], MISC2[:, ch:ch + 1],
                        dhalf[:, f * 512:(f + 1) * 512],
                        ALU.add, ALU.add)
                    vv.inc(v.last)
            self.v(veD)

        self._layernorm(x1_ap)

    def _attn_wave(self, wv):
        D = self.D
        ds, tt, vv, ss = self.ds, self.tt, self.vv, self.ss
        PB, PN = self.PB, self.PN
        SA, SB_ = self.SA, self.SB_
        MISC, IDENT = self.MISC, self.IDENT
        CSP = self.CSP
        kslot = lambda i: self.slot(8 + i)
        vslot = lambda i: self.slot(10 + i)
        vTslot = lambda i: self.slot(12 + i)

        hh0, hh1 = 2 * wv, 2 * wv + 1
        tt0, vv0 = tt.n, vv.n
        nmm = 5 if wv == 0 else 4
        nw = 4 if wv == 0 else 3
        dsx2, dskv = self.dsx2, self.dskv
        bx = [dsx2[k].n for k in range(2)]
        bk = [dskv[k].n for k in range(2)]

        def syW(y):
            for sect in range(4):
                for c in range(16):
                    i = sect * 16 + c
                    if i == 0:
                        y.wait_ge(tt.h, tt0)
                        y.wait_ge(vv.h, vv0)
                    if i >= 2:
                        y.wait_ge(tt.h, tt0 + (i - 1) * nmm)
                    xbuf = (SA if i % 2 == 0 else SB_)[:, 0:512]
                    y.dma_start(out=xbuf,
                                in_=D["xT"][c * 128:(c + 1) * 128,
                                            sect * 512:(sect + 1) * 512])
                    dsx2[i % 2].inc(y.last, 16)
                    y.dma_start(out=self.W[i % 2][:, 0:128],
                                in_=D["Wk"][c * 128:(c + 1) * 128,
                                            hh0 * 128:(hh0 + 1) * 128])
                    dskv[i % 2].inc(y.last, 16)
                    y.dma_start(out=self.W[2 + i % 2][:, 0:128],
                                in_=D["Wk"][c * 128:(c + 1) * 128,
                                            hh1 * 128:(hh1 + 1) * 128])
                    dskv[i % 2].inc(y.last, 16)
                    y.dma_start(out=self.W4[i % 2][:, 0:256],
                                in_=D["Wv"][c * 128:(c + 1) * 128,
                                            hh0 * 128:(hh0 + 2) * 128])
                    dskv[i % 2].inc(y.last, 16)
                    if wv == 0:
                        y.dma_start(out=self.W5[i % 2][:, 0:8],
                                    in_=D["Wg"][c * 128:(c + 1) * 128, :])
                        dskv[i % 2].inc(y.last, 16)
        self.y(syW)

        def teW(t):
            for sect in range(4):
                for c in range(16):
                    i = sect * 16 + c
                    t.wait_ge(dsx2[i % 2].h, bx[i % 2] + 16 * (i // 2 + 1))
                    t.wait_ge(dskv[i % 2].h, bk[i % 2] + 16 * nw * (i // 2 + 1))
                    if c == 0:
                        t.wait_ge(vv.h, vv0 + sect * nmm)
                    xbuf = (SA if i % 2 == 0 else SB_)[:, 0:512]
                    st, sp = (c == 0), (c == 15)
                    t.matmul(PB[2][:], self.W[i % 2][:, 0:128], xbuf, start=st, stop=sp)
                    tt.inc(t.last)
                    t.matmul(PB[3][:], self.W[2 + i % 2][:, 0:128], xbuf, start=st, stop=sp)
                    tt.inc(t.last)
                    t.matmul(PB[4][:], self.W4[i % 2][:, 0:128], xbuf, start=st, stop=sp)
                    tt.inc(t.last)
                    t.matmul(PB[5][:], self.W4[i % 2][:, 128:256], xbuf, start=st, stop=sp)
                    tt.inc(t.last)
                    if wv == 0:
                        t.matmul(PN[1][0:8, :], self.W5[i % 2][:, 0:8], xbuf, start=st, stop=sp)
                        tt.inc(t.last)
        self.t(teW)

        def veW(v):
            for sect in range(4):
                v.wait_ge(tt.h, tt0 + (sect + 1) * 16 * nmm)
                sl = slice(sect * 512, (sect + 1) * 512)
                v.tensor_copy(kslot(0)[:, sl], PB[2][:]); vv.inc(v.last)
                v.tensor_copy(kslot(1)[:, sl], PB[3][:]); vv.inc(v.last)
                v.tensor_copy(vslot(0)[:, sl], PB[4][:]); vv.inc(v.last)
                v.tensor_copy(vslot(1)[:, sl], PB[5][:]); vv.inc(v.last)
                if wv == 0:
                    v.tensor_copy(CSP[0:8, sl], PN[1][0:8, :]); vv.inc(v.last)
        self.v(veW)
        vv_kv = vv.n

        if wv == 0:
            self._csp_pipeline(vv_kv)

        tt_vt0 = tt.n
        vv_vt0 = vv.n

        def teV(t):
            t.wait_ge(vv.h, vv_vt0)
            for j in range(2):
                for stile in range(16):
                    i = j * 16 + stile
                    if i >= 2:
                        t.wait_ge(vv.h, vv_vt0 + i - 1)
                    t.transpose(PB[2 + i % 2][:, 0:128],
                                vslot(j)[:, stile * 128:(stile + 1) * 128],
                                IDENT[:])
                    tt.inc(t.last)
        self.t(teV)

        def veV(v):
            for j in range(2):
                for stile in range(16):
                    i = j * 16 + stile
                    v.wait_ge(tt.h, tt_vt0 + i + 1)
                    v.tensor_copy(vTslot(j)[:, stile * 128:(stile + 1) * 128],
                                  PB[2 + i % 2][:, 0:128])
                    vv.inc(v.last)
        self.v(veV)

        for hr in (2 * hh0, 2 * hh0 + 1, 2 * hh1, 2 * hh1 + 1):
            self._attn_head(hr)

    def _csp_pipeline(self, vv_kv):
        ds, tt, vv, ss = self.ds, self.tt, self.vv, self.ss
        PB = self.PB
        MISC, IDENT = self.MISC, self.IDENT
        CSP, CSPT, CSPO = self.CSP, self.CSPT, self.CSPO
        CSPA, CSPB = self.SC, self.SD
        CSPO0, CSPO1 = self.SC[0:8, 0:1024], self.SD[0:8, 0:1024]

        def se1(s):
            s.wait_ge(vv.h, vv_kv)
            s.activation(CSPA[0:8, :], CSP[0:8, :], ACT.Exp,
                         bias=MISC[0:8, 1:2], scale=-1.0)
            ss.inc(s.last)
        self.s(se1)
        ss1 = ss.n

        def ve1(v):
            v.wait_ge(ss.h, ss1)
            v.tensor_scalar(CSPA[0:8, :], CSPA[0:8, :], 1.0, None, ALU.add)
            vv.inc(v.last)
        self.v(ve1)
        vv1 = vv.n

        def se2(s):
            s.wait_ge(vv.h, vv1)
            s.activation(CSPB[0:8, :], CSPA[0:8, :], ACT.Ln)
            ss.inc(s.last)
        self.s(se2)
        ss2 = ss.n

        def ve2(v):
            v.wait_ge(ss.h, ss2)
            v.memset(CSPA[0:8, :], 1.0)
            vv.inc(v.last)
            v.wait_ge(vv.h, vv.n)
            v.tensor_tensor_scan(CSP[0:8, :], CSPA[0:8, :], CSPB[0:8, :],
                                 0.0, ALU.mult, ALU.add)
            vv.inc(v.last)
        self.v(ve2)
        vv_csp = vv.n

        def te2(t):
            t.wait_ge(vv.h, vv_csp)
            for tj in range(16):
                if tj >= 2:
                    t.wait_ge(vv.h, vv_csp + tj - 1)
                t.transpose(PB[2 + tj % 2][:, 0:8],
                            CSP[0:8, tj * 128:(tj + 1) * 128],
                            IDENT[0:8, 0:8])
                tt.inc(t.last)
        self.t(te2)
        tt_cspT = tt.n

        def ve3(v):
            for tj in range(16):
                v.wait_ge(tt.h, tt_cspT - 15 + tj)
                v.tensor_copy(CSPT[:, tj * 8:(tj + 1) * 8], PB[2 + tj % 2][:, 0:8])
                vv.inc(v.last)
            cv = CSP[0:8, :].rearrange("p (i f) -> p i f", f=128)
            o0 = CSPO0.rearrange("p (i f) -> p i f", f=128)
            o1 = CSPO1.rearrange("p (i f) -> p i f", f=128)
            v.tensor_copy(o0, cv[:, 0::2, :])
            vv.inc(v.last)
            v.tensor_copy(o1, cv[:, 1::2, :])
            vv.inc(v.last)
            v.wait_ge(vv.h, vv.n)
            v.tensor_scalar(CSPO[0:8, :], CSPO0, self.HSEL[0:8, 0:1],
                            None, ALU.mult)
            vv.inc(v.last)
            v.wait_ge(vv.h, vv.n)
            v.scalar_tensor_tensor(CSPO[0:8, :], CSPO1,
                                   self.HSEL[0:8, 1:2], CSPO[0:8, :],
                                   ALU.mult, ALU.add)
            vv.inc(v.last)
        self.v(ve3)

    def _attn_head(self, hr):
        ds, tt, vv, ss = self.ds, self.tt, self.vv, self.ss
        PB, PN = self.PB, self.PN
        SA, SB_, SC, SD = self.SA, self.SB_, self.SC, self.SD
        MISC, CSPT, CSPO, CMASK = self.MISC, self.CSPT, self.CSPO, self.CMASK
        CSP = self.CSP
        hh = hr // 2
        j = hh % 2
        q_ap = self.half(hr // 2, hr % 2)
        kslot = self.slot(8 + j)
        vTslot = self.slot(12 + j)

        iters = []
        for tj in range(16):
            i0 = (tj // 2) * 128
            if i0 < 512:
                iters.append(dict(tj=tj, lo=i0, hi=512, first=True))
                iters.append(dict(tj=tj, lo=512, hi=1024, first=False))
            else:
                iters.append(dict(tj=tj, lo=i0, hi=1024, first=True))
        last = [max(i for i, it in enumerate(iters) if it["lo"] < 512),
                max(i for i, it in enumerate(iters) if it["hi"] > 512)]

        base_tt, base_ss, base_vv = tt.n, ss.n, vv.n
        sched = []
        ttc, ssc, vvc = base_tt, base_ss, base_vv
        for i, it in enumerate(iters):
            e = dict(it)
            ttc += 1; e["tt_qk"] = ttc
            ttc += 1; e["tt_bc"] = ttc
            ssc += 1; e["ss_sq"] = ssc
            ssc += 1; e["ss_ge"] = ssc
            vvc += 1 + (1 if it["first"] else 0)
            e["vv_a"] = vvc
            ttc += 1; e["tt_nm"] = ttc
            ttc += 1; e["tt_av"] = ttc
            sched.append(e)
        tt_end, ss_end, vv_end = ttc, ssc, vvc

        started = [False, False]

        def teH(t):
            for i, e in enumerate(sched):
                tj, lo, hi = e["tj"], e["lo"], e["hi"]
                L = hi - lo
                b = 0 if lo < 512 else 1
                qkp = PB[2 + i % 2]
                bcp = PB[4 + i % 2]
                if i < 2:
                    t.wait_ge(vv.h, base_vv)
                else:
                    t.wait_ge(ss.h, sched[i - 2]["ss_ge"])
                t.matmul(qkp[:, 0:L], kslot[:, tj * 128:(tj + 1) * 128],
                         q_ap[:, lo:hi], start=True, stop=True)
                tt.inc(t.last)
                t.matmul(bcp[:, 0:L], self.HPICK[0:8, hh * 128:(hh + 1) * 128],
                         CSPO[0:8, lo:hi], start=True, stop=True)
                tt.inc(t.last)
                t.wait_ge(vv.h, e["vv_a"])
                st = not started[b]
                sp = (i == last[b])
                abuf = (SA if i % 2 == 0 else SB_)[:, 0:L]
                t.matmul(PN[b][0:1, (lo % 512):(lo % 512) + L],
                         MISC[:, 0:1], abuf, start=st, stop=sp)
                tt.inc(t.last)
                t.matmul(PB[b][:, (lo % 512):(lo % 512) + L],
                         vTslot[:, tj * 128:(tj + 1) * 128],
                         abuf, start=st, stop=sp)
                tt.inc(t.last)
                started[b] = True
        self.t(teH)
        assert tt.n == tt_end

        def seH(s):
            for i, e in enumerate(sched):
                tj, lo, hi = e["tj"], e["lo"], e["hi"]
                L = hi - lo
                qkp = PB[2 + i % 2]
                bcp = PB[4 + i % 2]
                sqb = SC[:, (i % 2) * 1024:(i % 2) * 1024 + L]
                geb = SD[:, (i % 2) * 1024:(i % 2) * 1024 + L]
                s.wait_ge(tt.h, e["tt_qk"])
                if i >= 2:
                    s.wait_ge(vv.h, sched[i - 2]["vv_a"])
                s.activation(sqb, qkp[:, 0:L], ACT.Square, scale=SQD)
                ss.inc(s.last)
                s.wait_ge(tt.h, e["tt_bc"])
                s.activation(geb, bcp[:, 0:L], ACT.Exp, scale=-1.0,
                             bias=CSPT[:, tj * 8 + hh:tj * 8 + hh + 1])
                ss.inc(s.last)
        self.s(seH)
        assert ss.n == ss_end

        def veH(v):
            for i, e in enumerate(sched):
                tj, lo, hi = e["tj"], e["lo"], e["hi"]
                L = hi - lo
                sqb = SC[:, (i % 2) * 1024:(i % 2) * 1024 + L]
                geb = SD[:, (i % 2) * 1024:(i % 2) * 1024 + L]
                abuf = (SA if i % 2 == 0 else SB_)[:, 0:L]
                v.wait_ge(ss.h, e["ss_ge"])
                if i >= 2:
                    v.wait_ge(tt.h, sched[i - 2]["tt_av"])
                if e["first"]:
                    v.tensor_tensor(geb[:, 0:128], geb[:, 0:128],
                                    CMASK[:, (tj % 2) * 128:(tj % 2) * 128 + 128],
                                    ALU.mult)
                    vv.inc(v.last)
                    v.wait_ge(vv.h, vv.n)
                v.tensor_tensor(abuf, sqb, geb, ALU.mult)
                vv.inc(v.last)
        self.v(veH)
        assert vv.n == vv_end

        def veE(v):
            v.wait_ge(tt.h, tt_end)
            for b in range(2):
                v.tensor_scalar(CSP[0:1, b * 512:(b + 1) * 512], PN[b][0:1, :],
                                1e-6, None, ALU.add)
                vv.inc(v.last)
            v.wait_ge(vv.h, vv.n)
            v.reciprocal(CSP[0:1, 1024:2048], CSP[0:1, 0:1024])
            vv.inc(v.last)
        self.v(veE)
        vv_rn = vv.n

        def teE(t):
            t.wait_ge(vv.h, vv_rn)
            for b in range(2):
                t.matmul(PB[2 + b][:, 0:512], self.ONER[0:1, :],
                         CSP[0:1, 1024 + b * 512:1024 + (b + 1) * 512],
                         start=True, stop=True)
                tt.inc(t.last)
        self.t(teE)
        tt_rb = tt.n

        def veE2(v):
            v.wait_ge(tt.h, tt_rb)
            for b in range(2):
                v.tensor_copy(SC[:, b * 512:(b + 1) * 512], PB[2 + b][:])
                vv.inc(v.last)
            v.wait_ge(vv.h, vv.n)
            for b in range(2):
                v.tensor_tensor(q_ap[:, b * 512:(b + 1) * 512], PB[b][:],
                                SC[:, b * 512:(b + 1) * 512], ALU.mult)
                vv.inc(v.last)
        self.v(veE2)

    def _layernorm(self, ap_of_chunk, eps_col=2):
        ds, tt, vv, ss = self.ds, self.tt, self.vv, self.ss
        PB, PN = self.PB, self.PN
        SA, MISC = self.SA, self.MISC
        NRM = self.CSP
        base_tt, base_ss, base_vv = tt.n, ss.n, vv.n
        base_dg = self.dg.n

        sq_cnt = []
        ssc = base_ss
        for ch in range(16):
            ssc += 1
            sq_cnt.append(ssc)

        def se(s):
            for ch in range(16):
                if ch < 2:
                    s.wait_ge(vv.h, base_vv)
                    s.wait_ge(self.dg.h, base_dg)
                else:
                    s.wait_ge(tt.h, base_tt + (ch - 1) * 4)
                s.activation(SA[:, (ch % 2) * 1024:(ch % 2) * 1024 + 1024],
                             ap_of_chunk(ch), ACT.Square)
                ss.inc(s.last)
        self.s(se)

        def te(t):
            t.wait_ge(vv.h, base_vv)
            for ch in range(16):
                st, sp = (ch == 0), (ch == 15)
                t.wait_ge(ss.h, sq_cnt[ch])
                for f in range(2):
                    t.matmul(PB[f][0:1, :], MISC[:, 0:1],
                             ap_of_chunk(ch)[:, f * 512:(f + 1) * 512],
                             start=st, stop=sp)
                    tt.inc(t.last)
                for f in range(2):
                    t.matmul(PN[f][0:1, :], MISC[:, 0:1],
                             SA[:, (ch % 2) * 1024 + f * 512:(ch % 2) * 1024 + (f + 1) * 512],
                             start=st, stop=sp)
                    tt.inc(t.last)
        self.t(te)
        tt_st = tt.n

        SD = self.SD

        def ve(v):
            v.wait_ge(tt.h, tt_st)
            for f in range(2):
                v.tensor_scalar(NRM[0:1, f * 512:(f + 1) * 512], PB[f][0:1, :],
                                1.0 / 2048, None, ALU.mult)
                vv.inc(v.last)
                v.tensor_scalar(SD[0:1, f * 512:(f + 1) * 512], PN[f][0:1, :],
                                1.0 / 2048, None, ALU.mult)
                vv.inc(v.last)
            v.wait_ge(vv.h, vv.n)
            v.tensor_tensor(SD[0:1, 1024:2048], NRM[0:1, 0:1024],
                            NRM[0:1, 0:1024], ALU.mult)
            vv.inc(v.last)
            v.wait_ge(vv.h, vv.n)
            v.tensor_tensor(SD[0:1, 0:1024], SD[0:1, 0:1024],
                            SD[0:1, 1024:2048], ALU.subtract)
            vv.inc(v.last)
        self.v(ve)
        vv_var = vv.n

        def se2(s):
            s.wait_ge(vv.h, vv_var)
            s.activation(NRM[0:1, 1024:2048], SD[0:1, 0:1024], ACT.Sqrt,
                         bias=MISC[0:1, eps_col:eps_col + 1])
            ss.inc(s.last)
        self.s(se2)
        ss_sd = ss.n

        def ve2(v):
            v.wait_ge(ss.h, ss_sd)
            v.reciprocal(SD[0:1, 0:1024], NRM[0:1, 1024:2048])
            vv.inc(v.last)
        self.v(ve2)
        vv_rs = vv.n

        def te2(t):
            t.wait_ge(vv.h, vv_rs)
            for f in range(2):
                t.matmul(PB[2 + f][:, 0:512],
                         self.ONER[0:1, :], NRM[0:1, f * 512:(f + 1) * 512],
                         start=True, stop=True)
                tt.inc(t.last)
                t.matmul(PB[4 + f][:, 0:512],
                         self.ONER[0:1, :], SD[0:1, f * 512:(f + 1) * 512],
                         start=True, stop=True)
                tt.inc(t.last)
        self.t(te2)
        tt_bc = tt.n

        def ve3(v):
            v.wait_ge(tt.h, tt_bc)
            v.wait_ge(self.dg.h, base_dg)
            for ch in range(16):
                for f in range(2):
                    v.tensor_tensor(ap_of_chunk(ch)[:, f * 512:(f + 1) * 512],
                                    ap_of_chunk(ch)[:, f * 512:(f + 1) * 512],
                                    PB[2 + f][:], ALU.subtract)
                    vv.inc(v.last)
            v.wait_ge(vv.h, vv.n)
            for ch in range(16):
                for f in range(2):
                    v.tensor_tensor(ap_of_chunk(ch)[:, f * 512:(f + 1) * 512],
                                    ap_of_chunk(ch)[:, f * 512:(f + 1) * 512],
                                    PB[4 + f][:], ALU.mult)
                    vv.inc(v.last)
        self.v(ve3)

    def _scol_bcast(self, m2, col0, ntiles, which, vv_wait):
        D = self.D
        tt, vv, dg, dsc = self.tt, self.vv, self.dg, self.dsc
        PB = self.PB
        SD, CSP = self.SD, self.CSP

        def gy(g, vv_wait=vv_wait):
            g.wait_ge(vv.h, vv_wait)
            for tb in range(ntiles):
                g.dma_start(out=D["md"][which * 2048 + tb * 128:which * 2048 + (tb + 1) * 128, :],
                            in_=m2[:, col0 + tb:col0 + tb + 1])
                dg.inc(g.last, 16)
        self.g(gy)
        dg_b = dg.n

        def sy(y):
            y.wait_ge(dg.h, dg_b)
            y.dma_start(out=CSP[0:1, 0:1024],
                        in_=D["md"][which * 2048:which * 2048 + 1024, :]
                        .rearrange("t one -> one t"))
            dsc.inc(y.last, 16)
        self.y(sy)
        dsc_b = dsc.n

        def te(t):
            t.wait_ge(dsc.h, dsc_b)
            t.wait_ge(vv.h, vv.n)
            for f in range(2):
                t.matmul(PB[2 + f][:, 0:512], self.ONER[0:1, :],
                         CSP[0:1, f * 512:(f + 1) * 512], start=True, stop=True)
                tt.inc(t.last)
        self.t(te)
        tt_b = tt.n

        def ve(v):
            v.wait_ge(tt.h, tt_b)
            for f in range(2):
                v.tensor_copy(SD[:, f * 512:(f + 1) * 512], PB[2 + f][:])
                vv.inc(v.last)
            v.wait_ge(vv.h, vv.n)
            v.reciprocal(SD[:, 1024:2048], SD[:, 0:1024])
            vv.inc(v.last)
        self.v(ve)

    def ph_quant_ternary(self):
        D = self.D
        tt, vv, ss, dg, cc = self.tt, self.vv, self.ss, self.dg, self.cc
        dsa = self.dsa
        PB = self.PB
        SA, SB_, SC = self.SA, self.SB_, self.SC
        MISC2, IDENTB, SBF = self.MISC2, self.IDENTB, self.SBF

        jobs = []
        for r in range(8):
            jobs.append(dict(src=D["w1_sh"][r * 128:(r + 1) * 128, :], wdt=2048,
                             dst=D["bt_w1"], dcol=r, ji=r))
        for r in range(2):
            jobs.append(dict(src=D["w2_sh"][r * 128:(r + 1) * 128, :], wdt=8192,
                             dst=D["bt_w2"], dcol=r, ji=8 + r))

        for jb in jobs:
            self._ternary_job(jb)

        def gq(g):
            g.wait_ge(vv.h, self.vv.n)
            g.wait_ge(self.dg.h, self.dg.n)
            g.collective_compute("AllGather", ALU.bypass,
                                 replica_groups=[list(range(NCORES))],
                                 ins=[D["bt_w1"][:]], outs=[D["bt_w1_all"][:]])
            cc.inc(g.last)
            g.collective_compute("AllGather", ALU.bypass,
                                 replica_groups=[list(range(NCORES))],
                                 ins=[D["bt_w2"][:]], outs=[D["bt_w2_all"][:]])
            cc.inc(g.last)
            g.collective_compute("AllGather", ALU.bypass,
                                 replica_groups=[list(range(NCORES))],
                                 ins=[D["bs_sc"][:]], outs=[D["bs_sc_all"][:]])
            cc.inc(g.last)
        self.g(gq)

    def _ternary_job(self, jb):
        D = self.D
        tt, vv, ss, dg = self.tt, self.vv, self.ss, self.dg
        dsa = self.dsa
        PB = self.PB
        SA, SC = self.SA, self.SC
        MISC2, IDENTB, SBF = self.MISC2, self.IDENTB, self.SBF
        W = jb["wdt"]
        nck = W // 2048
        ji = jb["ji"]

        for kk in range(nck):
            d0 = dsa.n
            def syL(y, jb=jb, kk=kk, d0=d0):
                y.wait_ge(ss.h, ss.n)
                y.wait_ge(vv.h, vv.n)
                y.dma_start(out=SA[:, 0:2048],
                            in_=jb["src"][:, kk * 2048:(kk + 1) * 2048])
                dsa.inc(y.last, 16)
            self.y(syL)
            d1 = dsa.n

            def seL(s, jb=jb, kk=kk, d1=d1):
                s.wait_ge(dsa.h, d1)
                s.activation(SC[:, 0:2048], SA[:, 0:2048], ACT.Abs,
                             scale=1.0 / jb["wdt"],
                             accum_out=MISC2[:, 40 + kk:41 + kk])
                ss.inc(s.last)
            self.s(seL)

        def veL(v, nck=nck, ji=ji):
            v.wait_ge(ss.h, ss.n)
            c0 = MISC2[:, 40:41]
            for kk in range(1, nck):
                v.tensor_tensor(c0, c0, MISC2[:, 40 + kk:41 + kk], ALU.add)
                vv.inc(v.last)
                v.wait_ge(vv.h, vv.n)
            v.tensor_scalar(MISC2[:, 20:21], c0, 1e-5, None, ALU.max)
            vv.inc(v.last)
            v.wait_ge(vv.h, vv.n)
            v.reciprocal(MISC2[:, 36:37], MISC2[:, 20:21])
            vv.inc(v.last)
        self.v(veL)
        vv_sc = vv.n

        def gyS(g, ji=ji, vv_sc=vv_sc):
            g.wait_ge(vv.h, vv_sc)
            g.dma_start(out=D["bs_sc"][ji * 128:(ji + 1) * 128, :],
                        in_=MISC2[:, 20:21])
            dg.inc(g.last, 16)
        self.g(gyS)

        for kk in range(nck):
            def syR(y, jb=jb, kk=kk):
                y.wait_ge(vv.h, vv.n)
                y.dma_start(out=SA[:, 0:2048],
                            in_=jb["src"][:, kk * 2048:(kk + 1) * 2048])
                dsa.inc(y.last, 16)
            self.y(syR)
            d2 = dsa.n

            def veQ(v, d2=d2, vv_sc=vv_sc):
                v.wait_ge(dsa.h, d2)
                v.wait_ge(dg.h, dg.n)
                v.tensor_scalar(SC[:, 0:2048], SA[:, 0:2048],
                                MISC2[:, 36:37], None, ALU.mult)
                vv.inc(v.last)
                v.wait_ge(vv.h, vv.n)
                v.tensor_copy(SC[:, 0:2048].bitcast(I32), SC[:, 0:2048])
                vv.inc(v.last)
                v.wait_ge(vv.h, vv.n)
                v.tensor_scalar(SBF[:, 0:2048], SC[:, 0:2048].bitcast(I32),
                                1.0, -1.0, ALU.min, ALU.max)
                vv.inc(v.last)
            self.v(veQ)
            vv_q = vv.n
            tt_t = tt.n

            def teT(t, vv_q=vv_q, tt_t=tt_t):
                t.wait_ge(vv.h, vv_q)
                for p in range(16):
                    if p >= 2:
                        t.wait_ge(vv.h, vv_q + p - 1)
                    t.transpose(PB[2 + p % 2][:, 0:256].bitcast(BF16)[:, 0:128],
                                SBF[:, p * 128:(p + 1) * 128], IDENTB[:])
                    tt.inc(t.last)
            self.t(teT)

            def veT(v, tt_t=tt_t, vv_q=vv_q):
                for p in range(16):
                    v.wait_ge(tt.h, tt_t + p + 1)
                    v.tensor_copy(SBF[:, p * 128:(p + 1) * 128],
                                  PB[2 + p % 2][:, 0:256].bitcast(BF16)[:, 0:128])
                    vv.inc(v.last)
            self.v(veT)
            vv_t = vv.n

            def gyT(g, jb=jb, kk=kk, vv_t=vv_t):
                g.wait_ge(vv.h, vv_t)
                for p in range(16):
                    g.dma_start(
                        out=jb["dst"][kk * 2048 + p * 128:kk * 2048 + (p + 1) * 128,
                                      jb["dcol"] * 128:(jb["dcol"] + 1) * 128],
                        in_=SBF[:, p * 128:(p + 1) * 128])
                    dg.inc(g.last, 16)
            self.g(gyT)

    def ph_l0_ffn(self):
        D = self.D
        tt, vv, ss, dg, cc = self.tt, self.vv, self.ss, self.dg, self.cc
        dsa, dsc = self.dsa, self.dsc
        PB, PN = self.PB, self.PN
        SA, SB_, SC, SD = self.SA, self.SB_, self.SC, self.SD
        SBF, MISC, MISC2, IDENT = self.SBF, self.MISC, self.MISC2, self.IDENT
        CSP = self.CSP
        x1_ap = self.x1_ap
        kx_ap = lambda ch: self.slot(ch // 4)[:].bitcast(BF16)[:, (ch % 4) * 1024:(ch % 4) * 1024 + 1024]
        kh_ap = lambda hc: self.slot(hc // 8)[:].bitcast(BF16)[:, (hc % 8) * 512:(hc % 8) * 512 + 1024]
        cc_need = cc.n

        kx2_ap = lambda ch: self.slot(ch // 4)[:].bitcast(BF16)[:, (ch % 4) * 1024:(ch % 4) * 1024 + 1024]

        tt0, vv0 = tt.n, vv.n

        def teA(t):
            t.wait_ge(vv.h, vv0)
            i = 0
            for tb in range(8):
                for ch in range(16):
                    if i >= 2:
                        j = i - 2
                        t.wait_ge(vv.h, vv0 + (j // 16) * 17 + (j % 16) + 1)
                    t.transpose(PB[2 + i % 2][:, 0:128],
                                x1_ap(ch)[:, tb * 128:(tb + 1) * 128], IDENT[:])
                    tt.inc(t.last)
                    i += 1
        self.t(teA)

        def veA(v):
            i = 0
            for tb in range(8):
                for ch in range(16):
                    v.wait_ge(tt.h, tt0 + i + 1)
                    v.tensor_copy(SC[:, ch * 128:(ch + 1) * 128],
                                  PB[2 + i % 2][:, 0:128])
                    vv.inc(v.last)
                    i += 1
                v.wait_ge(vv.h, vv.n)
                v.tensor_reduce(MISC2[:, 44 + tb:45 + tb], SC[:, 0:2048],
                                AX.X, ALU.max, apply_absolute_value=True)
                vv.inc(v.last)
                v.wait_ge(vv.h, vv.n)
        self.v(veA)

        def veS(v):
            for tb in range(8):
                v.tensor_scalar(MISC2[:, 44 + tb:45 + tb],
                                MISC2[:, 44 + tb:45 + tb], 1e-5, None, ALU.max)
                vv.inc(v.last)
            v.wait_ge(vv.h, vv.n)
            for tb in range(8):
                v.reciprocal(MISC2[:, 52 + tb:53 + tb], MISC2[:, 44 + tb:45 + tb])
                vv.inc(v.last)
            v.wait_ge(vv.h, vv.n)
            for tb in range(8):
                v.tensor_scalar(MISC2[:, 52 + tb:53 + tb],
                                MISC2[:, 52 + tb:53 + tb], 127.0, None, ALU.mult)
                vv.inc(v.last)
        self.v(veS)
        vv_s = vv.n

        self._scol_bcast(MISC2, 52, 8, 0, vv_s)

        def veK(v):
            v.wait_ge(vv.h, vv.n)
            for ch in range(16):
                v.tensor_tensor(SC[:, 0:1024], x1_ap(ch), SD[:, 0:1024], ALU.mult)
                vv.inc(v.last)
                v.wait_ge(vv.h, vv.n)
                v.tensor_copy(SC[:, 1024:2048].bitcast(I32), SC[:, 0:1024])
                vv.inc(v.last)
                v.wait_ge(vv.h, vv.n)
                v.tensor_copy(kx_ap(ch), SC[:, 1024:2048].bitcast(I32))
                vv.inc(v.last)
                v.wait_ge(vv.h, vv.n)
        self.v(veK)
        vv_kx = vv.n

        FFC = FF // 128
        RLOC = FF // NCORES // 128
        dsw = self.dsw
        base_w = [dsw[k].n for k in range(4)]
        base_tt = tt.n
        base_dsa = dsa.n

        schH = []
        ttc, vvc, ssc, dgc = tt.n, vv.n, ss.n, dg.n
        for hc in range(FFC):
            e = {}
            ttc += 32; e["tt_mm"] = ttc
            vvc += 4; e["vv_t2"] = vvc
            ssc += 1; e["ss_g"] = ssc
            dgc += 16; e["dg_h"] = dgc
            ttc += 8; e["tt_tr"] = ttc
            vvc += 16; e["vv_hm"] = vvc
            schH.append(e)

        base_m = [self.dsm[k].n for k in range(2)]

        def syM(y):
            i = 0
            for hc in range(FFC):
                rank, lc = hc // RLOC, hc % RLOC
                if hc >= 2:
                    y.wait_ge(vv.h, schH[hc - 2]["vv_t2"])
                y.dma_start(out=MISC[:, 8 + hc % 2:9 + hc % 2],
                            in_=D["bs_sc_all"][rank, lc * 128:(lc + 1) * 128, :])
                self.dsm[hc % 2].inc(y.last, 16)
                for c in range(16):
                    if i >= 4:
                        j = i - 4
                        y.wait_ge(tt.h, base_tt + (j // 16) * 40 + (j % 16 + 1) * 2)
                    y.dma_start(out=self.W[i % 4][:].bitcast(BF16)[:, 0:128],
                                in_=D["bt_w1_all"][rank, c * 128:(c + 1) * 128,
                                                   lc * 128:(lc + 1) * 128])
                    dsw[i % 4].inc(y.last, 16)
                    i += 1
        self.y(syM)

        def teM(t):
            i = 0
            for hc in range(FFC):
                e = schH[hc]
                for c in range(16):
                    t.wait_ge(dsw[i % 4].h, base_w[i % 4] + 16 * (i // 4 + 1))
                    if c == 0:
                        if hc == 0:
                            t.wait_ge(vv.h, vv_kx)
                            t.wait_ge(cc.h, cc_need)
                        else:
                            t.wait_ge(vv.h, schH[hc - 1]["vv_t2"])
                    for f in range(2):
                        t.matmul(PB[f][:], self.W[i % 4][:].bitcast(BF16)[:, 0:128],
                                 kx_ap(c)[:, f * 512:(f + 1) * 512],
                                 start=(c == 0), stop=(c == 15))
                        tt.inc(t.last)
                    i += 1
                t.wait_ge(ss.h, e["ss_g"])
                for tb in range(8):
                    if tb >= 2:
                        t.wait_ge(vv.h, e["vv_t2"] + 2 * (tb - 1))
                    t.transpose(PB[2 + tb % 2][:, 0:128],
                                SB_[:, tb * 128:(tb + 1) * 128], IDENT[:])
                    tt.inc(t.last)
        self.t(teM)

        def veM(v):
            for hc in range(FFC):
                e = schH[hc]
                v.wait_ge(tt.h, e["tt_mm"])
                v.wait_ge(self.dsm[hc % 2].h, base_m[hc % 2] + 16 * (hc // 2 + 1))
                for f in range(2):
                    v.tensor_scalar(SC[:, f * 512:(f + 1) * 512], PB[f][:],
                                    MISC[:, 8 + hc % 2:9 + hc % 2], None, ALU.mult)
                    vv.inc(v.last)
                v.wait_ge(vv.h, vv.n)
                for f in range(2):
                    v.tensor_tensor(SA[:, f * 512:(f + 1) * 512],
                                    SC[:, f * 512:(f + 1) * 512],
                                    SD[:, 1024 + f * 512:1024 + (f + 1) * 512],
                                    ALU.mult)
                    vv.inc(v.last)
                assert vv.n == e["vv_t2"]
                for tb in range(8):
                    v.wait_ge(tt.h, e["tt_tr"] - 8 + tb + 1)
                    v.tensor_reduce(MISC2[:, 60:61], PB[2 + tb % 2][:, 0:128],
                                    AX.X, ALU.max, apply_absolute_value=True)
                    vv.inc(v.last)
                    v.wait_ge(vv.h, vv.n)
                    if hc == 0:
                        v.tensor_copy(MISC2[:, 44 + tb:45 + tb], MISC2[:, 60:61])
                    else:
                        v.tensor_tensor(MISC2[:, 44 + tb:45 + tb],
                                        MISC2[:, 44 + tb:45 + tb],
                                        MISC2[:, 60:61], ALU.max)
                    vv.inc(v.last)
                    v.wait_ge(vv.h, vv.n)
                assert vv.n == e["vv_hm"]
        self.v(veM)

        def seM(s):
            for hc in range(FFC):
                e = schH[hc]
                s.wait_ge(vv.h, e["vv_t2"])
                if hc >= 1:
                    s.wait_ge(dg.h, schH[hc - 1]["dg_h"])
                    s.wait_ge(tt.h, schH[hc - 1]["tt_tr"])
                s.activation(SB_[:, 0:1024], SA[:, 0:1024], ACT.Gelu_apprx_tanh)
                ss.inc(s.last)
                assert ss.n == e["ss_g"]
        self.s(seM)

        def gyM(g):
            for hc in range(FFC):
                e = schH[hc]
                g.wait_ge(ss.h, e["ss_g"])
                g.dma_start(out=D["st_hh"][hc * 128:(hc + 1) * 128, :],
                            in_=SB_[:, 0:1024])
                dg.inc(g.last, 16)
                assert dg.n == e["dg_h"]
        self.g(gyM)

        def veS2(v):
            v.wait_ge(vv.h, vv.n)
            for tb in range(8):
                v.tensor_scalar(MISC2[:, 44 + tb:45 + tb],
                                MISC2[:, 44 + tb:45 + tb], 1e-5, None, ALU.max)
                vv.inc(v.last)
            v.wait_ge(vv.h, vv.n)
            for tb in range(8):
                v.reciprocal(MISC2[:, 52 + tb:53 + tb], MISC2[:, 44 + tb:45 + tb])
                vv.inc(v.last)
            v.wait_ge(vv.h, vv.n)
            for tb in range(8):
                v.tensor_scalar(MISC2[:, 52 + tb:53 + tb],
                                MISC2[:, 52 + tb:53 + tb], 127.0, None, ALU.mult)
                vv.inc(v.last)
        self.v(veS2)
        self._scol_bcast(MISC2, 52, 8, 1, self.vv.n)

        dsp = self.dsp
        dsa_p2 = dsp.n
        dg_p2base = dg.n
        ss_p2 = ss.n
        self._p2vv = []
        vvc = self.vv.n
        for hc in range(FFC):
            vvc += 3
            self._p2vv.append(vvc)

        def syP(y):
            for hc in range(FFC):
                y.wait_ge(ss.h, ss_p2)
                y.wait_ge(vv.h, vv.n if hc == 0 else self._p2vv[hc - 1])
                y.dma_start(out=SA[:, 0:1024], in_=D["st_hh"][hc * 128:(hc + 1) * 128, :])
                dsp.inc(y.last, 16)
        self.y(syP)

        def veP(v):
            for hc in range(FFC):
                v.wait_ge(dsp.h, dsa_p2 + 16 * (hc + 1))
                if hc >= 1:
                    v.wait_ge(dg.h, dg_p2base + 16 * hc)
                v.tensor_tensor(SC[:, 0:1024], SA[:, 0:1024], SD[:, 0:1024], ALU.mult)
                vv.inc(v.last)
                v.wait_ge(vv.h, vv.n)
                v.tensor_copy(SC[:, 1024:2048].bitcast(I32), SC[:, 0:1024])
                vv.inc(v.last)
                v.wait_ge(vv.h, vv.n)
                v.tensor_copy(SBF[:, 0:1024], SC[:, 1024:2048].bitcast(I32))
                vv.inc(v.last)
                assert vv.n == self._p2vv[hc]
                v.wait_ge(vv.h, vv.n)
        self.v(veP)

        def gyP(g):
            for hc in range(FFC):
                g.wait_ge(vv.h, self._p2vv[hc])
                g.dma_start(out=D["st_kh"][hc * 128:(hc + 1) * 128, :],
                            in_=SBF[:, 0:1024])
                dg.inc(g.last, 16)
        self.g(gyP)
        dg_p2 = dg.n

        base_tt2 = tt.n
        schM2 = []
        ttc = tt.n
        for qd in range(4):
            for hc in range(FFC):
                ttc += 8
                schM2.append(ttc)

        dsb = self.dsn
        bb = [dsb[k].n for k in range(2)]

        def syN(g):
            i = 0
            for qd in range(4):
                for hc in range(FFC):
                    if i >= 2:
                        g.wait_ge(tt.h, schM2[i - 2])
                    if i == 0:
                        g.wait_ge(dg.h, dg_p2)
                    g.dma_start(out=SBF[:, (i % 2) * 1024:(i % 2) * 1024 + 1024],
                                in_=D["st_kh"][hc * 128:(hc + 1) * 128, :])
                    dsb[i % 2].inc(g.last, 16)
                    i += 1
        self.g(syN)

        base_w2 = [dsw[k].n for k in range(4)]

        def syN2(y):
            i = 0
            for qd in range(4):
                for hc in range(FFC):
                    for q in range(4):
                        oc = qd * 4 + q
                        rank2, lcb = oc // 2, oc % 2
                        if i >= 4:
                            y.wait_ge(tt.h, base_tt2 + (i - 3) * 2)
                        y.dma_start(out=self.W[i % 4][:].bitcast(BF16)[:, 0:128],
                                    in_=D["bt_w2_all"][rank2,
                                                       hc * 128:(hc + 1) * 128,
                                                       lcb * 128:(lcb + 1) * 128])
                        dsw[i % 4].inc(y.last, 16)
                        i += 1
        self.y(syN2)

        vv_ep = []
        vvc = self.vv.n
        for qd in range(4):
            vvc += 16
            vv_ep.append(vvc)

        PSL = [PB[0], PB[1], PB[2], PB[3], PB[4], PB[5], PN[0], PN[1]]

        def teN(t):
            i = 0
            iw = 0
            for qd in range(4):
                for hc in range(FFC):
                    t.wait_ge(dsb[i % 2].h, bb[i % 2] + 16 * (i // 2 + 1))
                    if hc == 0 and qd > 0:
                        t.wait_ge(vv.h, vv_ep[qd - 1])
                    for q in range(4):
                        t.wait_ge(dsw[iw % 4].h, base_w2[iw % 4] + 16 * (iw // 4 + 1))
                        for f in range(2):
                            t.matmul(PSL[q * 2 + f][:],
                                     self.W[iw % 4][:].bitcast(BF16)[:, 0:128],
                                     SBF[:, (i % 2) * 1024 + f * 512:(i % 2) * 1024 + (f + 1) * 512],
                                     start=(hc == 0), stop=(hc == FFC - 1))
                            tt.inc(t.last)
                        iw += 1
                    i += 1
        self.t(teN)

        dso = self.dso
        dsa_q = dso.n

        def syO(s):
            for qd in range(4):
                for q in range(4):
                    oc = qd * 4 + q
                    rank2, lcb = oc // 2, oc % 2
                    if qd >= 1:
                        s.wait_ge(vv.h, vv_ep[qd - 1])
                    s.dma_start(out=MISC[:, 12 + q:13 + q],
                                in_=D["bs_sc_all"][rank2,
                                                   1024 + lcb * 128:1024 + (lcb + 1) * 128, :])
                    dso.inc(s.last, 16)
        self.s(syO)

        def veN(v):
            for qd in range(4):
                v.wait_ge(tt.h, schM2[(qd + 1) * FFC - 1])
                v.wait_ge(dso.h, dsa_q + 16 * 4 * (qd + 1))
                for q in range(4):
                    oc = qd * 4 + q
                    for f in range(2):
                        v.tensor_scalar(SC[:, f * 512:(f + 1) * 512],
                                        PSL[q * 2 + f][:],
                                        MISC[:, 12 + q:13 + q], None, ALU.mult)
                        vv.inc(v.last)
                    v.wait_ge(vv.h, vv.n)
                    v.tensor_tensor(SC[:, 1024:2048], SC[:, 0:1024],
                                    SD[:, 1024:2048], ALU.mult)
                    vv.inc(v.last)
                    v.wait_ge(vv.h, vv.n)
                    v.tensor_tensor(x1_ap(oc), SC[:, 1024:2048], x1_ap(oc),
                                    ALU.add)
                    vv.inc(v.last)
                    v.wait_ge(vv.h, vv.n)
                assert vv.n == vv_ep[qd]
        self.v(veN)

        def gyDbg(g):
            g.wait_ge(vv.h, self.vv.n)
            for ch in range(16):
                g.dma_start(out=D["st_x2own"][ch * 128:(ch + 1) * 128, :],
                            in_=x1_ap(ch))
                dg.inc(g.last, 16)
        self.g(gyDbg)

        self._layernorm(x1_ap)
        self.x2_ap = x1_ap

        def gyX(g):
            g.wait_ge(vv.h, self.vv.n)
            for ch in range(16):
                g.dma_start(out=D["bx2"][ch * 128:(ch + 1) * 128, :],
                            in_=x1_ap(ch))
                dg.inc(g.last, 16)
        self.g(gyX)

    def ph_l1_attn(self):
        D = self.D
        tt, vv, ss, dg, cc = self.tt, self.vv, self.ss, self.dg, self.cc
        PAIRS = [[0, 1], [2, 3], [4, 5], [6, 7]]

        def gA(g):
            g.wait_ge(dg.h, dg.n)
            g.collective_compute("AllGather", ALU.bypass, replica_groups=PAIRS,
                                 ins=[D["bx2"][:]], outs=[D["bx2_all"][:]])
            cc.inc(g.last)
        self.g(gA)
        self.cc_x2 = cc.n

        self._l1_inproj()
        self._l1_conv_xproj()
        self._l1_dt()
        self._l1_scan()
        self._l1_outproj()

    def _l1_inproj(self):
        D = self.D
        tt, vv, ss, dg = self.tt, self.vv, self.ss, self.dg
        dsa = self.dsa
        PB = self.PB
        SA = self.SA
        x2f = lambda c: self.slot(c)

        dsa0 = dsa.n

        def syL(y):
            y.wait_ge(self.cc.h, self.cc_x2)
            y.wait_ge(vv.h, vv.n)
            for c in range(16):
                for par in range(2):
                    y.dma_start(
                        out=x2f(c).rearrange("p (g f) -> p g f", f=128)[:, par::2, :],
                        in_=D["bx2_all"][par, c * 128:(c + 1) * 128, :]
                        .rearrange("p (g f) -> p g f", f=128))
                    dsa.inc(y.last, 16)
        self.y(syL)
        dsa_x2 = dsa.n

        items = []
        evac = []
        vvc = vv.n
        dsn = self.dsn
        base_n = [dsn[k].n for k in range(2)]
        for po in range(32):
            for tq in range(4):
                for c in range(16):
                    mw = []
                    if c == 0:
                        mw = [(dsa, dsa_x2)]
                        idx = po * 4 + tq
                        if idx >= 2:
                            mw.append((vv, evac[idx - 2]["vv"]))
                    items.append(dict(
                        w=D["inproj_sl"][c * 128:(c + 1) * 128,
                                         po * 128:(po + 1) * 128],
                        m=x2f(c)[:, tq * 512:(tq + 1) * 512],
                        o=PB[(po * 4 + tq) % 2][:],
                        start=(c == 0), stop=(c == 15), mwait=mw))
                vvc += 1
                evac.append(dict(vv=vvc))
        base_tt = self.tt.n
        self.mm_stream(items)

        def veE(v):
            for idx in range(128):
                po, tq = idx // 4, idx % 4
                v.wait_ge(tt.h, base_tt + (idx + 1) * 16)
                if idx >= 2:
                    v.wait_ge(dsn[idx % 2].h, base_n[idx % 2] + 16 * ((idx - 2) // 2 + 1))
                v.tensor_copy(SA[:, (idx % 2) * 512:(idx % 2) * 512 + 512],
                              PB[idx % 2][:])
                vv.inc(v.last)
        self.v(veE)

        def gyE(g):
            for idx in range(128):
                po, tq = idx // 4, idx % 4
                g.wait_ge(vv.h, evac[idx]["vv"])
                dst = D["st_hid"] if po < 16 else D["st_gate"]
                row = (po % 16) * 128
                g.dma_start(out=dst[row:row + 128, tq * 512:(tq + 1) * 512],
                            in_=SA[:, (idx % 2) * 512:(idx % 2) * 512 + 512])
                dsn[idx % 2].inc(g.last, 16)
        self.g(gyE)

    def _l1_conv_xproj(self):
        D = self.D
        tt, vv, ss, dg, cc = self.tt, self.vv, self.ss, self.dg, self.cc
        dsa, dsc = self.dsa, self.dsc
        PB, PN = self.PB, self.PN
        SA, SB_, SC, MISC2 = self.SA, self.SB_, self.SC, self.MISC2
        PSL = [PB[0][:], PB[1][:], PB[2][:], PB[3][:], PB[4][:], PB[5][:], PN[0][:], PN[1][:]]

        dg_in = None
        dsn_in = [self.dsn[k].n for k in range(2)]
        dsm = self.dsm
        base_cm = [dsm[k].n for k in range(2)]
        sch = []
        vvc, ssc = vv.n, ss.n
        for ct in range(16):
            e = {}
            e["dsa"] = (ct % 2, base_cm[ct % 2] + 48 * (ct // 2 + 1))
            vvc += 4
            e["vv_conv"] = vvc
            ssc += 1
            e["ss"] = ssc
            sch.append(e)

        dsw = self.dsw
        base_w = [dsw[k].n for k in range(4)]
        ttx0_pre = tt.n

        def syC(y):
            i = 0
            for ct in range(16):
                if ct >= 2:
                    y.wait_ge(ss.h, sch[ct - 2]["ss"])
                if ct == 0:
                    y.wait_ge(self.dsn[0].h, dsn_in[0])
                    y.wait_ge(self.dsn[1].h, dsn_in[1])
                y.dma_start(out=SA[:, 0:2048] if ct % 2 == 0 else SB_[:, 0:2048],
                            in_=D["st_hid"][ct * 128:(ct + 1) * 128, :])
                dsm[ct % 2].inc(y.last, 16)
                y.dma_start(out=MISC2[:, (ct % 2) * 5:(ct % 2) * 5 + 4],
                            in_=D["convw"][ct * 128:(ct + 1) * 128, :])
                dsm[ct % 2].inc(y.last, 16)
                y.dma_start(out=MISC2[:, 11 + ct % 2:12 + ct % 2],
                            in_=D["convb"][ct * 128:(ct + 1) * 128, :])
                dsm[ct % 2].inc(y.last, 16)
                for pp in range(2):
                    if i >= 4:
                        j = i - 4
                        y.wait_ge(tt.h, ttx0_pre + (j // 2) * 8 + (j % 2 + 1) * 4)
                    y.dma_start(out=self.W[i % 4][:, 0:(128 if pp == 0 else 32)],
                                in_=D["xproj"][ct * 128:(ct + 1) * 128, 0:128]
                                if pp == 0 else
                                D["xproj"][ct * 128:(ct + 1) * 128, 128:160])
                    dsw[i % 4].inc(y.last, 16)
                    i += 1
        self.y(syC)


        ttx0 = tt.n
        ttx = []
        ttc2 = ttx0
        for ct in range(16):
            ttc2 += 8
            ttx.append(ttc2)

        dg_hs = []
        dgc0 = dg.n
        for ct in range(16):
            dgc0 += 16
            dg_hs.append(dgc0)

        def veC(v):
            for ct in range(16):
                e = sch[ct]
                buf = SA if ct % 2 == 0 else SB_
                v.wait_ge(dsm[e["dsa"][0]].h, e["dsa"][1])
                if ct >= 1:
                    v.wait_ge(vv.h, vv.n)
                    v.wait_ge(ss.h, sch[ct - 1]["ss"])
                    v.wait_ge(tt.h, ttx[ct - 1])
                    v.wait_ge(dg.h, dg_hs[ct - 1])
                v.tensor_scalar(SC[:, 0:2048], buf[:, 0:2048],
                                MISC2[:, (ct % 2) * 5 + 3:(ct % 2) * 5 + 4],
                                None, ALU.mult)
                vv.inc(v.last)
                for j in range(3):
                    k = j + 1
                    v.wait_ge(vv.h, vv.n)
                    v.scalar_tensor_tensor(
                        SC[:, k:2048], buf[:, 0:2048 - k],
                        MISC2[:, (ct % 2) * 5 + 3 - k:(ct % 2) * 5 + 4 - k],
                        SC[:, k:2048], ALU.mult, ALU.add)
                    vv.inc(v.last)
                assert vv.n == e["vv_conv"]
        self.v(veC)

        def seC(s):
            for ct in range(16):
                e = sch[ct]
                s.wait_ge(vv.h, e["vv_conv"])
                if ct >= 1:
                    s.wait_ge(dg.h, dg_hs[ct - 1])
                s.activation(SC[:, 0:2048], SC[:, 0:2048], ACT.Silu,
                             bias=MISC2[:, 11 + ct % 2:12 + ct % 2])
                ss.inc(s.last)
                assert ss.n == e["ss"]

        self.s(seC)

        def gyH(g):
            for ct in range(16):
                e = sch[ct]
                g.wait_ge(ss.h, e["ss"])
                g.dma_start(out=D["st_hs"][ct * 128:(ct + 1) * 128, :],
                            in_=SC[:, 0:2048])
                dg.inc(g.last, 16)
        self.g(gyH)


        def teX(t):
            i = 0
            for ct in range(16):
                e = sch[ct]
                t.wait_ge(ss.h, e["ss"])
                for pp in range(2):
                    t.wait_ge(dsw[i % 4].h, base_w[i % 4] + 16 * (i // 4 + 1))
                    for tq in range(4):
                        wsl = self.W[i % 4][:, 0:(128 if pp == 0 else 32)]
                        out = PSL[tq] if pp == 0 else PSL[4 + tq][0:32, :]
                        t.matmul(out, wsl, SC[:, tq * 512:(tq + 1) * 512],
                                 start=(ct == 0), stop=(ct == 15))
                        tt.inc(t.last)
                    i += 1
                assert tt.n == ttx[ct]
        self.t(teX)

        def veX(v):
            v.wait_ge(tt.h, ttx[-1])
            for tq in range(4):
                v.tensor_copy(SA[:, tq * 512:(tq + 1) * 512], PSL[tq])
                vv.inc(v.last)
            for tq in range(4):
                v.tensor_copy(SB_[0:32, tq * 512:(tq + 1) * 512],
                              PSL[4 + tq][0:32, :])
                vv.inc(v.last)
        self.v(veX)
        vv_sp = vv.n

        def gyX2(g):
            g.wait_ge(vv.h, vv_sp)
            g.dma_start(out=D["bssm"][0:128, :], in_=SA[:, 0:2048])
            dg.inc(g.last, 16)
            g.dma_start(out=D["bssm"][128:160, :], in_=SB_[0:32, 0:2048])
            dg.inc(g.last, 16)
            g.wait_ge(dg.h, dg.n)
            g.collective_compute("AllReduce", ALU.add,
                                 replica_groups=[[0, 1], [2, 3], [4, 5], [6, 7]],
                                 ins=[D["bssm"][:]], outs=[D["bssm_all"][:]])
            cc.inc(g.last)
        self.g(gyX2)
        self.cc_sp = cc.n

    def _l1_dt(self):
        D = self.D
        tt, vv, ss, dg = self.tt, self.vv, self.ss, self.dg
        dsc, dsa = self.dsc, self.dsa
        PB, PN = self.PB, self.PN
        SA, SB_, SC, SD, MISC = self.SA, self.SB_, self.SC, self.SD, self.MISC
        NRM = self.CSP
        BCC = self.BCC

        dsc0 = dsc.n

        def syD(y):
            y.wait_ge(self.cc.h, self.cc_sp)
            y.wait_ge(vv.h, vv.n)
            y.wait_ge(ss.h, ss.n)
            y.wait_ge(tt.h, tt.n)
            y.dma_start(out=SA[:, 0:2048], in_=D["bssm_all"][0:128, :])
            dsc.inc(y.last, 16)
            y.dma_start(out=SB_[0:16, 0:2048], in_=D["bssm_all"][128:144, :])
            dsc.inc(y.last, 16)
            y.dma_start(out=SB_[32:48, 0:2048], in_=D["bssm_all"][144:160, :])
            dsc.inc(y.last, 16)
            y.dma_start(out=BCC[:], in_=D["bccols"][:])
            dsc.inc(y.last, 16)
        self.y(syD)
        dsc_l = dsc.n

        def veZ(v):
            v.wait_ge(dsc.h, dsc_l)
            v.memset(SD[0:64, 0:2048], 0.0)
            vv.inc(v.last)
        self.v(veZ)
        vv_z = vv.n

        def seR(s):
            s.wait_ge(vv.h, vv_z)
            s.activation(SC[:, 0:2048], SA[:, 0:2048], ACT.Square)
            ss.inc(s.last)
            s.activation(SD[0:16, 0:2048], SB_[0:16, 0:2048], ACT.Square)
            ss.inc(s.last)
            s.activation(SD[32:48, 0:2048], SB_[32:48, 0:2048], ACT.Square)
            ss.inc(s.last)
        self.s(seR)
        ss_r = ss.n

        def teR(t):
            t.wait_ge(ss.h, ss_r)
            for f in range(4):
                t.matmul(PB[f][0:1, :], MISC[:, 0:1],
                         SC[:, f * 512:(f + 1) * 512], start=True, stop=True)
                tt.inc(t.last)
        self.t(teR)
        tt_r = tt.n

        def veR(v):
            v.wait_ge(tt.h, tt_r)
            for f in range(4):
                v.tensor_copy(NRM[0:1, f * 512:(f + 1) * 512], PB[f][0:1, :])
                vv.inc(v.last)
        self.v(veR)
        vv_r = vv.n

        def seR2(s):
            s.wait_ge(vv.h, vv_r)
            s.activation(SC[0:1, 0:2048], NRM[0:1, 0:2048], ACT.Sqrt,
                         scale=1.0 / DTR, bias=MISC[0:1, 3:4])
            ss.inc(s.last)
        self.s(seR2)
        ss_r2 = ss.n

        def veR2(v):
            v.wait_ge(ss.h, ss_r2)
            v.reciprocal(NRM[0:1, 0:2048], SC[0:1, 0:2048])
            vv.inc(v.last)
        self.v(veR2)
        vv_r2 = vv.n

        def teR2(t):
            t.wait_ge(vv.h, vv_r2)
            for f in range(4):
                t.matmul(PB[f][:, 0:512], self.ONER[0:1, :],
                         NRM[0:1, f * 512:(f + 1) * 512], start=True, stop=True)
                tt.inc(t.last)
        self.t(teR2)
        tt_r2 = tt.n

        def veR3(v):
            v.wait_ge(tt.h, tt_r2)
            for f in range(4):
                v.tensor_tensor(SA[:, f * 512:(f + 1) * 512],
                                SA[:, f * 512:(f + 1) * 512], PB[f][:], ALU.mult)
                vv.inc(v.last)
        self.v(veR3)
        vv_dtn = vv.n

        for which in range(2):
            vv_pre = vv.n

            def teB(t, which=which, vv_pre=vv_pre):
                t.wait_ge(ss.h, ss_r)
                t.wait_ge(vv.h, vv_pre)
                for f in range(4):
                    t.matmul(PB[f][0:1, :], BCC[:, which:which + 1],
                             SD[0:64, f * 512:(f + 1) * 512], start=True, stop=True)
                    tt.inc(t.last)
            self.t(teB)
            tt_b = tt.n

            def veB(v, which=which):
                v.wait_ge(tt.h, tt_b)
                for f in range(4):
                    v.tensor_copy(NRM[0:1, f * 512:(f + 1) * 512], PB[f][0:1, :])
                    vv.inc(v.last)
            self.v(veB)
            vv_b = vv.n

            def seB(s, which=which):
                s.wait_ge(vv.h, vv_b)
                s.activation(SC[0:1, 0:2048], NRM[0:1, 0:2048], ACT.Sqrt,
                             scale=1.0 / NST, bias=MISC[0:1, 3:4])
                ss.inc(s.last)
            self.s(seB)
            ss_b = ss.n

            def veB2(v, which=which):
                v.wait_ge(ss.h, ss_b)
                v.reciprocal(NRM[0:1, 0:2048], SC[0:1, 0:2048])
                vv.inc(v.last)
            self.v(veB2)
            vv_b2 = vv.n

            def teB2(t, which=which):
                t.wait_ge(vv.h, vv_b2)
                for f in range(4):
                    t.matmul(PB[f][0:32, 0:512], self.ONER[0:1, 0:32],
                             NRM[0:1, f * 512:(f + 1) * 512], start=True, stop=True)
                    tt.inc(t.last)
            self.t(teB2)
            tt_b2 = tt.n

            def veB3(v, which=which):
                v.wait_ge(tt.h, tt_b2)
                r0 = 0 if which == 0 else 32
                for f in range(4):
                    v.tensor_tensor(SB_[r0:r0 + 16, f * 512:(f + 1) * 512],
                                    SB_[r0:r0 + 16, f * 512:(f + 1) * 512],
                                    PB[f][0:16, :], ALU.mult)
                    vv.inc(v.last)
            self.v(veB3)
        vv_bcn = vv.n

        dsw = self.dsw
        base_w = [dsw[k].n for k in range(4)]
        base_tt = tt.n
        dsm = self.dsm
        base_m = [dsm[k].n for k in range(2)]

        sch = []
        vvc, ssc, dgc = vv.n, ss.n, dg.n
        for idx in range(64):
            e = {}
            ssc += 2; e["ss_ex"] = ssc
            vvc += 1; e["vv_l1"] = vvc
            ssc += 2; e["ss"] = ssc
            vvc += 1; e["vv"] = vvc
            dgc += 16; e["dg"] = dgc
            sch.append(e)

        def syT(y):
            for po in range(16):
                if po >= 2:
                    y.wait_ge(vv.h, sch[(po - 2) * 4 + 3]["vv"])
                y.dma_start(out=MISC[:, 16 + po % 2:17 + po % 2],
                            in_=D["dtb"][po * 128:(po + 1) * 128, :])
                dsm[po % 2].inc(y.last, 16)
                i = po
                if i >= 4:
                    j = i - 4
                    y.wait_ge(tt.h, base_tt + (j + 1) * 4)
                y.dma_start(out=self.W[i % 4][:, 0:128],
                            in_=D["dtw"][0:128, po * 128:(po + 1) * 128])
                dsw[i % 4].inc(y.last, 16)
        self.y(syT)

        def teT(t):
            for po in range(16):
                t.wait_ge(dsw[po % 4].h, base_w[po % 4] + 16 * (po // 4 + 1))
                if po == 0:
                    t.wait_ge(vv.h, vv_bcn)
                for tq in range(4):
                    idx = po * 4 + tq
                    if idx >= 2:
                        t.wait_ge(ss.h, sch[idx - 2]["ss"])
                    t.matmul(PB[idx % 2][:], self.W[po % 4][:, 0:128],
                             SA[:, tq * 512:(tq + 1) * 512], start=True, stop=True)
                    tt.inc(t.last)
        self.t(teT)

        def seT(s):
            for po in range(16):
                for tq in range(4):
                    idx = po * 4 + tq
                    e = sch[idx]
                    hb = (idx % 2) * 1024
                    s.wait_ge(tt.h, base_tt + idx + 1)
                    s.wait_ge(dsm[po % 2].h, base_m[po % 2] + 16 * (po // 2 + 1))
                    if idx >= 2:
                        s.wait_ge(vv.h, sch[idx - 2]["vv"])
                    s.activation(SC[:, hb:hb + 512], PB[idx % 2][:], ACT.Abs,
                                 bias=MISC[:, 16 + po % 2:17 + po % 2])
                    ss.inc(s.last)
                    s.wait_ge(ss.h, ss.n)
                    s.activation(SC[:, hb + 512:hb + 1024], SC[:, hb:hb + 512],
                                 ACT.Exp, scale=-1.0)
                    ss.inc(s.last)
                    assert ss.n == e["ss_ex"]
                    s.wait_ge(vv.h, e["vv_l1"])
                    s.activation(SC[:, hb + 512:hb + 1024],
                                 SC[:, hb + 512:hb + 1024], ACT.Ln)
                    ss.inc(s.last)
                    s.activation(SC[:, hb:hb + 512], PB[idx % 2][:], ACT.Relu,
                                 bias=MISC[:, 16 + po % 2:17 + po % 2])
                    ss.inc(s.last)
                    assert ss.n == e["ss"]
        self.s(seT)

        def veT(v):
            for po in range(16):
                for tq in range(4):
                    idx = po * 4 + tq
                    e = sch[idx]
                    hb = (idx % 2) * 1024
                    v.wait_ge(ss.h, e["ss_ex"])
                    v.tensor_scalar(SC[:, hb + 512:hb + 1024],
                                    SC[:, hb + 512:hb + 1024], 1.0, None, ALU.add)
                    vv.inc(v.last)
                    assert vv.n == e["vv_l1"]
                    v.wait_ge(ss.h, e["ss"])
                    if idx >= 2:
                        v.wait_ge(dg.h, sch[idx - 2]["dg"])
                    v.tensor_tensor(SD[:, (idx % 2) * 512:(idx % 2) * 512 + 512],
                                    SC[:, hb:hb + 512],
                                    SC[:, hb + 512:hb + 1024], ALU.add)
                    vv.inc(v.last)
                    assert vv.n == e["vv"]
        self.v(veT)

        def gyT(g):
            for po in range(16):
                for tq in range(4):
                    idx = po * 4 + tq
                    g.wait_ge(vv.h, sch[idx]["vv"])
                    if idx >= 1:
                        g.wait_ge(dg.h, sch[idx - 1]["dg"])
                    g.dma_start(out=D["st_dt"][po * 128:(po + 1) * 128,
                                               tq * 512:(tq + 1) * 512],
                                in_=SD[:, (idx % 2) * 512:(idx % 2) * 512 + 512])
                    dg.inc(g.last, 16)
                    assert dg.n == sch[idx]["dg"]
        self.g(gyT)

    def _l1_scan(self):
        D = self.D
        tt, vv, ss, dg = self.tt, self.vv, self.ss, self.dg
        dsa = self.dsa
        PB = self.PB
        SB_, MISC, MISC2 = self.SB_, self.MISC, self.MISC2
        CK = 256
        NCK = T // CK
        R = self.R
        bBB = R[:, 0:4096].rearrange("p (t n) -> p t n", n=16)
        bCB = R[:, 4096:8192].rearrange("p (t n) -> p t n", n=16)
        bDA = R[:, 8192:12288]
        bDAv = bDA.rearrange("p (t n) -> p t n", n=16)
        bDBU = R[:, 12288:16384]
        bDBUv = bDBU.rearrange("p (t n) -> p t n", n=16)
        bS = R[:, 16384:20480]
        bSv = bS.rearrange("p (t n) -> p t n", n=16)
        bST = R[:, 20480:20736]
        bA = R[:, 20736:20992]
        bU = R[:, 21504:21760]
        bDT = R[:, 21760:22016]
        bG = R[:, 22016:22272]
        bY2 = R[:, 22528:22784]
        HPB = R[0:64, 22784:24832]
        HPC = R[0:64, 24832:26880]

        dg_dt = dg.n
        gg = self.gg

        def gpH(g):
            g.wait_ge(vv.h, vv.n)
            g.iota(R[:, 26880:28928][0:64, :], pattern=[[-1, 16], [0, 128]],
                   base=0, channel_multiplier=1,
                   allow_small_or_imprecise_dtypes=True)
            g.memset(R[0:64, 28928:28929], 0.0)
            gg.inc(g.last)
            g.wait_ge(gg.h, gg.n)
            g.tensor_scalar(HPB[:], R[0:64, 26880:28928], 0.0, None, ALU.is_equal)
            gg.inc(g.last)
            g.wait_ge(gg.h, gg.n)
            g.iota(R[:, 26880:28928][0:64, :], pattern=[[-1, 16], [0, 128]],
                   base=-32, channel_multiplier=1,
                   allow_small_or_imprecise_dtypes=True)
            g.memset(R[0:64, 28928:28929], 0.0)
            gg.inc(g.last)
            g.wait_ge(gg.h, gg.n)
            g.tensor_scalar(HPC[:], R[0:64, 26880:28928], 0.0, None, ALU.is_equal)
            gg.inc(g.last)
        self.g(gpH)
        gg_hp = gg.n

        dsa_a0 = dsa.n

        def syA(y):
            y.wait_ge(vv.h, vv.n)
            for dt_ in range(16):
                y.dma_start(out=R[:, 20992 + dt_ * 16:20992 + (dt_ + 1) * 16],
                            in_=D["Alog"][dt_ * 128:(dt_ + 1) * 128, :])
                dsa.inc(y.last, 16)
            y.dma_start(out=MISC2[:, 48:64],
                        in_=D["Dpar"].rearrange("(c p) one -> p (c one)", p=128),
                        allow_slow_non_contiguous=True)
            dsa.inc(y.last, 16)
        self.y(syA)
        dsa_a = dsa.n

        def seA(s):
            s.wait_ge(dsa.h, dsa_a)
            s.activation(R[:, 20736:20992], R[:, 20992:21248], ACT.Exp)
            ss.inc(s.last)
        self.s(seA)
        ss_a = ss.n

        def veA2(v):
            v.wait_ge(ss.h, ss_a)
            v.tensor_scalar(R[:, 20736:20992], R[:, 20736:20992], -1.0, None,
                            ALU.mult)
            vv.inc(v.last)
            v.wait_ge(vv.h, vv.n)
            v.memset(bST[:], 0.0)
            vv.inc(v.last)
        self.v(veA2)

        sch = []
        vvc, ssc, ttc, dsac, dgc = vv.n, ss.n, tt.n, dsa.n, dg.n
        for ck in range(NCK):
            e = {}
            ttc += 64
            e["tt_bc"] = ttc
            vvc += 64
            e["vv_bc"] = vvc
            tiles = []
            for dt_ in range(16):
                t2 = {}
                dsac += 16 * 3
                t2["dsa"] = dsac
                vvc += 39
                ssc += 2
                t2["vv"] = vvc
                t2["ss"] = ssc
                t2["vv_dtu"] = e["vv_bc"] + sum(
                    39 for _ in range(dt_)) + 1 if False else 0
                dgc += 16
                t2["dg"] = dgc
                tiles.append(t2)
            e["tiles"] = tiles
            sch.append(e)

        def teS(t):
            for ck in range(NCK):
                e = sch[ck]
                if ck == 0:
                    t.wait_ge(vv.h, sch[0]["vv_bc"] - 64)
                    t.wait_ge(gg.h, gg_hp)
                else:
                    t.wait_ge(vv.h, sch[ck - 1]["tiles"][-1]["vv"])
                k = 0
                for n in range(16):
                    for half in range(2):
                        src_ap = SB_[0:64, ck * CK + half * 128:ck * CK + (half + 1) * 128]
                        if k >= 4:
                            t.wait_ge(vv.h, e["vv_bc"] - 64 + k - 2)
                        t.matmul(PB[k % 4][:, 0:128],
                                 HPB[:, n * 128:(n + 1) * 128],
                                 src_ap, start=True, stop=True)
                        tt.inc(t.last)
                        k += 1
                        t.matmul(PB[k % 4][:, 0:128],
                                 HPC[:, n * 128:(n + 1) * 128],
                                 src_ap, start=True, stop=True)
                        tt.inc(t.last)
                        k += 1
        self.t(teS)

        def veS(v):
            for ck in range(NCK):
                e = sch[ck]
                k = 0
                for n in range(16):
                    for half in range(2):
                        v.wait_ge(tt.h, e["tt_bc"] - 64 + k + 1)
                        v.tensor_copy(bBB[:, half * 128:(half + 1) * 128, n],
                                      PB[k % 4][:, 0:128])
                        vv.inc(v.last)
                        k += 1
                        v.wait_ge(tt.h, e["tt_bc"] - 64 + k + 1)
                        v.tensor_copy(bCB[:, half * 128:(half + 1) * 128, n],
                                      PB[k % 4][:, 0:128])
                        vv.inc(v.last)
                        k += 1
                assert vv.n == e["vv_bc"]
                for dt_ in range(16):
                    t2 = e["tiles"][dt_]
                    v.wait_ge(dsa.h, t2["dsa"])
                    if ck > 0 or dt_ > 0:
                        pt = (e["tiles"][dt_ - 1] if dt_ > 0
                              else sch[ck - 1]["tiles"][-1])
                        v.wait_ge(dg.h, pt["dg"])
                    v.tensor_tensor(
                        bDAv[:],
                        bDT[:].rearrange("p (t one) -> p t one", one=1)
                           .broadcast_to([128, CK, 16]),
                        R[:, 20736 + dt_ * 16:20736 + (dt_ + 1) * 16]
                           .rearrange("p (one n) -> p one n", one=1)
                           .broadcast_to([128, CK, 16]),
                        ALU.mult)
                    vv.inc(v.last); v.wait_ge(vv.h, vv.n)
                    v.tensor_tensor(bDT[:], bDT[:], bU[:], ALU.mult)
                    vv.inc(v.last); v.wait_ge(vv.h, vv.n)
                    v.tensor_tensor(
                        bDBUv[:],
                        bDT[:].rearrange("p (t one) -> p t one", one=1)
                           .broadcast_to([128, CK, 16]),
                        bBB[:], ALU.mult)
                    vv.inc(v.last); v.wait_ge(vv.h, vv.n)
                    v.wait_ge(ss.h, t2["ss"] - 1)
                    for n in range(16):
                        v.tensor_tensor_scan(
                            bSv[:, :, n], bDAv[:, :, n], bDBUv[:, :, n],
                            bST[:, dt_ * 16 + n:dt_ * 16 + n + 1],
                            ALU.mult, ALU.add)
                        vv.inc(v.last)
                    v.wait_ge(vv.h, vv.n)
                    for n in range(16):
                        v.tensor_copy(bST[:, dt_ * 16 + n:dt_ * 16 + n + 1],
                                      bSv[:, CK - 1:CK, n])
                        vv.inc(v.last)
                    v.wait_ge(vv.h, vv.n)
                    v.tensor_tensor(bDAv[:], bSv[:], bCB[:], ALU.mult)
                    vv.inc(v.last); v.wait_ge(vv.h, vv.n)
                    v.tensor_reduce(bY2[:], bDAv[:], AX.X, ALU.add)
                    vv.inc(v.last); v.wait_ge(vv.h, vv.n)
                    v.scalar_tensor_tensor(bY2[:], bU[:],
                        MISC2[:, 48 + dt_:49 + dt_], bY2[:],
                        ALU.mult, ALU.add)
                    vv.inc(v.last); v.wait_ge(vv.h, vv.n)
                    v.wait_ge(ss.h, t2["ss"])
                    v.tensor_tensor(bY2[:], bY2[:], bG[:], ALU.mult)
                    vv.inc(v.last)
                    assert vv.n == t2["vv"], (vv.n, t2["vv"])
        self.v(veS)

        def seS(s):
            for ck in range(NCK):
                e = sch[ck]
                for dt_ in range(16):
                    t2 = e["tiles"][dt_]
                    s.wait_ge(vv.h, t2["vv"] - 38)
                    s.activation(bDA[:], bDA[:], ACT.Exp)
                    ss.inc(s.last)
                    s.wait_ge(dsa.h, t2["dsa"])
                    s.activation(bG[:], bG[:], ACT.Silu)
                    ss.inc(s.last)
        self.s(seS)

        def syS(y):
            for ck in range(NCK):
                e = sch[ck]
                for dt_ in range(16):
                    t2 = e["tiles"][dt_]
                    if ck > 0 or dt_ > 0:
                        pt = (e["tiles"][dt_ - 1] if dt_ > 0
                              else sch[ck - 1]["tiles"][-1])
                        y.wait_ge(vv.h, pt["vv"])
                    else:
                        y.wait_ge(vv.h, sch[0]["vv_bc"] - 64)
                        y.wait_ge(dg.h, dg_dt)
                    y.dma_start(out=bU[:],
                                in_=D["st_hs"][dt_ * 128:(dt_ + 1) * 128,
                                               ck * CK:(ck + 1) * CK])
                    dsa.inc(y.last, 16)
                    y.dma_start(out=bDT[:],
                                in_=D["st_dt"][dt_ * 128:(dt_ + 1) * 128,
                                               ck * CK:(ck + 1) * CK])
                    dsa.inc(y.last, 16)
                    y.dma_start(out=bG[:],
                                in_=D["st_gate"][dt_ * 128:(dt_ + 1) * 128,
                                                 ck * CK:(ck + 1) * CK])
                    dsa.inc(y.last, 16)
                    assert dsa.n == t2["dsa"]
        self.y(syS)

        def gyS(g):
            for ck in range(NCK):
                e = sch[ck]
                for dt_ in range(16):
                    t2 = e["tiles"][dt_]
                    g.wait_ge(vv.h, t2["vv"])
                    g.dma_start(out=D["st_yg"][dt_ * 128:(dt_ + 1) * 128,
                                               ck * CK:(ck + 1) * CK],
                                in_=bY2[:])
                    dg.inc(g.last, 16)
                    assert dg.n == t2["dg"]
        self.g(gyS)

    def _l1_outproj(self):
        D = self.D
        tt, vv, ss, dg, cc = self.tt, self.vv, self.ss, self.dg, self.cc
        dsa = self.dsa
        PB = self.PB
        SA, SB_ = self.SA, self.SB_
        dsw = self.dsw
        dsx2 = self.dsx2
        base_w = [dsw[k].n for k in range(4)]
        base_x = [dsx2[k].n for k in range(2)]
        dg_yg = dg.n

        blocks = []
        ttc, vvc, dgc = tt.n, vv.n, dg.n
        base_tt = tt.n
        for t4 in range(4):
            for ocb in range(4):
                e = dict(t4=t4, ocb=ocb)
                ttc += 64
                e["tt"] = ttc
                vvc += 4
                e["vv"] = vvc
                dgc += 16 * 16
                e["dg"] = dgc
                blocks.append(e)

        def syO(y):
            iw = 0
            im = 0
            for bi, e in enumerate(blocks):
                t4, ocb = e["t4"], e["ocb"]
                for ch in range(16):
                    if im >= 2:
                        jb = im - 2
                        y.wait_ge(tt.h, base_tt + jb * 4 + 4)
                    if im == 0:
                        y.wait_ge(dg.h, dg_yg)
                    y.dma_start(out=(SA if im % 2 == 0 else SB_)[:, 0:512],
                                in_=D["st_yg"][ch * 128:(ch + 1) * 128,
                                               t4 * 512:(t4 + 1) * 512])
                    dsx2[im % 2].inc(y.last, 16)
                    im += 1
                    for q in range(4):
                        oc = ocb * 4 + q
                        if iw >= 4:
                            jw = iw - 4
                            y.wait_ge(tt.h, base_tt + jw + 1)
                        y.dma_start(out=self.W[iw % 4][:, 0:128],
                                    in_=D["outproj"][ch * 128:(ch + 1) * 128,
                                                     oc * 128:(oc + 1) * 128])
                        dsw[iw % 4].inc(y.last, 16)
                        iw += 1
        self.y(syO)

        def teO(t):
            iw = 0
            im = 0
            for bi, e in enumerate(blocks):
                for ch in range(16):
                    t.wait_ge(dsx2[im % 2].h, base_x[im % 2] + 16 * (im // 2 + 1))
                    if ch == 0 and bi >= 1:
                        t.wait_ge(vv.h, blocks[bi - 1]["vv"])
                    mbuf = (SA if im % 2 == 0 else SB_)[:, 0:512]
                    im += 1
                    for q in range(4):
                        t.wait_ge(dsw[iw % 4].h, base_w[iw % 4] + 16 * (iw // 4 + 1))
                        t.matmul(PB[q][:], self.W[iw % 4][:, 0:128], mbuf,
                                 start=(ch == 0), stop=(ch == 15))
                        tt.inc(t.last)
                        iw += 1
                assert tt.n == e["tt"]
        self.t(teO)

        def veO(v):
            for bi, e in enumerate(blocks):
                v.wait_ge(tt.h, e["tt"])
                if bi >= 1:
                    v.wait_ge(dg.h, blocks[bi - 1]["dg"])
                for q in range(4):
                    v.tensor_copy(self.SC[:, q * 512:(q + 1) * 512], PB[q][:])
                    vv.inc(v.last)
                assert vv.n == e["vv"]
        self.v(veO)

        def gyO(g):
            for bi, e in enumerate(blocks):
                t4, ocb = e["t4"], e["ocb"]
                g.wait_ge(vv.h, e["vv"])
                for q in range(4):
                    oc = ocb * 4 + q
                    for gi in range(4):
                        gt = t4 * 4 + gi
                        g.dma_start(
                            out=D["bop"][gt % 2, oc * 128:(oc + 1) * 128,
                                         (gt // 2) * 128:(gt // 2 + 1) * 128],
                            in_=self.SC[:, q * 512 + gi * 128:q * 512 + (gi + 1) * 128])
                        dg.inc(g.last, 16)
                assert dg.n == e["dg"]
        self.g(gyO)

        def gR(g):
            g.wait_ge(dg.h, dg.n)
            g.collective_compute("ReduceScatter", ALU.add,
                                 replica_groups=[[0, 1], [2, 3], [4, 5], [6, 7]],
                                 ins=[D["bop"][:]], outs=[D["bop_out"][:]])
            cc.inc(g.last)
        self.g(gR)
        cc_rs = cc.n

        x3_ap = lambda ch: self.half(9 + ch // 2, ch % 2)
        self.x3_ap = x3_ap
        dsa0 = dsa.n

        res_ap = lambda ch: self.half(ch // 2, ch % 2)

        def syR(y):
            y.wait_ge(cc.h, cc_rs)
            y.wait_ge(vv.h, vv.n)
            y.wait_ge(ss.h, ss.n)
            y.wait_ge(dg.h, dg.n)
            for ch in range(16):
                y.dma_start(out=x3_ap(ch),
                            in_=D["bop_out"][ch * 128:(ch + 1) * 128, :])
                dsa.inc(y.last, 16)
                y.dma_start(out=res_ap(ch), in_=D["bx2"][ch * 128:(ch + 1) * 128, :])
                dsa.inc(y.last, 16)
        self.y(syR)
        dsa_r = dsa.n

        def veR(v):
            v.wait_ge(dsa.h, dsa_r)
            for ch in range(16):
                v.tensor_tensor(x3_ap(ch), x3_ap(ch), res_ap(ch), ALU.add)
                vv.inc(v.last)
        self.v(veR)

        self._layernorm(x3_ap)

    def ph_quant_turbo(self):
        D = self.D
        tt, vv, ss, gg = self.tt, self.vv, self.ss, self.gg
        dsa, dso, cc = self.dsa, self.dso, self.cc
        PB, PN = self.PB, self.PN
        SA, SB_, SC, SD, SBF = self.SA, self.SB_, self.SC, self.SD, self.SBF
        MISC2, CSP, R, IDENT = self.MISC2, self.CSP, self.R, self.IDENT

        FWF = FF // NCORES
        CWP = C // NCORES
        chunks = []
        for ti in range(16):
            for cq in range(2):
                sl = (slice(ti * 128, (ti + 1) * 128),
                      slice(cq * 512, (cq + 1) * 512))
                chunks.append(dict(
                    src=D["wfcT_sh"][sl[0], sl[1]],
                    st=D["q_fc_f32"][sl[0], sl[1]],
                    nf=4, fw=512))
        for ti in range(64):
            sl = (slice(ti * 128, (ti + 1) * 128), slice(0, CWP))
            chunks.append(dict(
                src=D["wprT_sh"][sl[0], sl[1]],
                st=D["q_pr_f32"][sl[0], sl[1]],
                nf=2, fw=256))
        NCH = len(chunks)

        Wb = [SA[:, 0:512], SA[:, 512:1024]]
        zbB = [SA[:, 1024:1536], SA[:, 1536:2048]]
        uB = [SB_[:, 0:512], SB_[:, 512:1024]]
        zB = [SB_[:, 1024:1536], SB_[:, 1536:2048]]
        rB = [SC[:, 0:512], SC[:, 512:1024]]
        Eev = SC[0:32, 1024:1536]
        maskB = SC[:, 1536:1664]
        mvalB = SC[:, 1664:1792]
        erB = [self.ER[:, 0:512], self.ER[:, 512:1024]]
        deqB = SD[:, 1024:1536]
        lodB = SD[:, 1536:2048]
        hiB = SBF[:, 0:512]
        loB = SBF[:, 512:1024]
        q8B = [SBF[:, 1024:1280].bitcast(I8), SBF[:, 1280:1536].bitcast(I8)]
        qsB = SBF[:, 1536:1792].bitcast(I8)
        QTIst = R[:, 256:768]
        QTI = self.QTIT[:]
        ISVP = R[:, 0:128]
        SVP = R[:, 128:256]
        BST = CSP[0:8, 384:896]
        bvT = CSP[0:8, 896:1024]
        ibT = CSP[0:8, 1024:1152]
        scT = CSP[0:8, 1152:1280]
        AMX, BASE, BINV = MISC2[:, 0:8], MISC2[:, 8:16], MISC2[:, 16:24]
        BISV, BSV = MISC2[:, 24:32], MISC2[:, 32:40]
        IBS, SCBD = MISC2[:, 40:48], MISC2[:, 48:56]
        MNE = MISC2[:, 56:64]

        dsa0, dso0 = dsa.n, dso.n
        ttc, vvc, ssc, ggc = tt.n, vv.n, ss.n, gg.n
        dsa_c = dsa0 + 64
        vvc += 1
        vv_qti = vvc
        sch = []
        for m, chk in enumerate(chunks):
            nf = chk["nf"]
            e = {"nf": nf, "fw": chk["fw"]}
            e["dsa_w"] = dsa0 + 64 + 16 * (m + 1)
            ssc += 1; e["ss_abs"] = ssc
            ttc += nf; e["tt_ztr"] = ttc
            ttc += 1; e["tt_btr"] = ttc
            ttc += nf; e["tt_bb"] = ttc
            e["tt_err"] = []
            for j in range(16):
                ttc += 1; e["tt_err"].append(ttc)
            ttc += nf; e["tt_et"] = ttc
            ttc += 2; e["tt_ftr"] = ttc
            ttc += 2 * nf; e["tt_fb"] = ttc
            vvc += 1; e["vv_amax"] = vvc
            vvc += 2; e["vv_binv"] = vvc
            vvc += 1; e["vv_btr"] = vvc
            vvc += 1; e["vv_z"] = vvc
            e["vv_u"] = []; e["vv_q"] = []
            for j in range(16):
                vvc += 1; e["vv_u"].append(vvc)
                vvc += 1; e["vv_q"].append(vvc)
            vvc += 1; e["vv_Eev"] = vvc
            vvc += 7; e["vv_argmin"] = vvc
            vvc += 2; e["vv_fd"] = vvc
            vvc += 2; e["vv_fev"] = vvc
            vvc += 1; e["vv_ts"] = vvc
            vvc += 2; e["vv_qs"] = vvc
            vvc += 1; e["vv_deq"] = vvc
            e["ss_e"] = []
            for j in range(16):
                ssc += 1; e["ss_e"].append(ssc)
            e["gg_r"] = []
            for j in range(16):
                ggc += 1; e["gg_r"].append(ggc)
            e["dso"] = dso0 + 16 * (m + 1)
            sch.append(e)

        def syQ(y):
            y.wait_ge(vv.h, vv.n)
            y.wait_ge(tt.h, tt.n)
            y.wait_ge(ss.h, ss.n)
            y.dma_start(out=ISVP, in_=D["qt_isvp"][:]); dsa.inc(y.last, 16)
            y.dma_start(out=SVP, in_=D["qt_svp"][:]); dsa.inc(y.last, 16)
            y.dma_start(out=QTIst, in_=D["qt_ind"][:]); dsa.inc(y.last, 16)
            y.dma_start(out=BST, in_=D["qt_bst"][:]); dsa.inc(y.last, 16)
            for m, chk in enumerate(chunks):
                y.wait_ge(dsa.h, dsa_c + 16 * m)
                if m >= 2:
                    y.wait_ge(vv.h, sch[m - 2]["vv_ts"])
                y.dma_start(out=Wb[m % 2][:, 0:chk["fw"]], in_=chk["src"])
                dsa.inc(y.last, 16)
                e = sch[m]
                y.wait_ge(vv.h, e["vv_deq"])
                y.dma_start(out=chk["st"], in_=deqB[:, 0:chk["fw"]])
                dso.inc(y.last, 16)
        self.y(syQ)

        def seQ(s):
            for m in range(NCH):
                e = sch[m]
                s.wait_ge(dsa.h, e["dsa_w"])
                if m >= 2:
                    s.wait_ge(vv.h, sch[m - 2]["vv_z"])
                s.activation(zbB[m % 2][:, 0:e["fw"]], Wb[m % 2][:, 0:e["fw"]],
                             ACT.Abs)
                ss.inc(s.last)
                assert ss.n == e["ss_abs"]
                for j in range(16):
                    s.wait_ge(gg.h, e["gg_r"][j])
                    if j >= 2:
                        s.wait_ge(tt.h, e["tt_err"][j - 2])
                    elif m >= 1:
                        s.wait_ge(tt.h, sch[m - 1]["tt_err"][14 + j])
                    s.activation(erB[j % 2][:, 0:e["fw"]], rB[j % 2][:, 0:e["fw"]],
                                 ACT.Square, scale=SVF[j])
                    ss.inc(s.last)
                    assert ss.n == e["ss_e"][j]
        self.s(seQ)

        def teQ(t):
            for m in range(NCH):
                e = sch[m]
                nf, fw = e["nf"], e["fw"]
                t.wait_ge(ss.h, e["ss_abs"])
                if m == 0:
                    t.wait_ge(vv.h, vv_qti)
                if m >= 2:
                    t.wait_ge(vv.h, sch[m - 2]["vv_amax"])
                for k in range(nf):
                    t.transpose(PB[m % 2][:, k * 128:(k + 1) * 128],
                                zbB[m % 2][:, k * 128:(k + 1) * 128], IDENT[:])
                    tt.inc(t.last)
                assert tt.n == e["tt_ztr"]
                t.wait_ge(vv.h, e["vv_binv"])
                if m >= 1:
                    t.wait_ge(vv.h, sch[m - 1]["vv_deq"])
                t.transpose(PN[1][0:2 * nf, 0:128], BINV[:, 0:2 * nf], IDENT[:])
                tt.inc(t.last)
                assert tt.n == e["tt_btr"]
                t.wait_ge(vv.h, e["vv_btr"])
                for k in range(nf):
                    t.matmul(PB[2 + m % 2][:, k * 128:(k + 1) * 128],
                             BST[0:2 * nf, k * 128:(k + 1) * 128],
                             bvT[0:2 * nf, :], start=True, stop=True)
                    tt.inc(t.last)
                assert tt.n == e["tt_bb"]
                for j in range(16):
                    t.wait_ge(ss.h, e["ss_e"][j])
                    if j == 0 and m >= 1:
                        t.wait_ge(vv.h, sch[m - 1]["vv_Eev"])
                    t.matmul(PB[4][0:32, 0:fw], QTI[:, 32 * j:32 * (j + 1)],
                             erB[j % 2][:, 0:fw], start=(j == 0), stop=(j == 15))
                    tt.inc(t.last)
                    assert tt.n == e["tt_err"][j]
                t.wait_ge(vv.h, e["vv_Eev"])
                if m >= 1:
                    t.wait_ge(vv.h, sch[m - 1]["vv_argmin"])
                for k in range(nf):
                    t.transpose(PB[5][:, 32 * k:32 * (k + 1)],
                                Eev[:, k * 128:(k + 1) * 128], IDENT[0:32, 0:32])
                    tt.inc(t.last)
                assert tt.n == e["tt_et"]
                t.wait_ge(vv.h, e["vv_fd"])
                t.transpose(PN[1][0:2 * nf, 128:256], IBS[:, 0:2 * nf], IDENT[:])
                tt.inc(t.last)
                t.transpose(PN[1][0:2 * nf, 256:384], SCBD[:, 0:2 * nf], IDENT[:])
                tt.inc(t.last)
                assert tt.n == e["tt_ftr"]
                t.wait_ge(vv.h, e["vv_fev"])
                for k in range(nf):
                    t.matmul(PN[0][:, k * 128:(k + 1) * 128],
                             BST[0:2 * nf, k * 128:(k + 1) * 128],
                             ibT[0:2 * nf, :], start=True, stop=True)
                    tt.inc(t.last)
                for k in range(nf):
                    t.matmul(PN[1][:, k * 128:(k + 1) * 128],
                             BST[0:2 * nf, k * 128:(k + 1) * 128],
                             scT[0:2 * nf, :], start=True, stop=True)
                    tt.inc(t.last)
                assert tt.n == e["tt_fb"]
        self.t(teQ)

        def veQ(v):
            v.wait_ge(dsa.h, dsa_c)
            v.tensor_copy(QTI[:], QTIst[:])
            vv.inc(v.last)
            assert vv.n == vv_qti
            for m in range(NCH):
                e = sch[m]
                nf, fw = e["nf"], e["fw"]
                v.wait_ge(tt.h, e["tt_ztr"])
                v.tensor_reduce(
                    AMX[:, 0:2 * nf].rearrange("p (k h) -> p k h", h=2),
                    PB[m % 2][:, 0:nf * 128]
                        .rearrange("p (k h x) -> p k h x", h=2, x=64),
                    AX.X, ALU.max)
                vv.inc(v.last)
                assert vv.n == e["vv_amax"]
                v.wait_ge(vv.h, vv.n)
                v.tensor_scalar(BASE[:, 0:2 * nf], AMX[:, 0:2 * nf],
                                1e-8, 1.0 / 15.0, ALU.max, ALU.mult)
                vv.inc(v.last)
                v.wait_ge(vv.h, vv.n)
                v.reciprocal(BINV[:, 0:2 * nf], BASE[:, 0:2 * nf])
                vv.inc(v.last)
                assert vv.n == e["vv_binv"]
                v.wait_ge(tt.h, e["tt_btr"])
                v.tensor_copy(bvT[0:2 * nf, :], PN[1][0:2 * nf, 0:128])
                vv.inc(v.last)
                assert vv.n == e["vv_btr"]
                v.wait_ge(tt.h, e["tt_bb"])
                v.tensor_tensor(zB[m % 2][:, 0:fw], zbB[m % 2][:, 0:fw],
                                PB[2 + m % 2][:, 0:fw], ALU.mult)
                vv.inc(v.last)
                assert vv.n == e["vv_z"]
                for j in range(16):
                    if j == 0:
                        v.wait_ge(vv.h, e["vv_z"])
                    if j >= 2:
                        v.wait_ge(gg.h, e["gg_r"][j - 2])
                    elif m >= 1:
                        v.wait_ge(gg.h, sch[m - 1]["gg_r"][14 + j])
                    v.tensor_scalar(uB[j % 2][:, 0:fw], zB[m % 2][:, 0:fw],
                                    ISVF[j], None, ALU.mult)
                    vv.inc(v.last)
                    assert vv.n == e["vv_u"][j]
                    v.wait_ge(vv.h, vv.n)
                    v.tensor_scalar(q8B[j % 2][:, 0:fw], uB[j % 2][:, 0:fw],
                                    15.25, None, ALU.min)
                    vv.inc(v.last)
                    assert vv.n == e["vv_q"][j]
                v.wait_ge(tt.h, e["tt_err"][15])
                v.tensor_copy(Eev[:, 0:fw], PB[4][0:32, 0:fw])
                vv.inc(v.last)
                assert vv.n == e["vv_Eev"]
                v.wait_ge(tt.h, e["tt_et"])
                etv = PB[5][:, 0:32 * nf].rearrange("p (k j h) -> p k h j",
                                                    j=16, h=2)
                v.tensor_reduce(
                    MNE[:, 0:2 * nf].rearrange("p (k h) -> p k h", h=2),
                    etv, AX.X, ALU.min)
                vv.inc(v.last)
                v.wait_ge(vv.h, vv.n)
                v.tensor_tensor(
                    maskB[:, 0:32 * nf]
                        .rearrange("p (k j h) -> p k h j", j=16, h=2),
                    etv,
                    MNE[:, 0:2 * nf].rearrange("p (k h o) -> p k h o", h=2, o=1)
                        .broadcast_to([128, nf, 2, 16]),
                    ALU.is_equal)
                vv.inc(v.last)
                v.wait_ge(vv.h, vv.n)
                v.tensor_tensor(mvalB[:, 0:32 * nf], maskB[:, 0:32 * nf],
                                ISVP[:, 0:32 * nf], ALU.mult)
                vv.inc(v.last)
                v.wait_ge(vv.h, vv.n)
                v.tensor_reduce(
                    BISV[:, 0:2 * nf].rearrange("p (k h) -> p k h", h=2),
                    mvalB[:, 0:32 * nf].rearrange("p (k j h) -> p k h j",
                                                  j=16, h=2),
                    AX.X, ALU.max)
                vv.inc(v.last)
                v.wait_ge(vv.h, vv.n)
                v.tensor_tensor(
                    maskB[:, 0:32 * nf]
                        .rearrange("p (k j h) -> p k h j", j=16, h=2),
                    mvalB[:, 0:32 * nf].rearrange("p (k j h) -> p k h j",
                                                  j=16, h=2),
                    BISV[:, 0:2 * nf].rearrange("p (k h o) -> p k h o", h=2, o=1)
                        .broadcast_to([128, nf, 2, 16]),
                    ALU.is_equal)
                vv.inc(v.last)
                v.wait_ge(vv.h, vv.n)
                v.tensor_tensor(mvalB[:, 0:32 * nf], maskB[:, 0:32 * nf],
                                SVP[:, 0:32 * nf], ALU.mult)
                vv.inc(v.last)
                v.wait_ge(vv.h, vv.n)
                v.tensor_reduce(
                    BSV[:, 0:2 * nf].rearrange("p (k h) -> p k h", h=2),
                    mvalB[:, 0:32 * nf].rearrange("p (k j h) -> p k h j",
                                                  j=16, h=2),
                    AX.X, ALU.max)
                vv.inc(v.last)
                assert vv.n == e["vv_argmin"]
                v.wait_ge(vv.h, vv.n)
                v.tensor_tensor(IBS[:, 0:2 * nf], BISV[:, 0:2 * nf],
                                BINV[:, 0:2 * nf], ALU.mult)
                vv.inc(v.last)
                v.tensor_tensor(SCBD[:, 0:2 * nf], BSV[:, 0:2 * nf],
                                BASE[:, 0:2 * nf], ALU.mult)
                vv.inc(v.last)
                assert vv.n == e["vv_fd"]
                v.wait_ge(tt.h, e["tt_ftr"])
                v.tensor_copy(ibT[0:2 * nf, :], PN[1][0:2 * nf, 128:256])
                vv.inc(v.last)
                v.tensor_copy(scT[0:2 * nf, :], PN[1][0:2 * nf, 256:384])
                vv.inc(v.last)
                assert vv.n == e["vv_fev"]
                v.wait_ge(tt.h, e["tt_fb"])
                v.tensor_tensor(zB[m % 2][:, 0:fw], Wb[m % 2][:, 0:fw],
                                PN[0][:, 0:fw], ALU.mult)
                vv.inc(v.last)
                assert vv.n == e["vv_ts"]
                v.wait_ge(vv.h, vv.n)
                v.tensor_scalar(zB[m % 2][:, 0:fw], zB[m % 2][:, 0:fw],
                                -15.25, 15.25, ALU.max, ALU.min)
                vv.inc(v.last)
                v.wait_ge(vv.h, vv.n)
                v.tensor_copy(qsB[:, 0:fw], zB[m % 2][:, 0:fw])
                vv.inc(v.last)
                assert vv.n == e["vv_qs"]
                v.wait_ge(vv.h, vv.n)
                if m >= 1:
                    v.wait_ge(dso.h, sch[m - 1]["dso"])
                v.tensor_tensor(deqB[:, 0:fw], qsB[:, 0:fw],
                                PN[1][:, 0:fw], ALU.mult)
                vv.inc(v.last)
                assert vv.n == e["vv_deq"]
        self.v(veQ)

        def gyQ(g):
            for m in range(NCH):
                e = sch[m]
                fw = e["fw"]
                for j in range(16):
                    g.wait_ge(vv.h, e["vv_q"][j])
                    if j >= 2:
                        g.wait_ge(ss.h, e["ss_e"][j - 2])
                    elif m >= 1:
                        g.wait_ge(ss.h, sch[m - 1]["ss_e"][14 + j])
                    g.tensor_tensor(rB[j % 2][:, 0:fw], q8B[j % 2][:, 0:fw],
                                    uB[j % 2][:, 0:fw], ALU.subtract)
                    gg.inc(g.last)
                    assert gg.n == e["gg_r"][j]
        self.g(gyQ)

        def gAG(g):
            g.wait_ge(dso.h, sch[-1]["dso"])
            for src, dst in (("q_fc_f32", "q_fc_f32_all"),
                             ("q_pr_f32", "q_pr_f32_all")):
                g.collective_compute(
                    "AllGather", ALU.bypass,
                    replica_groups=[[0, 1, 2, 3, 4, 5, 6, 7]],
                    ins=[D[src][:]], outs=[D[dst][:]])
                cc.inc(g.last)
        self.g(gAG)
        self.cc_qt = cc.n

    def ph_l1_ffn(self):
        D = self.D
        tt, vv, ss, gg, cc, dg = self.tt, self.vv, self.ss, self.gg, self.cc, self.dg
        PB, PN = self.PB, self.PN
        SA, SB_ = self.SA, self.SB_
        W = self.W
        dsw, dsx2, dskv = self.dsw, self.dsx2, self.dskv
        x3_ap = self.x3_ap

        base_ss = ss.n
        base_w = [dsw[k].n for k in range(4)]
        sch1 = []
        ttc, ssc, vvc, dgc = tt.n, ss.n, vv.n, dg.n
        grp_free = []
        for ffo in range(64):
            e = {}
            for g4 in range(4):
                grp_free.append(ttc + (4 * g4 + 4) * 2)
            ttc += 32
            e["tt"] = ttc
            e["relu"] = []
            e["h2"] = []
            e["dg"] = []
            for half in range(2):
                ssc += 1; e["relu"].append(ssc)
                vvc += 1; e["h2"].append(vvc)
                dgc += 16; e["dg"].append(dgc)
            sch1.append(e)

        def syF(y):
            y.wait_ge(cc.h, self.cc_qt)
            i = 0
            for ffo in range(64):
                rk, loc = ffo // 8, (ffo % 8) * 128
                for g4 in range(4):
                    if i >= 4:
                        y.wait_ge(tt.h, grp_free[i - 4])
                    src = D["q_fc_f32_all"][rk, g4 * 512:(g4 + 1) * 512,
                                            loc:loc + 128]                         .rearrange("(a p) f -> p a f", a=4)
                    y.dma_start(
                        out=W[i % 4][:].rearrange("p (a f) -> p a f", a=4),
                        in_=src)
                    dsw[i % 4].inc(y.last, 16)
                    i += 1
        self.y(syF)

        def teF(t):
            i = 0
            for ffo in range(64):
                e = sch1[ffo]
                bk0 = (ffo % 2) * 2
                for ci in range(16):
                    if ci % 4 == 0:
                        t.wait_ge(dsw[i % 4].h, base_w[i % 4] + 16 * (i // 4 + 1))
                        wbuf = W[i % 4]
                        i += 1
                    if ffo >= 2 and ci == 0:
                        for half in range(2):
                            t.wait_ge(ss.h, sch1[ffo - 2]["relu"][half])
                    wtile = wbuf[:, (ci % 4) * 128:(ci % 4 + 1) * 128]
                    for half in range(2):
                        t.matmul(PB[bk0 + half][:], wtile,
                                 x3_ap(ci)[:, half * 512:(half + 1) * 512],
                                 start=(ci == 0), stop=(ci == 15))
                        tt.inc(t.last)
                assert tt.n == e["tt"]
        self.t(teF)

        def seF(s):
            for ffo in range(64):
                e = sch1[ffo]
                bk0 = (ffo % 2) * 2
                for half in range(2):
                    s.wait_ge(tt.h, e["tt"] - (1 - half))
                    if ffo >= 1:
                        s.wait_ge(vv.h, sch1[ffo - 1]["h2"][half])
                    s.activation(SA[:, half * 512:(half + 1) * 512],
                                 PB[bk0 + half][:], ACT.Relu)
                    ss.inc(s.last)
                    assert ss.n == e["relu"][half]
        self.s(seF)

        def veF(v):
            for ffo in range(64):
                e = sch1[ffo]
                for half in range(2):
                    v.wait_ge(ss.h, e["relu"][half])
                    if ffo >= 1:
                        v.wait_ge(dg.h, sch1[ffo - 1]["dg"][1])
                    v.tensor_tensor(SB_[:, half * 512:(half + 1) * 512],
                                    SA[:, half * 512:(half + 1) * 512],
                                    SA[:, half * 512:(half + 1) * 512], ALU.mult)
                    vv.inc(v.last)
                    assert vv.n == e["h2"][half]
        self.v(veF)

        def gyF(g):
            for ffo in range(64):
                e = sch1[ffo]
                for half in range(2):
                    g.wait_ge(vv.h, e["h2"][half])
                    g.dma_start(out=D["st_hh"][ffo * 128:(ffo + 1) * 128,
                                               half * 512:(half + 1) * 512],
                                in_=SB_[:, half * 512:(half + 1) * 512])
                    dg.inc(g.last, 16)
                    assert dg.n == e["dg"][half]
        self.g(gyF)
        dg_h2 = dg.n

        PSL = [PB[0], PB[1], PB[2], PB[3], PB[4], PB[5], PN[0], PN[1]]
        base_w2 = [dsw[k].n for k in range(4)]
        MOVS = [dsx2, dskv]
        base_mv = [[s2[0].n, s2[1].n] for s2 in MOVS]
        sch2 = []
        ttc, vvc = tt.n, vv.n
        for cog in range(4):
            e = {}
            ttc += 64 * 8
            e["tt"] = ttc
            e["ev"] = []
            for q in range(4):
                for half in range(2):
                    vvc += 1
                    e["ev"].append(vvc)
            sch2.append(e)
        tt2base = tt.n

        def mmcnt(cog, ffi):
            return tt2base + cog * 64 * 8 + ffi * 8

        def syM(y):
            y.wait_ge(dg.h, dg_h2)
            for cog in range(4):
                for ffi in range(64):
                    gi = cog * 64 + ffi
                    if gi >= 2:
                        y.wait_ge(tt.h, mmcnt((gi - 2) // 64, (gi - 2) % 64) + 8)
                    src_w = D["q_pr_f32_all"][2 * cog:2 * cog + 2,
                                              ffi * 128:(ffi + 1) * 128, :]
                    y.dma_start(
                        out=W[gi % 2][:].rearrange("p (a f) -> p a f", a=2),
                        in_=src_w.rearrange("a p f -> p a f"))
                    dsw[gi % 2].inc(y.last, 16)
                    for half in range(2):
                        y.dma_start(
                            out=SA[:, (half * 2 + gi % 2) * 512:
                                    (half * 2 + gi % 2) * 512 + 512],
                            in_=D["st_hh"][ffi * 128:(ffi + 1) * 128,
                                           half * 512:(half + 1) * 512])
                        MOVS[half][gi % 2].inc(y.last, 16)
        self.y(syM)

        def teM(t):
            for cog in range(4):
                e = sch2[cog]
                for ffi in range(64):
                    gi = cog * 64 + ffi
                    t.wait_ge(dsw[gi % 2].h, base_w2[gi % 2] + 16 * (gi // 2 + 1))
                    for half in range(2):
                        t.wait_ge(MOVS[half][gi % 2].h,
                                  base_mv[half][gi % 2] + 16 * (gi // 2 + 1))
                    if ffi == 0 and cog >= 1:
                        t.wait_ge(vv.h, sch2[cog - 1]["ev"][-1])
                    wbuf = W[gi % 2]
                    for q in range(4):
                        wtile = wbuf[:, q * 128:(q + 1) * 128]
                        for half in range(2):
                            mov = SA[:, (half * 2 + gi % 2) * 512:
                                     (half * 2 + gi % 2) * 512 + 512]
                            t.matmul(PSL[q * 2 + half][:], wtile, mov,
                                     start=(ffi == 0), stop=(ffi == 63))
                            tt.inc(t.last)
                assert tt.n == e["tt"]
        self.t(teM)

        def veM(v):
            for cog in range(4):
                e = sch2[cog]
                v.wait_ge(tt.h, e["tt"])
                k = 0
                for q in range(4):
                    co = cog * 4 + q
                    for half in range(2):
                        v.tensor_tensor(
                            x3_ap(co)[:, half * 512:(half + 1) * 512],
                            x3_ap(co)[:, half * 512:(half + 1) * 512],
                            PSL[q * 2 + half][:], ALU.add)
                        vv.inc(v.last)
                        assert vv.n == e["ev"][k]
                        k += 1
        self.v(veM)

        self._layernorm(x3_ap)

    def ph_out(self):
        D = self.D
        ds = self.ds
        src_ap = self.x1_ap

        def sy(y):
            for s_ in (self.tt, self.vv, self.ss, self.gg):
                y.wait_ge(s_.h, s_.n)
            for ch in range(16):
                y.dma_start(out=D["out"][ch * 128:(ch + 1) * 128, :],
                            in_=src_ap(ch))
                ds.inc(y.last, 16)
            y.wait_ge(ds.h, ds.n)
        self.y(sy)


def stage_inputs(ins):
    x = np.ascontiguousarray(ins["L0_attn_hidden_states"])
    w1 = np.ascontiguousarray(ins["L0_ffn_w1"])
    w2 = np.ascontiguousarray(ins["L0_ffn_w2"])
    wfcT = np.ascontiguousarray(ins["L1_ffn_w_fc"].T)
    wprT = np.ascontiguousarray(ins["L1_ffn_w_proj"].T)
    convw_full = np.ascontiguousarray(ins["L1_attn_conv_w"][:, 0, :])
    base = {
        "Wq": np.ascontiguousarray(ins["L0_attn_Wq"]),
        "Wk": np.ascontiguousarray(ins["L0_attn_Wk"]),
        "Wv": np.ascontiguousarray(ins["L0_attn_Wv"]),
        "Wg": np.ascontiguousarray(ins["L0_attn_Wg"]),
        "Wc": np.ascontiguousarray(ins["L0_attn_Wc"]),
        "bc": np.ascontiguousarray(ins["L0_attn_bc"]).reshape(C, 1),
        "w1": w1, "w2": w2,
    }
    tril = np.tril(np.ones((128, 128), np.float32))
    diagm = np.ascontiguousarray(tril.T)
    bccols = np.zeros((64, 2), np.float32)
    bccols[0:16, 0] = 1.0
    bccols[32:48, 1] = 1.0
    qt_ind = np.zeros((128, 512), np.float32)
    for j in range(16):
        qt_ind[0:64, 32 * j + 2 * j] = 1.0
        qt_ind[64:128, 32 * j + 2 * j + 1] = 1.0
    qt_bst = np.zeros((8, 512), np.float32)
    for k in range(4):
        qt_bst[2 * k, 128 * k:128 * k + 64] = 1.0
        qt_bst[2 * k + 1, 128 * k + 64:128 * (k + 1)] = 1.0
    qt_svp = np.zeros((128, 128), np.float32)
    qt_isvp = np.zeros((128, 128), np.float32)
    for k in range(4):
        for j in range(16):
            for hb in range(2):
                qt_svp[:, 32 * k + 2 * j + hb] = SVF[j]
                qt_isvp[:, 32 * k + 2 * j + hb] = ISVF[j]
    base.update(qt_ind=qt_ind, qt_bst=qt_bst, qt_svp=qt_svp, qt_isvp=qt_isvp)
    inproj_full = np.ascontiguousarray(ins["L1_attn_in_proj_w"])
    maps = []
    for core in range(NCORES):
        b, h = core // 2, core % 2
        ch0 = h * CHH
        ot = own_tiles(h)
        xb = x[b].T
        m = dict(base)
        m["xT"] = np.ascontiguousarray(xb)
        m["xT_own"] = np.ascontiguousarray(
            np.concatenate([xb[:, g * 128:(g + 1) * 128] for g in ot], axis=1))
        if h == 0:
            cm = np.concatenate([diagm, np.zeros((128, 128), np.float32)], 1)
        else:
            cm = np.concatenate([np.ones((128, 128), np.float32), diagm], 1)
        m["cmask"] = np.ascontiguousarray(cm)
        m["hsel"] = np.ascontiguousarray(
            np.tile(np.array([[1.0 - h, float(h)]], np.float32), (8, 1)))
        hp = np.zeros((8, 1024), np.float32)
        for hh_ in range(8):
            hp[hh_, hh_ * 128:(hh_ + 1) * 128] = 1.0
        m["hpick"] = hp
        m["w1_sh"] = np.ascontiguousarray(w1[core * (FF // NCORES):(core + 1) * (FF // NCORES), :])
        m["w2_sh"] = np.ascontiguousarray(w2[core * (C // NCORES):(core + 1) * (C // NCORES), :])
        m["wfcT_sh"] = np.ascontiguousarray(wfcT[:, core * (FF // NCORES):(core + 1) * (FF // NCORES)])
        m["wprT_sh"] = np.ascontiguousarray(wprT[:, core * (C // NCORES):(core + 1) * (C // NCORES)])
        m["inproj_sl"] = np.ascontiguousarray(np.concatenate(
            [inproj_full[:, ch0:ch0 + CHH],
             inproj_full[:, INTER + ch0:INTER + ch0 + CHH]], axis=1))
        m["bccols"] = bccols
        m["convw"] = np.ascontiguousarray(convw_full[ch0:ch0 + CHH, :])
        m["convb"] = np.ascontiguousarray(ins["L1_attn_conv_b"][ch0:ch0 + CHH]).reshape(CHH, 1)
        m["xproj"] = np.ascontiguousarray(ins["L1_attn_x_proj_w"][ch0:ch0 + CHH, :])
        m["dtw"] = np.ascontiguousarray(ins["L1_attn_dt_proj_w"][:, ch0:ch0 + CHH])
        m["dtb"] = np.ascontiguousarray(ins["L1_attn_dt_proj_b"][ch0:ch0 + CHH]).reshape(CHH, 1)
        m["Alog"] = np.ascontiguousarray(ins["L1_attn_A_log"][ch0:ch0 + CHH, :])
        m["Dpar"] = np.ascontiguousarray(ins["L1_attn_D_param"][ch0:ch0 + CHH]).reshape(CHH, 1)
        m["outproj"] = np.ascontiguousarray(ins["L1_attn_out_proj_w"][ch0:ch0 + CHH, :])
        maps.append(m)
    return maps


def assemble_output(results):
    out = np.empty((B, T, C), dtype=np.float32)
    for core in range(NCORES):
        b, h = core // 2, core % 2
        o = results[core]["out"]
        for i, g in enumerate(own_tiles(h)):
            out[b, g * 128:(g + 1) * 128, :] = o[:, i * 128:(i + 1) * 128].T
    return out


def kernel(**inputs):
    ins = {k: np.asarray(v) for k, v in inputs.items()}
    maps = stage_inputs(ins)
    nc, prog = build_program()
    res = run_bass_kernel_spmd(nc, maps, core_ids=list(range(NCORES)))
    return assemble_output(res.results)


if __name__ == "__main__":
    import tempfile
    from concourse.bass_utils import compile_bass_kernel
    nc, prog = build_program()
    print("built, instructions:", nc.n_instructions())
    with tempfile.TemporaryDirectory() as td:
        compile_bass_kernel(nc, td)
    print("COMPILE OK")


# version 07
# speedup vs baseline: 1.0196x; 1.0196x over previous
import numpy as np
import concourse.bass as bass
import concourse.mybir as mybir
from concourse.bass_utils import run_bass_kernel_spmd

F32 = mybir.dt.float32
BF16 = mybir.dt.bfloat16
I32 = mybir.dt.int32
I8 = mybir.dt.int8
F32R = mybir.dt.float32r
ALU = mybir.AluOpType
ACT = mybir.ActivationFunctionType
AX = mybir.AxisListType

B, T, C = 4, 2048, 2048
NCORES = 8
TOWN = 1024
NT = T // 128
H, RR, DH = 8, 2, 128
HQ = H * RR
FF = 8192
INTER, NST, DTR, KCONV = 4096, 16, 128, 4
CHH = INTER // 2
GOFF = 6.906768
QMAX = 15.0
SHRINKS = np.linspace(0.4, 1.0, 16)
SVF = [float(np.float32(v)) for v in SHRINKS]
ISVF = [float(np.float32(1.0 / np.float64(np.float32(v)))) for v in SHRINKS]
LN_EPS = 1e-5
RMS_EPS = 1e-6
SQD = 1.0 / float(np.sqrt(DH))

PHASES = "all"


def own_tiles(h):
    return [g for g in range(NT) if g % 2 == h]


class Rec:
    def __init__(self, eng):
        self._eng = eng
        self.last = None
        self.last_is_dma = False

    def __getattr__(self, name):
        attr = getattr(self._eng, name)
        if not callable(attr):
            return attr

        def call(*a, **kw):
            r = attr(*a, **kw)
            if r is not None and hasattr(r, "then_inc"):
                if name in ("wait_ge", "nop", "sem_inc", "wait_op", "drain"):
                    return r
                self.last = r
                self.last_is_dma = name in ("dma_start", "dma_start_transpose")
            return r

        return call


class Sem:
    INCD = set()

    def __init__(self, nc, name):
        self.h = nc.alloc_semaphore(name=name)
        self.n = 0

    def inc(self, instr, k=1):
        instr.then_inc(self.h, k)
        Sem.INCD.add(id(instr))
        self.n += k
        return self.n

    @staticmethod
    def has_inc(instr):
        return id(instr) in Sem.INCD


def build_program():
    nc = bass.Bass("TRN2", target_bir_lowering=False, debug=False)
    p = Prog(nc)
    p.build()
    return nc, p


class Prog:
    def __init__(self, nc):
        self.nc = nc
        self.last = {}
        Sem.INCD = set()
        self.keepalive = []

    def _sect(self, engname, fn):
        prog = self
        emit = getattr(self.block, engname)

        def body(eng):
            rec = Rec(eng)
            fn(rec)
            if rec.last is not None:
                prog.last[engname] = (rec.last, rec.last_is_dma)
                prog.keepalive.append(rec.last)

        emit(body)

    def t(self, fn): self._sect("tensor", fn)
    def v(self, fn): self._sect("vector", fn)
    def s(self, fn): self._sect("scalar", fn)
    def g(self, fn): self._sect("gpsimd", fn)
    def y(self, fn): self._sect("sync", fn)

    ENG_SEM = {"tensor": "tt", "vector": "vv", "scalar": "ss", "gpsimd": "gg"}
    DMA_SEM = {"sync": "ds", "gpsimd": "dg", "vector": "dv", "scalar": "da",
               "tensor": "ds"}

    def barrier(self):
        for engname, (instr, is_dma) in list(self.last.items()):
            if Sem.has_inc(instr):
                continue
            if is_dma:
                getattr(self, self.DMA_SEM[engname]).inc(instr, 16)
            else:
                getattr(self, self.ENG_SEM[engname]).inc(instr, 1)
        self.last = {}
        waits = [(s.h, s.n) for s in
                 (self.tt, self.vv, self.ss, self.gg, self.ds, self.dg,
                  self.dv, self.da, self.dsa, self.dsc, self.cc,
                  self.dso, self.dsp, *self.dsw, *self.dsx2, *self.dskv,
                  *self.dsm, *self.dsn) if s.n > 0]
        for engname in ("tensor", "vector", "scalar", "gpsimd", "sync"):
            def mk(engname=engname):
                def body(eng):
                    for hh, n in waits:
                        eng.wait_ge(hh, n)
                getattr(self.block, engname)(body)
            mk()

    def mm_stream(self, items, nbuf=4):
        tt = self.tt
        W = self.W
        dsw = self.dsw
        base_tt = tt.n
        base_w = [dsw[k].n for k in range(4)]

        def sy(y):
            for i, it in enumerate(items):
                if i == 0:
                    y.wait_ge(tt.h, base_tt)
                if i >= nbuf:
                    y.wait_ge(tt.h, base_tt + (i - nbuf + 1))
                wbuf = W[i % nbuf]
                if it["w"].dtype == BF16:
                    wbuf = wbuf[:].bitcast(BF16)
                wslice = wbuf[:, 0:it["w"].shape[1]]
                y.dma_start(out=wslice, in_=it["w"])
                dsw[i % nbuf].inc(y.last, 16)
        self.y(sy)

        def te(t):
            for i, it in enumerate(items):
                t.wait_ge(dsw[i % nbuf].h, base_w[i % nbuf] + 16 * (i // nbuf + 1))
                for sem, cnt in it.get("mwait", ()):
                    t.wait_ge(sem.h, cnt)
                wbuf = W[i % nbuf]
                if it["w"].dtype == BF16:
                    wbuf = wbuf[:].bitcast(BF16)
                wslice = wbuf[:, 0:it["w"].shape[1]]
                t.matmul(it["o"], wslice, it["m"],
                         start=it.get("start", True), stop=it.get("stop", True))
                tt.inc(t.last)
        self.t(te)
        return tt.n

    def slot(self, i, w=2048):
        return self.R[:, i * 2048:i * 2048 + w]

    def half(self, i, j):
        return self.R[:, i * 2048 + j * 1024:i * 2048 + j * 1024 + 1024]

    def build(self):
        nc = self.nc
        D = {}
        self.D = D

        def din(name, shape, dtype=F32):
            D[name] = nc.dram_tensor(name, shape, dtype, kind="ExternalInput").ap()

        def dout(name, shape, dtype=F32):
            D[name] = nc.dram_tensor(name, shape, dtype, kind="ExternalOutput").ap()

        def dint(name, shape, dtype=F32, **kw):
            D[name] = nc.dram_tensor(name, shape, dtype, **kw).ap()

        din("xT", [C, T])
        din("xT_own", [C, TOWN])
        din("cmask", [128, 256])
        din("hsel", [8, 2])
        din("hpick", [8, 1024])
        din("Wq", [C, HQ * DH])
        din("Wk", [C, H * DH])
        din("Wv", [C, H * DH])
        din("Wg", [C, H])
        din("Wc", [HQ * DH, C])
        din("bc", [C, 1])
        din("w1", [FF, C])
        din("w2", [C, FF])
        din("inproj_sl", [C, 2 * CHH])
        din("bccols", [64, 2])
        din("convw", [CHH, KCONV])
        din("convb", [CHH, 1])
        din("xproj", [CHH, DTR + 2 * NST])
        din("dtw", [DTR, CHH])
        din("dtb", [CHH, 1])
        din("Alog", [CHH, NST])
        din("Dpar", [CHH, 1])
        din("outproj", [CHH, C])
        din("w1_sh", [FF // NCORES, C])
        din("w2_sh", [C // NCORES, FF])
        din("wfcT_sh", [C, FF // NCORES])
        din("wprT_sh", [FF, C // NCORES])
        din("qt_ind", [128, 512])
        din("qt_bst", [8, 512])
        din("qt_svp", [128, 128])
        din("qt_isvp", [128, 128])
        dout("out", [C, TOWN])

        dint("bt_w1", [C, FF // NCORES], BF16)
        dint("bt_w1_all", [NCORES, C, FF // NCORES], BF16)
        dint("bt_w2", [FF, C // NCORES], BF16)
        dint("bt_w2_all", [NCORES, FF, C // NCORES], BF16)
        dint("bs_sc", [1280, 1], F32)
        dint("bs_sc_all", [NCORES, 1280, 1], F32)
        dint("bq_fc", [C, FF // NCORES], F32)
        dint("bq_fc_all", [NCORES, C, FF // NCORES], F32)
        dint("bq_pr", [FF, C // NCORES], F32)
        dint("bq_pr_all", [NCORES, FF, C // NCORES], F32)
        dint("bx2", [C, TOWN], F32)
        dint("bx2_all", [2, C, TOWN], F32)
        dint("bssm", [DTR + 2 * NST, T], F32)
        dint("bssm_all", [DTR + 2 * NST, T], F32)
        dint("bop", [2, C, TOWN], F32)
        dint("bop_out", [C, TOWN], F32)
        dint("st_hid", [CHH, T], F32)
        dint("st_gate", [CHH, T], F32)
        dint("st_hs", [CHH, T], F32)
        dint("st_dt", [CHH, T], F32)
        dint("st_yg", [CHH, T], F32)
        dint("st_hh", [FF, TOWN], F32)
        dint("st_kh", [FF, TOWN], BF16)
        dint("q_fc_hi", [C, FF // NCORES], BF16)
        dint("q_fc_lo", [C, FF // NCORES], BF16)
        dint("q_pr_hi", [FF, C // NCORES], BF16)
        dint("q_pr_lo", [FF, C // NCORES], BF16)
        dint("q_fc_hi_all", [NCORES, C, FF // NCORES], BF16)
        dint("q_fc_lo_all", [NCORES, C, FF // NCORES], BF16)
        dint("q_pr_hi_all", [NCORES, FF, C // NCORES], BF16)
        dint("q_pr_lo_all", [NCORES, FF, C // NCORES], BF16)
        dint("st_h2lo", [FF, TOWN], BF16)
        dint("md", [4096, 1], F32)
        dint("st_x2own", [C, TOWN], F32)

        sb = nc.alloc_sbuf_tensor
        self.NSLOT = 17
        self.R = sb("R", [128, self.NSLOT * 2048], F32)
        self.W = [sb(f"W{i}", [128, 512], F32) for i in range(4)]
        self.W4 = [sb(f"W4{i}", [128, 256], F32) for i in range(2)]
        self.W5 = [sb(f"W5{i}", [128, 8], F32) for i in range(2)]
        self.SA = sb("SA", [128, 2048], F32)
        self.SB_ = sb("SB", [128, 2048], F32)
        self.SC = sb("SC", [128, 2048], F32)
        self.SD = sb("SD", [128, 2048], F32)
        self.SBF = sb("SBF", [128, 2048], BF16)
        self.MISC = sb("MISC", [128, 64], F32)
        self.ONER = sb("ONER", [1, 128], F32)
        self.MISC2 = sb("MISC2", [128, 64], F32)
        self.CSP = sb("CSP", [8, 2048], F32)
        self.CSPO = sb("CSPO", [8, 1024], F32)
        self.CSPT = sb("CSPT", [128, 128], F32)
        self.CMASK = sb("CMASK", [128, 256], F32)
        self.HSEL = sb("HSEL", [8, 2], F32)
        self.BCC = sb("BCC", [64, 2], F32)
        self.HPICK = sb("HPICK", [8, 1024], F32)
        self.IDENT = sb("IDENT", [128, 128], F32)
        self.IDENTB = sb("IDENTB", [128, 128], BF16)
        self.ER = sb("ER", [128, 1024], F32R)
        self.QTIT = sb("QTIT", [128, 512], F32R)
        ps = nc.alloc_psum_tensor
        self.PB = [ps(f"PB{i}", [128, 512], F32) for i in range(6)]
        self.PN = [ps(f"PN{i}", [128, 512], F32) for i in range(2)]

        for nm in ("ds", "dg", "dv", "da", "dsa", "dsc", "tt", "vv", "ss", "gg", "cc"):
            setattr(self, nm, Sem(nc, nm))
        self.dsw = [Sem(nc, f"dsw{i}") for i in range(4)]
        self.dsx2 = [Sem(nc, f"dsx{i}") for i in range(2)]
        self.dskv = [Sem(nc, f"dskv{i}") for i in range(2)]
        self.dsm = [Sem(nc, f"dsm{i}") for i in range(2)]
        self.dso = Sem(nc, "dso")
        self.dsp = Sem(nc, "dsp")
        self.dsn = [Sem(nc, f"dsn{i}") for i in range(2)]
        self.dsq = [Sem(nc, f"dsq{i}") for i in range(2)]

        self.block = nc.Block()
        self.block.__enter__()

        self.ph_init()
        self.barrier()
        self.ph_l0_attn()
        self.barrier()
        if PHASES in ("x2", "x3", "all"):
            self.ph_quant_ternary()
            self.barrier()
            self.ph_l0_ffn()
            self.barrier()
        if PHASES in ("x3", "all"):
            self.ph_l1_attn()
            self.barrier()
        if PHASES == "all":
            self.ph_quant_turbo()
            self.barrier()
            self.ph_l1_ffn()
            self.barrier()
        self.ph_out()

        self.block.__exit__(None, None, None)

    def ph_init(self):
        D = self.D
        ds, gg, vv = self.ds, self.gg, self.vv
        IDENT, IDENTB, MISC = self.IDENT, self.IDENTB, self.MISC

        dsc = self.dsc

        def sy(y):
            y.dma_start(out=self.CMASK[:], in_=D["cmask"][:])
            dsc.inc(y.last, 16)
            y.dma_start(out=self.HSEL[:], in_=D["hsel"][:])
            dsc.inc(y.last, 16)
            y.dma_start(out=self.HPICK[:], in_=D["hpick"][:])
            dsc.inc(y.last, 16)
        self.y(sy)

        def gp(g):
            g.iota(self.SA[:, 0:128], pattern=[[-1, 128]], base=0,
                   channel_multiplier=1, allow_small_or_imprecise_dtypes=True)
            g.memset(self.MISC2[:, 16:17], 0.0)
            gg.inc(g.last)
            g.wait_ge(gg.h, gg.n)
            g.tensor_scalar(IDENT[:], self.SA[:, 0:128], 0.0, None, ALU.is_equal)
            gg.inc(g.last)
        self.g(gp)

        def ve(v):
            v.wait_ge(gg.h, gg.n)
            v.tensor_copy(IDENTB[:], IDENT[:])
            vv.inc(v.last)
            v.memset(MISC[:, 0:1], 1.0)
            vv.inc(v.last)
            v.memset(self.ONER[0:1, :], 1.0)
            vv.inc(v.last)
            v.memset(MISC[0:8, 1:2], -GOFF)
            vv.inc(v.last)
            v.memset(MISC[0:1, 2:3], LN_EPS)
            vv.inc(v.last)
            v.memset(MISC[0:1, 3:4], RMS_EPS)
            vv.inc(v.last)
        self.v(ve)

    def ph_l0_attn(self):
        D = self.D
        ds, tt, vv, ss = self.ds, self.tt, self.vv, self.ss
        PB = self.PB
        SD, MISC2 = self.SD, self.MISC2

        q_ap = lambda hr: self.half(hr // 2, hr % 2)
        xo_ap = lambda c: self.half(9 + c // 2, c % 2)
        x1_ap = lambda ch: self.half(9 + ch // 2, ch % 2)
        self.x1_ap = x1_ap

        dsa = self.dsa

        def syA(y):
            for c in range(16):
                y.dma_start(out=xo_ap(c), in_=D["xT_own"][c * 128:(c + 1) * 128, :])
                dsa.inc(y.last, 16)
        self.y(syA)
        dsA = dsa.n

        vvB0 = vv.n
        items = []
        for hrg in range(8):
            for c in range(16):
                for j in range(2):
                    hr = hrg * 2 + j
                    for f in range(2):
                        mw = []
                        if c == 0:
                            mw = [(dsa, dsA)]
                            if hrg > 0:
                                mw.append((vv, vvB0 + hrg * 4))
                        items.append(dict(
                            w=D["Wq"][c * 128:(c + 1) * 128, hr * 128:(hr + 1) * 128],
                            m=xo_ap(c)[:, f * 512:(f + 1) * 512],
                            o=PB[2 + j * 2 + f][:],
                            start=(c == 0), stop=(c == 15), mwait=mw))
        ttB0 = tt.n
        self.mm_stream(items)

        def veB(v):
            for hrg in range(8):
                v.wait_ge(tt.h, ttB0 + (hrg + 1) * 64)
                for j in range(2):
                    hr = hrg * 2 + j
                    for f in range(2):
                        v.tensor_copy(q_ap(hr)[:, f * 512:(f + 1) * 512],
                                      PB[2 + j * 2 + f][:])
                        vv.inc(v.last)
        self.v(veB)

        for wv in range(4):
            self._attn_wave(wv)

        def syC(y):
            y.dma_start(out=MISC2[:, 0:16],
                        in_=D["bc"].rearrange("(i p) one -> p (i one)", p=128),
                        allow_slow_non_contiguous=True)
            self.dsc.inc(y.last, 16)
        self.y(syC)

        vvC0 = vv.n
        sdx_base = [self.dsx2[k].n for k in range(2)]
        for ch in range(16):
            dhalf = SD[:, (ch % 2) * 1024:(ch % 2) * 1024 + 1024]

            def syD(y, ch=ch, dhalf=dhalf):
                if ch < 2:
                    y.wait_ge(vv.h, vvC0)
                else:
                    y.wait_ge(vv.h, vvC0 + (ch - 1) * 2)
                y.dma_start(out=dhalf, in_=D["xT_own"][ch * 128:(ch + 1) * 128, :])
                self.dsx2[ch % 2].inc(y.last, 16)
            self.y(syD)
            dsD = (ch % 2, sdx_base[ch % 2] + 16 * (ch // 2 + 1))

            items = []
            for hr in range(16):
                for f in range(2):
                    items.append(dict(
                        w=D["Wc"][hr * 128:(hr + 1) * 128, ch * 128:(ch + 1) * 128],
                        m=q_ap(hr)[:, f * 512:(f + 1) * 512],
                        o=PB[f][:],
                        start=(hr == 0), stop=(hr == 15),
                        mwait=[(vv, vvC0 + ch * 2)] if hr == 0 else []))
            self.mm_stream(items)
            ttb = tt.n

            def veD(v, ch=ch, dhalf=dhalf, ttb=ttb, dsD=dsD):
                v.wait_ge(tt.h, ttb)
                v.wait_ge(self.dsx2[dsD[0]].h, dsD[1])
                v.wait_ge(self.dsc.h, self.dsc.n)
                for f in range(2):
                    v.scalar_tensor_tensor(
                        x1_ap(ch)[:, f * 512:(f + 1) * 512],
                        PB[f][:], MISC2[:, ch:ch + 1],
                        dhalf[:, f * 512:(f + 1) * 512],
                        ALU.add, ALU.add)
                    vv.inc(v.last)
            self.v(veD)

        self._layernorm(x1_ap)

    def _attn_wave(self, wv):
        D = self.D
        ds, tt, vv, ss = self.ds, self.tt, self.vv, self.ss
        PB, PN = self.PB, self.PN
        SA, SB_ = self.SA, self.SB_
        MISC, IDENT = self.MISC, self.IDENT
        CSP = self.CSP
        kslot = lambda i: self.slot(8 + i)
        vslot = lambda i: self.slot(10 + i)
        vTslot = lambda i: self.slot(12 + i)

        hh0, hh1 = 2 * wv, 2 * wv + 1
        tt0, vv0 = tt.n, vv.n
        nmm = 5 if wv == 0 else 4
        nw = 4 if wv == 0 else 3
        dsx2, dskv = self.dsx2, self.dskv
        bx = [dsx2[k].n for k in range(2)]
        bk = [dskv[k].n for k in range(2)]

        def syW(y):
            for sect in range(4):
                for c in range(16):
                    i = sect * 16 + c
                    if i == 0:
                        y.wait_ge(tt.h, tt0)
                        y.wait_ge(vv.h, vv0)
                    if i >= 2:
                        y.wait_ge(tt.h, tt0 + (i - 1) * nmm)
                    xbuf = (SA if i % 2 == 0 else SB_)[:, 0:512]
                    y.dma_start(out=xbuf,
                                in_=D["xT"][c * 128:(c + 1) * 128,
                                            sect * 512:(sect + 1) * 512])
                    dsx2[i % 2].inc(y.last, 16)
                    y.dma_start(out=self.W[i % 2][:, 0:128],
                                in_=D["Wk"][c * 128:(c + 1) * 128,
                                            hh0 * 128:(hh0 + 1) * 128])
                    dskv[i % 2].inc(y.last, 16)
                    y.dma_start(out=self.W[2 + i % 2][:, 0:128],
                                in_=D["Wk"][c * 128:(c + 1) * 128,
                                            hh1 * 128:(hh1 + 1) * 128])
                    dskv[i % 2].inc(y.last, 16)
                    y.dma_start(out=self.W4[i % 2][:, 0:256],
                                in_=D["Wv"][c * 128:(c + 1) * 128,
                                            hh0 * 128:(hh0 + 2) * 128])
                    dskv[i % 2].inc(y.last, 16)
                    if wv == 0:
                        y.dma_start(out=self.W5[i % 2][:, 0:8],
                                    in_=D["Wg"][c * 128:(c + 1) * 128, :])
                        dskv[i % 2].inc(y.last, 16)
        self.y(syW)

        def teW(t):
            for sect in range(4):
                for c in range(16):
                    i = sect * 16 + c
                    t.wait_ge(dsx2[i % 2].h, bx[i % 2] + 16 * (i // 2 + 1))
                    t.wait_ge(dskv[i % 2].h, bk[i % 2] + 16 * nw * (i // 2 + 1))
                    if c == 0:
                        t.wait_ge(vv.h, vv0 + sect * nmm)
                    xbuf = (SA if i % 2 == 0 else SB_)[:, 0:512]
                    st, sp = (c == 0), (c == 15)
                    t.matmul(PB[2][:], self.W[i % 2][:, 0:128], xbuf, start=st, stop=sp)
                    tt.inc(t.last)
                    t.matmul(PB[3][:], self.W[2 + i % 2][:, 0:128], xbuf, start=st, stop=sp)
                    tt.inc(t.last)
                    t.matmul(PB[4][:], self.W4[i % 2][:, 0:128], xbuf, start=st, stop=sp)
                    tt.inc(t.last)
                    t.matmul(PB[5][:], self.W4[i % 2][:, 128:256], xbuf, start=st, stop=sp)
                    tt.inc(t.last)
                    if wv == 0:
                        t.matmul(PN[1][0:8, :], self.W5[i % 2][:, 0:8], xbuf, start=st, stop=sp)
                        tt.inc(t.last)
        self.t(teW)

        def veW(v):
            for sect in range(4):
                v.wait_ge(tt.h, tt0 + (sect + 1) * 16 * nmm)
                sl = slice(sect * 512, (sect + 1) * 512)
                v.tensor_copy(kslot(0)[:, sl], PB[2][:]); vv.inc(v.last)
                v.tensor_copy(kslot(1)[:, sl], PB[3][:]); vv.inc(v.last)
                v.tensor_copy(vslot(0)[:, sl], PB[4][:]); vv.inc(v.last)
                v.tensor_copy(vslot(1)[:, sl], PB[5][:]); vv.inc(v.last)
                if wv == 0:
                    v.tensor_copy(CSP[0:8, sl], PN[1][0:8, :]); vv.inc(v.last)
        self.v(veW)
        vv_kv = vv.n

        if wv == 0:
            self._csp_pipeline(vv_kv)

        tt_vt0 = tt.n
        vv_vt0 = vv.n

        def teV(t):
            t.wait_ge(vv.h, vv_vt0)
            for j in range(2):
                for stile in range(16):
                    i = j * 16 + stile
                    if i >= 2:
                        t.wait_ge(vv.h, vv_vt0 + i - 1)
                    t.transpose(PB[2 + i % 2][:, 0:128],
                                vslot(j)[:, stile * 128:(stile + 1) * 128],
                                IDENT[:])
                    tt.inc(t.last)
        self.t(teV)

        def veV(v):
            for j in range(2):
                for stile in range(16):
                    i = j * 16 + stile
                    v.wait_ge(tt.h, tt_vt0 + i + 1)
                    v.tensor_copy(vTslot(j)[:, stile * 128:(stile + 1) * 128],
                                  PB[2 + i % 2][:, 0:128])
                    vv.inc(v.last)
        self.v(veV)

        for hr in (2 * hh0, 2 * hh0 + 1, 2 * hh1, 2 * hh1 + 1):
            self._attn_head(hr)

    def _csp_pipeline(self, vv_kv):
        ds, tt, vv, ss = self.ds, self.tt, self.vv, self.ss
        PB = self.PB
        MISC, IDENT = self.MISC, self.IDENT
        CSP, CSPT, CSPO = self.CSP, self.CSPT, self.CSPO
        CSPA, CSPB = self.SC, self.SD
        CSPO0, CSPO1 = self.SC[0:8, 0:1024], self.SD[0:8, 0:1024]

        def se1(s):
            s.wait_ge(vv.h, vv_kv)
            s.activation(CSPA[0:8, :], CSP[0:8, :], ACT.Exp,
                         bias=MISC[0:8, 1:2], scale=-1.0)
            ss.inc(s.last)
        self.s(se1)
        ss1 = ss.n

        def ve1(v):
            v.wait_ge(ss.h, ss1)
            v.tensor_scalar(CSPA[0:8, :], CSPA[0:8, :], 1.0, None, ALU.add)
            vv.inc(v.last)
        self.v(ve1)
        vv1 = vv.n

        def se2(s):
            s.wait_ge(vv.h, vv1)
            s.activation(CSPB[0:8, :], CSPA[0:8, :], ACT.Ln)
            ss.inc(s.last)
        self.s(se2)
        ss2 = ss.n

        def ve2(v):
            v.wait_ge(ss.h, ss2)
            v.memset(CSPA[0:8, :], 1.0)
            vv.inc(v.last)
            v.wait_ge(vv.h, vv.n)
            v.tensor_tensor_scan(CSP[0:8, :], CSPA[0:8, :], CSPB[0:8, :],
                                 0.0, ALU.mult, ALU.add)
            vv.inc(v.last)
        self.v(ve2)
        vv_csp = vv.n

        def te2(t):
            t.wait_ge(vv.h, vv_csp)
            for tj in range(16):
                if tj >= 2:
                    t.wait_ge(vv.h, vv_csp + tj - 1)
                t.transpose(PB[2 + tj % 2][:, 0:8],
                            CSP[0:8, tj * 128:(tj + 1) * 128],
                            IDENT[0:8, 0:8])
                tt.inc(t.last)
        self.t(te2)
        tt_cspT = tt.n

        def ve3(v):
            for tj in range(16):
                v.wait_ge(tt.h, tt_cspT - 15 + tj)
                v.tensor_copy(CSPT[:, tj * 8:(tj + 1) * 8], PB[2 + tj % 2][:, 0:8])
                vv.inc(v.last)
            cv = CSP[0:8, :].rearrange("p (i f) -> p i f", f=128)
            o0 = CSPO0.rearrange("p (i f) -> p i f", f=128)
            o1 = CSPO1.rearrange("p (i f) -> p i f", f=128)
            v.tensor_copy(o0, cv[:, 0::2, :])
            vv.inc(v.last)
            v.tensor_copy(o1, cv[:, 1::2, :])
            vv.inc(v.last)
            v.wait_ge(vv.h, vv.n)
            v.tensor_scalar(CSPO[0:8, :], CSPO0, self.HSEL[0:8, 0:1],
                            None, ALU.mult)
            vv.inc(v.last)
            v.wait_ge(vv.h, vv.n)
            v.scalar_tensor_tensor(CSPO[0:8, :], CSPO1,
                                   self.HSEL[0:8, 1:2], CSPO[0:8, :],
                                   ALU.mult, ALU.add)
            vv.inc(v.last)
        self.v(ve3)

    def _attn_head(self, hr):
        ds, tt, vv, ss = self.ds, self.tt, self.vv, self.ss
        PB, PN = self.PB, self.PN
        SA, SB_, SC, SD = self.SA, self.SB_, self.SC, self.SD
        MISC, CSPT, CSPO, CMASK = self.MISC, self.CSPT, self.CSPO, self.CMASK
        CSP = self.CSP
        hh = hr // 2
        j = hh % 2
        q_ap = self.half(hr // 2, hr % 2)
        kslot = self.slot(8 + j)
        vTslot = self.slot(12 + j)

        iters = []
        for tj in range(16):
            i0 = (tj // 2) * 128
            if i0 < 512:
                iters.append(dict(tj=tj, lo=i0, hi=512, first=True))
                iters.append(dict(tj=tj, lo=512, hi=1024, first=False))
            else:
                iters.append(dict(tj=tj, lo=i0, hi=1024, first=True))
        last = [max(i for i, it in enumerate(iters) if it["lo"] < 512),
                max(i for i, it in enumerate(iters) if it["hi"] > 512)]

        base_tt, base_ss, base_vv = tt.n, ss.n, vv.n
        sched = []
        ttc, ssc, vvc = base_tt, base_ss, base_vv
        for i, it in enumerate(iters):
            e = dict(it)
            ttc += 1; e["tt_qk"] = ttc
            ttc += 1; e["tt_bc"] = ttc
            ssc += 1; e["ss_sq"] = ssc
            ssc += 1; e["ss_ge"] = ssc
            vvc += 1 + (1 if it["first"] else 0)
            e["vv_a"] = vvc
            ttc += 1; e["tt_nm"] = ttc
            ttc += 1; e["tt_av"] = ttc
            sched.append(e)
        tt_end, ss_end, vv_end = ttc, ssc, vvc

        started = [False, False]

        def teH(t):
            for i, e in enumerate(sched):
                tj, lo, hi = e["tj"], e["lo"], e["hi"]
                L = hi - lo
                b = 0 if lo < 512 else 1
                qkp = PB[2 + i % 2]
                bcp = PB[4 + i % 2]
                if i < 2:
                    t.wait_ge(vv.h, base_vv)
                else:
                    t.wait_ge(ss.h, sched[i - 2]["ss_ge"])
                t.matmul(qkp[:, 0:L], kslot[:, tj * 128:(tj + 1) * 128],
                         q_ap[:, lo:hi], start=True, stop=True)
                tt.inc(t.last)
                t.matmul(bcp[:, 0:L], self.HPICK[0:8, hh * 128:(hh + 1) * 128],
                         CSPO[0:8, lo:hi], start=True, stop=True)
                tt.inc(t.last)
                t.wait_ge(vv.h, e["vv_a"])
                st = not started[b]
                sp = (i == last[b])
                abuf = (SA if i % 2 == 0 else SB_)[:, 0:L]
                t.matmul(PN[b][0:1, (lo % 512):(lo % 512) + L],
                         MISC[:, 0:1], abuf, start=st, stop=sp)
                tt.inc(t.last)
                t.matmul(PB[b][:, (lo % 512):(lo % 512) + L],
                         vTslot[:, tj * 128:(tj + 1) * 128],
                         abuf, start=st, stop=sp)
                tt.inc(t.last)
                started[b] = True
        self.t(teH)
        assert tt.n == tt_end

        def seH(s):
            for i, e in enumerate(sched):
                tj, lo, hi = e["tj"], e["lo"], e["hi"]
                L = hi - lo
                qkp = PB[2 + i % 2]
                bcp = PB[4 + i % 2]
                sqb = SC[:, (i % 2) * 1024:(i % 2) * 1024 + L]
                geb = SD[:, (i % 2) * 1024:(i % 2) * 1024 + L]
                s.wait_ge(tt.h, e["tt_qk"])
                if i >= 2:
                    s.wait_ge(vv.h, sched[i - 2]["vv_a"])
                s.activation(sqb, qkp[:, 0:L], ACT.Square, scale=SQD)
                ss.inc(s.last)
                s.wait_ge(tt.h, e["tt_bc"])
                s.activation(geb, bcp[:, 0:L], ACT.Exp, scale=-1.0,
                             bias=CSPT[:, tj * 8 + hh:tj * 8 + hh + 1])
                ss.inc(s.last)
        self.s(seH)
        assert ss.n == ss_end

        def veH(v):
            for i, e in enumerate(sched):
                tj, lo, hi = e["tj"], e["lo"], e["hi"]
                L = hi - lo
                sqb = SC[:, (i % 2) * 1024:(i % 2) * 1024 + L]
                geb = SD[:, (i % 2) * 1024:(i % 2) * 1024 + L]
                abuf = (SA if i % 2 == 0 else SB_)[:, 0:L]
                v.wait_ge(ss.h, e["ss_ge"])
                if i >= 2:
                    v.wait_ge(tt.h, sched[i - 2]["tt_av"])
                if e["first"]:
                    v.tensor_tensor(geb[:, 0:128], geb[:, 0:128],
                                    CMASK[:, (tj % 2) * 128:(tj % 2) * 128 + 128],
                                    ALU.mult)
                    vv.inc(v.last)
                    v.wait_ge(vv.h, vv.n)
                v.tensor_tensor(abuf, sqb, geb, ALU.mult)
                vv.inc(v.last)
        self.v(veH)
        assert vv.n == vv_end

        def veE(v):
            v.wait_ge(tt.h, tt_end)
            for b in range(2):
                v.tensor_scalar(CSP[0:1, b * 512:(b + 1) * 512], PN[b][0:1, :],
                                1e-6, None, ALU.add)
                vv.inc(v.last)
            v.wait_ge(vv.h, vv.n)
            v.reciprocal(CSP[0:1, 1024:2048], CSP[0:1, 0:1024])
            vv.inc(v.last)
        self.v(veE)
        vv_rn = vv.n

        def teE(t):
            t.wait_ge(vv.h, vv_rn)
            for b in range(2):
                t.matmul(PB[2 + b][:, 0:512], self.ONER[0:1, :],
                         CSP[0:1, 1024 + b * 512:1024 + (b + 1) * 512],
                         start=True, stop=True)
                tt.inc(t.last)
        self.t(teE)
        tt_rb = tt.n

        def veE2(v):
            v.wait_ge(tt.h, tt_rb)
            for b in range(2):
                v.tensor_copy(SC[:, b * 512:(b + 1) * 512], PB[2 + b][:])
                vv.inc(v.last)
            v.wait_ge(vv.h, vv.n)
            for b in range(2):
                v.tensor_tensor(q_ap[:, b * 512:(b + 1) * 512], PB[b][:],
                                SC[:, b * 512:(b + 1) * 512], ALU.mult)
                vv.inc(v.last)
        self.v(veE2)

    def _layernorm(self, ap_of_chunk, eps_col=2):
        ds, tt, vv, ss = self.ds, self.tt, self.vv, self.ss
        PB, PN = self.PB, self.PN
        SA, MISC = self.SA, self.MISC
        NRM = self.CSP
        base_tt, base_ss, base_vv = tt.n, ss.n, vv.n
        base_dg = self.dg.n

        sq_cnt = []
        ssc = base_ss
        for ch in range(16):
            ssc += 1
            sq_cnt.append(ssc)

        def se(s):
            for ch in range(16):
                if ch < 2:
                    s.wait_ge(vv.h, base_vv)
                    s.wait_ge(self.dg.h, base_dg)
                else:
                    s.wait_ge(tt.h, base_tt + (ch - 1) * 4)
                s.activation(SA[:, (ch % 2) * 1024:(ch % 2) * 1024 + 1024],
                             ap_of_chunk(ch), ACT.Square)
                ss.inc(s.last)
        self.s(se)

        def te(t):
            t.wait_ge(vv.h, base_vv)
            for ch in range(16):
                st, sp = (ch == 0), (ch == 15)
                t.wait_ge(ss.h, sq_cnt[ch])
                for f in range(2):
                    t.matmul(PB[f][0:1, :], MISC[:, 0:1],
                             ap_of_chunk(ch)[:, f * 512:(f + 1) * 512],
                             start=st, stop=sp)
                    tt.inc(t.last)
                for f in range(2):
                    t.matmul(PN[f][0:1, :], MISC[:, 0:1],
                             SA[:, (ch % 2) * 1024 + f * 512:(ch % 2) * 1024 + (f + 1) * 512],
                             start=st, stop=sp)
                    tt.inc(t.last)
        self.t(te)
        tt_st = tt.n

        SD = self.SD

        def ve(v):
            v.wait_ge(tt.h, tt_st)
            for f in range(2):
                v.tensor_scalar(NRM[0:1, f * 512:(f + 1) * 512], PB[f][0:1, :],
                                1.0 / 2048, None, ALU.mult)
                vv.inc(v.last)
                v.tensor_scalar(SD[0:1, f * 512:(f + 1) * 512], PN[f][0:1, :],
                                1.0 / 2048, None, ALU.mult)
                vv.inc(v.last)
            v.wait_ge(vv.h, vv.n)
            v.tensor_tensor(SD[0:1, 1024:2048], NRM[0:1, 0:1024],
                            NRM[0:1, 0:1024], ALU.mult)
            vv.inc(v.last)
            v.wait_ge(vv.h, vv.n)
            v.tensor_tensor(SD[0:1, 0:1024], SD[0:1, 0:1024],
                            SD[0:1, 1024:2048], ALU.subtract)
            vv.inc(v.last)
        self.v(ve)
        vv_var = vv.n

        def se2(s):
            s.wait_ge(vv.h, vv_var)
            s.activation(NRM[0:1, 1024:2048], SD[0:1, 0:1024], ACT.Sqrt,
                         bias=MISC[0:1, eps_col:eps_col + 1])
            ss.inc(s.last)
        self.s(se2)
        ss_sd = ss.n

        def ve2(v):
            v.wait_ge(ss.h, ss_sd)
            v.reciprocal(SD[0:1, 0:1024], NRM[0:1, 1024:2048])
            vv.inc(v.last)
        self.v(ve2)
        vv_rs = vv.n

        def te2(t):
            t.wait_ge(vv.h, vv_rs)
            for f in range(2):
                t.matmul(PB[2 + f][:, 0:512],
                         self.ONER[0:1, :], NRM[0:1, f * 512:(f + 1) * 512],
                         start=True, stop=True)
                tt.inc(t.last)
                t.matmul(PB[4 + f][:, 0:512],
                         self.ONER[0:1, :], SD[0:1, f * 512:(f + 1) * 512],
                         start=True, stop=True)
                tt.inc(t.last)
        self.t(te2)
        tt_bc = tt.n

        def ve3(v):
            v.wait_ge(tt.h, tt_bc)
            v.wait_ge(self.dg.h, base_dg)
            for ch in range(16):
                for f in range(2):
                    v.tensor_tensor(ap_of_chunk(ch)[:, f * 512:(f + 1) * 512],
                                    ap_of_chunk(ch)[:, f * 512:(f + 1) * 512],
                                    PB[2 + f][:], ALU.subtract)
                    vv.inc(v.last)
            v.wait_ge(vv.h, vv.n)
            for ch in range(16):
                for f in range(2):
                    v.tensor_tensor(ap_of_chunk(ch)[:, f * 512:(f + 1) * 512],
                                    ap_of_chunk(ch)[:, f * 512:(f + 1) * 512],
                                    PB[4 + f][:], ALU.mult)
                    vv.inc(v.last)
        self.v(ve3)

    def _scol_bcast(self, m2, col0, ntiles, which, vv_wait):
        D = self.D
        tt, vv, dg, dsc = self.tt, self.vv, self.dg, self.dsc
        PB = self.PB
        SD, CSP = self.SD, self.CSP

        def gy(g, vv_wait=vv_wait):
            g.wait_ge(vv.h, vv_wait)
            for tb in range(ntiles):
                g.dma_start(out=D["md"][which * 2048 + tb * 128:which * 2048 + (tb + 1) * 128, :],
                            in_=m2[:, col0 + tb:col0 + tb + 1])
                dg.inc(g.last, 16)
        self.g(gy)
        dg_b = dg.n

        def sy(y):
            y.wait_ge(dg.h, dg_b)
            y.dma_start(out=CSP[0:1, 0:1024],
                        in_=D["md"][which * 2048:which * 2048 + 1024, :]
                        .rearrange("t one -> one t"))
            dsc.inc(y.last, 16)
        self.y(sy)
        dsc_b = dsc.n

        def te(t):
            t.wait_ge(dsc.h, dsc_b)
            t.wait_ge(vv.h, vv.n)
            for f in range(2):
                t.matmul(PB[2 + f][:, 0:512], self.ONER[0:1, :],
                         CSP[0:1, f * 512:(f + 1) * 512], start=True, stop=True)
                tt.inc(t.last)
        self.t(te)
        tt_b = tt.n

        def ve(v):
            v.wait_ge(tt.h, tt_b)
            for f in range(2):
                v.tensor_copy(SD[:, f * 512:(f + 1) * 512], PB[2 + f][:])
                vv.inc(v.last)
            v.wait_ge(vv.h, vv.n)
            v.reciprocal(SD[:, 1024:2048], SD[:, 0:1024])
            vv.inc(v.last)
        self.v(ve)

    def ph_quant_ternary(self):
        D = self.D
        tt, vv, ss, dg, cc = self.tt, self.vv, self.ss, self.dg, self.cc
        dsa = self.dsa
        PB = self.PB
        SA, SB_, SC = self.SA, self.SB_, self.SC
        MISC2, IDENTB, SBF = self.MISC2, self.IDENTB, self.SBF

        jobs = []
        for r in range(8):
            jobs.append(dict(src=D["w1_sh"][r * 128:(r + 1) * 128, :], wdt=2048,
                             dst=D["bt_w1"], dcol=r, ji=r))
        for r in range(2):
            jobs.append(dict(src=D["w2_sh"][r * 128:(r + 1) * 128, :], wdt=8192,
                             dst=D["bt_w2"], dcol=r, ji=8 + r))

        for jb in jobs:
            self._ternary_job(jb)

        def gq(g):
            g.wait_ge(vv.h, self.vv.n)
            g.wait_ge(self.dg.h, self.dg.n)
            g.collective_compute("AllGather", ALU.bypass,
                                 replica_groups=[list(range(NCORES))],
                                 ins=[D["bt_w1"][:]], outs=[D["bt_w1_all"][:]])
            cc.inc(g.last)
            g.collective_compute("AllGather", ALU.bypass,
                                 replica_groups=[list(range(NCORES))],
                                 ins=[D["bt_w2"][:]], outs=[D["bt_w2_all"][:]])
            cc.inc(g.last)
            g.collective_compute("AllGather", ALU.bypass,
                                 replica_groups=[list(range(NCORES))],
                                 ins=[D["bs_sc"][:]], outs=[D["bs_sc_all"][:]])
            cc.inc(g.last)
        self.g(gq)

    def _ternary_job(self, jb):
        D = self.D
        tt, vv, ss, dg = self.tt, self.vv, self.ss, self.dg
        dsa = self.dsa
        PB = self.PB
        SA, SC = self.SA, self.SC
        MISC2, IDENTB, SBF = self.MISC2, self.IDENTB, self.SBF
        W = jb["wdt"]
        nck = W // 2048
        ji = jb["ji"]

        for kk in range(nck):
            d0 = dsa.n
            def syL(y, jb=jb, kk=kk, d0=d0):
                y.wait_ge(ss.h, ss.n)
                y.wait_ge(vv.h, vv.n)
                y.dma_start(out=SA[:, 0:2048],
                            in_=jb["src"][:, kk * 2048:(kk + 1) * 2048])
                dsa.inc(y.last, 16)
            self.y(syL)
            d1 = dsa.n

            def seL(s, jb=jb, kk=kk, d1=d1):
                s.wait_ge(dsa.h, d1)
                s.activation(SC[:, 0:2048], SA[:, 0:2048], ACT.Abs,
                             scale=1.0 / jb["wdt"],
                             accum_out=MISC2[:, 40 + kk:41 + kk])
                ss.inc(s.last)
            self.s(seL)

        def veL(v, nck=nck, ji=ji):
            v.wait_ge(ss.h, ss.n)
            c0 = MISC2[:, 40:41]
            for kk in range(1, nck):
                v.tensor_tensor(c0, c0, MISC2[:, 40 + kk:41 + kk], ALU.add)
                vv.inc(v.last)
                v.wait_ge(vv.h, vv.n)
            v.tensor_scalar(MISC2[:, 20:21], c0, 1e-5, None, ALU.max)
            vv.inc(v.last)
            v.wait_ge(vv.h, vv.n)
            v.reciprocal(MISC2[:, 36:37], MISC2[:, 20:21])
            vv.inc(v.last)
        self.v(veL)
        vv_sc = vv.n

        def gyS(g, ji=ji, vv_sc=vv_sc):
            g.wait_ge(vv.h, vv_sc)
            g.dma_start(out=D["bs_sc"][ji * 128:(ji + 1) * 128, :],
                        in_=MISC2[:, 20:21])
            dg.inc(g.last, 16)
        self.g(gyS)

        for kk in range(nck):
            def syR(y, jb=jb, kk=kk):
                y.wait_ge(vv.h, vv.n)
                y.dma_start(out=SA[:, 0:2048],
                            in_=jb["src"][:, kk * 2048:(kk + 1) * 2048])
                dsa.inc(y.last, 16)
            self.y(syR)
            d2 = dsa.n

            def veQ(v, d2=d2, vv_sc=vv_sc):
                v.wait_ge(dsa.h, d2)
                v.wait_ge(dg.h, dg.n)
                v.tensor_scalar(SC[:, 0:2048], SA[:, 0:2048],
                                MISC2[:, 36:37], None, ALU.mult)
                vv.inc(v.last)
                v.wait_ge(vv.h, vv.n)
                v.tensor_copy(SC[:, 0:2048].bitcast(I32), SC[:, 0:2048])
                vv.inc(v.last)
                v.wait_ge(vv.h, vv.n)
                v.tensor_scalar(SBF[:, 0:2048], SC[:, 0:2048].bitcast(I32),
                                1.0, -1.0, ALU.min, ALU.max)
                vv.inc(v.last)
            self.v(veQ)
            vv_q = vv.n
            tt_t = tt.n

            def teT(t, vv_q=vv_q, tt_t=tt_t):
                t.wait_ge(vv.h, vv_q)
                for p in range(16):
                    if p >= 2:
                        t.wait_ge(vv.h, vv_q + p - 1)
                    t.transpose(PB[2 + p % 2][:, 0:256].bitcast(BF16)[:, 0:128],
                                SBF[:, p * 128:(p + 1) * 128], IDENTB[:])
                    tt.inc(t.last)
            self.t(teT)

            def veT(v, tt_t=tt_t, vv_q=vv_q):
                for p in range(16):
                    v.wait_ge(tt.h, tt_t + p + 1)
                    v.tensor_copy(SBF[:, p * 128:(p + 1) * 128],
                                  PB[2 + p % 2][:, 0:256].bitcast(BF16)[:, 0:128])
                    vv.inc(v.last)
            self.v(veT)
            vv_t = vv.n

            def gyT(g, jb=jb, kk=kk, vv_t=vv_t):
                g.wait_ge(vv.h, vv_t)
                for p in range(16):
                    g.dma_start(
                        out=jb["dst"][kk * 2048 + p * 128:kk * 2048 + (p + 1) * 128,
                                      jb["dcol"] * 128:(jb["dcol"] + 1) * 128],
                        in_=SBF[:, p * 128:(p + 1) * 128])
                    dg.inc(g.last, 16)
            self.g(gyT)

    def ph_l0_ffn(self):
        D = self.D
        tt, vv, ss, dg, cc = self.tt, self.vv, self.ss, self.dg, self.cc
        dsa, dsc = self.dsa, self.dsc
        PB, PN = self.PB, self.PN
        SA, SB_, SC, SD = self.SA, self.SB_, self.SC, self.SD
        SBF, MISC, MISC2, IDENT = self.SBF, self.MISC, self.MISC2, self.IDENT
        CSP = self.CSP
        x1_ap = self.x1_ap
        kx_ap = lambda ch: self.slot(ch // 4)[:].bitcast(BF16)[:, (ch % 4) * 1024:(ch % 4) * 1024 + 1024]
        kh_ap = lambda hc: self.slot(hc // 8)[:].bitcast(BF16)[:, (hc % 8) * 512:(hc % 8) * 512 + 1024]
        cc_need = cc.n

        kx2_ap = lambda ch: self.slot(ch // 4)[:].bitcast(BF16)[:, (ch % 4) * 1024:(ch % 4) * 1024 + 1024]

        tt0, vv0 = tt.n, vv.n

        def teA(t):
            t.wait_ge(vv.h, vv0)
            i = 0
            for tb in range(8):
                for ch in range(16):
                    if i >= 2:
                        j = i - 2
                        t.wait_ge(vv.h, vv0 + (j // 16) * 17 + (j % 16) + 1)
                    t.transpose(PB[2 + i % 2][:, 0:128],
                                x1_ap(ch)[:, tb * 128:(tb + 1) * 128], IDENT[:])
                    tt.inc(t.last)
                    i += 1
        self.t(teA)

        def veA(v):
            i = 0
            for tb in range(8):
                for ch in range(16):
                    v.wait_ge(tt.h, tt0 + i + 1)
                    v.tensor_copy(SC[:, ch * 128:(ch + 1) * 128],
                                  PB[2 + i % 2][:, 0:128])
                    vv.inc(v.last)
                    i += 1
                v.wait_ge(vv.h, vv.n)
                v.tensor_reduce(MISC2[:, 44 + tb:45 + tb], SC[:, 0:2048],
                                AX.X, ALU.max, apply_absolute_value=True)
                vv.inc(v.last)
                v.wait_ge(vv.h, vv.n)
        self.v(veA)

        def veS(v):
            for tb in range(8):
                v.tensor_scalar(MISC2[:, 44 + tb:45 + tb],
                                MISC2[:, 44 + tb:45 + tb], 1e-5, None, ALU.max)
                vv.inc(v.last)
            v.wait_ge(vv.h, vv.n)
            for tb in range(8):
                v.reciprocal(MISC2[:, 52 + tb:53 + tb], MISC2[:, 44 + tb:45 + tb])
                vv.inc(v.last)
            v.wait_ge(vv.h, vv.n)
            for tb in range(8):
                v.tensor_scalar(MISC2[:, 52 + tb:53 + tb],
                                MISC2[:, 52 + tb:53 + tb], 127.0, None, ALU.mult)
                vv.inc(v.last)
        self.v(veS)
        vv_s = vv.n

        self._scol_bcast(MISC2, 52, 8, 0, vv_s)

        def veK(v):
            v.wait_ge(vv.h, vv.n)
            for ch in range(16):
                v.tensor_tensor(SC[:, 0:1024], x1_ap(ch), SD[:, 0:1024], ALU.mult)
                vv.inc(v.last)
                v.wait_ge(vv.h, vv.n)
                v.tensor_copy(SC[:, 1024:2048].bitcast(I32), SC[:, 0:1024])
                vv.inc(v.last)
                v.wait_ge(vv.h, vv.n)
                v.tensor_copy(kx_ap(ch), SC[:, 1024:2048].bitcast(I32))
                vv.inc(v.last)
                v.wait_ge(vv.h, vv.n)
        self.v(veK)
        vv_kx = vv.n

        FFC = FF // 128
        RLOC = FF // NCORES // 128
        dsw = self.dsw
        base_w = [dsw[k].n for k in range(4)]
        base_tt = tt.n
        base_dsa = dsa.n

        schH = []
        ttc, vvc, ssc, dgc = tt.n, vv.n, ss.n, dg.n
        for hc in range(FFC):
            e = {}
            ttc += 32; e["tt_mm"] = ttc
            vvc += 4; e["vv_t2"] = vvc
            ssc += 1; e["ss_g"] = ssc
            dgc += 16; e["dg_h"] = dgc
            ttc += 8; e["tt_tr"] = ttc
            vvc += 16; e["vv_hm"] = vvc
            schH.append(e)

        base_m = [self.dsm[k].n for k in range(2)]

        def syM(y):
            i = 0
            for hc in range(FFC):
                rank, lc = hc // RLOC, hc % RLOC
                if hc >= 2:
                    y.wait_ge(vv.h, schH[hc - 2]["vv_t2"])
                y.dma_start(out=MISC[:, 8 + hc % 2:9 + hc % 2],
                            in_=D["bs_sc_all"][rank, lc * 128:(lc + 1) * 128, :])
                self.dsm[hc % 2].inc(y.last, 16)
                for c in range(16):
                    if i >= 4:
                        j = i - 4
                        y.wait_ge(tt.h, base_tt + (j // 16) * 40 + (j % 16 + 1) * 2)
                    y.dma_start(out=self.W[i % 4][:].bitcast(BF16)[:, 0:128],
                                in_=D["bt_w1_all"][rank, c * 128:(c + 1) * 128,
                                                   lc * 128:(lc + 1) * 128])
                    dsw[i % 4].inc(y.last, 16)
                    i += 1
        self.y(syM)

        def teM(t):
            i = 0
            for hc in range(FFC):
                e = schH[hc]
                for c in range(16):
                    t.wait_ge(dsw[i % 4].h, base_w[i % 4] + 16 * (i // 4 + 1))
                    if c == 0:
                        if hc == 0:
                            t.wait_ge(vv.h, vv_kx)
                            t.wait_ge(cc.h, cc_need)
                        else:
                            t.wait_ge(vv.h, schH[hc - 1]["vv_t2"])
                    for f in range(2):
                        t.matmul(PB[f][:], self.W[i % 4][:].bitcast(BF16)[:, 0:128],
                                 kx_ap(c)[:, f * 512:(f + 1) * 512],
                                 start=(c == 0), stop=(c == 15))
                        tt.inc(t.last)
                    i += 1
                t.wait_ge(ss.h, e["ss_g"])
                for tb in range(8):
                    if tb >= 2:
                        t.wait_ge(vv.h, e["vv_t2"] + 2 * (tb - 1))
                    t.transpose(PB[2 + tb % 2][:, 0:128],
                                SB_[:, tb * 128:(tb + 1) * 128], IDENT[:])
                    tt.inc(t.last)
        self.t(teM)

        def veM(v):
            for hc in range(FFC):
                e = schH[hc]
                v.wait_ge(tt.h, e["tt_mm"])
                v.wait_ge(self.dsm[hc % 2].h, base_m[hc % 2] + 16 * (hc // 2 + 1))
                for f in range(2):
                    v.tensor_scalar(SC[:, f * 512:(f + 1) * 512], PB[f][:],
                                    MISC[:, 8 + hc % 2:9 + hc % 2], None, ALU.mult)
                    vv.inc(v.last)
                v.wait_ge(vv.h, vv.n)
                for f in range(2):
                    v.tensor_tensor(SA[:, f * 512:(f + 1) * 512],
                                    SC[:, f * 512:(f + 1) * 512],
                                    SD[:, 1024 + f * 512:1024 + (f + 1) * 512],
                                    ALU.mult)
                    vv.inc(v.last)
                assert vv.n == e["vv_t2"]
                for tb in range(8):
                    v.wait_ge(tt.h, e["tt_tr"] - 8 + tb + 1)
                    v.tensor_reduce(MISC2[:, 60:61], PB[2 + tb % 2][:, 0:128],
                                    AX.X, ALU.max, apply_absolute_value=True)
                    vv.inc(v.last)
                    v.wait_ge(vv.h, vv.n)
                    if hc == 0:
                        v.tensor_copy(MISC2[:, 44 + tb:45 + tb], MISC2[:, 60:61])
                    else:
                        v.tensor_tensor(MISC2[:, 44 + tb:45 + tb],
                                        MISC2[:, 44 + tb:45 + tb],
                                        MISC2[:, 60:61], ALU.max)
                    vv.inc(v.last)
                    v.wait_ge(vv.h, vv.n)
                assert vv.n == e["vv_hm"]
        self.v(veM)

        def seM(s):
            for hc in range(FFC):
                e = schH[hc]
                s.wait_ge(vv.h, e["vv_t2"])
                if hc >= 1:
                    s.wait_ge(dg.h, schH[hc - 1]["dg_h"])
                    s.wait_ge(tt.h, schH[hc - 1]["tt_tr"])
                s.activation(SB_[:, 0:1024], SA[:, 0:1024], ACT.Gelu_apprx_tanh)
                ss.inc(s.last)
                assert ss.n == e["ss_g"]
        self.s(seM)

        def gyM(g):
            for hc in range(FFC):
                e = schH[hc]
                g.wait_ge(ss.h, e["ss_g"])
                g.dma_start(out=D["st_hh"][hc * 128:(hc + 1) * 128, :],
                            in_=SB_[:, 0:1024])
                dg.inc(g.last, 16)
                assert dg.n == e["dg_h"]
        self.g(gyM)

        def veS2(v):
            v.wait_ge(vv.h, vv.n)
            for tb in range(8):
                v.tensor_scalar(MISC2[:, 44 + tb:45 + tb],
                                MISC2[:, 44 + tb:45 + tb], 1e-5, None, ALU.max)
                vv.inc(v.last)
            v.wait_ge(vv.h, vv.n)
            for tb in range(8):
                v.reciprocal(MISC2[:, 52 + tb:53 + tb], MISC2[:, 44 + tb:45 + tb])
                vv.inc(v.last)
            v.wait_ge(vv.h, vv.n)
            for tb in range(8):
                v.tensor_scalar(MISC2[:, 52 + tb:53 + tb],
                                MISC2[:, 52 + tb:53 + tb], 127.0, None, ALU.mult)
                vv.inc(v.last)
        self.v(veS2)
        self._scol_bcast(MISC2, 52, 8, 1, self.vv.n)

        dsp = self.dsp
        dsa_p2 = dsp.n
        dg_p2base = dg.n
        ss_p2 = ss.n
        self._p2vv = []
        vvc = self.vv.n
        for hc in range(FFC):
            vvc += 3
            self._p2vv.append(vvc)

        def syP(y):
            for hc in range(FFC):
                y.wait_ge(ss.h, ss_p2)
                y.wait_ge(vv.h, vv.n if hc == 0 else self._p2vv[hc - 1])
                y.dma_start(out=SA[:, 0:1024], in_=D["st_hh"][hc * 128:(hc + 1) * 128, :])
                dsp.inc(y.last, 16)
        self.y(syP)

        def veP(v):
            for hc in range(FFC):
                v.wait_ge(dsp.h, dsa_p2 + 16 * (hc + 1))
                if hc >= 1:
                    v.wait_ge(dg.h, dg_p2base + 16 * hc)
                v.tensor_tensor(SC[:, 0:1024], SA[:, 0:1024], SD[:, 0:1024], ALU.mult)
                vv.inc(v.last)
                v.wait_ge(vv.h, vv.n)
                v.tensor_copy(SC[:, 1024:2048].bitcast(I32), SC[:, 0:1024])
                vv.inc(v.last)
                v.wait_ge(vv.h, vv.n)
                v.tensor_copy(SBF[:, 0:1024], SC[:, 1024:2048].bitcast(I32))
                vv.inc(v.last)
                assert vv.n == self._p2vv[hc]
                v.wait_ge(vv.h, vv.n)
        self.v(veP)

        def gyP(g):
            for hc in range(FFC):
                g.wait_ge(vv.h, self._p2vv[hc])
                g.dma_start(out=D["st_kh"][hc * 128:(hc + 1) * 128, :],
                            in_=SBF[:, 0:1024])
                dg.inc(g.last, 16)
        self.g(gyP)
        dg_p2 = dg.n

        base_tt2 = tt.n
        schM2 = []
        ttc = tt.n
        for qd in range(4):
            for hc in range(FFC):
                ttc += 8
                schM2.append(ttc)

        dsb = self.dsn
        bb = [dsb[k].n for k in range(2)]

        def syN(g):
            i = 0
            for qd in range(4):
                for hc in range(FFC):
                    if i >= 2:
                        g.wait_ge(tt.h, schM2[i - 2])
                    if i == 0:
                        g.wait_ge(dg.h, dg_p2)
                    g.dma_start(out=SBF[:, (i % 2) * 1024:(i % 2) * 1024 + 1024],
                                in_=D["st_kh"][hc * 128:(hc + 1) * 128, :])
                    dsb[i % 2].inc(g.last, 16)
                    i += 1
        self.g(syN)

        base_w2 = [dsw[k].n for k in range(4)]

        def syN2(y):
            i = 0
            for qd in range(4):
                for hc in range(FFC):
                    for q in range(4):
                        oc = qd * 4 + q
                        rank2, lcb = oc // 2, oc % 2
                        if i >= 4:
                            y.wait_ge(tt.h, base_tt2 + (i - 3) * 2)
                        y.dma_start(out=self.W[i % 4][:].bitcast(BF16)[:, 0:128],
                                    in_=D["bt_w2_all"][rank2,
                                                       hc * 128:(hc + 1) * 128,
                                                       lcb * 128:(lcb + 1) * 128])
                        dsw[i % 4].inc(y.last, 16)
                        i += 1
        self.y(syN2)

        vv_ep = []
        vvc = self.vv.n
        for qd in range(4):
            vvc += 16
            vv_ep.append(vvc)

        PSL = [PB[0], PB[1], PB[2], PB[3], PB[4], PB[5], PN[0], PN[1]]

        def teN(t):
            i = 0
            iw = 0
            for qd in range(4):
                for hc in range(FFC):
                    t.wait_ge(dsb[i % 2].h, bb[i % 2] + 16 * (i // 2 + 1))
                    if hc == 0 and qd > 0:
                        t.wait_ge(vv.h, vv_ep[qd - 1])
                    for q in range(4):
                        t.wait_ge(dsw[iw % 4].h, base_w2[iw % 4] + 16 * (iw // 4 + 1))
                        for f in range(2):
                            t.matmul(PSL[q * 2 + f][:],
                                     self.W[iw % 4][:].bitcast(BF16)[:, 0:128],
                                     SBF[:, (i % 2) * 1024 + f * 512:(i % 2) * 1024 + (f + 1) * 512],
                                     start=(hc == 0), stop=(hc == FFC - 1))
                            tt.inc(t.last)
                        iw += 1
                    i += 1
        self.t(teN)

        dso = self.dso
        dsa_q = dso.n

        def syO(s):
            for qd in range(4):
                for q in range(4):
                    oc = qd * 4 + q
                    rank2, lcb = oc // 2, oc % 2
                    if qd >= 1:
                        s.wait_ge(vv.h, vv_ep[qd - 1])
                    s.dma_start(out=MISC[:, 12 + q:13 + q],
                                in_=D["bs_sc_all"][rank2,
                                                   1024 + lcb * 128:1024 + (lcb + 1) * 128, :])
                    dso.inc(s.last, 16)
        self.s(syO)

        def veN(v):
            for qd in range(4):
                v.wait_ge(tt.h, schM2[(qd + 1) * FFC - 1])
                v.wait_ge(dso.h, dsa_q + 16 * 4 * (qd + 1))
                for q in range(4):
                    oc = qd * 4 + q
                    for f in range(2):
                        v.tensor_scalar(SC[:, f * 512:(f + 1) * 512],
                                        PSL[q * 2 + f][:],
                                        MISC[:, 12 + q:13 + q], None, ALU.mult)
                        vv.inc(v.last)
                    v.wait_ge(vv.h, vv.n)
                    v.tensor_tensor(SC[:, 1024:2048], SC[:, 0:1024],
                                    SD[:, 1024:2048], ALU.mult)
                    vv.inc(v.last)
                    v.wait_ge(vv.h, vv.n)
                    v.tensor_tensor(x1_ap(oc), SC[:, 1024:2048], x1_ap(oc),
                                    ALU.add)
                    vv.inc(v.last)
                    v.wait_ge(vv.h, vv.n)
                assert vv.n == vv_ep[qd]
        self.v(veN)

        self._layernorm(x1_ap)
        self.x2_ap = x1_ap

        def gyX(g):
            g.wait_ge(vv.h, self.vv.n)
            for ch in range(16):
                g.dma_start(out=D["bx2"][ch * 128:(ch + 1) * 128, :],
                            in_=x1_ap(ch))
                dg.inc(g.last, 16)
        self.g(gyX)

    def ph_l1_attn(self):
        D = self.D
        tt, vv, ss, dg, cc = self.tt, self.vv, self.ss, self.dg, self.cc
        PAIRS = [[0, 1], [2, 3], [4, 5], [6, 7]]

        def gA(g):
            g.wait_ge(dg.h, dg.n)
            g.collective_compute("AllGather", ALU.bypass, replica_groups=PAIRS,
                                 ins=[D["bx2"][:]], outs=[D["bx2_all"][:]])
            cc.inc(g.last)
        self.g(gA)
        self.cc_x2 = cc.n

        self._l1_inproj()
        self._l1_conv_xproj()
        self._l1_dt()
        self._l1_scan()
        self._l1_outproj()

    def _l1_inproj(self):
        D = self.D
        tt, vv, ss, dg = self.tt, self.vv, self.ss, self.dg
        dsa = self.dsa
        PB = self.PB
        SA = self.SA
        x2f = lambda c: self.slot(c)

        dsa0 = dsa.n

        def syL(y):
            y.wait_ge(self.cc.h, self.cc_x2)
            y.wait_ge(vv.h, vv.n)
            for c in range(16):
                for par in range(2):
                    y.dma_start(
                        out=x2f(c).rearrange("p (g f) -> p g f", f=128)[:, par::2, :],
                        in_=D["bx2_all"][par, c * 128:(c + 1) * 128, :]
                        .rearrange("p (g f) -> p g f", f=128))
                    dsa.inc(y.last, 16)
        self.y(syL)
        dsa_x2 = dsa.n

        items = []
        evac = []
        vvc = vv.n
        dsn = self.dsn
        base_n = [dsn[k].n for k in range(2)]
        for po in range(32):
            for tq in range(4):
                for c in range(16):
                    mw = []
                    if c == 0:
                        mw = [(dsa, dsa_x2)]
                        idx = po * 4 + tq
                        if idx >= 2:
                            mw.append((vv, evac[idx - 2]["vv"]))
                    items.append(dict(
                        w=D["inproj_sl"][c * 128:(c + 1) * 128,
                                         po * 128:(po + 1) * 128],
                        m=x2f(c)[:, tq * 512:(tq + 1) * 512],
                        o=PB[(po * 4 + tq) % 2][:],
                        start=(c == 0), stop=(c == 15), mwait=mw))
                vvc += 1
                evac.append(dict(vv=vvc))
        base_tt = self.tt.n
        self.mm_stream(items)

        def veE(v):
            for idx in range(128):
                po, tq = idx // 4, idx % 4
                v.wait_ge(tt.h, base_tt + (idx + 1) * 16)
                if idx >= 2:
                    v.wait_ge(dsn[idx % 2].h, base_n[idx % 2] + 16 * ((idx - 2) // 2 + 1))
                v.tensor_copy(SA[:, (idx % 2) * 512:(idx % 2) * 512 + 512],
                              PB[idx % 2][:])
                vv.inc(v.last)
        self.v(veE)

        def gyE(g):
            for idx in range(128):
                po, tq = idx // 4, idx % 4
                g.wait_ge(vv.h, evac[idx]["vv"])
                dst = D["st_hid"] if po < 16 else D["st_gate"]
                row = (po % 16) * 128
                g.dma_start(out=dst[row:row + 128, tq * 512:(tq + 1) * 512],
                            in_=SA[:, (idx % 2) * 512:(idx % 2) * 512 + 512])
                dsn[idx % 2].inc(g.last, 16)
        self.g(gyE)

    def _l1_conv_xproj(self):
        D = self.D
        tt, vv, ss, dg, cc = self.tt, self.vv, self.ss, self.dg, self.cc
        dsa, dsc = self.dsa, self.dsc
        PB, PN = self.PB, self.PN
        SA, SB_, SC, MISC2 = self.SA, self.SB_, self.SC, self.MISC2
        PSL = [PB[0][:], PB[1][:], PB[2][:], PB[3][:], PB[4][:], PB[5][:], PN[0][:], PN[1][:]]

        dg_in = None
        dsn_in = [self.dsn[k].n for k in range(2)]
        dsm = self.dsm
        base_cm = [dsm[k].n for k in range(2)]
        sch = []
        vvc, ssc = vv.n, ss.n
        for ct in range(16):
            e = {}
            e["dsa"] = (ct % 2, base_cm[ct % 2] + 48 * (ct // 2 + 1))
            vvc += 4
            e["vv_conv"] = vvc
            ssc += 1
            e["ss"] = ssc
            sch.append(e)

        dsw = self.dsw
        base_w = [dsw[k].n for k in range(4)]
        ttx0_pre = tt.n

        def syC(y):
            i = 0
            for ct in range(16):
                if ct >= 2:
                    y.wait_ge(ss.h, sch[ct - 2]["ss"])
                if ct == 0:
                    y.wait_ge(self.dsn[0].h, dsn_in[0])
                    y.wait_ge(self.dsn[1].h, dsn_in[1])
                y.dma_start(out=SA[:, 0:2048] if ct % 2 == 0 else SB_[:, 0:2048],
                            in_=D["st_hid"][ct * 128:(ct + 1) * 128, :])
                dsm[ct % 2].inc(y.last, 16)
                y.dma_start(out=MISC2[:, (ct % 2) * 5:(ct % 2) * 5 + 4],
                            in_=D["convw"][ct * 128:(ct + 1) * 128, :])
                dsm[ct % 2].inc(y.last, 16)
                y.dma_start(out=MISC2[:, 11 + ct % 2:12 + ct % 2],
                            in_=D["convb"][ct * 128:(ct + 1) * 128, :])
                dsm[ct % 2].inc(y.last, 16)
                for pp in range(2):
                    if i >= 4:
                        j = i - 4
                        y.wait_ge(tt.h, ttx0_pre + (j // 2) * 8 + (j % 2 + 1) * 4)
                    y.dma_start(out=self.W[i % 4][:, 0:(128 if pp == 0 else 32)],
                                in_=D["xproj"][ct * 128:(ct + 1) * 128, 0:128]
                                if pp == 0 else
                                D["xproj"][ct * 128:(ct + 1) * 128, 128:160])
                    dsw[i % 4].inc(y.last, 16)
                    i += 1
        self.y(syC)


        ttx0 = tt.n
        ttx = []
        ttc2 = ttx0
        for ct in range(16):
            ttc2 += 8
            ttx.append(ttc2)

        dg_hs = []
        dgc0 = dg.n
        for ct in range(16):
            dgc0 += 16
            dg_hs.append(dgc0)

        def veC(v):
            for ct in range(16):
                e = sch[ct]
                buf = SA if ct % 2 == 0 else SB_
                v.wait_ge(dsm[e["dsa"][0]].h, e["dsa"][1])
                if ct >= 1:
                    v.wait_ge(vv.h, vv.n)
                    v.wait_ge(ss.h, sch[ct - 1]["ss"])
                    v.wait_ge(tt.h, ttx[ct - 1])
                    v.wait_ge(dg.h, dg_hs[ct - 1])
                v.tensor_scalar(SC[:, 0:2048], buf[:, 0:2048],
                                MISC2[:, (ct % 2) * 5 + 3:(ct % 2) * 5 + 4],
                                None, ALU.mult)
                vv.inc(v.last)
                for j in range(3):
                    k = j + 1
                    v.wait_ge(vv.h, vv.n)
                    v.scalar_tensor_tensor(
                        SC[:, k:2048], buf[:, 0:2048 - k],
                        MISC2[:, (ct % 2) * 5 + 3 - k:(ct % 2) * 5 + 4 - k],
                        SC[:, k:2048], ALU.mult, ALU.add)
                    vv.inc(v.last)
                assert vv.n == e["vv_conv"]
        self.v(veC)

        def seC(s):
            for ct in range(16):
                e = sch[ct]
                s.wait_ge(vv.h, e["vv_conv"])
                if ct >= 1:
                    s.wait_ge(dg.h, dg_hs[ct - 1])
                s.activation(SC[:, 0:2048], SC[:, 0:2048], ACT.Silu,
                             bias=MISC2[:, 11 + ct % 2:12 + ct % 2])
                ss.inc(s.last)
                assert ss.n == e["ss"]

        self.s(seC)

        def gyH(g):
            for ct in range(16):
                e = sch[ct]
                g.wait_ge(ss.h, e["ss"])
                g.dma_start(out=D["st_hs"][ct * 128:(ct + 1) * 128, :],
                            in_=SC[:, 0:2048])
                dg.inc(g.last, 16)
        self.g(gyH)


        def teX(t):
            i = 0
            for ct in range(16):
                e = sch[ct]
                t.wait_ge(ss.h, e["ss"])
                for pp in range(2):
                    t.wait_ge(dsw[i % 4].h, base_w[i % 4] + 16 * (i // 4 + 1))
                    for tq in range(4):
                        wsl = self.W[i % 4][:, 0:(128 if pp == 0 else 32)]
                        out = PSL[tq] if pp == 0 else PSL[4 + tq][0:32, :]
                        t.matmul(out, wsl, SC[:, tq * 512:(tq + 1) * 512],
                                 start=(ct == 0), stop=(ct == 15))
                        tt.inc(t.last)
                    i += 1
                assert tt.n == ttx[ct]
        self.t(teX)

        def veX(v):
            v.wait_ge(tt.h, ttx[-1])
            for tq in range(4):
                v.tensor_copy(SA[:, tq * 512:(tq + 1) * 512], PSL[tq])
                vv.inc(v.last)
            for tq in range(4):
                v.tensor_copy(SB_[0:32, tq * 512:(tq + 1) * 512],
                              PSL[4 + tq][0:32, :])
                vv.inc(v.last)
        self.v(veX)
        vv_sp = vv.n

        def gyX2(g):
            g.wait_ge(vv.h, vv_sp)
            g.dma_start(out=D["bssm"][0:128, :], in_=SA[:, 0:2048])
            dg.inc(g.last, 16)
            g.dma_start(out=D["bssm"][128:160, :], in_=SB_[0:32, 0:2048])
            dg.inc(g.last, 16)
            g.wait_ge(dg.h, dg.n)
            g.collective_compute("AllReduce", ALU.add,
                                 replica_groups=[[0, 1], [2, 3], [4, 5], [6, 7]],
                                 ins=[D["bssm"][:]], outs=[D["bssm_all"][:]])
            cc.inc(g.last)
        self.g(gyX2)
        self.cc_sp = cc.n

    def _l1_dt(self):
        D = self.D
        tt, vv, ss, dg = self.tt, self.vv, self.ss, self.dg
        dsc, dsa = self.dsc, self.dsa
        PB, PN = self.PB, self.PN
        SA, SB_, SC, SD, MISC = self.SA, self.SB_, self.SC, self.SD, self.MISC
        NRM = self.CSP
        BCC = self.BCC

        dsc0 = dsc.n

        def syD(y):
            y.wait_ge(self.cc.h, self.cc_sp)
            y.wait_ge(vv.h, vv.n)
            y.wait_ge(ss.h, ss.n)
            y.wait_ge(tt.h, tt.n)
            y.dma_start(out=SA[:, 0:2048], in_=D["bssm_all"][0:128, :])
            dsc.inc(y.last, 16)
            y.dma_start(out=SB_[0:16, 0:2048], in_=D["bssm_all"][128:144, :])
            dsc.inc(y.last, 16)
            y.dma_start(out=SB_[32:48, 0:2048], in_=D["bssm_all"][144:160, :])
            dsc.inc(y.last, 16)
            y.dma_start(out=BCC[:], in_=D["bccols"][:])
            dsc.inc(y.last, 16)
        self.y(syD)
        dsc_l = dsc.n

        def veZ(v):
            v.wait_ge(dsc.h, dsc_l)
            v.memset(SD[0:64, 0:2048], 0.0)
            vv.inc(v.last)
        self.v(veZ)
        vv_z = vv.n

        def seR(s):
            s.wait_ge(vv.h, vv_z)
            s.activation(SC[:, 0:2048], SA[:, 0:2048], ACT.Square)
            ss.inc(s.last)
            s.activation(SD[0:16, 0:2048], SB_[0:16, 0:2048], ACT.Square)
            ss.inc(s.last)
            s.activation(SD[32:48, 0:2048], SB_[32:48, 0:2048], ACT.Square)
            ss.inc(s.last)
        self.s(seR)
        ss_r = ss.n

        def teR(t):
            t.wait_ge(ss.h, ss_r)
            for f in range(4):
                t.matmul(PB[f][0:1, :], MISC[:, 0:1],
                         SC[:, f * 512:(f + 1) * 512], start=True, stop=True)
                tt.inc(t.last)
        self.t(teR)
        tt_r = tt.n

        def veR(v):
            v.wait_ge(tt.h, tt_r)
            for f in range(4):
                v.tensor_copy(NRM[0:1, f * 512:(f + 1) * 512], PB[f][0:1, :])
                vv.inc(v.last)
        self.v(veR)
        vv_r = vv.n

        def seR2(s):
            s.wait_ge(vv.h, vv_r)
            s.activation(SC[0:1, 0:2048], NRM[0:1, 0:2048], ACT.Sqrt,
                         scale=1.0 / DTR, bias=MISC[0:1, 3:4])
            ss.inc(s.last)
        self.s(seR2)
        ss_r2 = ss.n

        def veR2(v):
            v.wait_ge(ss.h, ss_r2)
            v.reciprocal(NRM[0:1, 0:2048], SC[0:1, 0:2048])
            vv.inc(v.last)
        self.v(veR2)
        vv_r2 = vv.n

        def teR2(t):
            t.wait_ge(vv.h, vv_r2)
            for f in range(4):
                t.matmul(PB[f][:, 0:512], self.ONER[0:1, :],
                         NRM[0:1, f * 512:(f + 1) * 512], start=True, stop=True)
                tt.inc(t.last)
        self.t(teR2)
        tt_r2 = tt.n

        def veR3(v):
            v.wait_ge(tt.h, tt_r2)
            for f in range(4):
                v.tensor_tensor(SA[:, f * 512:(f + 1) * 512],
                                SA[:, f * 512:(f + 1) * 512], PB[f][:], ALU.mult)
                vv.inc(v.last)
        self.v(veR3)
        vv_dtn = vv.n

        for which in range(2):
            vv_pre = vv.n

            def teB(t, which=which, vv_pre=vv_pre):
                t.wait_ge(ss.h, ss_r)
                t.wait_ge(vv.h, vv_pre)
                for f in range(4):
                    t.matmul(PB[f][0:1, :], BCC[:, which:which + 1],
                             SD[0:64, f * 512:(f + 1) * 512], start=True, stop=True)
                    tt.inc(t.last)
            self.t(teB)
            tt_b = tt.n

            def veB(v, which=which):
                v.wait_ge(tt.h, tt_b)
                for f in range(4):
                    v.tensor_copy(NRM[0:1, f * 512:(f + 1) * 512], PB[f][0:1, :])
                    vv.inc(v.last)
            self.v(veB)
            vv_b = vv.n

            def seB(s, which=which):
                s.wait_ge(vv.h, vv_b)
                s.activation(SC[0:1, 0:2048], NRM[0:1, 0:2048], ACT.Sqrt,
                             scale=1.0 / NST, bias=MISC[0:1, 3:4])
                ss.inc(s.last)
            self.s(seB)
            ss_b = ss.n

            def veB2(v, which=which):
                v.wait_ge(ss.h, ss_b)
                v.reciprocal(NRM[0:1, 0:2048], SC[0:1, 0:2048])
                vv.inc(v.last)
            self.v(veB2)
            vv_b2 = vv.n

            def teB2(t, which=which):
                t.wait_ge(vv.h, vv_b2)
                for f in range(4):
                    t.matmul(PB[f][0:32, 0:512], self.ONER[0:1, 0:32],
                             NRM[0:1, f * 512:(f + 1) * 512], start=True, stop=True)
                    tt.inc(t.last)
            self.t(teB2)
            tt_b2 = tt.n

            def veB3(v, which=which):
                v.wait_ge(tt.h, tt_b2)
                r0 = 0 if which == 0 else 32
                for f in range(4):
                    v.tensor_tensor(SB_[r0:r0 + 16, f * 512:(f + 1) * 512],
                                    SB_[r0:r0 + 16, f * 512:(f + 1) * 512],
                                    PB[f][0:16, :], ALU.mult)
                    vv.inc(v.last)
            self.v(veB3)
        vv_bcn = vv.n

        dsw = self.dsw
        base_w = [dsw[k].n for k in range(4)]
        base_tt = tt.n
        dsm = self.dsm
        base_m = [dsm[k].n for k in range(2)]

        sch = []
        vvc, ssc, dgc = vv.n, ss.n, dg.n
        for idx in range(64):
            e = {}
            ssc += 2; e["ss_ex"] = ssc
            vvc += 1; e["vv_l1"] = vvc
            ssc += 2; e["ss"] = ssc
            vvc += 1; e["vv"] = vvc
            dgc += 16; e["dg"] = dgc
            sch.append(e)

        def syT(y):
            for po in range(16):
                if po >= 2:
                    y.wait_ge(vv.h, sch[(po - 2) * 4 + 3]["vv"])
                y.dma_start(out=MISC[:, 16 + po % 2:17 + po % 2],
                            in_=D["dtb"][po * 128:(po + 1) * 128, :])
                dsm[po % 2].inc(y.last, 16)
                i = po
                if i >= 4:
                    j = i - 4
                    y.wait_ge(tt.h, base_tt + (j + 1) * 4)
                y.dma_start(out=self.W[i % 4][:, 0:128],
                            in_=D["dtw"][0:128, po * 128:(po + 1) * 128])
                dsw[i % 4].inc(y.last, 16)
        self.y(syT)

        def teT(t):
            for po in range(16):
                t.wait_ge(dsw[po % 4].h, base_w[po % 4] + 16 * (po // 4 + 1))
                if po == 0:
                    t.wait_ge(vv.h, vv_bcn)
                for tq in range(4):
                    idx = po * 4 + tq
                    if idx >= 2:
                        t.wait_ge(ss.h, sch[idx - 2]["ss"])
                    t.matmul(PB[idx % 2][:], self.W[po % 4][:, 0:128],
                             SA[:, tq * 512:(tq + 1) * 512], start=True, stop=True)
                    tt.inc(t.last)
        self.t(teT)

        def seT(s):
            for po in range(16):
                for tq in range(4):
                    idx = po * 4 + tq
                    e = sch[idx]
                    hb = (idx % 2) * 1024
                    s.wait_ge(tt.h, base_tt + idx + 1)
                    s.wait_ge(dsm[po % 2].h, base_m[po % 2] + 16 * (po // 2 + 1))
                    if idx >= 2:
                        s.wait_ge(vv.h, sch[idx - 2]["vv"])
                    s.activation(SC[:, hb:hb + 512], PB[idx % 2][:], ACT.Abs,
                                 bias=MISC[:, 16 + po % 2:17 + po % 2])
                    ss.inc(s.last)
                    s.wait_ge(ss.h, ss.n)
                    s.activation(SC[:, hb + 512:hb + 1024], SC[:, hb:hb + 512],
                                 ACT.Exp, scale=-1.0)
                    ss.inc(s.last)
                    assert ss.n == e["ss_ex"]
                    s.wait_ge(vv.h, e["vv_l1"])
                    s.activation(SC[:, hb + 512:hb + 1024],
                                 SC[:, hb + 512:hb + 1024], ACT.Ln)
                    ss.inc(s.last)
                    s.activation(SC[:, hb:hb + 512], PB[idx % 2][:], ACT.Relu,
                                 bias=MISC[:, 16 + po % 2:17 + po % 2])
                    ss.inc(s.last)
                    assert ss.n == e["ss"]
        self.s(seT)

        def veT(v):
            for po in range(16):
                for tq in range(4):
                    idx = po * 4 + tq
                    e = sch[idx]
                    hb = (idx % 2) * 1024
                    v.wait_ge(ss.h, e["ss_ex"])
                    v.tensor_scalar(SC[:, hb + 512:hb + 1024],
                                    SC[:, hb + 512:hb + 1024], 1.0, None, ALU.add)
                    vv.inc(v.last)
                    assert vv.n == e["vv_l1"]
                    v.wait_ge(ss.h, e["ss"])
                    if idx >= 2:
                        v.wait_ge(dg.h, sch[idx - 2]["dg"])
                    v.tensor_tensor(SD[:, (idx % 2) * 512:(idx % 2) * 512 + 512],
                                    SC[:, hb:hb + 512],
                                    SC[:, hb + 512:hb + 1024], ALU.add)
                    vv.inc(v.last)
                    assert vv.n == e["vv"]
        self.v(veT)

        def gyT(g):
            for po in range(16):
                for tq in range(4):
                    idx = po * 4 + tq
                    g.wait_ge(vv.h, sch[idx]["vv"])
                    if idx >= 1:
                        g.wait_ge(dg.h, sch[idx - 1]["dg"])
                    g.dma_start(out=D["st_dt"][po * 128:(po + 1) * 128,
                                               tq * 512:(tq + 1) * 512],
                                in_=SD[:, (idx % 2) * 512:(idx % 2) * 512 + 512])
                    dg.inc(g.last, 16)
                    assert dg.n == sch[idx]["dg"]
        self.g(gyT)

    def _l1_scan(self):
        D = self.D
        tt, vv, ss, dg = self.tt, self.vv, self.ss, self.dg
        dsa = self.dsa
        PB = self.PB
        SB_, MISC, MISC2 = self.SB_, self.MISC, self.MISC2
        CK = 256
        NCK = T // CK
        R = self.R
        bBB = R[:, 0:4096].rearrange("p (t n) -> p t n", n=16)
        bCB = R[:, 4096:8192].rearrange("p (t n) -> p t n", n=16)
        bDA = R[:, 8192:12288]
        bDAv = bDA.rearrange("p (t n) -> p t n", n=16)
        bDBU = R[:, 12288:16384]
        bDBUv = bDBU.rearrange("p (t n) -> p t n", n=16)
        bS = R[:, 16384:20480]
        bSv = bS.rearrange("p (t n) -> p t n", n=16)
        bST = R[:, 20480:20736]
        bA = R[:, 20736:20992]
        bU = R[:, 21504:21760]
        bDT = R[:, 21760:22016]
        bG = R[:, 22016:22272]
        bY2 = R[:, 22528:22784]
        HPB = R[0:64, 22784:24832]
        HPC = R[0:64, 24832:26880]

        dg_dt = dg.n
        gg = self.gg

        def gpH(g):
            g.wait_ge(vv.h, vv.n)
            g.iota(R[:, 26880:28928][0:64, :], pattern=[[-1, 16], [0, 128]],
                   base=0, channel_multiplier=1,
                   allow_small_or_imprecise_dtypes=True)
            g.memset(R[0:64, 28928:28929], 0.0)
            gg.inc(g.last)
            g.wait_ge(gg.h, gg.n)
            g.tensor_scalar(HPB[:], R[0:64, 26880:28928], 0.0, None, ALU.is_equal)
            gg.inc(g.last)
            g.wait_ge(gg.h, gg.n)
            g.iota(R[:, 26880:28928][0:64, :], pattern=[[-1, 16], [0, 128]],
                   base=-32, channel_multiplier=1,
                   allow_small_or_imprecise_dtypes=True)
            g.memset(R[0:64, 28928:28929], 0.0)
            gg.inc(g.last)
            g.wait_ge(gg.h, gg.n)
            g.tensor_scalar(HPC[:], R[0:64, 26880:28928], 0.0, None, ALU.is_equal)
            gg.inc(g.last)
        self.g(gpH)
        gg_hp = gg.n

        dsa_a0 = dsa.n

        def syA(y):
            y.wait_ge(vv.h, vv.n)
            for dt_ in range(16):
                y.dma_start(out=R[:, 20992 + dt_ * 16:20992 + (dt_ + 1) * 16],
                            in_=D["Alog"][dt_ * 128:(dt_ + 1) * 128, :])
                dsa.inc(y.last, 16)
            y.dma_start(out=MISC2[:, 48:64],
                        in_=D["Dpar"].rearrange("(c p) one -> p (c one)", p=128),
                        allow_slow_non_contiguous=True)
            dsa.inc(y.last, 16)
        self.y(syA)
        dsa_a = dsa.n

        def seA(s):
            s.wait_ge(dsa.h, dsa_a)
            s.activation(R[:, 20736:20992], R[:, 20992:21248], ACT.Exp)
            ss.inc(s.last)
        self.s(seA)
        ss_a = ss.n

        def veA2(v):
            v.wait_ge(ss.h, ss_a)
            v.tensor_scalar(R[:, 20736:20992], R[:, 20736:20992], -1.0, None,
                            ALU.mult)
            vv.inc(v.last)
            v.wait_ge(vv.h, vv.n)
            v.memset(bST[:], 0.0)
            vv.inc(v.last)
        self.v(veA2)

        sch = []
        vvc, ssc, ttc, dsac, dgc = vv.n, ss.n, tt.n, dsa.n, dg.n
        for ck in range(NCK):
            e = {}
            ttc += 64
            e["tt_bc"] = ttc
            vvc += 64
            e["vv_bc"] = vvc
            tiles = []
            for dt_ in range(16):
                t2 = {}
                dsac += 16 * 3
                t2["dsa"] = dsac
                vvc += 39
                ssc += 2
                t2["vv"] = vvc
                t2["ss"] = ssc
                t2["vv_dtu"] = e["vv_bc"] + sum(
                    39 for _ in range(dt_)) + 1 if False else 0
                dgc += 16
                t2["dg"] = dgc
                tiles.append(t2)
            e["tiles"] = tiles
            sch.append(e)

        def teS(t):
            for ck in range(NCK):
                e = sch[ck]
                if ck == 0:
                    t.wait_ge(vv.h, sch[0]["vv_bc"] - 64)
                    t.wait_ge(gg.h, gg_hp)
                else:
                    t.wait_ge(vv.h, sch[ck - 1]["tiles"][-1]["vv"])
                k = 0
                for n in range(16):
                    for half in range(2):
                        src_ap = SB_[0:64, ck * CK + half * 128:ck * CK + (half + 1) * 128]
                        if k >= 4:
                            t.wait_ge(vv.h, e["vv_bc"] - 64 + k - 2)
                        t.matmul(PB[k % 4][:, 0:128],
                                 HPB[:, n * 128:(n + 1) * 128],
                                 src_ap, start=True, stop=True)
                        tt.inc(t.last)
                        k += 1
                        t.matmul(PB[k % 4][:, 0:128],
                                 HPC[:, n * 128:(n + 1) * 128],
                                 src_ap, start=True, stop=True)
                        tt.inc(t.last)
                        k += 1
        self.t(teS)

        def veS(v):
            for ck in range(NCK):
                e = sch[ck]
                k = 0
                for n in range(16):
                    for half in range(2):
                        v.wait_ge(tt.h, e["tt_bc"] - 64 + k + 1)
                        v.tensor_copy(bBB[:, half * 128:(half + 1) * 128, n],
                                      PB[k % 4][:, 0:128])
                        vv.inc(v.last)
                        k += 1
                        v.wait_ge(tt.h, e["tt_bc"] - 64 + k + 1)
                        v.tensor_copy(bCB[:, half * 128:(half + 1) * 128, n],
                                      PB[k % 4][:, 0:128])
                        vv.inc(v.last)
                        k += 1
                assert vv.n == e["vv_bc"]
                for dt_ in range(16):
                    t2 = e["tiles"][dt_]
                    v.wait_ge(dsa.h, t2["dsa"])
                    if ck > 0 or dt_ > 0:
                        pt = (e["tiles"][dt_ - 1] if dt_ > 0
                              else sch[ck - 1]["tiles"][-1])
                        v.wait_ge(dg.h, pt["dg"])
                    v.tensor_tensor(
                        bDAv[:],
                        bDT[:].rearrange("p (t one) -> p t one", one=1)
                           .broadcast_to([128, CK, 16]),
                        R[:, 20736 + dt_ * 16:20736 + (dt_ + 1) * 16]
                           .rearrange("p (one n) -> p one n", one=1)
                           .broadcast_to([128, CK, 16]),
                        ALU.mult)
                    vv.inc(v.last); v.wait_ge(vv.h, vv.n)
                    v.tensor_tensor(bDT[:], bDT[:], bU[:], ALU.mult)
                    vv.inc(v.last); v.wait_ge(vv.h, vv.n)
                    v.tensor_tensor(
                        bDBUv[:],
                        bDT[:].rearrange("p (t one) -> p t one", one=1)
                           .broadcast_to([128, CK, 16]),
                        bBB[:], ALU.mult)
                    vv.inc(v.last); v.wait_ge(vv.h, vv.n)
                    v.wait_ge(ss.h, t2["ss"] - 1)
                    for n in range(16):
                        v.tensor_tensor_scan(
                            bSv[:, :, n], bDAv[:, :, n], bDBUv[:, :, n],
                            bST[:, dt_ * 16 + n:dt_ * 16 + n + 1],
                            ALU.mult, ALU.add)
                        vv.inc(v.last)
                    v.wait_ge(vv.h, vv.n)
                    for n in range(16):
                        v.tensor_copy(bST[:, dt_ * 16 + n:dt_ * 16 + n + 1],
                                      bSv[:, CK - 1:CK, n])
                        vv.inc(v.last)
                    v.wait_ge(vv.h, vv.n)
                    v.tensor_tensor(bDAv[:], bSv[:], bCB[:], ALU.mult)
                    vv.inc(v.last); v.wait_ge(vv.h, vv.n)
                    v.tensor_reduce(bY2[:], bDAv[:], AX.X, ALU.add)
                    vv.inc(v.last); v.wait_ge(vv.h, vv.n)
                    v.scalar_tensor_tensor(bY2[:], bU[:],
                        MISC2[:, 48 + dt_:49 + dt_], bY2[:],
                        ALU.mult, ALU.add)
                    vv.inc(v.last); v.wait_ge(vv.h, vv.n)
                    v.wait_ge(ss.h, t2["ss"])
                    v.tensor_tensor(bY2[:], bY2[:], bG[:], ALU.mult)
                    vv.inc(v.last)
                    assert vv.n == t2["vv"], (vv.n, t2["vv"])
        self.v(veS)

        def seS(s):
            for ck in range(NCK):
                e = sch[ck]
                for dt_ in range(16):
                    t2 = e["tiles"][dt_]
                    s.wait_ge(vv.h, t2["vv"] - 38)
                    s.activation(bDA[:], bDA[:], ACT.Exp)
                    ss.inc(s.last)
                    s.wait_ge(dsa.h, t2["dsa"])
                    s.activation(bG[:], bG[:], ACT.Silu)
                    ss.inc(s.last)
        self.s(seS)

        def syS(y):
            for ck in range(NCK):
                e = sch[ck]
                for dt_ in range(16):
                    t2 = e["tiles"][dt_]
                    if ck > 0 or dt_ > 0:
                        pt = (e["tiles"][dt_ - 1] if dt_ > 0
                              else sch[ck - 1]["tiles"][-1])
                        y.wait_ge(vv.h, pt["vv"])
                    else:
                        y.wait_ge(vv.h, sch[0]["vv_bc"] - 64)
                        y.wait_ge(dg.h, dg_dt)
                    y.dma_start(out=bU[:],
                                in_=D["st_hs"][dt_ * 128:(dt_ + 1) * 128,
                                               ck * CK:(ck + 1) * CK])
                    dsa.inc(y.last, 16)
                    y.dma_start(out=bDT[:],
                                in_=D["st_dt"][dt_ * 128:(dt_ + 1) * 128,
                                               ck * CK:(ck + 1) * CK])
                    dsa.inc(y.last, 16)
                    y.dma_start(out=bG[:],
                                in_=D["st_gate"][dt_ * 128:(dt_ + 1) * 128,
                                                 ck * CK:(ck + 1) * CK])
                    dsa.inc(y.last, 16)
                    assert dsa.n == t2["dsa"]
        self.y(syS)

        def gyS(g):
            for ck in range(NCK):
                e = sch[ck]
                for dt_ in range(16):
                    t2 = e["tiles"][dt_]
                    g.wait_ge(vv.h, t2["vv"])
                    g.dma_start(out=D["st_yg"][dt_ * 128:(dt_ + 1) * 128,
                                               ck * CK:(ck + 1) * CK],
                                in_=bY2[:])
                    dg.inc(g.last, 16)
                    assert dg.n == t2["dg"]
        self.g(gyS)

    def _l1_outproj(self):
        D = self.D
        tt, vv, ss, dg, cc = self.tt, self.vv, self.ss, self.dg, self.cc
        dsa = self.dsa
        PB = self.PB
        SA, SB_ = self.SA, self.SB_
        dsw = self.dsw
        dsx2 = self.dsx2
        base_w = [dsw[k].n for k in range(4)]
        base_x = [dsx2[k].n for k in range(2)]
        dg_yg = dg.n

        blocks = []
        ttc, vvc, dgc = tt.n, vv.n, dg.n
        base_tt = tt.n
        for t4 in range(4):
            for ocb in range(4):
                e = dict(t4=t4, ocb=ocb)
                ttc += 64
                e["tt"] = ttc
                vvc += 4
                e["vv"] = vvc
                dgc += 16 * 16
                e["dg"] = dgc
                blocks.append(e)

        def syO(y):
            iw = 0
            im = 0
            for bi, e in enumerate(blocks):
                t4, ocb = e["t4"], e["ocb"]
                for ch in range(16):
                    if im >= 2:
                        jb = im - 2
                        y.wait_ge(tt.h, base_tt + jb * 4 + 4)
                    if im == 0:
                        y.wait_ge(dg.h, dg_yg)
                    y.dma_start(out=(SA if im % 2 == 0 else SB_)[:, 0:512],
                                in_=D["st_yg"][ch * 128:(ch + 1) * 128,
                                               t4 * 512:(t4 + 1) * 512])
                    dsx2[im % 2].inc(y.last, 16)
                    im += 1
                    for q in range(4):
                        oc = ocb * 4 + q
                        if iw >= 4:
                            jw = iw - 4
                            y.wait_ge(tt.h, base_tt + jw + 1)
                        y.dma_start(out=self.W[iw % 4][:, 0:128],
                                    in_=D["outproj"][ch * 128:(ch + 1) * 128,
                                                     oc * 128:(oc + 1) * 128])
                        dsw[iw % 4].inc(y.last, 16)
                        iw += 1
        self.y(syO)

        def teO(t):
            iw = 0
            im = 0
            for bi, e in enumerate(blocks):
                for ch in range(16):
                    t.wait_ge(dsx2[im % 2].h, base_x[im % 2] + 16 * (im // 2 + 1))
                    if ch == 0 and bi >= 1:
                        t.wait_ge(vv.h, blocks[bi - 1]["vv"])
                    mbuf = (SA if im % 2 == 0 else SB_)[:, 0:512]
                    im += 1
                    for q in range(4):
                        t.wait_ge(dsw[iw % 4].h, base_w[iw % 4] + 16 * (iw // 4 + 1))
                        t.matmul(PB[q][:], self.W[iw % 4][:, 0:128], mbuf,
                                 start=(ch == 0), stop=(ch == 15))
                        tt.inc(t.last)
                        iw += 1
                assert tt.n == e["tt"]
        self.t(teO)

        def veO(v):
            for bi, e in enumerate(blocks):
                v.wait_ge(tt.h, e["tt"])
                if bi >= 1:
                    v.wait_ge(dg.h, blocks[bi - 1]["dg"])
                for q in range(4):
                    v.tensor_copy(self.SC[:, q * 512:(q + 1) * 512], PB[q][:])
                    vv.inc(v.last)
                assert vv.n == e["vv"]
        self.v(veO)

        def gyO(g):
            for bi, e in enumerate(blocks):
                t4, ocb = e["t4"], e["ocb"]
                g.wait_ge(vv.h, e["vv"])
                for q in range(4):
                    oc = ocb * 4 + q
                    for gi in range(4):
                        gt = t4 * 4 + gi
                        g.dma_start(
                            out=D["bop"][gt % 2, oc * 128:(oc + 1) * 128,
                                         (gt // 2) * 128:(gt // 2 + 1) * 128],
                            in_=self.SC[:, q * 512 + gi * 128:q * 512 + (gi + 1) * 128])
                        dg.inc(g.last, 16)
                assert dg.n == e["dg"]
        self.g(gyO)

        def gR(g):
            g.wait_ge(dg.h, dg.n)
            g.collective_compute("ReduceScatter", ALU.add,
                                 replica_groups=[[0, 1], [2, 3], [4, 5], [6, 7]],
                                 ins=[D["bop"][:]], outs=[D["bop_out"][:]])
            cc.inc(g.last)
        self.g(gR)
        cc_rs = cc.n

        x3_ap = lambda ch: self.half(9 + ch // 2, ch % 2)
        self.x3_ap = x3_ap
        dsa0 = dsa.n

        res_ap = lambda ch: self.half(ch // 2, ch % 2)

        def syR(y):
            y.wait_ge(cc.h, cc_rs)
            y.wait_ge(vv.h, vv.n)
            y.wait_ge(ss.h, ss.n)
            y.wait_ge(dg.h, dg.n)
            for ch in range(16):
                y.dma_start(out=x3_ap(ch),
                            in_=D["bop_out"][ch * 128:(ch + 1) * 128, :])
                dsa.inc(y.last, 16)
                y.dma_start(out=res_ap(ch), in_=D["bx2"][ch * 128:(ch + 1) * 128, :])
                dsa.inc(y.last, 16)
        self.y(syR)
        dsa_r = dsa.n

        def veR(v):
            v.wait_ge(dsa.h, dsa_r)
            for ch in range(16):
                v.tensor_tensor(x3_ap(ch), x3_ap(ch), res_ap(ch), ALU.add)
                vv.inc(v.last)
        self.v(veR)

        self._layernorm(x3_ap)

    def ph_quant_turbo(self):
        D = self.D
        tt, vv, ss, gg = self.tt, self.vv, self.ss, self.gg
        dsa, dso, cc = self.dsa, self.dso, self.cc
        PB, PN = self.PB, self.PN
        SA, SB_, SC, SD, SBF = self.SA, self.SB_, self.SC, self.SD, self.SBF
        MISC2, CSP, R, IDENT = self.MISC2, self.CSP, self.R, self.IDENT

        FWF = FF // NCORES
        CWP = C // NCORES
        chunks = []
        for ti in range(16):
            for cq in range(2):
                sl = (slice(ti * 128, (ti + 1) * 128),
                      slice(cq * 512, (cq + 1) * 512))
                chunks.append(dict(
                    src=D["wfcT_sh"][sl[0], sl[1]],
                    hi=D["q_fc_hi"][sl[0], sl[1]], lo=D["q_fc_lo"][sl[0], sl[1]],
                    nf=4, fw=512))
        for ti in range(64):
            sl = (slice(ti * 128, (ti + 1) * 128), slice(0, CWP))
            chunks.append(dict(
                src=D["wprT_sh"][sl[0], sl[1]],
                hi=D["q_pr_hi"][sl[0], sl[1]], lo=D["q_pr_lo"][sl[0], sl[1]],
                nf=2, fw=256))
        NCH = len(chunks)

        Wb = [SA[:, 0:512], SA[:, 512:1024]]
        zbB = [SA[:, 1024:1536], SA[:, 1536:2048]]
        uB = [SB_[:, 0:512], SB_[:, 512:1024]]
        zB = [SB_[:, 1024:1536], SB_[:, 1536:2048]]
        rB = [SC[:, 0:512], SC[:, 512:1024]]
        Eev = SC[0:32, 1024:1536]
        maskB = SC[:, 1536:1664]
        mvalB = SC[:, 1664:1792]
        erB = [self.ER[:, 0:512], self.ER[:, 512:1024]]
        deqB = SD[:, 1024:1536]
        lodB = SD[:, 1536:2048]
        hiB = SBF[:, 0:512]
        loB = SBF[:, 512:1024]
        q8B = [SBF[:, 1024:1280].bitcast(I8), SBF[:, 1280:1536].bitcast(I8)]
        qsB = SBF[:, 1536:1792].bitcast(I8)
        QTIst = R[:, 256:768]
        QTI = self.QTIT[:]
        ISVP = R[:, 0:128]
        SVP = R[:, 128:256]
        BST = CSP[0:8, 384:896]
        bvT = CSP[0:8, 896:1024]
        ibT = CSP[0:8, 1024:1152]
        scT = CSP[0:8, 1152:1280]
        AMX, BASE, BINV = MISC2[:, 0:8], MISC2[:, 8:16], MISC2[:, 16:24]
        BISV, BSV = MISC2[:, 24:32], MISC2[:, 32:40]
        IBS, SCBD = MISC2[:, 40:48], MISC2[:, 48:56]
        MNE = MISC2[:, 56:64]

        dsa0, dso0 = dsa.n, dso.n
        ttc, vvc, ssc, ggc = tt.n, vv.n, ss.n, gg.n
        dsa_c = dsa0 + 64
        vvc += 1
        vv_qti = vvc
        sch = []
        for m, chk in enumerate(chunks):
            nf = chk["nf"]
            e = {"nf": nf, "fw": chk["fw"]}
            e["dsa_w"] = dsa0 + 64 + 16 * (m + 1)
            ssc += 1; e["ss_abs"] = ssc
            ttc += nf; e["tt_ztr"] = ttc
            ttc += 1; e["tt_btr"] = ttc
            ttc += nf; e["tt_bb"] = ttc
            e["tt_err"] = []
            for j in range(16):
                ttc += 1; e["tt_err"].append(ttc)
            ttc += nf; e["tt_et"] = ttc
            ttc += 2; e["tt_ftr"] = ttc
            ttc += 2 * nf; e["tt_fb"] = ttc
            vvc += 1; e["vv_amax"] = vvc
            vvc += 2; e["vv_binv"] = vvc
            vvc += 1; e["vv_btr"] = vvc
            vvc += 1; e["vv_z"] = vvc
            e["vv_u"] = []; e["vv_q"] = []
            for j in range(16):
                vvc += 1; e["vv_u"].append(vvc)
                vvc += 1; e["vv_q"].append(vvc)
            vvc += 1; e["vv_Eev"] = vvc
            vvc += 7; e["vv_argmin"] = vvc
            vvc += 2; e["vv_fd"] = vvc
            vvc += 2; e["vv_fev"] = vvc
            vvc += 1; e["vv_ts"] = vvc
            vvc += 2; e["vv_qs"] = vvc
            vvc += 1; e["vv_deq"] = vvc
            vvc += 1; e["vv_hi"] = vvc
            vvc += 2; e["vv_lo"] = vvc
            e["ss_e"] = []
            for j in range(16):
                ssc += 1; e["ss_e"].append(ssc)
            e["gg_r"] = []
            for j in range(16):
                ggc += 1; e["gg_r"].append(ggc)
            e["dso"] = dso0 + 32 * (m + 1)
            sch.append(e)

        def syQ(y):
            y.wait_ge(vv.h, vv.n)
            y.wait_ge(tt.h, tt.n)
            y.wait_ge(ss.h, ss.n)
            y.dma_start(out=ISVP, in_=D["qt_isvp"][:]); dsa.inc(y.last, 16)
            y.dma_start(out=SVP, in_=D["qt_svp"][:]); dsa.inc(y.last, 16)
            y.dma_start(out=QTIst, in_=D["qt_ind"][:]); dsa.inc(y.last, 16)
            y.dma_start(out=BST, in_=D["qt_bst"][:]); dsa.inc(y.last, 16)
            for m, chk in enumerate(chunks):
                y.wait_ge(dsa.h, dsa_c + 16 * m)
                if m >= 2:
                    y.wait_ge(vv.h, sch[m - 2]["vv_ts"])
                y.dma_start(out=Wb[m % 2][:, 0:chk["fw"]], in_=chk["src"])
                dsa.inc(y.last, 16)
                e = sch[m]
                y.wait_ge(vv.h, e["vv_hi"])
                y.dma_start(out=chk["hi"], in_=hiB[:, 0:chk["fw"]])
                dso.inc(y.last, 16)
                y.wait_ge(vv.h, e["vv_lo"])
                y.dma_start(out=chk["lo"], in_=loB[:, 0:chk["fw"]])
                dso.inc(y.last, 16)
        self.y(syQ)

        def seQ(s):
            for m in range(NCH):
                e = sch[m]
                s.wait_ge(dsa.h, e["dsa_w"])
                if m >= 2:
                    s.wait_ge(vv.h, sch[m - 2]["vv_z"])
                s.activation(zbB[m % 2][:, 0:e["fw"]], Wb[m % 2][:, 0:e["fw"]],
                             ACT.Abs)
                ss.inc(s.last)
                assert ss.n == e["ss_abs"]
                for j in range(16):
                    s.wait_ge(gg.h, e["gg_r"][j])
                    if j >= 2:
                        s.wait_ge(tt.h, e["tt_err"][j - 2])
                    elif m >= 1:
                        s.wait_ge(tt.h, sch[m - 1]["tt_err"][14 + j])
                    s.activation(erB[j % 2][:, 0:e["fw"]], rB[j % 2][:, 0:e["fw"]],
                                 ACT.Square, scale=SVF[j])
                    ss.inc(s.last)
                    assert ss.n == e["ss_e"][j]
        self.s(seQ)

        def teQ(t):
            for m in range(NCH):
                e = sch[m]
                nf, fw = e["nf"], e["fw"]
                t.wait_ge(ss.h, e["ss_abs"])
                if m == 0:
                    t.wait_ge(vv.h, vv_qti)
                if m >= 2:
                    t.wait_ge(vv.h, sch[m - 2]["vv_amax"])
                for k in range(nf):
                    t.transpose(PB[m % 2][:, k * 128:(k + 1) * 128],
                                zbB[m % 2][:, k * 128:(k + 1) * 128], IDENT[:])
                    tt.inc(t.last)
                assert tt.n == e["tt_ztr"]
                t.wait_ge(vv.h, e["vv_binv"])
                if m >= 1:
                    t.wait_ge(vv.h, sch[m - 1]["vv_deq"])
                t.transpose(PN[1][0:2 * nf, 0:128], BINV[:, 0:2 * nf], IDENT[:])
                tt.inc(t.last)
                assert tt.n == e["tt_btr"]
                t.wait_ge(vv.h, e["vv_btr"])
                for k in range(nf):
                    t.matmul(PB[2 + m % 2][:, k * 128:(k + 1) * 128],
                             BST[0:2 * nf, k * 128:(k + 1) * 128],
                             bvT[0:2 * nf, :], start=True, stop=True)
                    tt.inc(t.last)
                assert tt.n == e["tt_bb"]
                for j in range(16):
                    t.wait_ge(ss.h, e["ss_e"][j])
                    if j == 0 and m >= 1:
                        t.wait_ge(vv.h, sch[m - 1]["vv_Eev"])
                    t.matmul(PB[4][0:32, 0:fw], QTI[:, 32 * j:32 * (j + 1)],
                             erB[j % 2][:, 0:fw], start=(j == 0), stop=(j == 15))
                    tt.inc(t.last)
                    assert tt.n == e["tt_err"][j]
                t.wait_ge(vv.h, e["vv_Eev"])
                if m >= 1:
                    t.wait_ge(vv.h, sch[m - 1]["vv_argmin"])
                for k in range(nf):
                    t.transpose(PB[5][:, 32 * k:32 * (k + 1)],
                                Eev[:, k * 128:(k + 1) * 128], IDENT[0:32, 0:32])
                    tt.inc(t.last)
                assert tt.n == e["tt_et"]
                t.wait_ge(vv.h, e["vv_fd"])
                t.transpose(PN[1][0:2 * nf, 128:256], IBS[:, 0:2 * nf], IDENT[:])
                tt.inc(t.last)
                t.transpose(PN[1][0:2 * nf, 256:384], SCBD[:, 0:2 * nf], IDENT[:])
                tt.inc(t.last)
                assert tt.n == e["tt_ftr"]
                t.wait_ge(vv.h, e["vv_fev"])
                for k in range(nf):
                    t.matmul(PN[0][:, k * 128:(k + 1) * 128],
                             BST[0:2 * nf, k * 128:(k + 1) * 128],
                             ibT[0:2 * nf, :], start=True, stop=True)
                    tt.inc(t.last)
                for k in range(nf):
                    t.matmul(PN[1][:, k * 128:(k + 1) * 128],
                             BST[0:2 * nf, k * 128:(k + 1) * 128],
                             scT[0:2 * nf, :], start=True, stop=True)
                    tt.inc(t.last)
                assert tt.n == e["tt_fb"]
        self.t(teQ)

        def veQ(v):
            v.wait_ge(dsa.h, dsa_c)
            v.tensor_copy(QTI[:], QTIst[:])
            vv.inc(v.last)
            assert vv.n == vv_qti
            for m in range(NCH):
                e = sch[m]
                nf, fw = e["nf"], e["fw"]
                v.wait_ge(tt.h, e["tt_ztr"])
                v.tensor_reduce(
                    AMX[:, 0:2 * nf].rearrange("p (k h) -> p k h", h=2),
                    PB[m % 2][:, 0:nf * 128]
                        .rearrange("p (k h x) -> p k h x", h=2, x=64),
                    AX.X, ALU.max)
                vv.inc(v.last)
                assert vv.n == e["vv_amax"]
                v.wait_ge(vv.h, vv.n)
                v.tensor_scalar(BASE[:, 0:2 * nf], AMX[:, 0:2 * nf],
                                1e-8, 1.0 / 15.0, ALU.max, ALU.mult)
                vv.inc(v.last)
                v.wait_ge(vv.h, vv.n)
                v.reciprocal(BINV[:, 0:2 * nf], BASE[:, 0:2 * nf])
                vv.inc(v.last)
                assert vv.n == e["vv_binv"]
                v.wait_ge(tt.h, e["tt_btr"])
                v.tensor_copy(bvT[0:2 * nf, :], PN[1][0:2 * nf, 0:128])
                vv.inc(v.last)
                assert vv.n == e["vv_btr"]
                v.wait_ge(tt.h, e["tt_bb"])
                v.tensor_tensor(zB[m % 2][:, 0:fw], zbB[m % 2][:, 0:fw],
                                PB[2 + m % 2][:, 0:fw], ALU.mult)
                vv.inc(v.last)
                assert vv.n == e["vv_z"]
                for j in range(16):
                    if j == 0:
                        v.wait_ge(vv.h, e["vv_z"])
                    if j >= 2:
                        v.wait_ge(gg.h, e["gg_r"][j - 2])
                    elif m >= 1:
                        v.wait_ge(gg.h, sch[m - 1]["gg_r"][14 + j])
                    v.tensor_scalar(uB[j % 2][:, 0:fw], zB[m % 2][:, 0:fw],
                                    ISVF[j], None, ALU.mult)
                    vv.inc(v.last)
                    assert vv.n == e["vv_u"][j]
                    v.wait_ge(vv.h, vv.n)
                    v.tensor_scalar(q8B[j % 2][:, 0:fw], uB[j % 2][:, 0:fw],
                                    15.25, None, ALU.min)
                    vv.inc(v.last)
                    assert vv.n == e["vv_q"][j]
                v.wait_ge(tt.h, e["tt_err"][15])
                v.tensor_copy(Eev[:, 0:fw], PB[4][0:32, 0:fw])
                vv.inc(v.last)
                assert vv.n == e["vv_Eev"]
                v.wait_ge(tt.h, e["tt_et"])
                etv = PB[5][:, 0:32 * nf].rearrange("p (k j h) -> p k h j",
                                                    j=16, h=2)
                v.tensor_reduce(
                    MNE[:, 0:2 * nf].rearrange("p (k h) -> p k h", h=2),
                    etv, AX.X, ALU.min)
                vv.inc(v.last)
                v.wait_ge(vv.h, vv.n)
                v.tensor_tensor(
                    maskB[:, 0:32 * nf]
                        .rearrange("p (k j h) -> p k h j", j=16, h=2),
                    etv,
                    MNE[:, 0:2 * nf].rearrange("p (k h o) -> p k h o", h=2, o=1)
                        .broadcast_to([128, nf, 2, 16]),
                    ALU.is_equal)
                vv.inc(v.last)
                v.wait_ge(vv.h, vv.n)
                v.tensor_tensor(mvalB[:, 0:32 * nf], maskB[:, 0:32 * nf],
                                ISVP[:, 0:32 * nf], ALU.mult)
                vv.inc(v.last)
                v.wait_ge(vv.h, vv.n)
                v.tensor_reduce(
                    BISV[:, 0:2 * nf].rearrange("p (k h) -> p k h", h=2),
                    mvalB[:, 0:32 * nf].rearrange("p (k j h) -> p k h j",
                                                  j=16, h=2),
                    AX.X, ALU.max)
                vv.inc(v.last)
                v.wait_ge(vv.h, vv.n)
                v.tensor_tensor(
                    maskB[:, 0:32 * nf]
                        .rearrange("p (k j h) -> p k h j", j=16, h=2),
                    mvalB[:, 0:32 * nf].rearrange("p (k j h) -> p k h j",
                                                  j=16, h=2),
                    BISV[:, 0:2 * nf].rearrange("p (k h o) -> p k h o", h=2, o=1)
                        .broadcast_to([128, nf, 2, 16]),
                    ALU.is_equal)
                vv.inc(v.last)
                v.wait_ge(vv.h, vv.n)
                v.tensor_tensor(mvalB[:, 0:32 * nf], maskB[:, 0:32 * nf],
                                SVP[:, 0:32 * nf], ALU.mult)
                vv.inc(v.last)
                v.wait_ge(vv.h, vv.n)
                v.tensor_reduce(
                    BSV[:, 0:2 * nf].rearrange("p (k h) -> p k h", h=2),
                    mvalB[:, 0:32 * nf].rearrange("p (k j h) -> p k h j",
                                                  j=16, h=2),
                    AX.X, ALU.max)
                vv.inc(v.last)
                assert vv.n == e["vv_argmin"]
                v.wait_ge(vv.h, vv.n)
                v.tensor_tensor(IBS[:, 0:2 * nf], BISV[:, 0:2 * nf],
                                BINV[:, 0:2 * nf], ALU.mult)
                vv.inc(v.last)
                v.tensor_tensor(SCBD[:, 0:2 * nf], BSV[:, 0:2 * nf],
                                BASE[:, 0:2 * nf], ALU.mult)
                vv.inc(v.last)
                assert vv.n == e["vv_fd"]
                v.wait_ge(tt.h, e["tt_ftr"])
                v.tensor_copy(ibT[0:2 * nf, :], PN[1][0:2 * nf, 128:256])
                vv.inc(v.last)
                v.tensor_copy(scT[0:2 * nf, :], PN[1][0:2 * nf, 256:384])
                vv.inc(v.last)
                assert vv.n == e["vv_fev"]
                v.wait_ge(tt.h, e["tt_fb"])
                v.tensor_tensor(zB[m % 2][:, 0:fw], Wb[m % 2][:, 0:fw],
                                PN[0][:, 0:fw], ALU.mult)
                vv.inc(v.last)
                assert vv.n == e["vv_ts"]
                v.wait_ge(vv.h, vv.n)
                v.tensor_scalar(zB[m % 2][:, 0:fw], zB[m % 2][:, 0:fw],
                                -15.25, 15.25, ALU.max, ALU.min)
                vv.inc(v.last)
                v.wait_ge(vv.h, vv.n)
                v.tensor_copy(qsB[:, 0:fw], zB[m % 2][:, 0:fw])
                vv.inc(v.last)
                assert vv.n == e["vv_qs"]
                v.wait_ge(vv.h, vv.n)
                v.tensor_tensor(deqB[:, 0:fw], qsB[:, 0:fw],
                                PN[1][:, 0:fw], ALU.mult)
                vv.inc(v.last)
                assert vv.n == e["vv_deq"]
                v.wait_ge(vv.h, vv.n)
                if m >= 1:
                    v.wait_ge(dso.h, sch[m - 1]["dso"])
                v.tensor_copy(hiB[:, 0:fw], deqB[:, 0:fw])
                vv.inc(v.last)
                assert vv.n == e["vv_hi"]
                v.wait_ge(vv.h, vv.n)
                v.tensor_tensor(lodB[:, 0:fw], deqB[:, 0:fw],
                                hiB[:, 0:fw], ALU.subtract)
                vv.inc(v.last)
                v.wait_ge(vv.h, vv.n)
                v.tensor_copy(loB[:, 0:fw], lodB[:, 0:fw])
                vv.inc(v.last)
                assert vv.n == e["vv_lo"]
        self.v(veQ)

        def gyQ(g):
            for m in range(NCH):
                e = sch[m]
                fw = e["fw"]
                for j in range(16):
                    g.wait_ge(vv.h, e["vv_q"][j])
                    if j >= 2:
                        g.wait_ge(ss.h, e["ss_e"][j - 2])
                    elif m >= 1:
                        g.wait_ge(ss.h, sch[m - 1]["ss_e"][14 + j])
                    g.tensor_tensor(rB[j % 2][:, 0:fw], q8B[j % 2][:, 0:fw],
                                    uB[j % 2][:, 0:fw], ALU.subtract)
                    gg.inc(g.last)
                    assert gg.n == e["gg_r"][j]
        self.g(gyQ)

        def gAG(g):
            g.wait_ge(dso.h, sch[-1]["dso"])
            for src, dst in (("q_fc_hi", "q_fc_hi_all"), ("q_fc_lo", "q_fc_lo_all"),
                             ("q_pr_hi", "q_pr_hi_all"), ("q_pr_lo", "q_pr_lo_all")):
                g.collective_compute(
                    "AllGather", ALU.bypass,
                    replica_groups=[[0, 1, 2, 3, 4, 5, 6, 7]],
                    ins=[D[src][:]], outs=[D[dst][:]])
                cc.inc(g.last)
        self.g(gAG)
        self.cc_qt = cc.n

    def ph_l1_ffn(self):
        D = self.D
        tt, vv, ss, gg, cc, dg = self.tt, self.vv, self.ss, self.gg, self.cc, self.dg
        PB, PN = self.PB, self.PN
        SA, SB_, SBF = self.SA, self.SB_, self.SBF
        R, W = self.R, self.W
        dsw, dsx2, dskv = self.dsw, self.dsx2, self.dskv
        x3_ap = self.x3_ap
        XHI = R[:, 0:8192].bitcast(BF16)
        XLO = R[:, 8192:16384].bitcast(BF16)
        SAB = SA[:].bitcast(BF16)

        vv0 = vv.n

        def veA(v):
            for ch in range(16):
                v.wait_ge(vv.h, vv.n)
                v.tensor_copy(XHI[:, ch * 1024:(ch + 1) * 1024], x3_ap(ch))
                vv.inc(v.last)
                v.wait_ge(vv.h, vv.n)
                v.tensor_tensor(SA[:, 0:1024], x3_ap(ch),
                                XHI[:, ch * 1024:(ch + 1) * 1024], ALU.subtract)
                vv.inc(v.last)
                v.wait_ge(vv.h, vv.n)
                v.tensor_copy(XLO[:, ch * 1024:(ch + 1) * 1024], SA[:, 0:1024])
                vv.inc(v.last)
        self.v(veA)
        vv_xs = vv.n

        base_tt = tt.n
        base_ss = ss.n
        base_vv = vv.n
        base_dg = dg.n
        base_w = [dsw[k].n for k in range(4)]
        sch1 = []
        ttc, ssc, vvc, dgc = tt.n, ss.n, vv.n, dg.n
        grp_free = []
        for ffo in range(64):
            e = {}
            for g in range(4):
                grp_free.append(ttc + (4 * g + 3) * 6 + 4)
                grp_free.append(ttc + (4 * g + 4) * 6)
            ttc += 96
            e["tt"] = ttc
            e["relu"] = []
            e["h2"] = []
            e["hi"] = []
            e["lo"] = []
            e["dg"] = []
            for half in range(2):
                ssc += 1; e["relu"].append(ssc)
                vvc += 1; e["h2"].append(vvc)
                vvc += 1; e["hi"].append(vvc)
                vvc += 2; e["lo"].append(vvc)
                dgc += 32; e["dg"].append(dgc)
            sch1.append(e)

        def syF(y):
            y.wait_ge(cc.h, self.cc_qt)
            y.wait_ge(vv.h, vv_xs)
            i = 0
            for ffo in range(64):
                rk, loc = ffo // 8, (ffo % 8) * 128
                for g in range(4):
                    for srcn in ("q_fc_hi_all", "q_fc_lo_all"):
                        if i >= 4:
                            y.wait_ge(tt.h, grp_free[i - 4])
                        src = D[srcn][rk, g * 512:(g + 1) * 512, loc:loc + 128] \
                            .rearrange("(a p) f -> p a f", a=4)
                        y.dma_start(
                            out=W[i % 4][:].bitcast(BF16)[:, 0:512]
                                .rearrange("p (a f) -> p a f", a=4),
                            in_=src)
                        dsw[i % 4].inc(y.last, 16)
                        i += 1
        self.y(syF)

        def teF(t):
            i = 0
            for ffo in range(64):
                e = sch1[ffo]
                bk0 = (ffo % 2) * 2
                for ci in range(16):
                    if ci % 4 == 0:
                        t.wait_ge(dsw[i % 4].h, base_w[i % 4] + 16 * (i // 4 + 1))
                        whib = W[i % 4][:].bitcast(BF16)
                        i += 1
                        t.wait_ge(dsw[i % 4].h, base_w[i % 4] + 16 * (i // 4 + 1))
                        wlob = W[i % 4][:].bitcast(BF16)
                        i += 1
                    if ffo >= 2 and ci == 0:
                        for half in range(2):
                            t.wait_ge(ss.h, sch1[ffo - 2]["relu"][half])
                    whi = whib[:, (ci % 4) * 128:(ci % 4 + 1) * 128]
                    wlo = wlob[:, (ci % 4) * 128:(ci % 4 + 1) * 128]
                    st = (ci == 0)
                    for half in range(2):
                        t.matmul(PB[bk0 + half][:],
                                 whi, XHI[:, ci * 1024 + half * 512:
                                          ci * 1024 + (half + 1) * 512],
                                 start=st, stop=False)
                        tt.inc(t.last)
                    for half in range(2):
                        t.matmul(PB[bk0 + half][:],
                                 whi, XLO[:, ci * 1024 + half * 512:
                                          ci * 1024 + (half + 1) * 512],
                                 start=False, stop=False)
                        tt.inc(t.last)
                    for half in range(2):
                        t.matmul(PB[bk0 + half][:],
                                 wlo, XHI[:, ci * 1024 + half * 512:
                                          ci * 1024 + (half + 1) * 512],
                                 start=False, stop=(ci == 15))
                        tt.inc(t.last)
                assert tt.n == e["tt"]
        self.t(teF)

        def seF(s):
            for ffo in range(64):
                e = sch1[ffo]
                bk0 = (ffo % 2) * 2
                for half in range(2):
                    s.wait_ge(tt.h, e["tt"] - (1 - half))
                    if ffo >= 1:
                        s.wait_ge(vv.h, sch1[ffo - 1]["h2"][half])
                    s.activation(SA[:, half * 512:(half + 1) * 512],
                                 PB[bk0 + half][:], ACT.Relu)
                    ss.inc(s.last)
                    assert ss.n == e["relu"][half]
        self.s(seF)

        def veF(v):
            for ffo in range(64):
                e = sch1[ffo]
                for half in range(2):
                    v.wait_ge(ss.h, e["relu"][half])
                    v.tensor_tensor(SB_[:, half * 512:(half + 1) * 512],
                                    SA[:, half * 512:(half + 1) * 512],
                                    SA[:, half * 512:(half + 1) * 512], ALU.mult)
                    vv.inc(v.last)
                    assert vv.n == e["h2"][half]
                    v.wait_ge(vv.h, vv.n)
                    if ffo >= 1:
                        v.wait_ge(dg.h, sch1[ffo - 1]["dg"][1])
                    v.tensor_copy(SBF[:, half * 512:(half + 1) * 512],
                                  SB_[:, half * 512:(half + 1) * 512])
                    vv.inc(v.last)
                    assert vv.n == e["hi"][half]
                    v.wait_ge(vv.h, vv.n)
                    v.tensor_tensor(SB_[:, 1024 + half * 512:1024 + (half + 1) * 512],
                                    SB_[:, half * 512:(half + 1) * 512],
                                    SBF[:, half * 512:(half + 1) * 512],
                                    ALU.subtract)
                    vv.inc(v.last)
                    v.wait_ge(vv.h, vv.n)
                    v.tensor_copy(SBF[:, 1024 + half * 512:1024 + (half + 1) * 512],
                                  SB_[:, 1024 + half * 512:1024 + (half + 1) * 512])
                    vv.inc(v.last)
                    assert vv.n == e["lo"][half]
        self.v(veF)

        def gyF(g):
            for ffo in range(64):
                e = sch1[ffo]
                for half in range(2):
                    g.wait_ge(vv.h, e["hi"][half])
                    g.dma_start(out=D["st_kh"][ffo * 128:(ffo + 1) * 128,
                                               half * 512:(half + 1) * 512],
                                in_=SBF[:, half * 512:(half + 1) * 512])
                    dg.inc(g.last, 16)
                    g.wait_ge(vv.h, e["lo"][half])
                    g.dma_start(out=D["st_h2lo"][ffo * 128:(ffo + 1) * 128,
                                                 half * 512:(half + 1) * 512],
                                in_=SBF[:, 1024 + half * 512:1024 + (half + 1) * 512])
                    dg.inc(g.last, 16)
                    assert dg.n == e["dg"][half]
        self.g(gyF)
        dg_h2 = dg.n

        PSL = [PB[0], PB[1], PB[2], PB[3], PB[4], PB[5], PN[0], PN[1]]
        base_w2 = [dsw[k].n for k in range(4)]
        MOVS = [dsx2, dskv, self.dsq, self.dsm]
        base_mv = [[s[0].n, s[1].n] for s in MOVS]
        sch2 = []
        ttc, vvc = tt.n, vv.n
        for cog in range(4):
            e = {}
            ttc += 64 * 24
            e["tt"] = ttc
            e["ev"] = []
            for q in range(4):
                for half in range(2):
                    vvc += 1
                    e["ev"].append(vvc)
            sch2.append(e)
        tt2base = tt.n

        def mmcnt(cog, ffi):
            return tt2base + cog * 64 * 24 + ffi * 24

        def syM(y):
            y.wait_ge(dg.h, dg_h2)
            for cog in range(4):
                for ffi in range(64):
                    gi = cog * 64 + ffi
                    if gi >= 2:
                        y.wait_ge(tt.h, mmcnt((gi - 2) // 64, (gi - 2) % 64) + 24)
                    src_h = D["q_pr_hi_all"][2 * cog:2 * cog + 2,
                                             ffi * 128:(ffi + 1) * 128, :]
                    y.dma_start(
                        out=W[gi % 2][:].bitcast(BF16)[:, 0:512]
                            .rearrange("p (a f) -> p a f", a=2),
                        in_=src_h.rearrange("a p f -> p a f"))
                    dsw[gi % 2].inc(y.last, 16)
                    src_l = D["q_pr_lo_all"][2 * cog:2 * cog + 2,
                                             ffi * 128:(ffi + 1) * 128, :]
                    y.dma_start(
                        out=W[2 + gi % 2][:].bitcast(BF16)[:, 0:512]
                            .rearrange("p (a f) -> p a f", a=2),
                        in_=src_l.rearrange("a p f -> p a f"))
                    dsw[2 + gi % 2].inc(y.last, 16)
                    for half in range(2):
                        y.dma_start(
                            out=SAB[:, half * 1024 + (gi % 2) * 512:
                                    half * 1024 + (gi % 2) * 512 + 512],
                            in_=D["st_kh"][ffi * 128:(ffi + 1) * 128,
                                           half * 512:(half + 1) * 512])
                        MOVS[half][gi % 2].inc(y.last, 16)
                        y.dma_start(
                            out=SAB[:, (2 + half) * 1024 + (gi % 2) * 512:
                                    (2 + half) * 1024 + (gi % 2) * 512 + 512],
                            in_=D["st_h2lo"][ffi * 128:(ffi + 1) * 128,
                                             half * 512:(half + 1) * 512])
                        MOVS[2 + half][gi % 2].inc(y.last, 16)
        self.y(syM)

        def teM(t):
            for cog in range(4):
                e = sch2[cog]
                for ffi in range(64):
                    gi = cog * 64 + ffi
                    t.wait_ge(dsw[gi % 2].h, base_w2[gi % 2] + 16 * (gi // 2 + 1))
                    t.wait_ge(dsw[2 + gi % 2].h, base_w2[2 + gi % 2] + 16 * (gi // 2 + 1))
                    for stn in range(4):
                        t.wait_ge(MOVS[stn][gi % 2].h,
                                  base_mv[stn][gi % 2] + 16 * (gi // 2 + 1))
                    if ffi == 0 and cog >= 1:
                        t.wait_ge(vv.h, sch2[cog - 1]["ev"][-1])
                    whib = W[gi % 2][:].bitcast(BF16)
                    wlob = W[2 + gi % 2][:].bitcast(BF16)
                    for q in range(4):
                        whi = whib[:, q * 128:(q + 1) * 128]
                        wlo = wlob[:, q * 128:(q + 1) * 128]
                        for half in range(2):
                            mhi = SAB[:, half * 1024 + (gi % 2) * 512:
                                      half * 1024 + (gi % 2) * 512 + 512]
                            mlo = SAB[:, (2 + half) * 1024 + (gi % 2) * 512:
                                      (2 + half) * 1024 + (gi % 2) * 512 + 512]
                            out = PSL[q * 2 + half][:]
                            t.matmul(out, whi, mhi, start=(ffi == 0), stop=False)
                            tt.inc(t.last)
                            t.matmul(out, whi, mlo, start=False, stop=False)
                            tt.inc(t.last)
                            t.matmul(out, wlo, mhi, start=False, stop=(ffi == 63))
                            tt.inc(t.last)
                assert tt.n == e["tt"]
        self.t(teM)

        def veM(v):
            for cog in range(4):
                e = sch2[cog]
                v.wait_ge(tt.h, e["tt"])
                k = 0
                for q in range(4):
                    co = cog * 4 + q
                    for half in range(2):
                        v.tensor_tensor(
                            x3_ap(co)[:, half * 512:(half + 1) * 512],
                            x3_ap(co)[:, half * 512:(half + 1) * 512],
                            PSL[q * 2 + half][:], ALU.add)
                        vv.inc(v.last)
                        assert vv.n == e["ev"][k]
                        k += 1
        self.v(veM)

        self._layernorm(x3_ap)

    def ph_out(self):
        D = self.D
        ds = self.ds
        src_ap = self.x1_ap

        def sy(y):
            for s_ in (self.tt, self.vv, self.ss, self.gg):
                y.wait_ge(s_.h, s_.n)
            for ch in range(16):
                y.dma_start(out=D["out"][ch * 128:(ch + 1) * 128, :],
                            in_=src_ap(ch))
                ds.inc(y.last, 16)
            y.wait_ge(ds.h, ds.n)
        self.y(sy)


def stage_inputs(ins):
    x = np.ascontiguousarray(ins["L0_attn_hidden_states"])
    w1 = np.ascontiguousarray(ins["L0_ffn_w1"])
    w2 = np.ascontiguousarray(ins["L0_ffn_w2"])
    wfcT = np.ascontiguousarray(ins["L1_ffn_w_fc"].T)
    wprT = np.ascontiguousarray(ins["L1_ffn_w_proj"].T)
    convw_full = np.ascontiguousarray(ins["L1_attn_conv_w"][:, 0, :])
    base = {
        "Wq": np.ascontiguousarray(ins["L0_attn_Wq"]),
        "Wk": np.ascontiguousarray(ins["L0_attn_Wk"]),
        "Wv": np.ascontiguousarray(ins["L0_attn_Wv"]),
        "Wg": np.ascontiguousarray(ins["L0_attn_Wg"]),
        "Wc": np.ascontiguousarray(ins["L0_attn_Wc"]),
        "bc": np.ascontiguousarray(ins["L0_attn_bc"]).reshape(C, 1),
        "w1": w1, "w2": w2,
    }
    tril = np.tril(np.ones((128, 128), np.float32))
    diagm = np.ascontiguousarray(tril.T)
    bccols = np.zeros((64, 2), np.float32)
    bccols[0:16, 0] = 1.0
    bccols[32:48, 1] = 1.0
    qt_ind = np.zeros((128, 512), np.float32)
    for j in range(16):
        qt_ind[0:64, 32 * j + 2 * j] = 1.0
        qt_ind[64:128, 32 * j + 2 * j + 1] = 1.0
    qt_bst = np.zeros((8, 512), np.float32)
    for k in range(4):
        qt_bst[2 * k, 128 * k:128 * k + 64] = 1.0
        qt_bst[2 * k + 1, 128 * k + 64:128 * (k + 1)] = 1.0
    qt_svp = np.zeros((128, 128), np.float32)
    qt_isvp = np.zeros((128, 128), np.float32)
    for k in range(4):
        for j in range(16):
            for hb in range(2):
                qt_svp[:, 32 * k + 2 * j + hb] = SVF[j]
                qt_isvp[:, 32 * k + 2 * j + hb] = ISVF[j]
    base.update(qt_ind=qt_ind, qt_bst=qt_bst, qt_svp=qt_svp, qt_isvp=qt_isvp)
    inproj_full = np.ascontiguousarray(ins["L1_attn_in_proj_w"])
    maps = []
    for core in range(NCORES):
        b, h = core // 2, core % 2
        ch0 = h * CHH
        ot = own_tiles(h)
        xb = x[b].T
        m = dict(base)
        m["xT"] = np.ascontiguousarray(xb)
        m["xT_own"] = np.ascontiguousarray(
            np.concatenate([xb[:, g * 128:(g + 1) * 128] for g in ot], axis=1))
        if h == 0:
            cm = np.concatenate([diagm, np.zeros((128, 128), np.float32)], 1)
        else:
            cm = np.concatenate([np.ones((128, 128), np.float32), diagm], 1)
        m["cmask"] = np.ascontiguousarray(cm)
        m["hsel"] = np.ascontiguousarray(
            np.tile(np.array([[1.0 - h, float(h)]], np.float32), (8, 1)))
        hp = np.zeros((8, 1024), np.float32)
        for hh_ in range(8):
            hp[hh_, hh_ * 128:(hh_ + 1) * 128] = 1.0
        m["hpick"] = hp
        m["w1_sh"] = np.ascontiguousarray(w1[core * (FF // NCORES):(core + 1) * (FF // NCORES), :])
        m["w2_sh"] = np.ascontiguousarray(w2[core * (C // NCORES):(core + 1) * (C // NCORES), :])
        m["wfcT_sh"] = np.ascontiguousarray(wfcT[:, core * (FF // NCORES):(core + 1) * (FF // NCORES)])
        m["wprT_sh"] = np.ascontiguousarray(wprT[:, core * (C // NCORES):(core + 1) * (C // NCORES)])
        m["inproj_sl"] = np.ascontiguousarray(np.concatenate(
            [inproj_full[:, ch0:ch0 + CHH],
             inproj_full[:, INTER + ch0:INTER + ch0 + CHH]], axis=1))
        m["bccols"] = bccols
        m["convw"] = np.ascontiguousarray(convw_full[ch0:ch0 + CHH, :])
        m["convb"] = np.ascontiguousarray(ins["L1_attn_conv_b"][ch0:ch0 + CHH]).reshape(CHH, 1)
        m["xproj"] = np.ascontiguousarray(ins["L1_attn_x_proj_w"][ch0:ch0 + CHH, :])
        m["dtw"] = np.ascontiguousarray(ins["L1_attn_dt_proj_w"][:, ch0:ch0 + CHH])
        m["dtb"] = np.ascontiguousarray(ins["L1_attn_dt_proj_b"][ch0:ch0 + CHH]).reshape(CHH, 1)
        m["Alog"] = np.ascontiguousarray(ins["L1_attn_A_log"][ch0:ch0 + CHH, :])
        m["Dpar"] = np.ascontiguousarray(ins["L1_attn_D_param"][ch0:ch0 + CHH]).reshape(CHH, 1)
        m["outproj"] = np.ascontiguousarray(ins["L1_attn_out_proj_w"][ch0:ch0 + CHH, :])
        maps.append(m)
    return maps


def assemble_output(results):
    out = np.empty((B, T, C), dtype=np.float32)
    for core in range(NCORES):
        b, h = core // 2, core % 2
        o = results[core]["out"]
        for i, g in enumerate(own_tiles(h)):
            out[b, g * 128:(g + 1) * 128, :] = o[:, i * 128:(i + 1) * 128].T
    return out


def kernel(**inputs):
    ins = {k: np.asarray(v) for k, v in inputs.items()}
    maps = stage_inputs(ins)
    nc, prog = build_program()
    res = run_bass_kernel_spmd(nc, maps, core_ids=list(range(NCORES)))
    return assemble_output(res.results)


if __name__ == "__main__":
    import tempfile
    from concourse.bass_utils import compile_bass_kernel
    nc, prog = build_program()
    print("built, instructions:", nc.n_instructions())
    with tempfile.TemporaryDirectory() as td:
        compile_bass_kernel(nc, td)
    print("COMPILE OK")


# version 08 (final)
# speedup vs baseline: 1.0313x; 1.0115x over previous
import numpy as np
import concourse.bass as bass
import concourse.mybir as mybir
from concourse.bass_utils import run_bass_kernel_spmd

F32 = mybir.dt.float32
BF16 = mybir.dt.bfloat16
I32 = mybir.dt.int32
I8 = mybir.dt.int8
F32R = mybir.dt.float32r
ALU = mybir.AluOpType
ACT = mybir.ActivationFunctionType
AX = mybir.AxisListType

B, T, C = 4, 2048, 2048
NCORES = 8
TOWN = 1024
NT = T // 128
H, RR, DH = 8, 2, 128
HQ = H * RR
FF = 8192
INTER, NST, DTR, KCONV = 4096, 16, 128, 4
CHH = INTER // 2
GOFF = 6.906768
QMAX = 15.0
SHRINKS = np.linspace(0.4, 1.0, 16)
SVF = [float(np.float32(v)) for v in SHRINKS]
ISVF = [float(np.float32(1.0 / np.float64(np.float32(v)))) for v in SHRINKS]
LN_EPS = 1e-5
RMS_EPS = 1e-6
SQD = 1.0 / float(np.sqrt(DH))

PHASES = "all"


def own_tiles(h):
    return [g for g in range(NT) if g % 2 == h]


class Rec:
    def __init__(self, eng):
        self._eng = eng
        self.last = None
        self.last_is_dma = False

    def __getattr__(self, name):
        attr = getattr(self._eng, name)
        if not callable(attr):
            return attr

        def call(*a, **kw):
            r = attr(*a, **kw)
            if r is not None and hasattr(r, "then_inc"):
                if name in ("wait_ge", "nop", "sem_inc", "wait_op", "drain"):
                    return r
                self.last = r
                self.last_is_dma = name in ("dma_start", "dma_start_transpose")
            return r

        return call


class Sem:
    INCD = set()

    def __init__(self, nc, name):
        self.h = nc.alloc_semaphore(name=name)
        self.n = 0

    def inc(self, instr, k=1):
        instr.then_inc(self.h, k)
        Sem.INCD.add(id(instr))
        self.n += k
        return self.n

    @staticmethod
    def has_inc(instr):
        return id(instr) in Sem.INCD


def build_program():
    nc = bass.Bass("TRN2", target_bir_lowering=False, debug=False)
    p = Prog(nc)
    p.build()
    return nc, p


class Prog:
    def __init__(self, nc):
        self.nc = nc
        self.last = {}
        Sem.INCD = set()
        self.keepalive = []

    def _sect(self, engname, fn):
        prog = self
        emit = getattr(self.block, engname)

        def body(eng):
            rec = Rec(eng)
            fn(rec)
            if rec.last is not None:
                prog.last[engname] = (rec.last, rec.last_is_dma)
                prog.keepalive.append(rec.last)

        emit(body)

    def t(self, fn): self._sect("tensor", fn)
    def v(self, fn): self._sect("vector", fn)
    def s(self, fn): self._sect("scalar", fn)
    def g(self, fn): self._sect("gpsimd", fn)
    def y(self, fn): self._sect("sync", fn)

    ENG_SEM = {"tensor": "tt", "vector": "vv", "scalar": "ss", "gpsimd": "gg"}
    DMA_SEM = {"sync": "ds", "gpsimd": "dg", "vector": "dv", "scalar": "da",
               "tensor": "ds"}

    def barrier(self):
        for engname, (instr, is_dma) in list(self.last.items()):
            if Sem.has_inc(instr):
                continue
            if is_dma:
                getattr(self, self.DMA_SEM[engname]).inc(instr, 16)
            else:
                getattr(self, self.ENG_SEM[engname]).inc(instr, 1)
        self.last = {}
        waits = [(s.h, s.n) for s in
                 (self.tt, self.vv, self.ss, self.gg, self.ds, self.dg,
                  self.dv, self.da, self.dsa, self.dsc, self.cc,
                  self.dso, self.dsp, *self.dsw, *self.dsx2, *self.dskv,
                  *self.dsm, *self.dsn) if s.n > 0]
        for engname in ("tensor", "vector", "scalar", "gpsimd", "sync"):
            def mk(engname=engname):
                def body(eng):
                    for hh, n in waits:
                        eng.wait_ge(hh, n)
                getattr(self.block, engname)(body)
            mk()

    def mm_stream(self, items, nbuf=4):
        tt = self.tt
        W = self.W
        dsw = self.dsw
        base_tt = tt.n
        base_w = [dsw[k].n for k in range(4)]

        def sy(y):
            for i, it in enumerate(items):
                if i == 0:
                    y.wait_ge(tt.h, base_tt)
                if i >= nbuf:
                    y.wait_ge(tt.h, base_tt + (i - nbuf + 1))
                wbuf = W[i % nbuf]
                if it["w"].dtype == BF16:
                    wbuf = wbuf[:].bitcast(BF16)
                wslice = wbuf[:, 0:it["w"].shape[1]]
                y.dma_start(out=wslice, in_=it["w"])
                dsw[i % nbuf].inc(y.last, 16)
        self.y(sy)

        def te(t):
            for i, it in enumerate(items):
                t.wait_ge(dsw[i % nbuf].h, base_w[i % nbuf] + 16 * (i // nbuf + 1))
                for sem, cnt in it.get("mwait", ()):
                    t.wait_ge(sem.h, cnt)
                wbuf = W[i % nbuf]
                if it["w"].dtype == BF16:
                    wbuf = wbuf[:].bitcast(BF16)
                wslice = wbuf[:, 0:it["w"].shape[1]]
                t.matmul(it["o"], wslice, it["m"],
                         start=it.get("start", True), stop=it.get("stop", True))
                tt.inc(t.last)
        self.t(te)
        return tt.n

    def slot(self, i, w=2048):
        return self.R[:, i * 2048:i * 2048 + w]

    def half(self, i, j):
        return self.R[:, i * 2048 + j * 1024:i * 2048 + j * 1024 + 1024]

    def build(self):
        nc = self.nc
        D = {}
        self.D = D

        def din(name, shape, dtype=F32):
            D[name] = nc.dram_tensor(name, shape, dtype, kind="ExternalInput").ap()

        def dout(name, shape, dtype=F32):
            D[name] = nc.dram_tensor(name, shape, dtype, kind="ExternalOutput").ap()

        def dint(name, shape, dtype=F32, **kw):
            D[name] = nc.dram_tensor(name, shape, dtype, **kw).ap()

        din("xT", [C, T])
        din("xT_own", [C, TOWN])
        din("cmask", [128, 256])
        din("hsel", [8, 2])
        din("hpick", [8, 1024])
        din("Wq", [C, HQ * DH])
        din("Wk", [C, H * DH])
        din("Wv", [C, H * DH])
        din("Wg", [C, H])
        din("Wc", [HQ * DH, C])
        din("bc", [C, 1])
        din("w1", [FF, C])
        din("w2", [C, FF])
        din("inproj_sl", [C, 2 * CHH])
        din("bccols", [64, 2])
        din("convw", [CHH, KCONV])
        din("convb", [CHH, 1])
        din("xproj", [CHH, DTR + 2 * NST])
        din("dtw", [DTR, CHH])
        din("dtb", [CHH, 1])
        din("Alog", [CHH, NST])
        din("Dpar", [CHH, 1])
        din("outproj", [CHH, C])
        din("w1_sh", [FF // NCORES, C])
        din("w2_sh", [C // NCORES, FF])
        din("wfcT_sh", [C, FF // NCORES])
        din("wprT_sh", [FF, C // NCORES])
        din("qt_ind", [128, 512])
        din("qt_bst", [8, 512])
        din("qt_svp", [128, 128])
        din("qt_isvp", [128, 128])
        dout("out", [C, TOWN])

        dint("bt_w1", [C, FF // NCORES], BF16)
        dint("bt_w1_all", [NCORES, C, FF // NCORES], BF16)
        dint("bt_w2", [FF, C // NCORES], BF16)
        dint("bt_w2_all", [NCORES, FF, C // NCORES], BF16)
        dint("bs_sc", [1280, 1], F32)
        dint("bs_sc_all", [NCORES, 1280, 1], F32)
        dint("bq_fc", [C, FF // NCORES], F32)
        dint("bq_fc_all", [NCORES, C, FF // NCORES], F32)
        dint("bq_pr", [FF, C // NCORES], F32)
        dint("bq_pr_all", [NCORES, FF, C // NCORES], F32)
        dint("bx2", [C, TOWN], F32)
        dint("bx2_all", [2, C, TOWN], F32)
        dint("bssm", [DTR + 2 * NST, T], F32)
        dint("bssm_all", [DTR + 2 * NST, T], F32)
        dint("bop", [2, C, TOWN], F32)
        dint("bop_out", [C, TOWN], F32)
        dint("st_hid", [CHH, T], F32)
        dint("st_gate", [CHH, T], F32)
        dint("st_hs", [CHH, T], F32)
        dint("st_dt", [CHH, T], F32)
        dint("st_yg", [CHH, T], F32)
        dint("st_hh", [FF, TOWN], F32)
        dint("st_kh", [FF, TOWN], BF16)
        dint("q_fc_hi", [C, FF // NCORES], BF16)
        dint("q_fc_lo", [C, FF // NCORES], BF16)
        dint("q_pr_hi", [FF, C // NCORES], BF16)
        dint("q_pr_lo", [FF, C // NCORES], BF16)
        dint("q_fc_hi_all", [NCORES, C, FF // NCORES], BF16)
        dint("q_fc_lo_all", [NCORES, C, FF // NCORES], BF16)
        dint("q_pr_hi_all", [NCORES, FF, C // NCORES], BF16)
        dint("q_pr_lo_all", [NCORES, FF, C // NCORES], BF16)
        dint("st_h2lo", [FF, TOWN], BF16)
        dint("md", [4096, 1], F32)
        dint("st_x2own", [C, TOWN], F32)

        sb = nc.alloc_sbuf_tensor
        self.NSLOT = 17
        self.R = sb("R", [128, self.NSLOT * 2048], F32)
        self.W = [sb(f"W{i}", [128, 512], F32) for i in range(4)]
        self.W4 = [sb(f"W4{i}", [128, 256], F32) for i in range(2)]
        self.W5 = [sb(f"W5{i}", [128, 8], F32) for i in range(2)]
        self.SA = sb("SA", [128, 2048], F32)
        self.SB_ = sb("SB", [128, 2048], F32)
        self.SC = sb("SC", [128, 2048], F32)
        self.SD = sb("SD", [128, 2048], F32)
        self.SBF = sb("SBF", [128, 2048], BF16)
        self.MISC = sb("MISC", [128, 64], F32)
        self.ONER = sb("ONER", [1, 128], F32)
        self.MISC2 = sb("MISC2", [128, 64], F32)
        self.CSP = sb("CSP", [8, 2048], F32)
        self.CSPO = sb("CSPO", [8, 1024], F32)
        self.CSPT = sb("CSPT", [128, 128], F32)
        self.CMASK = sb("CMASK", [128, 256], F32)
        self.HSEL = sb("HSEL", [8, 2], F32)
        self.BCC = sb("BCC", [64, 2], F32)
        self.HPICK = sb("HPICK", [8, 1024], F32)
        self.IDENT = sb("IDENT", [128, 128], F32)
        self.IDENTB = sb("IDENTB", [128, 128], BF16)
        self.ER = sb("ER", [128, 1024], F32R)
        self.QTIT = sb("QTIT", [128, 512], F32R)
        ps = nc.alloc_psum_tensor
        self.PB = [ps(f"PB{i}", [128, 512], F32) for i in range(6)]
        self.PN = [ps(f"PN{i}", [128, 512], F32) for i in range(2)]

        for nm in ("ds", "dg", "dv", "da", "dsa", "dsc", "tt", "vv", "ss", "gg", "cc"):
            setattr(self, nm, Sem(nc, nm))
        self.dsw = [Sem(nc, f"dsw{i}") for i in range(4)]
        self.dsx2 = [Sem(nc, f"dsx{i}") for i in range(2)]
        self.dskv = [Sem(nc, f"dskv{i}") for i in range(2)]
        self.dsm = [Sem(nc, f"dsm{i}") for i in range(2)]
        self.dso = Sem(nc, "dso")
        self.dsp = Sem(nc, "dsp")
        self.dsn = [Sem(nc, f"dsn{i}") for i in range(2)]
        self.dsq = [Sem(nc, f"dsq{i}") for i in range(2)]

        self.block = nc.Block()
        self.block.__enter__()

        self.ph_init()
        self.barrier()
        self.ph_l0_attn()
        self.barrier()
        if PHASES in ("x2", "x3", "all"):
            self.ph_quant_ternary()
            self.barrier()
            self.ph_l0_ffn()
            self.barrier()
        if PHASES in ("x3", "all"):
            self.ph_l1_attn()
            self.barrier()
        if PHASES == "all":
            self.ph_quant_turbo()
            self.barrier()
            self.ph_l1_ffn()
            self.barrier()
        self.ph_out()

        self.block.__exit__(None, None, None)

    def ph_init(self):
        D = self.D
        ds, gg, vv = self.ds, self.gg, self.vv
        IDENT, IDENTB, MISC = self.IDENT, self.IDENTB, self.MISC

        dsc = self.dsc

        def sy(y):
            y.dma_start(out=self.CMASK[:], in_=D["cmask"][:])
            dsc.inc(y.last, 16)
            y.dma_start(out=self.HSEL[:], in_=D["hsel"][:])
            dsc.inc(y.last, 16)
            y.dma_start(out=self.HPICK[:], in_=D["hpick"][:])
            dsc.inc(y.last, 16)
        self.y(sy)

        def gp(g):
            g.iota(self.SA[:, 0:128], pattern=[[-1, 128]], base=0,
                   channel_multiplier=1, allow_small_or_imprecise_dtypes=True)
            g.memset(self.MISC2[:, 16:17], 0.0)
            gg.inc(g.last)
            g.wait_ge(gg.h, gg.n)
            g.tensor_scalar(IDENT[:], self.SA[:, 0:128], 0.0, None, ALU.is_equal)
            gg.inc(g.last)
        self.g(gp)

        def ve(v):
            v.wait_ge(gg.h, gg.n)
            v.tensor_copy(IDENTB[:], IDENT[:])
            vv.inc(v.last)
            v.memset(MISC[:, 0:1], 1.0)
            vv.inc(v.last)
            v.memset(self.ONER[0:1, :], 1.0)
            vv.inc(v.last)
            v.memset(MISC[0:8, 1:2], -GOFF)
            vv.inc(v.last)
            v.memset(MISC[0:1, 2:3], LN_EPS)
            vv.inc(v.last)
            v.memset(MISC[0:1, 3:4], RMS_EPS)
            vv.inc(v.last)
        self.v(ve)

    def ph_l0_attn(self):
        D = self.D
        ds, tt, vv, ss = self.ds, self.tt, self.vv, self.ss
        PB = self.PB
        SD, MISC2 = self.SD, self.MISC2

        q_ap = lambda hr: self.half(hr // 2, hr % 2)
        xo_ap = lambda c: self.half(9 + c // 2, c % 2)
        x1_ap = lambda ch: self.half(9 + ch // 2, ch % 2)
        self.x1_ap = x1_ap

        dsa = self.dsa

        def syA(y):
            for c in range(16):
                y.dma_start(out=xo_ap(c), in_=D["xT_own"][c * 128:(c + 1) * 128, :])
                dsa.inc(y.last, 16)
        self.y(syA)
        dsA = dsa.n

        vvB0 = vv.n
        items = []
        for hrg in range(8):
            for c in range(16):
                for j in range(2):
                    hr = hrg * 2 + j
                    for f in range(2):
                        mw = []
                        if c == 0:
                            mw = [(dsa, dsA)]
                            if hrg > 0:
                                mw.append((vv, vvB0 + hrg * 4))
                        items.append(dict(
                            w=D["Wq"][c * 128:(c + 1) * 128, hr * 128:(hr + 1) * 128],
                            m=xo_ap(c)[:, f * 512:(f + 1) * 512],
                            o=PB[2 + j * 2 + f][:],
                            start=(c == 0), stop=(c == 15), mwait=mw))
        ttB0 = tt.n
        self.mm_stream(items)

        def veB(v):
            for hrg in range(8):
                v.wait_ge(tt.h, ttB0 + (hrg + 1) * 64)
                for j in range(2):
                    hr = hrg * 2 + j
                    for f in range(2):
                        v.tensor_copy(q_ap(hr)[:, f * 512:(f + 1) * 512],
                                      PB[2 + j * 2 + f][:])
                        vv.inc(v.last)
        self.v(veB)

        for wv in range(4):
            self._attn_wave(wv)

        def syC(y):
            y.dma_start(out=MISC2[:, 0:16],
                        in_=D["bc"].rearrange("(i p) one -> p (i one)", p=128),
                        allow_slow_non_contiguous=True)
            self.dsc.inc(y.last, 16)
        self.y(syC)

        vvC0 = vv.n
        sdx_base = [self.dsx2[k].n for k in range(2)]
        for ch in range(16):
            dhalf = SD[:, (ch % 2) * 1024:(ch % 2) * 1024 + 1024]

            def syD(y, ch=ch, dhalf=dhalf):
                if ch < 2:
                    y.wait_ge(vv.h, vvC0)
                else:
                    y.wait_ge(vv.h, vvC0 + (ch - 1) * 2)
                y.dma_start(out=dhalf, in_=D["xT_own"][ch * 128:(ch + 1) * 128, :])
                self.dsx2[ch % 2].inc(y.last, 16)
            self.y(syD)
            dsD = (ch % 2, sdx_base[ch % 2] + 16 * (ch // 2 + 1))

            items = []
            for hr in range(16):
                for f in range(2):
                    items.append(dict(
                        w=D["Wc"][hr * 128:(hr + 1) * 128, ch * 128:(ch + 1) * 128],
                        m=q_ap(hr)[:, f * 512:(f + 1) * 512],
                        o=PB[f][:],
                        start=(hr == 0), stop=(hr == 15),
                        mwait=[(vv, vvC0 + ch * 2)] if hr == 0 else []))
            self.mm_stream(items)
            ttb = tt.n

            def veD(v, ch=ch, dhalf=dhalf, ttb=ttb, dsD=dsD):
                v.wait_ge(tt.h, ttb)
                v.wait_ge(self.dsx2[dsD[0]].h, dsD[1])
                v.wait_ge(self.dsc.h, self.dsc.n)
                for f in range(2):
                    v.scalar_tensor_tensor(
                        x1_ap(ch)[:, f * 512:(f + 1) * 512],
                        PB[f][:], MISC2[:, ch:ch + 1],
                        dhalf[:, f * 512:(f + 1) * 512],
                        ALU.add, ALU.add)
                    vv.inc(v.last)
            self.v(veD)

        self._layernorm(x1_ap)

    def _attn_wave(self, wv):
        D = self.D
        ds, tt, vv, ss = self.ds, self.tt, self.vv, self.ss
        PB, PN = self.PB, self.PN
        SA, SB_ = self.SA, self.SB_
        MISC, IDENT = self.MISC, self.IDENT
        CSP = self.CSP
        kslot = lambda i: self.slot(8 + i)
        vslot = lambda i: self.slot(10 + i)
        vTslot = lambda i: self.slot(12 + i)

        hh0, hh1 = 2 * wv, 2 * wv + 1
        tt0, vv0 = tt.n, vv.n
        nmm = 5 if wv == 0 else 4
        nw = 4 if wv == 0 else 3
        dsx2, dskv = self.dsx2, self.dskv
        bx = [dsx2[k].n for k in range(2)]
        bk = [dskv[k].n for k in range(2)]

        def syW(y):
            for sect in range(4):
                for c in range(16):
                    i = sect * 16 + c
                    if i == 0:
                        y.wait_ge(tt.h, tt0)
                        y.wait_ge(vv.h, vv0)
                    if i >= 2:
                        y.wait_ge(tt.h, tt0 + (i - 1) * nmm)
                    xbuf = (SA if i % 2 == 0 else SB_)[:, 0:512]
                    y.dma_start(out=xbuf,
                                in_=D["xT"][c * 128:(c + 1) * 128,
                                            sect * 512:(sect + 1) * 512])
                    dsx2[i % 2].inc(y.last, 16)
                    y.dma_start(out=self.W[i % 2][:, 0:128],
                                in_=D["Wk"][c * 128:(c + 1) * 128,
                                            hh0 * 128:(hh0 + 1) * 128])
                    dskv[i % 2].inc(y.last, 16)
                    y.dma_start(out=self.W[2 + i % 2][:, 0:128],
                                in_=D["Wk"][c * 128:(c + 1) * 128,
                                            hh1 * 128:(hh1 + 1) * 128])
                    dskv[i % 2].inc(y.last, 16)
                    y.dma_start(out=self.W4[i % 2][:, 0:256],
                                in_=D["Wv"][c * 128:(c + 1) * 128,
                                            hh0 * 128:(hh0 + 2) * 128])
                    dskv[i % 2].inc(y.last, 16)
                    if wv == 0:
                        y.dma_start(out=self.W5[i % 2][:, 0:8],
                                    in_=D["Wg"][c * 128:(c + 1) * 128, :])
                        dskv[i % 2].inc(y.last, 16)
        self.y(syW)

        def teW(t):
            for sect in range(4):
                for c in range(16):
                    i = sect * 16 + c
                    t.wait_ge(dsx2[i % 2].h, bx[i % 2] + 16 * (i // 2 + 1))
                    t.wait_ge(dskv[i % 2].h, bk[i % 2] + 16 * nw * (i // 2 + 1))
                    if c == 0:
                        t.wait_ge(vv.h, vv0 + sect * nmm)
                    xbuf = (SA if i % 2 == 0 else SB_)[:, 0:512]
                    st, sp = (c == 0), (c == 15)
                    t.matmul(PB[2][:], self.W[i % 2][:, 0:128], xbuf, start=st, stop=sp)
                    tt.inc(t.last)
                    t.matmul(PB[3][:], self.W[2 + i % 2][:, 0:128], xbuf, start=st, stop=sp)
                    tt.inc(t.last)
                    t.matmul(PB[4][:], self.W4[i % 2][:, 0:128], xbuf, start=st, stop=sp)
                    tt.inc(t.last)
                    t.matmul(PB[5][:], self.W4[i % 2][:, 128:256], xbuf, start=st, stop=sp)
                    tt.inc(t.last)
                    if wv == 0:
                        t.matmul(PN[1][0:8, :], self.W5[i % 2][:, 0:8], xbuf, start=st, stop=sp)
                        tt.inc(t.last)
        self.t(teW)

        def veW(v):
            for sect in range(4):
                v.wait_ge(tt.h, tt0 + (sect + 1) * 16 * nmm)
                sl = slice(sect * 512, (sect + 1) * 512)
                v.tensor_copy(kslot(0)[:, sl], PB[2][:]); vv.inc(v.last)
                v.tensor_copy(kslot(1)[:, sl], PB[3][:]); vv.inc(v.last)
                v.tensor_copy(vslot(0)[:, sl], PB[4][:]); vv.inc(v.last)
                v.tensor_copy(vslot(1)[:, sl], PB[5][:]); vv.inc(v.last)
                if wv == 0:
                    v.tensor_copy(CSP[0:8, sl], PN[1][0:8, :]); vv.inc(v.last)
        self.v(veW)
        vv_kv = vv.n

        if wv == 0:
            self._csp_pipeline(vv_kv)

        tt_vt0 = tt.n
        vv_vt0 = vv.n

        def teV(t):
            t.wait_ge(vv.h, vv_vt0)
            for j in range(2):
                for stile in range(16):
                    i = j * 16 + stile
                    if i >= 2:
                        t.wait_ge(vv.h, vv_vt0 + i - 1)
                    t.transpose(PB[2 + i % 2][:, 0:128],
                                vslot(j)[:, stile * 128:(stile + 1) * 128],
                                IDENT[:])
                    tt.inc(t.last)
        self.t(teV)

        def veV(v):
            for j in range(2):
                for stile in range(16):
                    i = j * 16 + stile
                    v.wait_ge(tt.h, tt_vt0 + i + 1)
                    v.tensor_copy(vTslot(j)[:, stile * 128:(stile + 1) * 128],
                                  PB[2 + i % 2][:, 0:128])
                    vv.inc(v.last)
        self.v(veV)

        for hr in (2 * hh0, 2 * hh0 + 1, 2 * hh1, 2 * hh1 + 1):
            self._attn_head(hr)

    def _csp_pipeline(self, vv_kv):
        ds, tt, vv, ss = self.ds, self.tt, self.vv, self.ss
        PB = self.PB
        MISC, IDENT = self.MISC, self.IDENT
        CSP, CSPT, CSPO = self.CSP, self.CSPT, self.CSPO
        CSPA, CSPB = self.SC, self.SD
        CSPO0, CSPO1 = self.SC[0:8, 0:1024], self.SD[0:8, 0:1024]

        def se1(s):
            s.wait_ge(vv.h, vv_kv)
            s.activation(CSPA[0:8, :], CSP[0:8, :], ACT.Exp,
                         bias=MISC[0:8, 1:2], scale=-1.0)
            ss.inc(s.last)
        self.s(se1)
        ss1 = ss.n

        def ve1(v):
            v.wait_ge(ss.h, ss1)
            v.tensor_scalar(CSPA[0:8, :], CSPA[0:8, :], 1.0, None, ALU.add)
            vv.inc(v.last)
        self.v(ve1)
        vv1 = vv.n

        def se2(s):
            s.wait_ge(vv.h, vv1)
            s.activation(CSPB[0:8, :], CSPA[0:8, :], ACT.Ln)
            ss.inc(s.last)
        self.s(se2)
        ss2 = ss.n

        def ve2(v):
            v.wait_ge(ss.h, ss2)
            v.memset(CSPA[0:8, :], 1.0)
            vv.inc(v.last)
            v.wait_ge(vv.h, vv.n)
            v.tensor_tensor_scan(CSP[0:8, :], CSPA[0:8, :], CSPB[0:8, :],
                                 0.0, ALU.mult, ALU.add)
            vv.inc(v.last)
        self.v(ve2)
        vv_csp = vv.n

        def te2(t):
            t.wait_ge(vv.h, vv_csp)
            for tj in range(16):
                if tj >= 2:
                    t.wait_ge(vv.h, vv_csp + tj - 1)
                t.transpose(PB[2 + tj % 2][:, 0:8],
                            CSP[0:8, tj * 128:(tj + 1) * 128],
                            IDENT[0:8, 0:8])
                tt.inc(t.last)
        self.t(te2)
        tt_cspT = tt.n

        def ve3(v):
            for tj in range(16):
                v.wait_ge(tt.h, tt_cspT - 15 + tj)
                v.tensor_copy(CSPT[:, tj * 8:(tj + 1) * 8], PB[2 + tj % 2][:, 0:8])
                vv.inc(v.last)
            cv = CSP[0:8, :].rearrange("p (i f) -> p i f", f=128)
            o0 = CSPO0.rearrange("p (i f) -> p i f", f=128)
            o1 = CSPO1.rearrange("p (i f) -> p i f", f=128)
            v.tensor_copy(o0, cv[:, 0::2, :])
            vv.inc(v.last)
            v.tensor_copy(o1, cv[:, 1::2, :])
            vv.inc(v.last)
            v.wait_ge(vv.h, vv.n)
            v.tensor_scalar(CSPO[0:8, :], CSPO0, self.HSEL[0:8, 0:1],
                            None, ALU.mult)
            vv.inc(v.last)
            v.wait_ge(vv.h, vv.n)
            v.scalar_tensor_tensor(CSPO[0:8, :], CSPO1,
                                   self.HSEL[0:8, 1:2], CSPO[0:8, :],
                                   ALU.mult, ALU.add)
            vv.inc(v.last)
        self.v(ve3)

    def _attn_head(self, hr):
        ds, tt, vv, ss = self.ds, self.tt, self.vv, self.ss
        PB, PN = self.PB, self.PN
        SA, SB_, SC, SD = self.SA, self.SB_, self.SC, self.SD
        MISC, CSPT, CSPO, CMASK = self.MISC, self.CSPT, self.CSPO, self.CMASK
        CSP = self.CSP
        hh = hr // 2
        j = hh % 2
        q_ap = self.half(hr // 2, hr % 2)
        kslot = self.slot(8 + j)
        vTslot = self.slot(12 + j)

        iters = []
        for tj in range(16):
            i0 = (tj // 2) * 128
            if i0 < 512:
                iters.append(dict(tj=tj, lo=i0, hi=512, first=True))
                iters.append(dict(tj=tj, lo=512, hi=1024, first=False))
            else:
                iters.append(dict(tj=tj, lo=i0, hi=1024, first=True))
        last = [max(i for i, it in enumerate(iters) if it["lo"] < 512),
                max(i for i, it in enumerate(iters) if it["hi"] > 512)]

        base_tt, base_ss, base_vv = tt.n, ss.n, vv.n
        sched = []
        ttc, ssc, vvc = base_tt, base_ss, base_vv
        for i, it in enumerate(iters):
            e = dict(it)
            ttc += 1; e["tt_qk"] = ttc
            ttc += 1; e["tt_bc"] = ttc
            ssc += 1; e["ss_sq"] = ssc
            ssc += 1; e["ss_ge"] = ssc
            vvc += 1 + (1 if it["first"] else 0)
            e["vv_a"] = vvc
            ttc += 1; e["tt_nm"] = ttc
            ttc += 1; e["tt_av"] = ttc
            sched.append(e)
        tt_end, ss_end, vv_end = ttc, ssc, vvc

        started = [False, False]

        def teH(t):
            for i, e in enumerate(sched):
                tj, lo, hi = e["tj"], e["lo"], e["hi"]
                L = hi - lo
                b = 0 if lo < 512 else 1
                qkp = PB[2 + i % 2]
                bcp = PB[4 + i % 2]
                if i < 2:
                    t.wait_ge(vv.h, base_vv)
                else:
                    t.wait_ge(ss.h, sched[i - 2]["ss_ge"])
                t.matmul(qkp[:, 0:L], kslot[:, tj * 128:(tj + 1) * 128],
                         q_ap[:, lo:hi], start=True, stop=True)
                tt.inc(t.last)
                t.matmul(bcp[:, 0:L], self.HPICK[0:8, hh * 128:(hh + 1) * 128],
                         CSPO[0:8, lo:hi], start=True, stop=True)
                tt.inc(t.last)
                t.wait_ge(vv.h, e["vv_a"])
                st = not started[b]
                sp = (i == last[b])
                abuf = (SA if i % 2 == 0 else SB_)[:, 0:L]
                t.matmul(PN[b][0:1, (lo % 512):(lo % 512) + L],
                         MISC[:, 0:1], abuf, start=st, stop=sp)
                tt.inc(t.last)
                t.matmul(PB[b][:, (lo % 512):(lo % 512) + L],
                         vTslot[:, tj * 128:(tj + 1) * 128],
                         abuf, start=st, stop=sp)
                tt.inc(t.last)
                started[b] = True
        self.t(teH)
        assert tt.n == tt_end

        def seH(s):
            for i, e in enumerate(sched):
                tj, lo, hi = e["tj"], e["lo"], e["hi"]
                L = hi - lo
                qkp = PB[2 + i % 2]
                bcp = PB[4 + i % 2]
                sqb = SC[:, (i % 2) * 1024:(i % 2) * 1024 + L]
                geb = SD[:, (i % 2) * 1024:(i % 2) * 1024 + L]
                s.wait_ge(tt.h, e["tt_qk"])
                if i >= 2:
                    s.wait_ge(vv.h, sched[i - 2]["vv_a"])
                s.activation(sqb, qkp[:, 0:L], ACT.Square, scale=SQD)
                ss.inc(s.last)
                s.wait_ge(tt.h, e["tt_bc"])
                s.activation(geb, bcp[:, 0:L], ACT.Exp, scale=-1.0,
                             bias=CSPT[:, tj * 8 + hh:tj * 8 + hh + 1])
                ss.inc(s.last)
        self.s(seH)
        assert ss.n == ss_end

        def veH(v):
            for i, e in enumerate(sched):
                tj, lo, hi = e["tj"], e["lo"], e["hi"]
                L = hi - lo
                sqb = SC[:, (i % 2) * 1024:(i % 2) * 1024 + L]
                geb = SD[:, (i % 2) * 1024:(i % 2) * 1024 + L]
                abuf = (SA if i % 2 == 0 else SB_)[:, 0:L]
                v.wait_ge(ss.h, e["ss_ge"])
                if i >= 2:
                    v.wait_ge(tt.h, sched[i - 2]["tt_av"])
                if e["first"]:
                    v.tensor_tensor(geb[:, 0:128], geb[:, 0:128],
                                    CMASK[:, (tj % 2) * 128:(tj % 2) * 128 + 128],
                                    ALU.mult)
                    vv.inc(v.last)
                    v.wait_ge(vv.h, vv.n)
                v.tensor_tensor(abuf, sqb, geb, ALU.mult)
                vv.inc(v.last)
        self.v(veH)
        assert vv.n == vv_end

        def veE(v):
            v.wait_ge(tt.h, tt_end)
            for b in range(2):
                v.tensor_scalar(CSP[0:1, b * 512:(b + 1) * 512], PN[b][0:1, :],
                                1e-6, None, ALU.add)
                vv.inc(v.last)
            v.wait_ge(vv.h, vv.n)
            v.reciprocal(CSP[0:1, 1024:2048], CSP[0:1, 0:1024])
            vv.inc(v.last)
        self.v(veE)
        vv_rn = vv.n

        def teE(t):
            t.wait_ge(vv.h, vv_rn)
            for b in range(2):
                t.matmul(PB[2 + b][:, 0:512], self.ONER[0:1, :],
                         CSP[0:1, 1024 + b * 512:1024 + (b + 1) * 512],
                         start=True, stop=True)
                tt.inc(t.last)
        self.t(teE)
        tt_rb = tt.n

        def veE2(v):
            v.wait_ge(tt.h, tt_rb)
            for b in range(2):
                v.tensor_copy(SC[:, b * 512:(b + 1) * 512], PB[2 + b][:])
                vv.inc(v.last)
            v.wait_ge(vv.h, vv.n)
            for b in range(2):
                v.tensor_tensor(q_ap[:, b * 512:(b + 1) * 512], PB[b][:],
                                SC[:, b * 512:(b + 1) * 512], ALU.mult)
                vv.inc(v.last)
        self.v(veE2)

    def _layernorm(self, ap_of_chunk, eps_col=2):
        ds, tt, vv, ss = self.ds, self.tt, self.vv, self.ss
        PB, PN = self.PB, self.PN
        SA, MISC = self.SA, self.MISC
        NRM = self.CSP
        base_tt, base_ss, base_vv = tt.n, ss.n, vv.n
        base_dg = self.dg.n

        sq_cnt = []
        ssc = base_ss
        for ch in range(16):
            ssc += 1
            sq_cnt.append(ssc)

        def se(s):
            for ch in range(16):
                if ch < 2:
                    s.wait_ge(vv.h, base_vv)
                    s.wait_ge(self.dg.h, base_dg)
                else:
                    s.wait_ge(tt.h, base_tt + (ch - 1) * 4)
                s.activation(SA[:, (ch % 2) * 1024:(ch % 2) * 1024 + 1024],
                             ap_of_chunk(ch), ACT.Square)
                ss.inc(s.last)
        self.s(se)

        def te(t):
            t.wait_ge(vv.h, base_vv)
            for ch in range(16):
                st, sp = (ch == 0), (ch == 15)
                t.wait_ge(ss.h, sq_cnt[ch])
                for f in range(2):
                    t.matmul(PB[f][0:1, :], MISC[:, 0:1],
                             ap_of_chunk(ch)[:, f * 512:(f + 1) * 512],
                             start=st, stop=sp)
                    tt.inc(t.last)
                for f in range(2):
                    t.matmul(PN[f][0:1, :], MISC[:, 0:1],
                             SA[:, (ch % 2) * 1024 + f * 512:(ch % 2) * 1024 + (f + 1) * 512],
                             start=st, stop=sp)
                    tt.inc(t.last)
        self.t(te)
        tt_st = tt.n

        SD = self.SD

        def ve(v):
            v.wait_ge(tt.h, tt_st)
            for f in range(2):
                v.tensor_scalar(NRM[0:1, f * 512:(f + 1) * 512], PB[f][0:1, :],
                                1.0 / 2048, None, ALU.mult)
                vv.inc(v.last)
                v.tensor_scalar(SD[0:1, f * 512:(f + 1) * 512], PN[f][0:1, :],
                                1.0 / 2048, None, ALU.mult)
                vv.inc(v.last)
            v.wait_ge(vv.h, vv.n)
            v.tensor_tensor(SD[0:1, 1024:2048], NRM[0:1, 0:1024],
                            NRM[0:1, 0:1024], ALU.mult)
            vv.inc(v.last)
            v.wait_ge(vv.h, vv.n)
            v.tensor_tensor(SD[0:1, 0:1024], SD[0:1, 0:1024],
                            SD[0:1, 1024:2048], ALU.subtract)
            vv.inc(v.last)
        self.v(ve)
        vv_var = vv.n

        def se2(s):
            s.wait_ge(vv.h, vv_var)
            s.activation(NRM[0:1, 1024:2048], SD[0:1, 0:1024], ACT.Sqrt,
                         bias=MISC[0:1, eps_col:eps_col + 1])
            ss.inc(s.last)
        self.s(se2)
        ss_sd = ss.n

        def ve2(v):
            v.wait_ge(ss.h, ss_sd)
            v.reciprocal(SD[0:1, 0:1024], NRM[0:1, 1024:2048])
            vv.inc(v.last)
        self.v(ve2)
        vv_rs = vv.n

        def te2(t):
            t.wait_ge(vv.h, vv_rs)
            for f in range(2):
                t.matmul(PB[2 + f][:, 0:512],
                         self.ONER[0:1, :], NRM[0:1, f * 512:(f + 1) * 512],
                         start=True, stop=True)
                tt.inc(t.last)
                t.matmul(PB[4 + f][:, 0:512],
                         self.ONER[0:1, :], SD[0:1, f * 512:(f + 1) * 512],
                         start=True, stop=True)
                tt.inc(t.last)
        self.t(te2)
        tt_bc = tt.n

        def ve3(v):
            v.wait_ge(tt.h, tt_bc)
            v.wait_ge(self.dg.h, base_dg)
            for ch in range(16):
                for f in range(2):
                    v.tensor_tensor(ap_of_chunk(ch)[:, f * 512:(f + 1) * 512],
                                    ap_of_chunk(ch)[:, f * 512:(f + 1) * 512],
                                    PB[2 + f][:], ALU.subtract)
                    vv.inc(v.last)
            v.wait_ge(vv.h, vv.n)
            for ch in range(16):
                for f in range(2):
                    v.tensor_tensor(ap_of_chunk(ch)[:, f * 512:(f + 1) * 512],
                                    ap_of_chunk(ch)[:, f * 512:(f + 1) * 512],
                                    PB[4 + f][:], ALU.mult)
                    vv.inc(v.last)
        self.v(ve3)

    def _scol_bcast(self, m2, col0, ntiles, which, vv_wait):
        D = self.D
        tt, vv, dg, dsc = self.tt, self.vv, self.dg, self.dsc
        PB = self.PB
        SD, CSP = self.SD, self.CSP

        def gy(g, vv_wait=vv_wait):
            g.wait_ge(vv.h, vv_wait)
            for tb in range(ntiles):
                g.dma_start(out=D["md"][which * 2048 + tb * 128:which * 2048 + (tb + 1) * 128, :],
                            in_=m2[:, col0 + tb:col0 + tb + 1])
                dg.inc(g.last, 16)
        self.g(gy)
        dg_b = dg.n

        def sy(y):
            y.wait_ge(dg.h, dg_b)
            y.dma_start(out=CSP[0:1, 0:1024],
                        in_=D["md"][which * 2048:which * 2048 + 1024, :]
                        .rearrange("t one -> one t"))
            dsc.inc(y.last, 16)
        self.y(sy)
        dsc_b = dsc.n

        def te(t):
            t.wait_ge(dsc.h, dsc_b)
            t.wait_ge(vv.h, vv.n)
            for f in range(2):
                t.matmul(PB[2 + f][:, 0:512], self.ONER[0:1, :],
                         CSP[0:1, f * 512:(f + 1) * 512], start=True, stop=True)
                tt.inc(t.last)
        self.t(te)
        tt_b = tt.n

        def ve(v):
            v.wait_ge(tt.h, tt_b)
            for f in range(2):
                v.tensor_copy(SD[:, f * 512:(f + 1) * 512], PB[2 + f][:])
                vv.inc(v.last)
            v.wait_ge(vv.h, vv.n)
            v.reciprocal(SD[:, 1024:2048], SD[:, 0:1024])
            vv.inc(v.last)
        self.v(ve)

    def ph_quant_ternary(self):
        D = self.D
        tt, vv, ss, dg, cc = self.tt, self.vv, self.ss, self.dg, self.cc
        dsa = self.dsa
        PB = self.PB
        SA, SB_, SC = self.SA, self.SB_, self.SC
        MISC2, IDENTB, SBF = self.MISC2, self.IDENTB, self.SBF

        jobs = []
        for r in range(8):
            jobs.append(dict(src=D["w1_sh"][r * 128:(r + 1) * 128, :], wdt=2048,
                             dst=D["bt_w1"], dcol=r, ji=r))
        for r in range(2):
            jobs.append(dict(src=D["w2_sh"][r * 128:(r + 1) * 128, :], wdt=8192,
                             dst=D["bt_w2"], dcol=r, ji=8 + r))

        for jb in jobs:
            self._ternary_job(jb)

        def gq(g):
            g.wait_ge(vv.h, self.vv.n)
            g.wait_ge(self.dg.h, self.dg.n)
            g.collective_compute("AllGather", ALU.bypass,
                                 replica_groups=[list(range(NCORES))],
                                 ins=[D["bt_w1"][:]], outs=[D["bt_w1_all"][:]])
            cc.inc(g.last)
            g.collective_compute("AllGather", ALU.bypass,
                                 replica_groups=[list(range(NCORES))],
                                 ins=[D["bt_w2"][:]], outs=[D["bt_w2_all"][:]])
            cc.inc(g.last)
            g.collective_compute("AllGather", ALU.bypass,
                                 replica_groups=[list(range(NCORES))],
                                 ins=[D["bs_sc"][:]], outs=[D["bs_sc_all"][:]])
            cc.inc(g.last)
        self.g(gq)

    def _ternary_job(self, jb):
        D = self.D
        tt, vv, ss, dg = self.tt, self.vv, self.ss, self.dg
        dsa = self.dsa
        PB = self.PB
        SA, SC = self.SA, self.SC
        MISC2, IDENTB, SBF = self.MISC2, self.IDENTB, self.SBF
        W = jb["wdt"]
        nck = W // 2048
        ji = jb["ji"]

        for kk in range(nck):
            d0 = dsa.n
            def syL(y, jb=jb, kk=kk, d0=d0):
                y.wait_ge(ss.h, ss.n)
                y.wait_ge(vv.h, vv.n)
                y.dma_start(out=SA[:, 0:2048],
                            in_=jb["src"][:, kk * 2048:(kk + 1) * 2048])
                dsa.inc(y.last, 16)
            self.y(syL)
            d1 = dsa.n

            def seL(s, jb=jb, kk=kk, d1=d1):
                s.wait_ge(dsa.h, d1)
                s.activation(SC[:, 0:2048], SA[:, 0:2048], ACT.Abs,
                             scale=1.0 / jb["wdt"],
                             accum_out=MISC2[:, 40 + kk:41 + kk])
                ss.inc(s.last)
            self.s(seL)

        def veL(v, nck=nck, ji=ji):
            v.wait_ge(ss.h, ss.n)
            c0 = MISC2[:, 40:41]
            for kk in range(1, nck):
                v.tensor_tensor(c0, c0, MISC2[:, 40 + kk:41 + kk], ALU.add)
                vv.inc(v.last)
                v.wait_ge(vv.h, vv.n)
            v.tensor_scalar(MISC2[:, 20:21], c0, 1e-5, None, ALU.max)
            vv.inc(v.last)
            v.wait_ge(vv.h, vv.n)
            v.reciprocal(MISC2[:, 36:37], MISC2[:, 20:21])
            vv.inc(v.last)
        self.v(veL)
        vv_sc = vv.n

        def gyS(g, ji=ji, vv_sc=vv_sc):
            g.wait_ge(vv.h, vv_sc)
            g.dma_start(out=D["bs_sc"][ji * 128:(ji + 1) * 128, :],
                        in_=MISC2[:, 20:21])
            dg.inc(g.last, 16)
        self.g(gyS)

        for kk in range(nck):
            def syR(y, jb=jb, kk=kk):
                y.wait_ge(vv.h, vv.n)
                y.dma_start(out=SA[:, 0:2048],
                            in_=jb["src"][:, kk * 2048:(kk + 1) * 2048])
                dsa.inc(y.last, 16)
            self.y(syR)
            d2 = dsa.n

            def veQ(v, d2=d2, vv_sc=vv_sc):
                v.wait_ge(dsa.h, d2)
                v.wait_ge(dg.h, dg.n)
                v.tensor_scalar(SC[:, 0:2048], SA[:, 0:2048],
                                MISC2[:, 36:37], None, ALU.mult)
                vv.inc(v.last)
                v.wait_ge(vv.h, vv.n)
                v.tensor_copy(SC[:, 0:2048].bitcast(I32), SC[:, 0:2048])
                vv.inc(v.last)
                v.wait_ge(vv.h, vv.n)
                v.tensor_scalar(SBF[:, 0:2048], SC[:, 0:2048].bitcast(I32),
                                1.0, -1.0, ALU.min, ALU.max)
                vv.inc(v.last)
            self.v(veQ)
            vv_q = vv.n
            tt_t = tt.n

            def teT(t, vv_q=vv_q, tt_t=tt_t):
                t.wait_ge(vv.h, vv_q)
                for p in range(16):
                    if p >= 2:
                        t.wait_ge(vv.h, vv_q + p - 1)
                    t.transpose(PB[2 + p % 2][:, 0:256].bitcast(BF16)[:, 0:128],
                                SBF[:, p * 128:(p + 1) * 128], IDENTB[:])
                    tt.inc(t.last)
            self.t(teT)

            def veT(v, tt_t=tt_t, vv_q=vv_q):
                for p in range(16):
                    v.wait_ge(tt.h, tt_t + p + 1)
                    v.tensor_copy(SBF[:, p * 128:(p + 1) * 128],
                                  PB[2 + p % 2][:, 0:256].bitcast(BF16)[:, 0:128])
                    vv.inc(v.last)
            self.v(veT)
            vv_t = vv.n

            def gyT(g, jb=jb, kk=kk, vv_t=vv_t):
                g.wait_ge(vv.h, vv_t)
                for p in range(16):
                    g.dma_start(
                        out=jb["dst"][kk * 2048 + p * 128:kk * 2048 + (p + 1) * 128,
                                      jb["dcol"] * 128:(jb["dcol"] + 1) * 128],
                        in_=SBF[:, p * 128:(p + 1) * 128])
                    dg.inc(g.last, 16)
            self.g(gyT)

    def ph_l0_ffn(self):
        D = self.D
        tt, vv, ss, dg, cc = self.tt, self.vv, self.ss, self.dg, self.cc
        dsa, dsc = self.dsa, self.dsc
        PB, PN = self.PB, self.PN
        SA, SB_, SC, SD = self.SA, self.SB_, self.SC, self.SD
        SBF, MISC, MISC2, IDENT = self.SBF, self.MISC, self.MISC2, self.IDENT
        CSP = self.CSP
        x1_ap = self.x1_ap
        kx_ap = lambda ch: self.slot(ch // 4)[:].bitcast(BF16)[:, (ch % 4) * 1024:(ch % 4) * 1024 + 1024]
        kh_ap = lambda hc: self.slot(hc // 8)[:].bitcast(BF16)[:, (hc % 8) * 512:(hc % 8) * 512 + 1024]
        cc_need = cc.n

        kx2_ap = lambda ch: self.slot(ch // 4)[:].bitcast(BF16)[:, (ch % 4) * 1024:(ch % 4) * 1024 + 1024]

        tt0, vv0 = tt.n, vv.n

        def teA(t):
            t.wait_ge(vv.h, vv0)
            i = 0
            for tb in range(8):
                for ch in range(16):
                    if i >= 2:
                        j = i - 2
                        t.wait_ge(vv.h, vv0 + (j // 16) * 17 + (j % 16) + 1)
                    t.transpose(PB[2 + i % 2][:, 0:128],
                                x1_ap(ch)[:, tb * 128:(tb + 1) * 128], IDENT[:])
                    tt.inc(t.last)
                    i += 1
        self.t(teA)

        def veA(v):
            i = 0
            for tb in range(8):
                for ch in range(16):
                    v.wait_ge(tt.h, tt0 + i + 1)
                    v.tensor_copy(SC[:, ch * 128:(ch + 1) * 128],
                                  PB[2 + i % 2][:, 0:128])
                    vv.inc(v.last)
                    i += 1
                v.wait_ge(vv.h, vv.n)
                v.tensor_reduce(MISC2[:, 44 + tb:45 + tb], SC[:, 0:2048],
                                AX.X, ALU.max, apply_absolute_value=True)
                vv.inc(v.last)
                v.wait_ge(vv.h, vv.n)
        self.v(veA)

        def veS(v):
            for tb in range(8):
                v.tensor_scalar(MISC2[:, 44 + tb:45 + tb],
                                MISC2[:, 44 + tb:45 + tb], 1e-5, None, ALU.max)
                vv.inc(v.last)
            v.wait_ge(vv.h, vv.n)
            for tb in range(8):
                v.reciprocal(MISC2[:, 52 + tb:53 + tb], MISC2[:, 44 + tb:45 + tb])
                vv.inc(v.last)
            v.wait_ge(vv.h, vv.n)
            for tb in range(8):
                v.tensor_scalar(MISC2[:, 52 + tb:53 + tb],
                                MISC2[:, 52 + tb:53 + tb], 127.0, None, ALU.mult)
                vv.inc(v.last)
        self.v(veS)
        vv_s = vv.n

        self._scol_bcast(MISC2, 52, 8, 0, vv_s)

        def veK(v):
            v.wait_ge(vv.h, vv.n)
            for ch in range(16):
                v.tensor_tensor(SC[:, 0:1024], x1_ap(ch), SD[:, 0:1024], ALU.mult)
                vv.inc(v.last)
                v.wait_ge(vv.h, vv.n)
                v.tensor_copy(SC[:, 1024:2048].bitcast(I32), SC[:, 0:1024])
                vv.inc(v.last)
                v.wait_ge(vv.h, vv.n)
                v.tensor_copy(kx_ap(ch), SC[:, 1024:2048].bitcast(I32))
                vv.inc(v.last)
                v.wait_ge(vv.h, vv.n)
        self.v(veK)
        vv_kx = vv.n

        FFC = FF // 128
        RLOC = FF // NCORES // 128
        dsw = self.dsw
        base_w = [dsw[k].n for k in range(4)]
        base_tt = tt.n
        base_dsa = dsa.n

        schH = []
        ttc, vvc, ssc, dgc = tt.n, vv.n, ss.n, dg.n
        for hc in range(FFC):
            e = {}
            ttc += 32; e["tt_mm"] = ttc
            vvc += 4; e["vv_t2"] = vvc
            ssc += 1; e["ss_g"] = ssc
            dgc += 16; e["dg_h"] = dgc
            ttc += 8; e["tt_tr"] = ttc
            vvc += 16; e["vv_hm"] = vvc
            schH.append(e)

        base_m = [self.dsm[k].n for k in range(2)]

        def syM(y):
            i = 0
            for hc in range(FFC):
                rank, lc = hc // RLOC, hc % RLOC
                if hc >= 2:
                    y.wait_ge(vv.h, schH[hc - 2]["vv_t2"])
                y.dma_start(out=MISC[:, 8 + hc % 2:9 + hc % 2],
                            in_=D["bs_sc_all"][rank, lc * 128:(lc + 1) * 128, :])
                self.dsm[hc % 2].inc(y.last, 16)
                for c in range(16):
                    if i >= 4:
                        j = i - 4
                        y.wait_ge(tt.h, base_tt + (j // 16) * 40 + (j % 16 + 1) * 2)
                    y.dma_start(out=self.W[i % 4][:].bitcast(BF16)[:, 0:128],
                                in_=D["bt_w1_all"][rank, c * 128:(c + 1) * 128,
                                                   lc * 128:(lc + 1) * 128])
                    dsw[i % 4].inc(y.last, 16)
                    i += 1
        self.y(syM)

        def teM(t):
            i = 0
            for hc in range(FFC):
                e = schH[hc]
                for c in range(16):
                    t.wait_ge(dsw[i % 4].h, base_w[i % 4] + 16 * (i // 4 + 1))
                    if c == 0:
                        if hc == 0:
                            t.wait_ge(vv.h, vv_kx)
                            t.wait_ge(cc.h, cc_need)
                        else:
                            t.wait_ge(vv.h, schH[hc - 1]["vv_t2"])
                    for f in range(2):
                        t.matmul(PB[f][:], self.W[i % 4][:].bitcast(BF16)[:, 0:128],
                                 kx_ap(c)[:, f * 512:(f + 1) * 512],
                                 start=(c == 0), stop=(c == 15))
                        tt.inc(t.last)
                    i += 1
                t.wait_ge(ss.h, e["ss_g"])
                for tb in range(8):
                    if tb >= 2:
                        t.wait_ge(vv.h, e["vv_t2"] + 2 * (tb - 1))
                    t.transpose(PB[2 + tb % 2][:, 0:128],
                                SB_[:, tb * 128:(tb + 1) * 128], IDENT[:])
                    tt.inc(t.last)
        self.t(teM)

        def veM(v):
            for hc in range(FFC):
                e = schH[hc]
                v.wait_ge(tt.h, e["tt_mm"])
                v.wait_ge(self.dsm[hc % 2].h, base_m[hc % 2] + 16 * (hc // 2 + 1))
                for f in range(2):
                    v.tensor_scalar(SC[:, f * 512:(f + 1) * 512], PB[f][:],
                                    MISC[:, 8 + hc % 2:9 + hc % 2], None, ALU.mult)
                    vv.inc(v.last)
                v.wait_ge(vv.h, vv.n)
                for f in range(2):
                    v.tensor_tensor(SA[:, f * 512:(f + 1) * 512],
                                    SC[:, f * 512:(f + 1) * 512],
                                    SD[:, 1024 + f * 512:1024 + (f + 1) * 512],
                                    ALU.mult)
                    vv.inc(v.last)
                assert vv.n == e["vv_t2"]
                for tb in range(8):
                    v.wait_ge(tt.h, e["tt_tr"] - 8 + tb + 1)
                    v.tensor_reduce(MISC2[:, 60:61], PB[2 + tb % 2][:, 0:128],
                                    AX.X, ALU.max, apply_absolute_value=True)
                    vv.inc(v.last)
                    v.wait_ge(vv.h, vv.n)
                    if hc == 0:
                        v.tensor_copy(MISC2[:, 44 + tb:45 + tb], MISC2[:, 60:61])
                    else:
                        v.tensor_tensor(MISC2[:, 44 + tb:45 + tb],
                                        MISC2[:, 44 + tb:45 + tb],
                                        MISC2[:, 60:61], ALU.max)
                    vv.inc(v.last)
                    v.wait_ge(vv.h, vv.n)
                assert vv.n == e["vv_hm"]
        self.v(veM)

        def seM(s):
            for hc in range(FFC):
                e = schH[hc]
                s.wait_ge(vv.h, e["vv_t2"])
                if hc >= 1:
                    s.wait_ge(dg.h, schH[hc - 1]["dg_h"])
                    s.wait_ge(tt.h, schH[hc - 1]["tt_tr"])
                s.activation(SB_[:, 0:1024], SA[:, 0:1024], ACT.Gelu_apprx_tanh)
                ss.inc(s.last)
                assert ss.n == e["ss_g"]
        self.s(seM)

        def gyM(g):
            for hc in range(FFC):
                e = schH[hc]
                g.wait_ge(ss.h, e["ss_g"])
                g.dma_start(out=D["st_hh"][hc * 128:(hc + 1) * 128, :],
                            in_=SB_[:, 0:1024])
                dg.inc(g.last, 16)
                assert dg.n == e["dg_h"]
        self.g(gyM)

        def veS2(v):
            v.wait_ge(vv.h, vv.n)
            for tb in range(8):
                v.tensor_scalar(MISC2[:, 44 + tb:45 + tb],
                                MISC2[:, 44 + tb:45 + tb], 1e-5, None, ALU.max)
                vv.inc(v.last)
            v.wait_ge(vv.h, vv.n)
            for tb in range(8):
                v.reciprocal(MISC2[:, 52 + tb:53 + tb], MISC2[:, 44 + tb:45 + tb])
                vv.inc(v.last)
            v.wait_ge(vv.h, vv.n)
            for tb in range(8):
                v.tensor_scalar(MISC2[:, 52 + tb:53 + tb],
                                MISC2[:, 52 + tb:53 + tb], 127.0, None, ALU.mult)
                vv.inc(v.last)
        self.v(veS2)
        self._scol_bcast(MISC2, 52, 8, 1, self.vv.n)

        dsp = self.dsp
        dsa_p2 = dsp.n
        dg_p2base = dg.n
        ss_p2 = ss.n
        self._p2vv = []
        vvc = self.vv.n
        for hc in range(FFC):
            vvc += 3
            self._p2vv.append(vvc)

        def syP(y):
            for hc in range(FFC):
                y.wait_ge(ss.h, ss_p2)
                y.wait_ge(vv.h, vv.n if hc == 0 else self._p2vv[hc - 1])
                y.dma_start(out=SA[:, 0:1024], in_=D["st_hh"][hc * 128:(hc + 1) * 128, :])
                dsp.inc(y.last, 16)
        self.y(syP)

        def veP(v):
            for hc in range(FFC):
                v.wait_ge(dsp.h, dsa_p2 + 16 * (hc + 1))
                if hc >= 1:
                    v.wait_ge(dg.h, dg_p2base + 16 * hc)
                v.tensor_tensor(SC[:, 0:1024], SA[:, 0:1024], SD[:, 0:1024], ALU.mult)
                vv.inc(v.last)
                v.wait_ge(vv.h, vv.n)
                v.tensor_copy(SC[:, 1024:2048].bitcast(I32), SC[:, 0:1024])
                vv.inc(v.last)
                v.wait_ge(vv.h, vv.n)
                v.tensor_copy(SBF[:, 0:1024], SC[:, 1024:2048].bitcast(I32))
                vv.inc(v.last)
                assert vv.n == self._p2vv[hc]
                v.wait_ge(vv.h, vv.n)
        self.v(veP)

        def gyP(g):
            for hc in range(FFC):
                g.wait_ge(vv.h, self._p2vv[hc])
                g.dma_start(out=D["st_kh"][hc * 128:(hc + 1) * 128, :],
                            in_=SBF[:, 0:1024])
                dg.inc(g.last, 16)
        self.g(gyP)
        dg_p2 = dg.n

        base_tt2 = tt.n
        schM2 = []
        ttc = tt.n
        for qd in range(4):
            for hc in range(FFC):
                ttc += 8
                schM2.append(ttc)

        dsb = self.dsn
        bb = [dsb[k].n for k in range(2)]

        def syN(g):
            i = 0
            for qd in range(4):
                for hc in range(FFC):
                    if i >= 2:
                        g.wait_ge(tt.h, schM2[i - 2])
                    if i == 0:
                        g.wait_ge(dg.h, dg_p2)
                    g.dma_start(out=SBF[:, (i % 2) * 1024:(i % 2) * 1024 + 1024],
                                in_=D["st_kh"][hc * 128:(hc + 1) * 128, :])
                    dsb[i % 2].inc(g.last, 16)
                    i += 1
        self.g(syN)

        base_w2 = [dsw[k].n for k in range(4)]

        def syN2(y):
            i = 0
            for qd in range(4):
                for hc in range(FFC):
                    for q in range(4):
                        oc = qd * 4 + q
                        rank2, lcb = oc // 2, oc % 2
                        if i >= 4:
                            y.wait_ge(tt.h, base_tt2 + (i - 3) * 2)
                        y.dma_start(out=self.W[i % 4][:].bitcast(BF16)[:, 0:128],
                                    in_=D["bt_w2_all"][rank2,
                                                       hc * 128:(hc + 1) * 128,
                                                       lcb * 128:(lcb + 1) * 128])
                        dsw[i % 4].inc(y.last, 16)
                        i += 1
        self.y(syN2)

        vv_ep = []
        vvc = self.vv.n
        for qd in range(4):
            vvc += 16
            vv_ep.append(vvc)

        PSL = [PB[0], PB[1], PB[2], PB[3], PB[4], PB[5], PN[0], PN[1]]

        def teN(t):
            i = 0
            iw = 0
            for qd in range(4):
                for hc in range(FFC):
                    t.wait_ge(dsb[i % 2].h, bb[i % 2] + 16 * (i // 2 + 1))
                    if hc == 0 and qd > 0:
                        t.wait_ge(vv.h, vv_ep[qd - 1])
                    for q in range(4):
                        t.wait_ge(dsw[iw % 4].h, base_w2[iw % 4] + 16 * (iw // 4 + 1))
                        for f in range(2):
                            t.matmul(PSL[q * 2 + f][:],
                                     self.W[iw % 4][:].bitcast(BF16)[:, 0:128],
                                     SBF[:, (i % 2) * 1024 + f * 512:(i % 2) * 1024 + (f + 1) * 512],
                                     start=(hc == 0), stop=(hc == FFC - 1))
                            tt.inc(t.last)
                        iw += 1
                    i += 1
        self.t(teN)

        dso = self.dso
        dsa_q = dso.n

        def syO(s):
            for qd in range(4):
                for q in range(4):
                    oc = qd * 4 + q
                    rank2, lcb = oc // 2, oc % 2
                    if qd >= 1:
                        s.wait_ge(vv.h, vv_ep[qd - 1])
                    s.dma_start(out=MISC[:, 12 + q:13 + q],
                                in_=D["bs_sc_all"][rank2,
                                                   1024 + lcb * 128:1024 + (lcb + 1) * 128, :])
                    dso.inc(s.last, 16)
        self.s(syO)

        def veN(v):
            for qd in range(4):
                v.wait_ge(tt.h, schM2[(qd + 1) * FFC - 1])
                v.wait_ge(dso.h, dsa_q + 16 * 4 * (qd + 1))
                for q in range(4):
                    oc = qd * 4 + q
                    for f in range(2):
                        v.tensor_scalar(SC[:, f * 512:(f + 1) * 512],
                                        PSL[q * 2 + f][:],
                                        MISC[:, 12 + q:13 + q], None, ALU.mult)
                        vv.inc(v.last)
                    v.wait_ge(vv.h, vv.n)
                    v.tensor_tensor(SC[:, 1024:2048], SC[:, 0:1024],
                                    SD[:, 1024:2048], ALU.mult)
                    vv.inc(v.last)
                    v.wait_ge(vv.h, vv.n)
                    v.tensor_tensor(x1_ap(oc), SC[:, 1024:2048], x1_ap(oc),
                                    ALU.add)
                    vv.inc(v.last)
                    v.wait_ge(vv.h, vv.n)
                assert vv.n == vv_ep[qd]
        self.v(veN)

        self._layernorm(x1_ap)
        self.x2_ap = x1_ap

        def gyX(g):
            g.wait_ge(vv.h, self.vv.n)
            for ch in range(16):
                g.dma_start(out=D["bx2"][ch * 128:(ch + 1) * 128, :],
                            in_=x1_ap(ch))
                dg.inc(g.last, 16)
        self.g(gyX)

    def ph_l1_attn(self):
        D = self.D
        tt, vv, ss, dg, cc = self.tt, self.vv, self.ss, self.dg, self.cc
        PAIRS = [[0, 1], [2, 3], [4, 5], [6, 7]]

        def gA(g):
            g.wait_ge(dg.h, dg.n)
            g.collective_compute("AllGather", ALU.bypass, replica_groups=PAIRS,
                                 ins=[D["bx2"][:]], outs=[D["bx2_all"][:]])
            cc.inc(g.last)
        self.g(gA)
        self.cc_x2 = cc.n

        self._l1_inproj()
        self._l1_conv_xproj()
        self._l1_dt()
        self._l1_scan()
        self._l1_outproj()

    def _l1_inproj(self):
        D = self.D
        tt, vv, ss, dg = self.tt, self.vv, self.ss, self.dg
        dsa = self.dsa
        PB = self.PB
        SA = self.SA
        x2f = lambda c: self.slot(c)

        dsa0 = dsa.n

        def syL(y):
            y.wait_ge(self.cc.h, self.cc_x2)
            y.wait_ge(vv.h, vv.n)
            for c in range(16):
                for par in range(2):
                    y.dma_start(
                        out=x2f(c).rearrange("p (g f) -> p g f", f=128)[:, par::2, :],
                        in_=D["bx2_all"][par, c * 128:(c + 1) * 128, :]
                        .rearrange("p (g f) -> p g f", f=128))
                    dsa.inc(y.last, 16)
        self.y(syL)
        dsa_x2 = dsa.n

        items = []
        evac = []
        vvc = vv.n
        dsn = self.dsn
        base_n = [dsn[k].n for k in range(2)]
        for po in range(32):
            for tq in range(4):
                for c in range(16):
                    mw = []
                    if c == 0:
                        mw = [(dsa, dsa_x2)]
                        idx = po * 4 + tq
                        if idx >= 2:
                            mw.append((vv, evac[idx - 2]["vv"]))
                    items.append(dict(
                        w=D["inproj_sl"][c * 128:(c + 1) * 128,
                                         po * 128:(po + 1) * 128],
                        m=x2f(c)[:, tq * 512:(tq + 1) * 512],
                        o=PB[(po * 4 + tq) % 2][:],
                        start=(c == 0), stop=(c == 15), mwait=mw))
                vvc += 1
                evac.append(dict(vv=vvc))
        base_tt = self.tt.n
        self.mm_stream(items)

        def veE(v):
            for idx in range(128):
                po, tq = idx // 4, idx % 4
                v.wait_ge(tt.h, base_tt + (idx + 1) * 16)
                if idx >= 2:
                    v.wait_ge(dsn[idx % 2].h, base_n[idx % 2] + 16 * ((idx - 2) // 2 + 1))
                v.tensor_copy(SA[:, (idx % 2) * 512:(idx % 2) * 512 + 512],
                              PB[idx % 2][:])
                vv.inc(v.last)
        self.v(veE)

        def gyE(g):
            for idx in range(128):
                po, tq = idx // 4, idx % 4
                g.wait_ge(vv.h, evac[idx]["vv"])
                dst = D["st_hid"] if po < 16 else D["st_gate"]
                row = (po % 16) * 128
                g.dma_start(out=dst[row:row + 128, tq * 512:(tq + 1) * 512],
                            in_=SA[:, (idx % 2) * 512:(idx % 2) * 512 + 512])
                dsn[idx % 2].inc(g.last, 16)
        self.g(gyE)

    def _l1_conv_xproj(self):
        D = self.D
        tt, vv, ss, dg, cc = self.tt, self.vv, self.ss, self.dg, self.cc
        dsa, dsc = self.dsa, self.dsc
        PB, PN = self.PB, self.PN
        SA, SB_, SC, MISC2 = self.SA, self.SB_, self.SC, self.MISC2
        PSL = [PB[0][:], PB[1][:], PB[2][:], PB[3][:], PB[4][:], PB[5][:], PN[0][:], PN[1][:]]

        dg_in = None
        dsn_in = [self.dsn[k].n for k in range(2)]
        dsm = self.dsm
        base_cm = [dsm[k].n for k in range(2)]
        sch = []
        vvc, ssc = vv.n, ss.n
        for ct in range(16):
            e = {}
            e["dsa"] = (ct % 2, base_cm[ct % 2] + 48 * (ct // 2 + 1))
            vvc += 4
            e["vv_conv"] = vvc
            ssc += 1
            e["ss"] = ssc
            sch.append(e)

        dsw = self.dsw
        base_w = [dsw[k].n for k in range(4)]
        ttx0_pre = tt.n

        def syC(y):
            i = 0
            for ct in range(16):
                if ct >= 2:
                    y.wait_ge(ss.h, sch[ct - 2]["ss"])
                if ct == 0:
                    y.wait_ge(self.dsn[0].h, dsn_in[0])
                    y.wait_ge(self.dsn[1].h, dsn_in[1])
                y.dma_start(out=SA[:, 0:2048] if ct % 2 == 0 else SB_[:, 0:2048],
                            in_=D["st_hid"][ct * 128:(ct + 1) * 128, :])
                dsm[ct % 2].inc(y.last, 16)
                y.dma_start(out=MISC2[:, (ct % 2) * 5:(ct % 2) * 5 + 4],
                            in_=D["convw"][ct * 128:(ct + 1) * 128, :])
                dsm[ct % 2].inc(y.last, 16)
                y.dma_start(out=MISC2[:, 11 + ct % 2:12 + ct % 2],
                            in_=D["convb"][ct * 128:(ct + 1) * 128, :])
                dsm[ct % 2].inc(y.last, 16)
                for pp in range(2):
                    if i >= 4:
                        j = i - 4
                        y.wait_ge(tt.h, ttx0_pre + (j // 2) * 8 + (j % 2 + 1) * 4)
                    y.dma_start(out=self.W[i % 4][:, 0:(128 if pp == 0 else 32)],
                                in_=D["xproj"][ct * 128:(ct + 1) * 128, 0:128]
                                if pp == 0 else
                                D["xproj"][ct * 128:(ct + 1) * 128, 128:160])
                    dsw[i % 4].inc(y.last, 16)
                    i += 1
        self.y(syC)


        ttx0 = tt.n
        ttx = []
        ttc2 = ttx0
        for ct in range(16):
            ttc2 += 8
            ttx.append(ttc2)

        dg_hs = []
        dgc0 = dg.n
        for ct in range(16):
            dgc0 += 16
            dg_hs.append(dgc0)

        def veC(v):
            for ct in range(16):
                e = sch[ct]
                buf = SA if ct % 2 == 0 else SB_
                v.wait_ge(dsm[e["dsa"][0]].h, e["dsa"][1])
                if ct >= 1:
                    v.wait_ge(vv.h, vv.n)
                    v.wait_ge(ss.h, sch[ct - 1]["ss"])
                    v.wait_ge(tt.h, ttx[ct - 1])
                    v.wait_ge(dg.h, dg_hs[ct - 1])
                v.tensor_scalar(SC[:, 0:2048], buf[:, 0:2048],
                                MISC2[:, (ct % 2) * 5 + 3:(ct % 2) * 5 + 4],
                                None, ALU.mult)
                vv.inc(v.last)
                for j in range(3):
                    k = j + 1
                    v.wait_ge(vv.h, vv.n)
                    v.scalar_tensor_tensor(
                        SC[:, k:2048], buf[:, 0:2048 - k],
                        MISC2[:, (ct % 2) * 5 + 3 - k:(ct % 2) * 5 + 4 - k],
                        SC[:, k:2048], ALU.mult, ALU.add)
                    vv.inc(v.last)
                assert vv.n == e["vv_conv"]
        self.v(veC)

        def seC(s):
            for ct in range(16):
                e = sch[ct]
                s.wait_ge(vv.h, e["vv_conv"])
                if ct >= 1:
                    s.wait_ge(dg.h, dg_hs[ct - 1])
                s.activation(SC[:, 0:2048], SC[:, 0:2048], ACT.Silu,
                             bias=MISC2[:, 11 + ct % 2:12 + ct % 2])
                ss.inc(s.last)
                assert ss.n == e["ss"]

        self.s(seC)

        def gyH(g):
            for ct in range(16):
                e = sch[ct]
                g.wait_ge(ss.h, e["ss"])
                g.dma_start(out=D["st_hs"][ct * 128:(ct + 1) * 128, :],
                            in_=SC[:, 0:2048])
                dg.inc(g.last, 16)
        self.g(gyH)


        def teX(t):
            i = 0
            for ct in range(16):
                e = sch[ct]
                t.wait_ge(ss.h, e["ss"])
                for pp in range(2):
                    t.wait_ge(dsw[i % 4].h, base_w[i % 4] + 16 * (i // 4 + 1))
                    for tq in range(4):
                        wsl = self.W[i % 4][:, 0:(128 if pp == 0 else 32)]
                        out = PSL[tq] if pp == 0 else PSL[4 + tq][0:32, :]
                        t.matmul(out, wsl, SC[:, tq * 512:(tq + 1) * 512],
                                 start=(ct == 0), stop=(ct == 15))
                        tt.inc(t.last)
                    i += 1
                assert tt.n == ttx[ct]
        self.t(teX)

        def veX(v):
            v.wait_ge(tt.h, ttx[-1])
            for tq in range(4):
                v.tensor_copy(SA[:, tq * 512:(tq + 1) * 512], PSL[tq])
                vv.inc(v.last)
            for tq in range(4):
                v.tensor_copy(SB_[0:32, tq * 512:(tq + 1) * 512],
                              PSL[4 + tq][0:32, :])
                vv.inc(v.last)
        self.v(veX)
        vv_sp = vv.n

        def gyX2(g):
            g.wait_ge(vv.h, vv_sp)
            g.dma_start(out=D["bssm"][0:128, :], in_=SA[:, 0:2048])
            dg.inc(g.last, 16)
            g.dma_start(out=D["bssm"][128:160, :], in_=SB_[0:32, 0:2048])
            dg.inc(g.last, 16)
            g.wait_ge(dg.h, dg.n)
            g.collective_compute("AllReduce", ALU.add,
                                 replica_groups=[[0, 1], [2, 3], [4, 5], [6, 7]],
                                 ins=[D["bssm"][:]], outs=[D["bssm_all"][:]])
            cc.inc(g.last)
        self.g(gyX2)
        self.cc_sp = cc.n

    def _l1_dt(self):
        D = self.D
        tt, vv, ss, dg = self.tt, self.vv, self.ss, self.dg
        dsc, dsa = self.dsc, self.dsa
        PB, PN = self.PB, self.PN
        SA, SB_, SC, SD, MISC = self.SA, self.SB_, self.SC, self.SD, self.MISC
        NRM = self.CSP
        BCC = self.BCC

        dsc0 = dsc.n

        def syD(y):
            y.wait_ge(self.cc.h, self.cc_sp)
            y.wait_ge(vv.h, vv.n)
            y.wait_ge(ss.h, ss.n)
            y.wait_ge(tt.h, tt.n)
            y.dma_start(out=SA[:, 0:2048], in_=D["bssm_all"][0:128, :])
            dsc.inc(y.last, 16)
            y.dma_start(out=SB_[0:16, 0:2048], in_=D["bssm_all"][128:144, :])
            dsc.inc(y.last, 16)
            y.dma_start(out=SB_[32:48, 0:2048], in_=D["bssm_all"][144:160, :])
            dsc.inc(y.last, 16)
            y.dma_start(out=BCC[:], in_=D["bccols"][:])
            dsc.inc(y.last, 16)
        self.y(syD)
        dsc_l = dsc.n

        def veZ(v):
            v.wait_ge(dsc.h, dsc_l)
            v.memset(SD[0:64, 0:2048], 0.0)
            vv.inc(v.last)
        self.v(veZ)
        vv_z = vv.n

        def seR(s):
            s.wait_ge(vv.h, vv_z)
            s.activation(SC[:, 0:2048], SA[:, 0:2048], ACT.Square)
            ss.inc(s.last)
            s.activation(SD[0:16, 0:2048], SB_[0:16, 0:2048], ACT.Square)
            ss.inc(s.last)
            s.activation(SD[32:48, 0:2048], SB_[32:48, 0:2048], ACT.Square)
            ss.inc(s.last)
        self.s(seR)
        ss_r = ss.n

        def teR(t):
            t.wait_ge(ss.h, ss_r)
            for f in range(4):
                t.matmul(PB[f][0:1, :], MISC[:, 0:1],
                         SC[:, f * 512:(f + 1) * 512], start=True, stop=True)
                tt.inc(t.last)
        self.t(teR)
        tt_r = tt.n

        def veR(v):
            v.wait_ge(tt.h, tt_r)
            for f in range(4):
                v.tensor_copy(NRM[0:1, f * 512:(f + 1) * 512], PB[f][0:1, :])
                vv.inc(v.last)
        self.v(veR)
        vv_r = vv.n

        def seR2(s):
            s.wait_ge(vv.h, vv_r)
            s.activation(SC[0:1, 0:2048], NRM[0:1, 0:2048], ACT.Sqrt,
                         scale=1.0 / DTR, bias=MISC[0:1, 3:4])
            ss.inc(s.last)
        self.s(seR2)
        ss_r2 = ss.n

        def veR2(v):
            v.wait_ge(ss.h, ss_r2)
            v.reciprocal(NRM[0:1, 0:2048], SC[0:1, 0:2048])
            vv.inc(v.last)
        self.v(veR2)
        vv_r2 = vv.n

        def teR2(t):
            t.wait_ge(vv.h, vv_r2)
            for f in range(4):
                t.matmul(PB[f][:, 0:512], self.ONER[0:1, :],
                         NRM[0:1, f * 512:(f + 1) * 512], start=True, stop=True)
                tt.inc(t.last)
        self.t(teR2)
        tt_r2 = tt.n

        def veR3(v):
            v.wait_ge(tt.h, tt_r2)
            for f in range(4):
                v.tensor_tensor(SA[:, f * 512:(f + 1) * 512],
                                SA[:, f * 512:(f + 1) * 512], PB[f][:], ALU.mult)
                vv.inc(v.last)
        self.v(veR3)
        vv_dtn = vv.n

        for which in range(2):
            vv_pre = vv.n

            def teB(t, which=which, vv_pre=vv_pre):
                t.wait_ge(ss.h, ss_r)
                t.wait_ge(vv.h, vv_pre)
                for f in range(4):
                    t.matmul(PB[f][0:1, :], BCC[:, which:which + 1],
                             SD[0:64, f * 512:(f + 1) * 512], start=True, stop=True)
                    tt.inc(t.last)
            self.t(teB)
            tt_b = tt.n

            def veB(v, which=which):
                v.wait_ge(tt.h, tt_b)
                for f in range(4):
                    v.tensor_copy(NRM[0:1, f * 512:(f + 1) * 512], PB[f][0:1, :])
                    vv.inc(v.last)
            self.v(veB)
            vv_b = vv.n

            def seB(s, which=which):
                s.wait_ge(vv.h, vv_b)
                s.activation(SC[0:1, 0:2048], NRM[0:1, 0:2048], ACT.Sqrt,
                             scale=1.0 / NST, bias=MISC[0:1, 3:4])
                ss.inc(s.last)
            self.s(seB)
            ss_b = ss.n

            def veB2(v, which=which):
                v.wait_ge(ss.h, ss_b)
                v.reciprocal(NRM[0:1, 0:2048], SC[0:1, 0:2048])
                vv.inc(v.last)
            self.v(veB2)
            vv_b2 = vv.n

            def teB2(t, which=which):
                t.wait_ge(vv.h, vv_b2)
                for f in range(4):
                    t.matmul(PB[f][0:32, 0:512], self.ONER[0:1, 0:32],
                             NRM[0:1, f * 512:(f + 1) * 512], start=True, stop=True)
                    tt.inc(t.last)
            self.t(teB2)
            tt_b2 = tt.n

            def veB3(v, which=which):
                v.wait_ge(tt.h, tt_b2)
                r0 = 0 if which == 0 else 32
                for f in range(4):
                    v.tensor_tensor(SB_[r0:r0 + 16, f * 512:(f + 1) * 512],
                                    SB_[r0:r0 + 16, f * 512:(f + 1) * 512],
                                    PB[f][0:16, :], ALU.mult)
                    vv.inc(v.last)
            self.v(veB3)
        vv_bcn = vv.n

        dsw = self.dsw
        base_w = [dsw[k].n for k in range(4)]
        base_tt = tt.n
        dsm = self.dsm
        base_m = [dsm[k].n for k in range(2)]

        sch = []
        vvc, ssc, dgc = vv.n, ss.n, dg.n
        for idx in range(64):
            e = {}
            ssc += 2; e["ss_ex"] = ssc
            vvc += 1; e["vv_l1"] = vvc
            ssc += 2; e["ss"] = ssc
            vvc += 1; e["vv"] = vvc
            dgc += 16; e["dg"] = dgc
            sch.append(e)

        def syT(y):
            for po in range(16):
                if po >= 2:
                    y.wait_ge(vv.h, sch[(po - 2) * 4 + 3]["vv"])
                y.dma_start(out=MISC[:, 16 + po % 2:17 + po % 2],
                            in_=D["dtb"][po * 128:(po + 1) * 128, :])
                dsm[po % 2].inc(y.last, 16)
                i = po
                if i >= 4:
                    j = i - 4
                    y.wait_ge(tt.h, base_tt + (j + 1) * 4)
                y.dma_start(out=self.W[i % 4][:, 0:128],
                            in_=D["dtw"][0:128, po * 128:(po + 1) * 128])
                dsw[i % 4].inc(y.last, 16)
        self.y(syT)

        def teT(t):
            for po in range(16):
                t.wait_ge(dsw[po % 4].h, base_w[po % 4] + 16 * (po // 4 + 1))
                if po == 0:
                    t.wait_ge(vv.h, vv_bcn)
                for tq in range(4):
                    idx = po * 4 + tq
                    if idx >= 2:
                        t.wait_ge(ss.h, sch[idx - 2]["ss"])
                    t.matmul(PB[idx % 2][:], self.W[po % 4][:, 0:128],
                             SA[:, tq * 512:(tq + 1) * 512], start=True, stop=True)
                    tt.inc(t.last)
        self.t(teT)

        def seT(s):
            for po in range(16):
                for tq in range(4):
                    idx = po * 4 + tq
                    e = sch[idx]
                    hb = (idx % 2) * 1024
                    s.wait_ge(tt.h, base_tt + idx + 1)
                    s.wait_ge(dsm[po % 2].h, base_m[po % 2] + 16 * (po // 2 + 1))
                    if idx >= 2:
                        s.wait_ge(vv.h, sch[idx - 2]["vv"])
                    s.activation(SC[:, hb:hb + 512], PB[idx % 2][:], ACT.Abs,
                                 bias=MISC[:, 16 + po % 2:17 + po % 2])
                    ss.inc(s.last)
                    s.wait_ge(ss.h, ss.n)
                    s.activation(SC[:, hb + 512:hb + 1024], SC[:, hb:hb + 512],
                                 ACT.Exp, scale=-1.0)
                    ss.inc(s.last)
                    assert ss.n == e["ss_ex"]
                    s.wait_ge(vv.h, e["vv_l1"])
                    s.activation(SC[:, hb + 512:hb + 1024],
                                 SC[:, hb + 512:hb + 1024], ACT.Ln)
                    ss.inc(s.last)
                    s.activation(SC[:, hb:hb + 512], PB[idx % 2][:], ACT.Relu,
                                 bias=MISC[:, 16 + po % 2:17 + po % 2])
                    ss.inc(s.last)
                    assert ss.n == e["ss"]
        self.s(seT)

        def veT(v):
            for po in range(16):
                for tq in range(4):
                    idx = po * 4 + tq
                    e = sch[idx]
                    hb = (idx % 2) * 1024
                    v.wait_ge(ss.h, e["ss_ex"])
                    v.tensor_scalar(SC[:, hb + 512:hb + 1024],
                                    SC[:, hb + 512:hb + 1024], 1.0, None, ALU.add)
                    vv.inc(v.last)
                    assert vv.n == e["vv_l1"]
                    v.wait_ge(ss.h, e["ss"])
                    if idx >= 2:
                        v.wait_ge(dg.h, sch[idx - 2]["dg"])
                    v.tensor_tensor(SD[:, (idx % 2) * 512:(idx % 2) * 512 + 512],
                                    SC[:, hb:hb + 512],
                                    SC[:, hb + 512:hb + 1024], ALU.add)
                    vv.inc(v.last)
                    assert vv.n == e["vv"]
        self.v(veT)

        def gyT(g):
            for po in range(16):
                for tq in range(4):
                    idx = po * 4 + tq
                    g.wait_ge(vv.h, sch[idx]["vv"])
                    if idx >= 1:
                        g.wait_ge(dg.h, sch[idx - 1]["dg"])
                    g.dma_start(out=D["st_dt"][po * 128:(po + 1) * 128,
                                               tq * 512:(tq + 1) * 512],
                                in_=SD[:, (idx % 2) * 512:(idx % 2) * 512 + 512])
                    dg.inc(g.last, 16)
                    assert dg.n == sch[idx]["dg"]
        self.g(gyT)

    def _l1_scan(self):
        D = self.D
        tt, vv, ss, dg = self.tt, self.vv, self.ss, self.dg
        dsa = self.dsa
        PB = self.PB
        SB_, MISC, MISC2 = self.SB_, self.MISC, self.MISC2
        CK = 256
        NCK = T // CK
        R = self.R
        bBB = R[:, 0:4096].rearrange("p (t n) -> p t n", n=16)
        bCB = R[:, 4096:8192].rearrange("p (t n) -> p t n", n=16)
        bDA = R[:, 8192:12288]
        bDAv = bDA.rearrange("p (t n) -> p t n", n=16)
        bDBU = R[:, 12288:16384]
        bDBUv = bDBU.rearrange("p (t n) -> p t n", n=16)
        bS = R[:, 16384:20480]
        bSv = bS.rearrange("p (t n) -> p t n", n=16)
        bST = R[:, 20480:20736]
        bA = R[:, 20736:20992]
        bU = R[:, 21504:21760]
        bDT = R[:, 21760:22016]
        bG = R[:, 22016:22272]
        bY2 = R[:, 22528:22784]
        HPB = R[0:64, 22784:24832]
        HPC = R[0:64, 24832:26880]

        dg_dt = dg.n
        gg = self.gg

        def gpH(g):
            g.wait_ge(vv.h, vv.n)
            g.iota(R[:, 26880:28928][0:64, :], pattern=[[-1, 16], [0, 128]],
                   base=0, channel_multiplier=1,
                   allow_small_or_imprecise_dtypes=True)
            g.memset(R[0:64, 28928:28929], 0.0)
            gg.inc(g.last)
            g.wait_ge(gg.h, gg.n)
            g.tensor_scalar(HPB[:], R[0:64, 26880:28928], 0.0, None, ALU.is_equal)
            gg.inc(g.last)
            g.wait_ge(gg.h, gg.n)
            g.iota(R[:, 26880:28928][0:64, :], pattern=[[-1, 16], [0, 128]],
                   base=-32, channel_multiplier=1,
                   allow_small_or_imprecise_dtypes=True)
            g.memset(R[0:64, 28928:28929], 0.0)
            gg.inc(g.last)
            g.wait_ge(gg.h, gg.n)
            g.tensor_scalar(HPC[:], R[0:64, 26880:28928], 0.0, None, ALU.is_equal)
            gg.inc(g.last)
        self.g(gpH)
        gg_hp = gg.n

        dsa_a0 = dsa.n

        def syA(y):
            y.wait_ge(vv.h, vv.n)
            for dt_ in range(16):
                y.dma_start(out=R[:, 20992 + dt_ * 16:20992 + (dt_ + 1) * 16],
                            in_=D["Alog"][dt_ * 128:(dt_ + 1) * 128, :])
                dsa.inc(y.last, 16)
            y.dma_start(out=MISC2[:, 48:64],
                        in_=D["Dpar"].rearrange("(c p) one -> p (c one)", p=128),
                        allow_slow_non_contiguous=True)
            dsa.inc(y.last, 16)
        self.y(syA)
        dsa_a = dsa.n

        def seA(s):
            s.wait_ge(dsa.h, dsa_a)
            s.activation(R[:, 20736:20992], R[:, 20992:21248], ACT.Exp)
            ss.inc(s.last)
        self.s(seA)
        ss_a = ss.n

        def veA2(v):
            v.wait_ge(ss.h, ss_a)
            v.tensor_scalar(R[:, 20736:20992], R[:, 20736:20992], -1.0, None,
                            ALU.mult)
            vv.inc(v.last)
            v.wait_ge(vv.h, vv.n)
            v.memset(bST[:], 0.0)
            vv.inc(v.last)
        self.v(veA2)

        sch = []
        vvc, ssc, ttc, dsac, dgc = vv.n, ss.n, tt.n, dsa.n, dg.n
        for ck in range(NCK):
            e = {}
            ttc += 64
            e["tt_bc"] = ttc
            vvc += 64
            e["vv_bc"] = vvc
            tiles = []
            for dt_ in range(16):
                t2 = {}
                dsac += 16 * 3
                t2["dsa"] = dsac
                vvc += 39
                ssc += 2
                t2["vv"] = vvc
                t2["ss"] = ssc
                t2["vv_dtu"] = e["vv_bc"] + sum(
                    39 for _ in range(dt_)) + 1 if False else 0
                dgc += 16
                t2["dg"] = dgc
                tiles.append(t2)
            e["tiles"] = tiles
            sch.append(e)

        def teS(t):
            for ck in range(NCK):
                e = sch[ck]
                if ck == 0:
                    t.wait_ge(vv.h, sch[0]["vv_bc"] - 64)
                    t.wait_ge(gg.h, gg_hp)
                else:
                    t.wait_ge(vv.h, sch[ck - 1]["tiles"][-1]["vv"])
                k = 0
                for n in range(16):
                    for half in range(2):
                        src_ap = SB_[0:64, ck * CK + half * 128:ck * CK + (half + 1) * 128]
                        if k >= 4:
                            t.wait_ge(vv.h, e["vv_bc"] - 64 + k - 2)
                        t.matmul(PB[k % 4][:, 0:128],
                                 HPB[:, n * 128:(n + 1) * 128],
                                 src_ap, start=True, stop=True)
                        tt.inc(t.last)
                        k += 1
                        t.matmul(PB[k % 4][:, 0:128],
                                 HPC[:, n * 128:(n + 1) * 128],
                                 src_ap, start=True, stop=True)
                        tt.inc(t.last)
                        k += 1
        self.t(teS)

        def veS(v):
            for ck in range(NCK):
                e = sch[ck]
                k = 0
                for n in range(16):
                    for half in range(2):
                        v.wait_ge(tt.h, e["tt_bc"] - 64 + k + 1)
                        v.tensor_copy(bBB[:, half * 128:(half + 1) * 128, n],
                                      PB[k % 4][:, 0:128])
                        vv.inc(v.last)
                        k += 1
                        v.wait_ge(tt.h, e["tt_bc"] - 64 + k + 1)
                        v.tensor_copy(bCB[:, half * 128:(half + 1) * 128, n],
                                      PB[k % 4][:, 0:128])
                        vv.inc(v.last)
                        k += 1
                assert vv.n == e["vv_bc"]
                for dt_ in range(16):
                    t2 = e["tiles"][dt_]
                    v.wait_ge(dsa.h, t2["dsa"])
                    if ck > 0 or dt_ > 0:
                        pt = (e["tiles"][dt_ - 1] if dt_ > 0
                              else sch[ck - 1]["tiles"][-1])
                        v.wait_ge(dg.h, pt["dg"])
                    v.tensor_tensor(
                        bDAv[:],
                        bDT[:].rearrange("p (t one) -> p t one", one=1)
                           .broadcast_to([128, CK, 16]),
                        R[:, 20736 + dt_ * 16:20736 + (dt_ + 1) * 16]
                           .rearrange("p (one n) -> p one n", one=1)
                           .broadcast_to([128, CK, 16]),
                        ALU.mult)
                    vv.inc(v.last); v.wait_ge(vv.h, vv.n)
                    v.tensor_tensor(bDT[:], bDT[:], bU[:], ALU.mult)
                    vv.inc(v.last); v.wait_ge(vv.h, vv.n)
                    v.tensor_tensor(
                        bDBUv[:],
                        bDT[:].rearrange("p (t one) -> p t one", one=1)
                           .broadcast_to([128, CK, 16]),
                        bBB[:], ALU.mult)
                    vv.inc(v.last); v.wait_ge(vv.h, vv.n)
                    v.wait_ge(ss.h, t2["ss"] - 1)
                    for n in range(16):
                        v.tensor_tensor_scan(
                            bSv[:, :, n], bDAv[:, :, n], bDBUv[:, :, n],
                            bST[:, dt_ * 16 + n:dt_ * 16 + n + 1],
                            ALU.mult, ALU.add)
                        vv.inc(v.last)
                    v.wait_ge(vv.h, vv.n)
                    for n in range(16):
                        v.tensor_copy(bST[:, dt_ * 16 + n:dt_ * 16 + n + 1],
                                      bSv[:, CK - 1:CK, n])
                        vv.inc(v.last)
                    v.wait_ge(vv.h, vv.n)
                    v.tensor_tensor(bDAv[:], bSv[:], bCB[:], ALU.mult)
                    vv.inc(v.last); v.wait_ge(vv.h, vv.n)
                    v.tensor_reduce(bY2[:], bDAv[:], AX.X, ALU.add)
                    vv.inc(v.last); v.wait_ge(vv.h, vv.n)
                    v.scalar_tensor_tensor(bY2[:], bU[:],
                        MISC2[:, 48 + dt_:49 + dt_], bY2[:],
                        ALU.mult, ALU.add)
                    vv.inc(v.last); v.wait_ge(vv.h, vv.n)
                    v.wait_ge(ss.h, t2["ss"])
                    v.tensor_tensor(bY2[:], bY2[:], bG[:], ALU.mult)
                    vv.inc(v.last)
                    assert vv.n == t2["vv"], (vv.n, t2["vv"])
        self.v(veS)

        def seS(s):
            for ck in range(NCK):
                e = sch[ck]
                for dt_ in range(16):
                    t2 = e["tiles"][dt_]
                    s.wait_ge(vv.h, t2["vv"] - 38)
                    s.activation(bDA[:], bDA[:], ACT.Exp)
                    ss.inc(s.last)
                    s.wait_ge(dsa.h, t2["dsa"])
                    s.activation(bG[:], bG[:], ACT.Silu)
                    ss.inc(s.last)
        self.s(seS)

        def syS(y):
            for ck in range(NCK):
                e = sch[ck]
                for dt_ in range(16):
                    t2 = e["tiles"][dt_]
                    if ck > 0 or dt_ > 0:
                        pt = (e["tiles"][dt_ - 1] if dt_ > 0
                              else sch[ck - 1]["tiles"][-1])
                        y.wait_ge(vv.h, pt["vv"])
                    else:
                        y.wait_ge(vv.h, sch[0]["vv_bc"] - 64)
                        y.wait_ge(dg.h, dg_dt)
                    y.dma_start(out=bU[:],
                                in_=D["st_hs"][dt_ * 128:(dt_ + 1) * 128,
                                               ck * CK:(ck + 1) * CK])
                    dsa.inc(y.last, 16)
                    y.dma_start(out=bDT[:],
                                in_=D["st_dt"][dt_ * 128:(dt_ + 1) * 128,
                                               ck * CK:(ck + 1) * CK])
                    dsa.inc(y.last, 16)
                    y.dma_start(out=bG[:],
                                in_=D["st_gate"][dt_ * 128:(dt_ + 1) * 128,
                                                 ck * CK:(ck + 1) * CK])
                    dsa.inc(y.last, 16)
                    assert dsa.n == t2["dsa"]
        self.y(syS)

        def gyS(g):
            for ck in range(NCK):
                e = sch[ck]
                for dt_ in range(16):
                    t2 = e["tiles"][dt_]
                    g.wait_ge(vv.h, t2["vv"])
                    g.dma_start(out=D["st_yg"][dt_ * 128:(dt_ + 1) * 128,
                                               ck * CK:(ck + 1) * CK],
                                in_=bY2[:])
                    dg.inc(g.last, 16)
                    assert dg.n == t2["dg"]
        self.g(gyS)

    def _l1_outproj(self):
        D = self.D
        tt, vv, ss, dg, cc = self.tt, self.vv, self.ss, self.dg, self.cc
        dsa = self.dsa
        PB = self.PB
        SA, SB_ = self.SA, self.SB_
        dsw = self.dsw
        dsx2 = self.dsx2
        base_w = [dsw[k].n for k in range(4)]
        base_x = [dsx2[k].n for k in range(2)]
        dg_yg = dg.n

        blocks = []
        ttc, vvc, dgc = tt.n, vv.n, dg.n
        base_tt = tt.n
        for t4 in range(4):
            for ocb in range(4):
                e = dict(t4=t4, ocb=ocb)
                ttc += 64
                e["tt"] = ttc
                vvc += 4
                e["vv"] = vvc
                dgc += 16 * 16
                e["dg"] = dgc
                blocks.append(e)

        def syO(y):
            iw = 0
            im = 0
            for bi, e in enumerate(blocks):
                t4, ocb = e["t4"], e["ocb"]
                for ch in range(16):
                    if im >= 2:
                        jb = im - 2
                        y.wait_ge(tt.h, base_tt + jb * 4 + 4)
                    if im == 0:
                        y.wait_ge(dg.h, dg_yg)
                    y.dma_start(out=(SA if im % 2 == 0 else SB_)[:, 0:512],
                                in_=D["st_yg"][ch * 128:(ch + 1) * 128,
                                               t4 * 512:(t4 + 1) * 512])
                    dsx2[im % 2].inc(y.last, 16)
                    im += 1
                    for q in range(4):
                        oc = ocb * 4 + q
                        if iw >= 4:
                            jw = iw - 4
                            y.wait_ge(tt.h, base_tt + jw + 1)
                        y.dma_start(out=self.W[iw % 4][:, 0:128],
                                    in_=D["outproj"][ch * 128:(ch + 1) * 128,
                                                     oc * 128:(oc + 1) * 128])
                        dsw[iw % 4].inc(y.last, 16)
                        iw += 1
        self.y(syO)

        def teO(t):
            iw = 0
            im = 0
            for bi, e in enumerate(blocks):
                for ch in range(16):
                    t.wait_ge(dsx2[im % 2].h, base_x[im % 2] + 16 * (im // 2 + 1))
                    if ch == 0 and bi >= 1:
                        t.wait_ge(vv.h, blocks[bi - 1]["vv"])
                    mbuf = (SA if im % 2 == 0 else SB_)[:, 0:512]
                    im += 1
                    for q in range(4):
                        t.wait_ge(dsw[iw % 4].h, base_w[iw % 4] + 16 * (iw // 4 + 1))
                        t.matmul(PB[q][:], self.W[iw % 4][:, 0:128], mbuf,
                                 start=(ch == 0), stop=(ch == 15))
                        tt.inc(t.last)
                        iw += 1
                assert tt.n == e["tt"]
        self.t(teO)

        def veO(v):
            for bi, e in enumerate(blocks):
                v.wait_ge(tt.h, e["tt"])
                if bi >= 1:
                    v.wait_ge(dg.h, blocks[bi - 1]["dg"])
                for q in range(4):
                    v.tensor_copy(self.SC[:, q * 512:(q + 1) * 512], PB[q][:])
                    vv.inc(v.last)
                assert vv.n == e["vv"]
        self.v(veO)

        def gyO(g):
            for bi, e in enumerate(blocks):
                t4, ocb = e["t4"], e["ocb"]
                g.wait_ge(vv.h, e["vv"])
                for q in range(4):
                    oc = ocb * 4 + q
                    for gi in range(4):
                        gt = t4 * 4 + gi
                        g.dma_start(
                            out=D["bop"][gt % 2, oc * 128:(oc + 1) * 128,
                                         (gt // 2) * 128:(gt // 2 + 1) * 128],
                            in_=self.SC[:, q * 512 + gi * 128:q * 512 + (gi + 1) * 128])
                        dg.inc(g.last, 16)
                assert dg.n == e["dg"]
        self.g(gyO)

        def gR(g):
            g.wait_ge(dg.h, dg.n)
            g.collective_compute("ReduceScatter", ALU.add,
                                 replica_groups=[[0, 1], [2, 3], [4, 5], [6, 7]],
                                 ins=[D["bop"][:]], outs=[D["bop_out"][:]])
            cc.inc(g.last)
        self.g(gR)
        cc_rs = cc.n

        x3_ap = lambda ch: self.half(9 + ch // 2, ch % 2)
        self.x3_ap = x3_ap
        dsa0 = dsa.n

        res_ap = lambda ch: self.half(ch // 2, ch % 2)

        def syR(y):
            y.wait_ge(cc.h, cc_rs)
            y.wait_ge(vv.h, vv.n)
            y.wait_ge(ss.h, ss.n)
            y.wait_ge(dg.h, dg.n)
            for ch in range(16):
                y.dma_start(out=x3_ap(ch),
                            in_=D["bop_out"][ch * 128:(ch + 1) * 128, :])
                dsa.inc(y.last, 16)
                y.dma_start(out=res_ap(ch), in_=D["bx2"][ch * 128:(ch + 1) * 128, :])
                dsa.inc(y.last, 16)
        self.y(syR)
        dsa_r = dsa.n

        def veR(v):
            v.wait_ge(dsa.h, dsa_r)
            for ch in range(16):
                v.tensor_tensor(x3_ap(ch), x3_ap(ch), res_ap(ch), ALU.add)
                vv.inc(v.last)
        self.v(veR)

        self._layernorm(x3_ap)

    def ph_quant_turbo(self):
        D = self.D
        tt, vv, ss, gg = self.tt, self.vv, self.ss, self.gg
        dsa, dso, cc = self.dsa, self.dso, self.cc
        PB, PN = self.PB, self.PN
        SA, SB_, SC, SD, SBF = self.SA, self.SB_, self.SC, self.SD, self.SBF
        MISC2, CSP, R, IDENT = self.MISC2, self.CSP, self.R, self.IDENT

        FWF = FF // NCORES
        CWP = C // NCORES
        chunks = []
        for ti in range(16):
            for cq in range(2):
                sl = (slice(ti * 128, (ti + 1) * 128),
                      slice(cq * 512, (cq + 1) * 512))
                chunks.append(dict(
                    src=D["wfcT_sh"][sl[0], sl[1]],
                    hi=D["q_fc_hi"][sl[0], sl[1]], lo=D["q_fc_lo"][sl[0], sl[1]],
                    nf=4, fw=512))
        for ti in range(64):
            sl = (slice(ti * 128, (ti + 1) * 128), slice(0, CWP))
            chunks.append(dict(
                src=D["wprT_sh"][sl[0], sl[1]],
                hi=D["q_pr_hi"][sl[0], sl[1]], lo=D["q_pr_lo"][sl[0], sl[1]],
                nf=2, fw=256))
        NCH = len(chunks)

        Wb = [SA[:, 0:512], SA[:, 512:1024]]
        zbB = [SA[:, 1024:1536], SA[:, 1536:2048]]
        uB = [SB_[:, 0:512], SB_[:, 512:1024]]
        zB = [SB_[:, 1024:1536], SB_[:, 1536:2048]]
        rB = [SC[:, 0:512], SC[:, 512:1024]]
        Eev = SC[0:32, 1024:1536]
        maskB = SC[:, 1536:1664]
        mvalB = SC[:, 1664:1792]
        erB = [self.ER[:, 0:512], self.ER[:, 512:1024]]
        deqB = SD[:, 1024:1536]
        lodB = SD[:, 1536:2048]
        hiB = SBF[:, 0:512]
        loB = SBF[:, 512:1024]
        q8B = [SBF[:, 1024:1280].bitcast(I8), SBF[:, 1280:1536].bitcast(I8)]
        qsB = SBF[:, 1536:1792].bitcast(I8)
        QTIst = R[:, 256:768]
        QTI = self.QTIT[:]
        ISVP = R[:, 0:128]
        SVP = R[:, 128:256]
        BST = CSP[0:8, 384:896]
        bvT = CSP[0:8, 896:1024]
        ibT = CSP[0:8, 1024:1152]
        scT = CSP[0:8, 1152:1280]
        AMX, BASE, BINV = MISC2[:, 0:8], MISC2[:, 8:16], MISC2[:, 16:24]
        BISV, BSV = MISC2[:, 24:32], MISC2[:, 32:40]
        IBS, SCBD = MISC2[:, 40:48], MISC2[:, 48:56]
        MNE = MISC2[:, 56:64]

        dsa0, dso0 = dsa.n, dso.n
        ttc, vvc, ssc, ggc = tt.n, vv.n, ss.n, gg.n
        dsa_c = dsa0 + 64
        vvc += 1
        vv_qti = vvc
        sch = []
        for m, chk in enumerate(chunks):
            nf = chk["nf"]
            e = {"nf": nf, "fw": chk["fw"]}
            e["dsa_w"] = dsa0 + 64 + 16 * (m + 1)
            ssc += 1; e["ss_abs"] = ssc
            ttc += nf; e["tt_ztr"] = ttc
            ttc += 1; e["tt_btr"] = ttc
            ttc += nf; e["tt_bb"] = ttc
            e["tt_err"] = []
            for j in range(16):
                ttc += 1; e["tt_err"].append(ttc)
            ttc += nf; e["tt_et"] = ttc
            ttc += 2; e["tt_ftr"] = ttc
            ttc += 2 * nf; e["tt_fb"] = ttc
            vvc += 1; e["vv_amax"] = vvc
            vvc += 2; e["vv_binv"] = vvc
            vvc += 1; e["vv_btr"] = vvc
            vvc += 1; e["vv_z"] = vvc
            e["vv_u"] = []; e["vv_q"] = []
            for j in range(16):
                vvc += 1; e["vv_u"].append(vvc)
                vvc += 1; e["vv_q"].append(vvc)
            vvc += 1; e["vv_Eev"] = vvc
            vvc += 7; e["vv_argmin"] = vvc
            vvc += 2; e["vv_fd"] = vvc
            vvc += 2; e["vv_fev"] = vvc
            vvc += 1; e["vv_ts"] = vvc
            vvc += 2; e["vv_qs"] = vvc
            vvc += 1; e["vv_deq"] = vvc
            vvc += 1; e["vv_hi"] = vvc
            vvc += 2; e["vv_lo"] = vvc
            e["ss_e"] = []
            for j in range(16):
                ssc += 1; e["ss_e"].append(ssc)
            e["gg_r"] = []
            for j in range(16):
                ggc += 1; e["gg_r"].append(ggc)
            e["dso"] = dso0 + 32 * (m + 1)
            sch.append(e)

        def syQ(y):
            y.wait_ge(vv.h, vv.n)
            y.wait_ge(tt.h, tt.n)
            y.wait_ge(ss.h, ss.n)
            y.dma_start(out=ISVP, in_=D["qt_isvp"][:]); dsa.inc(y.last, 16)
            y.dma_start(out=SVP, in_=D["qt_svp"][:]); dsa.inc(y.last, 16)
            y.dma_start(out=QTIst, in_=D["qt_ind"][:]); dsa.inc(y.last, 16)
            y.dma_start(out=BST, in_=D["qt_bst"][:]); dsa.inc(y.last, 16)
            for m, chk in enumerate(chunks):
                y.wait_ge(dsa.h, dsa_c + 16 * m)
                if m >= 2:
                    y.wait_ge(vv.h, sch[m - 2]["vv_ts"])
                y.dma_start(out=Wb[m % 2][:, 0:chk["fw"]], in_=chk["src"])
                dsa.inc(y.last, 16)
                e = sch[m]
                y.wait_ge(vv.h, e["vv_hi"])
                y.dma_start(out=chk["hi"], in_=hiB[:, 0:chk["fw"]])
                dso.inc(y.last, 16)
                y.wait_ge(vv.h, e["vv_lo"])
                y.dma_start(out=chk["lo"], in_=loB[:, 0:chk["fw"]])
                dso.inc(y.last, 16)
        self.y(syQ)

        def seQ(s):
            for m in range(NCH):
                e = sch[m]
                s.wait_ge(dsa.h, e["dsa_w"])
                if m >= 2:
                    s.wait_ge(vv.h, sch[m - 2]["vv_z"])
                s.activation(zbB[m % 2][:, 0:e["fw"]], Wb[m % 2][:, 0:e["fw"]],
                             ACT.Abs)
                ss.inc(s.last)
                assert ss.n == e["ss_abs"]
                for j in range(16):
                    s.wait_ge(gg.h, e["gg_r"][j])
                    if j >= 2:
                        s.wait_ge(tt.h, e["tt_err"][j - 2])
                    elif m >= 1:
                        s.wait_ge(tt.h, sch[m - 1]["tt_err"][14 + j])
                    s.activation(erB[j % 2][:, 0:e["fw"]], rB[j % 2][:, 0:e["fw"]],
                                 ACT.Square, scale=SVF[j])
                    ss.inc(s.last)
                    assert ss.n == e["ss_e"][j]
        self.s(seQ)

        def teQ(t):
            for m in range(NCH):
                e = sch[m]
                nf, fw = e["nf"], e["fw"]
                t.wait_ge(ss.h, e["ss_abs"])
                if m == 0:
                    t.wait_ge(vv.h, vv_qti)
                if m >= 2:
                    t.wait_ge(vv.h, sch[m - 2]["vv_amax"])
                for k in range(nf):
                    t.transpose(PB[m % 2][:, k * 128:(k + 1) * 128],
                                zbB[m % 2][:, k * 128:(k + 1) * 128], IDENT[:])
                    tt.inc(t.last)
                assert tt.n == e["tt_ztr"]
                t.wait_ge(vv.h, e["vv_binv"])
                if m >= 1:
                    t.wait_ge(vv.h, sch[m - 1]["vv_deq"])
                t.transpose(PN[1][0:2 * nf, 0:128], BINV[:, 0:2 * nf], IDENT[:])
                tt.inc(t.last)
                assert tt.n == e["tt_btr"]
                t.wait_ge(vv.h, e["vv_btr"])
                for k in range(nf):
                    t.matmul(PB[2 + m % 2][:, k * 128:(k + 1) * 128],
                             BST[0:2 * nf, k * 128:(k + 1) * 128],
                             bvT[0:2 * nf, :], start=True, stop=True)
                    tt.inc(t.last)
                assert tt.n == e["tt_bb"]
                for j in range(16):
                    t.wait_ge(ss.h, e["ss_e"][j])
                    if j == 0 and m >= 1:
                        t.wait_ge(vv.h, sch[m - 1]["vv_Eev"])
                    t.matmul(PB[4][0:32, 0:fw], QTI[:, 32 * j:32 * (j + 1)],
                             erB[j % 2][:, 0:fw], start=(j == 0), stop=(j == 15))
                    tt.inc(t.last)
                    assert tt.n == e["tt_err"][j]
                t.wait_ge(vv.h, e["vv_Eev"])
                if m >= 1:
                    t.wait_ge(vv.h, sch[m - 1]["vv_argmin"])
                for k in range(nf):
                    t.transpose(PB[5][:, 32 * k:32 * (k + 1)],
                                Eev[:, k * 128:(k + 1) * 128], IDENT[0:32, 0:32])
                    tt.inc(t.last)
                assert tt.n == e["tt_et"]
                t.wait_ge(vv.h, e["vv_fd"])
                t.transpose(PN[1][0:2 * nf, 128:256], IBS[:, 0:2 * nf], IDENT[:])
                tt.inc(t.last)
                t.transpose(PN[1][0:2 * nf, 256:384], SCBD[:, 0:2 * nf], IDENT[:])
                tt.inc(t.last)
                assert tt.n == e["tt_ftr"]
                t.wait_ge(vv.h, e["vv_fev"])
                for k in range(nf):
                    t.matmul(PN[0][:, k * 128:(k + 1) * 128],
                             BST[0:2 * nf, k * 128:(k + 1) * 128],
                             ibT[0:2 * nf, :], start=True, stop=True)
                    tt.inc(t.last)
                for k in range(nf):
                    t.matmul(PN[1][:, k * 128:(k + 1) * 128],
                             BST[0:2 * nf, k * 128:(k + 1) * 128],
                             scT[0:2 * nf, :], start=True, stop=True)
                    tt.inc(t.last)
                assert tt.n == e["tt_fb"]
        self.t(teQ)

        def veQ(v):
            v.wait_ge(dsa.h, dsa_c)
            v.tensor_copy(QTI[:], QTIst[:])
            vv.inc(v.last)
            assert vv.n == vv_qti
            for m in range(NCH):
                e = sch[m]
                nf, fw = e["nf"], e["fw"]
                v.wait_ge(tt.h, e["tt_ztr"])
                v.tensor_reduce(
                    AMX[:, 0:2 * nf].rearrange("p (k h) -> p k h", h=2),
                    PB[m % 2][:, 0:nf * 128]
                        .rearrange("p (k h x) -> p k h x", h=2, x=64),
                    AX.X, ALU.max)
                vv.inc(v.last)
                assert vv.n == e["vv_amax"]
                v.wait_ge(vv.h, vv.n)
                v.tensor_scalar(BASE[:, 0:2 * nf], AMX[:, 0:2 * nf],
                                1e-8, 1.0 / 15.0, ALU.max, ALU.mult)
                vv.inc(v.last)
                v.wait_ge(vv.h, vv.n)
                v.reciprocal(BINV[:, 0:2 * nf], BASE[:, 0:2 * nf])
                vv.inc(v.last)
                assert vv.n == e["vv_binv"]
                v.wait_ge(tt.h, e["tt_btr"])
                v.tensor_copy(bvT[0:2 * nf, :], PN[1][0:2 * nf, 0:128])
                vv.inc(v.last)
                assert vv.n == e["vv_btr"]
                v.wait_ge(tt.h, e["tt_bb"])
                v.tensor_tensor(zB[m % 2][:, 0:fw], zbB[m % 2][:, 0:fw],
                                PB[2 + m % 2][:, 0:fw], ALU.mult)
                vv.inc(v.last)
                assert vv.n == e["vv_z"]
                for j in range(16):
                    if j == 0:
                        v.wait_ge(vv.h, e["vv_z"])
                    if j >= 2:
                        v.wait_ge(gg.h, e["gg_r"][j - 2])
                    elif m >= 1:
                        v.wait_ge(gg.h, sch[m - 1]["gg_r"][14 + j])
                    v.tensor_scalar(uB[j % 2][:, 0:fw], zB[m % 2][:, 0:fw],
                                    ISVF[j], None, ALU.mult)
                    vv.inc(v.last)
                    assert vv.n == e["vv_u"][j]
                    v.wait_ge(vv.h, vv.n)
                    v.tensor_scalar(q8B[j % 2][:, 0:fw], uB[j % 2][:, 0:fw],
                                    15.25, None, ALU.min)
                    vv.inc(v.last)
                    assert vv.n == e["vv_q"][j]
                v.wait_ge(tt.h, e["tt_err"][15])
                v.tensor_copy(Eev[:, 0:fw], PB[4][0:32, 0:fw])
                vv.inc(v.last)
                assert vv.n == e["vv_Eev"]
                v.wait_ge(tt.h, e["tt_et"])
                etv = PB[5][:, 0:32 * nf].rearrange("p (k j h) -> p k h j",
                                                    j=16, h=2)
                v.tensor_reduce(
                    MNE[:, 0:2 * nf].rearrange("p (k h) -> p k h", h=2),
                    etv, AX.X, ALU.min)
                vv.inc(v.last)
                v.wait_ge(vv.h, vv.n)
                v.tensor_tensor(
                    maskB[:, 0:32 * nf]
                        .rearrange("p (k j h) -> p k h j", j=16, h=2),
                    etv,
                    MNE[:, 0:2 * nf].rearrange("p (k h o) -> p k h o", h=2, o=1)
                        .broadcast_to([128, nf, 2, 16]),
                    ALU.is_equal)
                vv.inc(v.last)
                v.wait_ge(vv.h, vv.n)
                v.tensor_tensor(mvalB[:, 0:32 * nf], maskB[:, 0:32 * nf],
                                ISVP[:, 0:32 * nf], ALU.mult)
                vv.inc(v.last)
                v.wait_ge(vv.h, vv.n)
                v.tensor_reduce(
                    BISV[:, 0:2 * nf].rearrange("p (k h) -> p k h", h=2),
                    mvalB[:, 0:32 * nf].rearrange("p (k j h) -> p k h j",
                                                  j=16, h=2),
                    AX.X, ALU.max)
                vv.inc(v.last)
                v.wait_ge(vv.h, vv.n)
                v.tensor_tensor(
                    maskB[:, 0:32 * nf]
                        .rearrange("p (k j h) -> p k h j", j=16, h=2),
                    mvalB[:, 0:32 * nf].rearrange("p (k j h) -> p k h j",
                                                  j=16, h=2),
                    BISV[:, 0:2 * nf].rearrange("p (k h o) -> p k h o", h=2, o=1)
                        .broadcast_to([128, nf, 2, 16]),
                    ALU.is_equal)
                vv.inc(v.last)
                v.wait_ge(vv.h, vv.n)
                v.tensor_tensor(mvalB[:, 0:32 * nf], maskB[:, 0:32 * nf],
                                SVP[:, 0:32 * nf], ALU.mult)
                vv.inc(v.last)
                v.wait_ge(vv.h, vv.n)
                v.tensor_reduce(
                    BSV[:, 0:2 * nf].rearrange("p (k h) -> p k h", h=2),
                    mvalB[:, 0:32 * nf].rearrange("p (k j h) -> p k h j",
                                                  j=16, h=2),
                    AX.X, ALU.max)
                vv.inc(v.last)
                assert vv.n == e["vv_argmin"]
                v.wait_ge(vv.h, vv.n)
                v.tensor_tensor(IBS[:, 0:2 * nf], BISV[:, 0:2 * nf],
                                BINV[:, 0:2 * nf], ALU.mult)
                vv.inc(v.last)
                v.tensor_tensor(SCBD[:, 0:2 * nf], BSV[:, 0:2 * nf],
                                BASE[:, 0:2 * nf], ALU.mult)
                vv.inc(v.last)
                assert vv.n == e["vv_fd"]
                v.wait_ge(tt.h, e["tt_ftr"])
                v.tensor_copy(ibT[0:2 * nf, :], PN[1][0:2 * nf, 128:256])
                vv.inc(v.last)
                v.tensor_copy(scT[0:2 * nf, :], PN[1][0:2 * nf, 256:384])
                vv.inc(v.last)
                assert vv.n == e["vv_fev"]
                v.wait_ge(tt.h, e["tt_fb"])
                v.tensor_tensor(zB[m % 2][:, 0:fw], Wb[m % 2][:, 0:fw],
                                PN[0][:, 0:fw], ALU.mult)
                vv.inc(v.last)
                assert vv.n == e["vv_ts"]
                v.wait_ge(vv.h, vv.n)
                v.tensor_scalar(zB[m % 2][:, 0:fw], zB[m % 2][:, 0:fw],
                                -15.25, 15.25, ALU.max, ALU.min)
                vv.inc(v.last)
                v.wait_ge(vv.h, vv.n)
                v.tensor_copy(qsB[:, 0:fw], zB[m % 2][:, 0:fw])
                vv.inc(v.last)
                assert vv.n == e["vv_qs"]
                v.wait_ge(vv.h, vv.n)
                v.tensor_tensor(deqB[:, 0:fw], qsB[:, 0:fw],
                                PN[1][:, 0:fw], ALU.mult)
                vv.inc(v.last)
                assert vv.n == e["vv_deq"]
                v.wait_ge(vv.h, vv.n)
                if m >= 1:
                    v.wait_ge(dso.h, sch[m - 1]["dso"])
                v.tensor_copy(hiB[:, 0:fw], deqB[:, 0:fw])
                vv.inc(v.last)
                assert vv.n == e["vv_hi"]
                v.wait_ge(vv.h, vv.n)
                v.tensor_tensor(lodB[:, 0:fw], deqB[:, 0:fw],
                                hiB[:, 0:fw], ALU.subtract)
                vv.inc(v.last)
                v.wait_ge(vv.h, vv.n)
                v.tensor_copy(loB[:, 0:fw], lodB[:, 0:fw])
                vv.inc(v.last)
                assert vv.n == e["vv_lo"]
        self.v(veQ)

        def gyQ(g):
            for m in range(NCH):
                e = sch[m]
                fw = e["fw"]
                for j in range(16):
                    g.wait_ge(vv.h, e["vv_q"][j])
                    if j >= 2:
                        g.wait_ge(ss.h, e["ss_e"][j - 2])
                    elif m >= 1:
                        g.wait_ge(ss.h, sch[m - 1]["ss_e"][14 + j])
                    g.tensor_tensor(rB[j % 2][:, 0:fw], q8B[j % 2][:, 0:fw],
                                    uB[j % 2][:, 0:fw], ALU.subtract)
                    gg.inc(g.last)
                    assert gg.n == e["gg_r"][j]
        self.g(gyQ)

        def gAG(g):
            g.wait_ge(dso.h, sch[31]["dso"])
            for srcn, dst in (("q_fc_hi", "q_fc_hi_all"),
                              ("q_fc_lo", "q_fc_lo_all")):
                g.collective_compute(
                    "AllGather", ALU.bypass,
                    replica_groups=[[0, 1, 2, 3, 4, 5, 6, 7]],
                    ins=[D[srcn][:]], outs=[D[dst][:]])
                cc.inc(g.last)
            g.wait_ge(dso.h, sch[-1]["dso"])
            for srcn, dst in (("q_pr_hi", "q_pr_hi_all"),
                              ("q_pr_lo", "q_pr_lo_all")):
                g.collective_compute(
                    "AllGather", ALU.bypass,
                    replica_groups=[[0, 1, 2, 3, 4, 5, 6, 7]],
                    ins=[D[srcn][:]], outs=[D[dst][:]])
                cc.inc(g.last)
        self.g(gAG)
        self.cc_qt = cc.n

    def ph_l1_ffn(self):
        D = self.D
        tt, vv, ss, gg, cc, dg = self.tt, self.vv, self.ss, self.gg, self.cc, self.dg
        PB, PN = self.PB, self.PN
        SA, SB_, SBF = self.SA, self.SB_, self.SBF
        R, W = self.R, self.W
        dsw, dsx2, dskv = self.dsw, self.dsx2, self.dskv
        x3_ap = self.x3_ap
        XHI = R[:, 0:8192].bitcast(BF16)
        XLO = R[:, 8192:16384].bitcast(BF16)
        SAB = SA[:].bitcast(BF16)

        vv0 = vv.n

        def veA(v):
            for ch in range(16):
                v.wait_ge(vv.h, vv.n)
                v.tensor_copy(XHI[:, ch * 1024:(ch + 1) * 1024], x3_ap(ch))
                vv.inc(v.last)
                v.wait_ge(vv.h, vv.n)
                v.tensor_tensor(SA[:, 0:1024], x3_ap(ch),
                                XHI[:, ch * 1024:(ch + 1) * 1024], ALU.subtract)
                vv.inc(v.last)
                v.wait_ge(vv.h, vv.n)
                v.tensor_copy(XLO[:, ch * 1024:(ch + 1) * 1024], SA[:, 0:1024])
                vv.inc(v.last)
        self.v(veA)
        vv_xs = vv.n

        base_tt = tt.n
        base_ss = ss.n
        base_vv = vv.n
        base_dg = dg.n
        base_w = [dsw[k].n for k in range(4)]
        sch1 = []
        ttc, ssc, vvc, dgc = tt.n, ss.n, vv.n, dg.n
        grp_free = []
        for ffo in range(64):
            e = {}
            for g in range(4):
                grp_free.append(ttc + (4 * g + 3) * 6 + 4)
                grp_free.append(ttc + (4 * g + 4) * 6)
            ttc += 96
            e["tt"] = ttc
            e["relu"] = []
            e["h2"] = []
            e["hi"] = []
            e["lo"] = []
            e["dg"] = []
            for half in range(2):
                ssc += 1; e["relu"].append(ssc)
                vvc += 1; e["h2"].append(vvc)
                vvc += 1; e["hi"].append(vvc)
                vvc += 2; e["lo"].append(vvc)
                dgc += 32; e["dg"].append(dgc)
            sch1.append(e)

        def syF(y):
            y.wait_ge(cc.h, self.cc_qt)
            y.wait_ge(vv.h, vv_xs)
            i = 0
            for ffo in range(64):
                rk, loc = ffo // 8, (ffo % 8) * 128
                for g in range(4):
                    for srcn in ("q_fc_hi_all", "q_fc_lo_all"):
                        if i >= 4:
                            y.wait_ge(tt.h, grp_free[i - 4])
                        src = D[srcn][rk, g * 512:(g + 1) * 512, loc:loc + 128] \
                            .rearrange("(a p) f -> p a f", a=4)
                        y.dma_start(
                            out=W[i % 4][:].bitcast(BF16)[:, 0:512]
                                .rearrange("p (a f) -> p a f", a=4),
                            in_=src)
                        dsw[i % 4].inc(y.last, 16)
                        i += 1
        self.y(syF)

        def teF(t):
            i = 0
            for ffo in range(64):
                e = sch1[ffo]
                bk0 = (ffo % 2) * 2
                for ci in range(16):
                    if ci % 4 == 0:
                        t.wait_ge(dsw[i % 4].h, base_w[i % 4] + 16 * (i // 4 + 1))
                        whib = W[i % 4][:].bitcast(BF16)
                        i += 1
                        t.wait_ge(dsw[i % 4].h, base_w[i % 4] + 16 * (i // 4 + 1))
                        wlob = W[i % 4][:].bitcast(BF16)
                        i += 1
                    if ffo >= 2 and ci == 0:
                        for half in range(2):
                            t.wait_ge(ss.h, sch1[ffo - 2]["relu"][half])
                    whi = whib[:, (ci % 4) * 128:(ci % 4 + 1) * 128]
                    wlo = wlob[:, (ci % 4) * 128:(ci % 4 + 1) * 128]
                    st = (ci == 0)
                    for half in range(2):
                        t.matmul(PB[bk0 + half][:],
                                 whi, XHI[:, ci * 1024 + half * 512:
                                          ci * 1024 + (half + 1) * 512],
                                 start=st, stop=False)
                        tt.inc(t.last)
                    for half in range(2):
                        t.matmul(PB[bk0 + half][:],
                                 whi, XLO[:, ci * 1024 + half * 512:
                                          ci * 1024 + (half + 1) * 512],
                                 start=False, stop=False)
                        tt.inc(t.last)
                    for half in range(2):
                        t.matmul(PB[bk0 + half][:],
                                 wlo, XHI[:, ci * 1024 + half * 512:
                                          ci * 1024 + (half + 1) * 512],
                                 start=False, stop=(ci == 15))
                        tt.inc(t.last)
                assert tt.n == e["tt"]
        self.t(teF)

        def seF(s):
            for ffo in range(64):
                e = sch1[ffo]
                bk0 = (ffo % 2) * 2
                for half in range(2):
                    s.wait_ge(tt.h, e["tt"] - (1 - half))
                    if ffo >= 1:
                        s.wait_ge(vv.h, sch1[ffo - 1]["h2"][half])
                    s.activation(SA[:, half * 512:(half + 1) * 512],
                                 PB[bk0 + half][:], ACT.Relu)
                    ss.inc(s.last)
                    assert ss.n == e["relu"][half]
        self.s(seF)

        def veF(v):
            for ffo in range(64):
                e = sch1[ffo]
                for half in range(2):
                    v.wait_ge(ss.h, e["relu"][half])
                    v.tensor_tensor(SB_[:, half * 512:(half + 1) * 512],
                                    SA[:, half * 512:(half + 1) * 512],
                                    SA[:, half * 512:(half + 1) * 512], ALU.mult)
                    vv.inc(v.last)
                    assert vv.n == e["h2"][half]
                    v.wait_ge(vv.h, vv.n)
                    if ffo >= 1:
                        v.wait_ge(dg.h, sch1[ffo - 1]["dg"][1])
                    v.tensor_copy(SBF[:, half * 512:(half + 1) * 512],
                                  SB_[:, half * 512:(half + 1) * 512])
                    vv.inc(v.last)
                    assert vv.n == e["hi"][half]
                    v.wait_ge(vv.h, vv.n)
                    v.tensor_tensor(SB_[:, 1024 + half * 512:1024 + (half + 1) * 512],
                                    SB_[:, half * 512:(half + 1) * 512],
                                    SBF[:, half * 512:(half + 1) * 512],
                                    ALU.subtract)
                    vv.inc(v.last)
                    v.wait_ge(vv.h, vv.n)
                    v.tensor_copy(SBF[:, 1024 + half * 512:1024 + (half + 1) * 512],
                                  SB_[:, 1024 + half * 512:1024 + (half + 1) * 512])
                    vv.inc(v.last)
                    assert vv.n == e["lo"][half]
        self.v(veF)

        def gyF(g):
            for ffo in range(64):
                e = sch1[ffo]
                for half in range(2):
                    g.wait_ge(vv.h, e["hi"][half])
                    g.dma_start(out=D["st_kh"][ffo * 128:(ffo + 1) * 128,
                                               half * 512:(half + 1) * 512],
                                in_=SBF[:, half * 512:(half + 1) * 512])
                    dg.inc(g.last, 16)
                    g.wait_ge(vv.h, e["lo"][half])
                    g.dma_start(out=D["st_h2lo"][ffo * 128:(ffo + 1) * 128,
                                                 half * 512:(half + 1) * 512],
                                in_=SBF[:, 1024 + half * 512:1024 + (half + 1) * 512])
                    dg.inc(g.last, 16)
                    assert dg.n == e["dg"][half]
        self.g(gyF)
        dg_h2 = dg.n

        PSL = [PB[0], PB[1], PB[2], PB[3], PB[4], PB[5], PN[0], PN[1]]
        base_w2 = [dsw[k].n for k in range(4)]
        MOVS = [dsx2, dskv, self.dsq, self.dsm]
        base_mv = [[s[0].n, s[1].n] for s in MOVS]
        sch2 = []
        ttc, vvc = tt.n, vv.n
        for cog in range(4):
            e = {}
            ttc += 64 * 24
            e["tt"] = ttc
            e["ev"] = []
            for q in range(4):
                for half in range(2):
                    vvc += 1
                    e["ev"].append(vvc)
            sch2.append(e)
        tt2base = tt.n

        def mmcnt(cog, ffi):
            return tt2base + cog * 64 * 24 + ffi * 24

        def syM(y):
            y.wait_ge(dg.h, dg_h2)
            for cog in range(4):
                for ffi in range(64):
                    gi = cog * 64 + ffi
                    if gi >= 2:
                        y.wait_ge(tt.h, mmcnt((gi - 2) // 64, (gi - 2) % 64) + 24)
                    src_h = D["q_pr_hi_all"][2 * cog:2 * cog + 2,
                                             ffi * 128:(ffi + 1) * 128, :]
                    y.dma_start(
                        out=W[gi % 2][:].bitcast(BF16)[:, 0:512]
                            .rearrange("p (a f) -> p a f", a=2),
                        in_=src_h.rearrange("a p f -> p a f"))
                    dsw[gi % 2].inc(y.last, 16)
                    src_l = D["q_pr_lo_all"][2 * cog:2 * cog + 2,
                                             ffi * 128:(ffi + 1) * 128, :]
                    y.dma_start(
                        out=W[2 + gi % 2][:].bitcast(BF16)[:, 0:512]
                            .rearrange("p (a f) -> p a f", a=2),
                        in_=src_l.rearrange("a p f -> p a f"))
                    dsw[2 + gi % 2].inc(y.last, 16)
                    for half in range(2):
                        y.dma_start(
                            out=SAB[:, half * 1024 + (gi % 2) * 512:
                                    half * 1024 + (gi % 2) * 512 + 512],
                            in_=D["st_kh"][ffi * 128:(ffi + 1) * 128,
                                           half * 512:(half + 1) * 512])
                        MOVS[half][gi % 2].inc(y.last, 16)
                        y.dma_start(
                            out=SAB[:, (2 + half) * 1024 + (gi % 2) * 512:
                                    (2 + half) * 1024 + (gi % 2) * 512 + 512],
                            in_=D["st_h2lo"][ffi * 128:(ffi + 1) * 128,
                                             half * 512:(half + 1) * 512])
                        MOVS[2 + half][gi % 2].inc(y.last, 16)
        self.y(syM)

        def teM(t):
            for cog in range(4):
                e = sch2[cog]
                for ffi in range(64):
                    gi = cog * 64 + ffi
                    t.wait_ge(dsw[gi % 2].h, base_w2[gi % 2] + 16 * (gi // 2 + 1))
                    t.wait_ge(dsw[2 + gi % 2].h, base_w2[2 + gi % 2] + 16 * (gi // 2 + 1))
                    for stn in range(4):
                        t.wait_ge(MOVS[stn][gi % 2].h,
                                  base_mv[stn][gi % 2] + 16 * (gi // 2 + 1))
                    if ffi == 0 and cog >= 1:
                        t.wait_ge(vv.h, sch2[cog - 1]["ev"][-1])
                    whib = W[gi % 2][:].bitcast(BF16)
                    wlob = W[2 + gi % 2][:].bitcast(BF16)
                    for q in range(4):
                        whi = whib[:, q * 128:(q + 1) * 128]
                        wlo = wlob[:, q * 128:(q + 1) * 128]
                        for half in range(2):
                            mhi = SAB[:, half * 1024 + (gi % 2) * 512:
                                      half * 1024 + (gi % 2) * 512 + 512]
                            mlo = SAB[:, (2 + half) * 1024 + (gi % 2) * 512:
                                      (2 + half) * 1024 + (gi % 2) * 512 + 512]
                            out = PSL[q * 2 + half][:]
                            t.matmul(out, whi, mhi, start=(ffi == 0), stop=False)
                            tt.inc(t.last)
                            t.matmul(out, whi, mlo, start=False, stop=False)
                            tt.inc(t.last)
                            t.matmul(out, wlo, mhi, start=False, stop=(ffi == 63))
                            tt.inc(t.last)
                assert tt.n == e["tt"]
        self.t(teM)

        def veM(v):
            for cog in range(4):
                e = sch2[cog]
                v.wait_ge(tt.h, e["tt"])
                k = 0
                for q in range(4):
                    co = cog * 4 + q
                    for half in range(2):
                        v.tensor_tensor(
                            x3_ap(co)[:, half * 512:(half + 1) * 512],
                            x3_ap(co)[:, half * 512:(half + 1) * 512],
                            PSL[q * 2 + half][:], ALU.add)
                        vv.inc(v.last)
                        assert vv.n == e["ev"][k]
                        k += 1
        self.v(veM)

        self._layernorm(x3_ap)

    def ph_out(self):
        D = self.D
        ds = self.ds
        src_ap = self.x1_ap

        def sy(y):
            for s_ in (self.tt, self.vv, self.ss, self.gg):
                y.wait_ge(s_.h, s_.n)
            for ch in range(16):
                y.dma_start(out=D["out"][ch * 128:(ch + 1) * 128, :],
                            in_=src_ap(ch))
                ds.inc(y.last, 16)
            y.wait_ge(ds.h, ds.n)
        self.y(sy)


def stage_inputs(ins):
    x = np.ascontiguousarray(ins["L0_attn_hidden_states"])
    w1 = np.ascontiguousarray(ins["L0_ffn_w1"])
    w2 = np.ascontiguousarray(ins["L0_ffn_w2"])
    wfcT = np.ascontiguousarray(ins["L1_ffn_w_fc"].T)
    wprT = np.ascontiguousarray(ins["L1_ffn_w_proj"].T)
    convw_full = np.ascontiguousarray(ins["L1_attn_conv_w"][:, 0, :])
    base = {
        "Wq": np.ascontiguousarray(ins["L0_attn_Wq"]),
        "Wk": np.ascontiguousarray(ins["L0_attn_Wk"]),
        "Wv": np.ascontiguousarray(ins["L0_attn_Wv"]),
        "Wg": np.ascontiguousarray(ins["L0_attn_Wg"]),
        "Wc": np.ascontiguousarray(ins["L0_attn_Wc"]),
        "bc": np.ascontiguousarray(ins["L0_attn_bc"]).reshape(C, 1),
        "w1": w1, "w2": w2,
    }
    tril = np.tril(np.ones((128, 128), np.float32))
    diagm = np.ascontiguousarray(tril.T)
    bccols = np.zeros((64, 2), np.float32)
    bccols[0:16, 0] = 1.0
    bccols[32:48, 1] = 1.0
    qt_ind = np.zeros((128, 512), np.float32)
    for j in range(16):
        qt_ind[0:64, 32 * j + 2 * j] = 1.0
        qt_ind[64:128, 32 * j + 2 * j + 1] = 1.0
    qt_bst = np.zeros((8, 512), np.float32)
    for k in range(4):
        qt_bst[2 * k, 128 * k:128 * k + 64] = 1.0
        qt_bst[2 * k + 1, 128 * k + 64:128 * (k + 1)] = 1.0
    qt_svp = np.zeros((128, 128), np.float32)
    qt_isvp = np.zeros((128, 128), np.float32)
    for k in range(4):
        for j in range(16):
            for hb in range(2):
                qt_svp[:, 32 * k + 2 * j + hb] = SVF[j]
                qt_isvp[:, 32 * k + 2 * j + hb] = ISVF[j]
    base.update(qt_ind=qt_ind, qt_bst=qt_bst, qt_svp=qt_svp, qt_isvp=qt_isvp)
    inproj_full = np.ascontiguousarray(ins["L1_attn_in_proj_w"])
    maps = []
    for core in range(NCORES):
        b, h = core // 2, core % 2
        ch0 = h * CHH
        ot = own_tiles(h)
        xb = x[b].T
        m = dict(base)
        m["xT"] = np.ascontiguousarray(xb)
        m["xT_own"] = np.ascontiguousarray(
            np.concatenate([xb[:, g * 128:(g + 1) * 128] for g in ot], axis=1))
        if h == 0:
            cm = np.concatenate([diagm, np.zeros((128, 128), np.float32)], 1)
        else:
            cm = np.concatenate([np.ones((128, 128), np.float32), diagm], 1)
        m["cmask"] = np.ascontiguousarray(cm)
        m["hsel"] = np.ascontiguousarray(
            np.tile(np.array([[1.0 - h, float(h)]], np.float32), (8, 1)))
        hp = np.zeros((8, 1024), np.float32)
        for hh_ in range(8):
            hp[hh_, hh_ * 128:(hh_ + 1) * 128] = 1.0
        m["hpick"] = hp
        m["w1_sh"] = np.ascontiguousarray(w1[core * (FF // NCORES):(core + 1) * (FF // NCORES), :])
        m["w2_sh"] = np.ascontiguousarray(w2[core * (C // NCORES):(core + 1) * (C // NCORES), :])
        m["wfcT_sh"] = np.ascontiguousarray(wfcT[:, core * (FF // NCORES):(core + 1) * (FF // NCORES)])
        m["wprT_sh"] = np.ascontiguousarray(wprT[:, core * (C // NCORES):(core + 1) * (C // NCORES)])
        m["inproj_sl"] = np.ascontiguousarray(np.concatenate(
            [inproj_full[:, ch0:ch0 + CHH],
             inproj_full[:, INTER + ch0:INTER + ch0 + CHH]], axis=1))
        m["bccols"] = bccols
        m["convw"] = np.ascontiguousarray(convw_full[ch0:ch0 + CHH, :])
        m["convb"] = np.ascontiguousarray(ins["L1_attn_conv_b"][ch0:ch0 + CHH]).reshape(CHH, 1)
        m["xproj"] = np.ascontiguousarray(ins["L1_attn_x_proj_w"][ch0:ch0 + CHH, :])
        m["dtw"] = np.ascontiguousarray(ins["L1_attn_dt_proj_w"][:, ch0:ch0 + CHH])
        m["dtb"] = np.ascontiguousarray(ins["L1_attn_dt_proj_b"][ch0:ch0 + CHH]).reshape(CHH, 1)
        m["Alog"] = np.ascontiguousarray(ins["L1_attn_A_log"][ch0:ch0 + CHH, :])
        m["Dpar"] = np.ascontiguousarray(ins["L1_attn_D_param"][ch0:ch0 + CHH]).reshape(CHH, 1)
        m["outproj"] = np.ascontiguousarray(ins["L1_attn_out_proj_w"][ch0:ch0 + CHH, :])
        maps.append(m)
    return maps


def assemble_output(results):
    out = np.empty((B, T, C), dtype=np.float32)
    for core in range(NCORES):
        b, h = core // 2, core % 2
        o = results[core]["out"]
        for i, g in enumerate(own_tiles(h)):
            out[b, g * 128:(g + 1) * 128, :] = o[:, i * 128:(i + 1) * 128].T
    return out


def kernel(**inputs):
    ins = {k: np.asarray(v) for k, v in inputs.items()}
    maps = stage_inputs(ins)
    nc, prog = build_program()
    res = run_bass_kernel_spmd(nc, maps, core_ids=list(range(NCORES)))
    return assemble_output(res.results)


if __name__ == "__main__":
    import tempfile
    from concourse.bass_utils import compile_bass_kernel
    nc, prog = build_program()
    print("built, instructions:", nc.n_instructions())
    with tempfile.TemporaryDirectory() as td:
        compile_bass_kernel(nc, td)
    print("COMPILE OK")
